# version 02
# baseline (speedup 1.0000x reference)
import numpy as np
import ml_dtypes
import concourse.bass as bass
import concourse.mybir as mybir
from concourse.bass_utils import run_bass_kernel_spmd
from concourse.alu_op_type import AluOpType as OP
from contextlib import ExitStack, contextmanager

F32 = mybir.dt.float32
BF16 = mybir.dt.bfloat16
AF = mybir.ActivationFunctionType
AX = mybir.AxisListType.X
BF16NP = ml_dtypes.bfloat16

T = 4096
D = 1024
TT = 512
NT128 = T // 128

ENGINES = ("sync", "tensor", "scalar", "vector", "gpsimd")


class Sem:
    def __init__(self, h):
        self.h = h
        self.val = 0

    def inc(self, n=1):
        self.val += n
        return self.val


class Build:

    def __init__(self, name="mk"):
        self.nc = bass.Bass("TRN2", target_bir_lowering=False)
        self.ctx = ExitStack()
        self.dram = {}
        self.input_names = []
        self.output_names = []
        self._block = None
        self._sems = []
        self._swsems = []
        self._sem_i = 0
        self._swsem_i = 0
        self.bar = None
        self.stage_n = 0
        self.name = name

    def din(self, name, shape, dt):
        if name in self.dram:
            return self.dram[name]
        ap = self.nc.dram_tensor(name, list(shape), dt, kind="ExternalInput").ap()
        self.dram[name] = ap
        self.input_names.append(name)
        return ap

    def dout(self, name, shape, dt):
        ap = self.nc.dram_tensor(name, list(shape), dt, kind="ExternalOutput").ap()
        self.dram[name] = ap
        self.output_names.append(name)
        return ap

    def dtmp(self, name, shape, dt):
        if name in self.dram:
            return self.dram[name]
        ap = self.nc.dram_tensor(name, list(shape), dt, kind="Internal").ap()
        self.dram[name] = ap
        return ap

    def open(self, consts=()):
        for i in range(40):
            self._sems.append(Sem(self.ctx.enter_context(self.nc.semaphore(f"sp{i}"))))
        for i in range(24):
            self._swsems.append(Sem(self.ctx.enter_context(self.nc.semaphore(f"sw{i}"))))
        self.bar = Sem(self.ctx.enter_context(self.nc.semaphore("bar")))
        self._block = self.ctx.enter_context(self.nc.Block())
        cvals = [0.0, 1.0, 1e-5] + [v for v in consts if v not in (0.0, 1.0, 1e-5)]
        self._csb = self.ctx.enter_context(
            self.nc.sbuf_tensor("constf", [128, len(cvals)], F32))
        with self.stage("consts"):
            def _m(v):
                for j, val in enumerate(cvals):
                    v.memset(self._csb[:, j:j + 1], float(val))
            self.sec("vector", _m)
        for j, val in enumerate(cvals):
            self.nc.const_aps.register(F32, float(val), self._csb[:, j:j + 1])

    def close(self):
        self.ctx.close()
        return self.nc

    def sem(self):
        s = self._sems[self._sem_i % len(self._sems)]
        self._sem_i += 1
        return s

    def sem2(self):
        return [self.sem(), self.sem()]

    def sem_sw(self):
        s = self._swsems[self._swsem_i % len(self._swsems)]
        self._swsem_i += 1
        return s

    def sem2_sw(self):
        return [self.sem_sw(), self.sem_sw()]

    def sec(self, eng, fn):
        getattr(self._block, eng)(fn)

    @contextmanager
    def stage(self, name=""):
        k = self.stage_n
        base = 6 * k

        def _wsync(q):
            q.wait_ge(self.bar.h, base)
            for s in self._sems:
                q.sem_clear(s.h)
            q.sem_inc(self.bar.h, 1)
        self.sec("sync", _wsync)
        for s in self._sems:
            s.val = 0
        for e in ENGINES:
            if e == "sync":
                continue

            def _w(q, base=base):
                q.wait_ge(self.bar.h, base + 1)
            self.sec(e, _w)
        sctx = ExitStack()
        try:
            yield sctx
        finally:
            sctx.close()
            self.stage_n += 1
            used = [(s.h, s.val) for s in self._sems if s.val > 0]
            used += [(s.h, s.val) for s in self._swsems if s.val > 0]
            for e in ENGINES:
                def _i(q, e=e):
                    for (h, v) in used:
                        q.wait_ge(h, v)
                    if e in ("vector", "scalar", "gpsimd"):
                        q.drain()
                    q.sem_inc(self.bar.h, 1)
                self.sec(e, _i)

    def sb(self, sctx, name, shape, dt):
        return sctx.enter_context(
            self.nc.sbuf_tensor(f"{name}_{self.stage_n}", list(shape), dt))

    def ps(self, sctx, name, shape, dt=F32):
        return sctx.enter_context(
            self.nc.psum_tensor(f"{name}_{self.stage_n}", list(shape), dt))


def cdiv(a, b):
    return (a + b - 1) // b


def chainer(q, s):
    def step(ins):
        tgt = s.inc(1)
        ins.then_inc(s.h, 1)
        q.wait_ge(s.h, tgt)
    return step


def st_ln(b, xr, plus=(), write_x=None, z=None, zT=None, zT32=None,
          gamma=None, beta=None, variant="ln", ident=None, ident32=None):
    with b.stage("ln") as sctx:
        xa = [b.sb(sctx, f"ln_xa{i}", [128, D], F32) for i in range(2)]
        zc = [b.sb(sctx, f"ln_zc{i}", [128, D], F32) for i in range(2)]
        sq = [b.sb(sctx, f"ln_sq{i}", [128, D], F32) for i in range(2)]
        dl = [[b.sb(sctx, f"ln_d{i}_{j}", [128, D], F32) for j in range(len(plus))]
              for i in range(2)]
        co = [b.sb(sctx, f"ln_co{i}", [128, 4], F32) for i in range(2)]
        zbf = [b.sb(sctx, f"ln_zbf{i}", [128, D], BF16) for i in range(2)] if zT is not None else None
        s_pro = b.sem()
        s_ld2 = b.sem2()
        s_st2 = b.sem2()
        s_zt2 = b.sem2()
        s_v, s_s, s_t = b.sem(), b.sem(), b.sem()

        use_gb = gamma is not None
        if use_gb:
            gb = b.sb(sctx, "ln_gb", [128, 2 * D], F32)
            ones1 = b.sb(sctx, "ln_ones", [1, 128], F32)
            grow = b.sb(sctx, "ln_grow", [1, 2 * D], F32)
            p_gb = b.ps(sctx, "ln_pgb", [128, 2048], F32)
        if zT is not None:
            id_sb = b.sb(sctx, "ln_id", [128, 128], BF16)
            p_tr = [b.ps(sctx, f"ln_ptr{i}", [128, 128], BF16) for i in range(2)]
            ztr = [b.sb(sctx, f"ln_ztr{i}", [128, 128], BF16) for i in range(2)]
        if zT32 is not None:
            id32_sb = b.sb(sctx, "ln_id32", [128, 128], F32)
            p_tr32 = [b.ps(sctx, f"ln_p32{i}", [128, 128], F32) for i in range(2)]
            ztr32 = [b.sb(sctx, f"ln_z32{i}", [128, 128], F32) for i in range(2)]

        def _ld0(sync):
            if use_gb:
                sync.dma_start(out=grow[0:1, 0:D], in_=gamma).then_inc(s_pro.h, 16)
                sync.dma_start(out=grow[0:1, D:2 * D], in_=beta).then_inc(s_pro.h, 16)
            if zT is not None:
                sync.dma_start(out=id_sb[:], in_=ident).then_inc(s_pro.h, 16)
            if zT32 is not None:
                sync.dma_start(out=id32_sb[:], in_=ident32).then_inc(s_pro.h, 16)
        b.sec("sync", _ld0)
        s_pro.inc(16 * ((2 if use_gb else 0) + (1 if zT is not None else 0) +
                        (1 if zT32 is not None else 0)))
        ld_pro = s_pro.val

        if use_gb:
            def _v0(v):
                v.memset(ones1[:], 1.0).then_inc(s_v.h, 1)
            b.sec("vector", _v0)
            s_v.inc(1)
            v_ones = s_v.val

            def _t0(t):
                t.wait_ge(s_pro.h, ld_pro)
                t.wait_ge(s_v.h, v_ones)
                for j in range(4):
                    t.matmul(p_gb[:, j * 512:(j + 1) * 512], ones1[:],
                             grow[0:1, j * 512:(j + 1) * 512], start=True,
                             stop=True).then_inc(s_t.h, 1)
            b.sec("tensor", _t0)
            s_t.inc(4)
            t_gb = s_t.val

            def _v1(v):
                v.wait_ge(s_t.h, t_gb)
                v.tensor_copy(out=gb[:], in_=p_gb[:]).then_inc(s_v.h, 1)
            b.sec("vector", _v1)
            s_v.inc(1)
        gb_ready = s_v.val

        v_done = []
        st_done = []
        zt_ev = []
        zt_st = []

        n_z = D // 128
        for i in range(NT128):
            bi = i % 2
            r0 = i * 128

            def _ld(sync, i=i, bi=bi, r0=r0):
                if i >= 2:
                    sync.wait_ge(s_v.h, v_done[i - 2])
                    sync.wait_ge(s_st2[bi].h, st_done[i - 2])
                sync.dma_start(out=xa[bi][:], in_=xr[r0:r0 + 128, :]).then_inc(s_ld2[bi].h, 16)
                for j in range(len(plus)):
                    sync.dma_start(out=dl[bi][j][:],
                                   in_=plus[j][0][r0:r0 + 128, :]).then_inc(s_ld2[bi].h, 16)
            b.sec("sync", _ld)
            s_ld2[bi].inc(16 * (1 + len(plus)))
            ld_i = s_ld2[bi].val

            def _va(v, bi=bi, ld_i=ld_i):
                st = chainer(v, s_v)
                v.wait_ge(s_ld2[bi].h, ld_i)
                if use_gb:
                    v.wait_ge(s_v.h, gb_ready)
                for j in range(len(plus)):
                    st(v.scalar_tensor_tensor(out=xa[bi][:], in0=dl[bi][j][:],
                                              scalar=float(plus[j][1]), in1=xa[bi][:],
                                              op0=OP.mult, op1=OP.add))
                st(v.reduce_sum(out=co[bi][:, 0:1], in_=xa[bi][:], axis=AX))
                st(v.tensor_scalar(out=co[bi][:, 0:1], in0=co[bi][:, 0:1],
                                   scalar1=1.0 / D, scalar2=None, op0=OP.mult))
                v.tensor_scalar(out=zc[bi][:], in0=xa[bi][:], scalar1=co[bi][:, 0:1],
                                scalar2=None, op0=OP.subtract).then_inc(s_v.h, 1)
            b.sec("vector", _va)
            s_v.inc(1)
            centered = s_v.val

            def _sa(sc, bi=bi, centered=centered):
                sc.wait_ge(s_v.h, centered)
                sc.activation(out=sq[bi][:], in_=zc[bi][:], func=AF.Square).then_inc(s_s.h, 1)
            b.sec("scalar", _sa)
            s_s.inc(1)
            squared = s_s.val

            def _vb(v, bi=bi, squared=squared):
                v.wait_ge(s_s.h, squared)
                v.reduce_sum(out=co[bi][:, 1:2], in_=sq[bi][:], axis=AX).then_inc(s_v.h, 1)
            b.sec("vector", _vb)
            s_v.inc(1)
            ssq = s_v.val

            def _sb_(sc, bi=bi, ssq=ssq):
                sc.wait_ge(s_v.h, ssq)
                if variant == "ln":
                    sc.activation(out=co[bi][:, 2:3], in_=co[bi][:, 1:2],
                                  func=AF.Sqrt, scale=1.0 / D, bias=1e-5).then_inc(s_s.h, 1)
                else:
                    st = chainer(sc, s_s)
                    st(sc.activation(out=co[bi][:, 3:4], in_=co[bi][:, 1:2],
                                     func=AF.Sqrt, scale=1.0 / (D - 1)))
                    sc.activation(out=co[bi][:, 2:3], in_=co[bi][:, 3:4],
                                  func=AF.Sqrt, bias=1e-5).then_inc(s_s.h, 1)
            b.sec("scalar", _sb_)
            s_s.inc(1)
            denom = s_s.val

            def _vc(v, bi=bi, denom=denom):
                st = chainer(v, s_v)
                v.wait_ge(s_s.h, denom)
                st(v.reciprocal(out=co[bi][:, 3:4], in_=co[bi][:, 2:3]))
                st(v.tensor_scalar(out=zc[bi][:], in0=zc[bi][:], scalar1=co[bi][:, 3:4],
                                   scalar2=None, op0=OP.mult))
                if use_gb:
                    st(v.tensor_tensor(out=zc[bi][:], in0=zc[bi][:], in1=gb[:, 0:D], op=OP.mult))
                    st(v.tensor_tensor(out=zc[bi][:], in0=zc[bi][:], in1=gb[:, D:2 * D], op=OP.add))
                if zbf is not None:
                    st(v.tensor_copy(out=zbf[bi][:], in_=zc[bi][:]))
                v.sem_inc(s_v.h, 1)
            b.sec("vector", _vc)
            s_v.inc(1)
            z_ready = s_v.val
            v_done.append(z_ready)

            n_st = (1 if z is not None else 0) + (1 if (write_x is not None and plus) else 0)

            def _st(sc, i=i, bi=bi, r0=r0, z_ready=z_ready):
                sc.wait_ge(s_v.h, z_ready)
                if z is not None:
                    sc.dma_start(out=z[r0:r0 + 128, :], in_=zc[bi][:]).then_inc(s_st2[bi].h, 16)
                if write_x is not None and plus:
                    sc.dma_start(out=write_x[r0:r0 + 128, :], in_=xa[bi][:]).then_inc(s_st2[bi].h, 16)
            b.sec("scalar", _st)
            s_st2[bi].inc(16 * n_st)
            st_done.append(s_st2[bi].val)

            for (on, dst, ptr_, ztr_, zsrc) in (
                    [("bf", zT, p_tr, ztr, None)] if zT is not None else []) + (
                    [("f32", zT32, p_tr32, ztr32, None)] if zT32 is not None else []):
                for c in range(n_z):
                    k = len(zt_ev)
                    pk = k % 2

                    def _tr(t, bi=bi, c=c, k=k, pk=pk, z_ready=z_ready, on=on,
                            ptr_=ptr_):
                        t.wait_ge(s_v.h, z_ready)
                        if k <= 1:
                            t.wait_ge(s_pro.h, ld_pro)
                        if k >= 2:
                            t.wait_ge(s_v.h, zt_ev[k - 2])
                        srct = zbf[bi] if on == "bf" else zc[bi]
                        idt = id_sb if on == "bf" else id32_sb
                        t.transpose(ptr_[pk][:], srct[:, c * 128:(c + 1) * 128],
                                    idt[:]).then_inc(s_t.h, 1)
                    b.sec("tensor", _tr)
                    s_t.inc(1)
                    trk = s_t.val

                    def _ev(v, pk=pk, k=k, trk=trk, ptr_=ptr_, ztr_=ztr_):
                        v.wait_ge(s_t.h, trk)
                        if k >= 2:
                            v.wait_ge(s_zt2[pk].h, zt_st[k - 2])
                        v.tensor_copy(out=ztr_[pk][:], in_=ptr_[pk][:]).then_inc(s_v.h, 1)
                    b.sec("vector", _ev)
                    s_v.inc(1)
                    zt_ev.append(s_v.val)
                    evk = s_v.val

                    def _st2(sc, c=c, i=i, pk=pk, evk=evk, dst=dst, ztr_=ztr_):
                        sc.wait_ge(s_v.h, evk)
                        sc.dma_start(out=dst[c * 128:(c + 1) * 128, i * 128:(i + 1) * 128],
                                     in_=ztr_[pk][:]).then_inc(s_zt2[pk].h, 16)
                    b.sec("scalar", _st2)
                    s_zt2[pk].inc(16)
                    zt_st.append(s_zt2[pk].val)

        def _fin(sc):
            sc.wait_ge(s_st2[0].h, s_st2[0].val)
            sc.wait_ge(s_st2[1].h, s_st2[1].val)
            sc.wait_ge(s_zt2[0].h, s_zt2[0].val)
            sc.wait_ge(s_zt2[1].h, s_zt2[1].val)
        b.sec("scalar", _fin)


class _EC:
    pass


def _copy_evac(E):
    return E.q.tensor_copy(out=E.out, in_=E.psum)


def st_mm_fm(b, name, segs, N, outs, evac_s=None, evac_v=_copy_evac, cols=(),
             aux=(), Tn=T, out_dt=BF16, group_bytes=6 * 1024 * 1024):
    Nc = cdiv(N, 128)
    dual = any("w2" in s for s in segs)
    with b.stage(name) as sctx:
        s_pro, s_t, s_s, s_v = (b.sem() for _ in range(4))
        s_wl = b.sem_sw()
        s_ld2, s_la2 = b.sem2(), b.sem2()
        s_ldc2, s_st2 = b.sem2_sw(), b.sem2_sw()

        use_panels = any(s.get("wlayout") == "panels" for s in segs)
        if use_panels:
            pb = sum(s["K"] * 256 * (2 if "w2" in s else 1) for s in segs)
            G = max(1, min(Nc, group_bytes // pb))
        else:
            G = Nc
        groups = [list(range(g, min(g + G, Nc))) for g in range(0, Nc, G)]

        for si, s in enumerate(segs):
            Kc = s["K"] // 128
            sdt = s.get("dt", BF16)
            if s.get("wlayout", "rows") == "rows":
                s["_w"] = [b.sb(sctx, f"{name}_w{si}k{kc}", [128, N], sdt) for kc in range(Kc)]
                if "w2" in s:
                    s["_w2"] = [b.sb(sctx, f"{name}_v{si}k{kc}", [128, N], sdt) for kc in range(Kc)]
            else:
                s["_w"] = [b.sb(sctx, f"{name}_wp{si}j{j}", [128, s["K"]], sdt) for j in range(G)]
                if "w2" in s:
                    s["_w2"] = [b.sb(sctx, f"{name}_vp{si}j{j}", [128, s["K"]], sdt) for j in range(G)]
            s["_x"] = [[b.sb(sctx, f"{name}_x{si}b{t}k{kc}", [128, TT], sdt)
                        for kc in range(Kc)] for t in range(2)]

        p = [b.ps(sctx, f"{name}_p{i}", [128, TT], F32) for i in range(2)]
        p2 = [b.ps(sctx, f"{name}_q{i}", [128, TT], F32) for i in range(2)] if dual else None
        ob = [b.sb(sctx, f"{name}_o{i}", [128, TT], out_dt) for i in range(2)]
        tmp = [b.sb(sctx, f"{name}_t{i}", [128, TT], F32) for i in range(2)]
        axb = [[b.sb(sctx, f"{name}_a{i}j{j}", [128, TT], aux[j][1]) for j in range(len(aux))]
               for i in range(2)]
        colsb = b.sb(sctx, f"{name}_c", [128, max(1, Nc * len(cols))], F32) if cols else None

        def _pro(sync):
            for j, c in enumerate(cols):
                sync.dma_start(out=colsb[:, j * Nc:(j + 1) * Nc],
                               in_=c).then_inc(s_pro.h, 16)
        b.sec("sync", _pro)
        s_pro.inc(16 * len(cols))
        cols_ld = s_pro.val

        def _prow(g):
            for si, s in enumerate(segs):
                if s.get("wlayout", "rows") == "rows":
                    for kc in range(s["K"] // 128):
                        g.dma_start(out=s["_w"][kc][:],
                                    in_=s["w"][kc * 128:(kc + 1) * 128, :]).then_inc(s_wl.h, 16)
                        if "w2" in s:
                            g.dma_start(out=s["_w2"][kc][:],
                                        in_=s["w2"][kc * 128:(kc + 1) * 128, :]).then_inc(s_wl.h, 16)
        b.sec("gpsimd", _prow)
        for s in segs:
            if s.get("wlayout", "rows") == "rows":
                s_wl.inc(16 * (s["K"] // 128) * (2 if "w2" in s else 1))

        def outmap(r):
            for (dap, g0, g1) in outs:
                if g0 <= r < g1:
                    return dap, r - g0
            raise AssertionError(f"row {r} unmapped")

        mm_at = {}
        ev_v_at = {}
        psum_free = {}
        st_at = {}
        aux_free = {}
        xfree = [0, 0]
        idx = 0
        for gi, group in enumerate(groups):
            if use_panels:
                wait_mm = mm_at[idx - 1] if idx > 0 else 0

                def _pg(g, group=tuple(group), wait_mm=wait_mm):
                    g.wait_ge(s_t.h, wait_mm)
                    for si, s in enumerate(segs):
                        if s.get("wlayout") == "panels":
                            for jj, ncx in enumerate(group):
                                g.dma_start(
                                    out=s["_w"][jj][:],
                                    in_=s["w"][ncx].rearrange("(kc kp) m -> kp kc m", kp=128),
                                ).then_inc(s_wl.h, 16)
                                if "w2" in s:
                                    g.dma_start(
                                        out=s["_w2"][jj][:],
                                        in_=s["w2"][ncx].rearrange("(kc kp) m -> kp kc m", kp=128),
                                    ).then_inc(s_wl.h, 16)
                b.sec("gpsimd", _pg)
                for s in segs:
                    if s.get("wlayout") == "panels":
                        s_wl.inc(16 * len(group) * (2 if "w2" in s else 1))
            wl_ready = s_wl.val

            for t in range(Tn // TT):
                slot = t % 2
                plain = [s for s in segs if not s.get("cast")]
                casts = [s for s in segs if s.get("cast")]

                def _lx(sync, slot=slot, t=t, w=xfree[slot]):
                    sync.wait_ge(s_t.h, w)
                    for s in plain:
                        for kc in range(s["K"] // 128):
                            sync.dma_start(out=s["_x"][slot][kc][:],
                                           in_=s["x"][kc * 128:(kc + 1) * 128,
                                                      t * TT:(t + 1) * TT]).then_inc(s_ld2[slot].h, 16)
                b.sec("sync", _lx)
                if casts:
                    def _lxc(g, slot=slot, t=t, w=xfree[slot]):
                        g.wait_ge(s_t.h, w)
                        for s in casts:
                            for kc in range(s["K"] // 128):
                                g.dma_start(out=s["_x"][slot][kc][:],
                                            in_=s["x"][kc * 128:(kc + 1) * 128,
                                                       t * TT:(t + 1) * TT]).then_inc(s_ldc2[slot].h, 16)
                    b.sec("gpsimd", _lxc)
                s_ld2[slot].inc(16 * sum(s["K"] // 128 for s in plain))
                s_ldc2[slot].inc(16 * sum(s["K"] // 128 for s in casts))
                x_ld = s_ld2[slot].val
                xc_ld = s_ldc2[slot].val

                for jj, ncx in enumerate(group):
                    pp = idx % 2
                    cw = min(128, N - ncx * 128)
                    odap, olo = outmap(ncx * 128)

                    if aux:
                        def _la(sync, idx=idx, pp=pp, ncx=ncx, cw=cw, t=t):
                            if idx >= 2:
                                sync.wait_ge(s_v.h, aux_free[idx - 2])
                            for j, (ad, adt) in enumerate(aux):
                                sync.dma_start(
                                    out=axb[pp][j][0:cw, :],
                                    in_=ad[ncx * 128:ncx * 128 + cw,
                                           t * TT:(t + 1) * TT]).then_inc(s_la2[pp].h, 16)
                        b.sec("sync", _la)
                        s_la2[pp].inc(16 * len(aux))
                    la_tile = s_la2[pp].val

                    def _mm(tq, idx=idx, pp=pp, jj=jj, ncx=ncx, cw=cw, x_ld=x_ld,
                            wl_ready=wl_ready, slot=slot, xc_ld=xc_ld):
                        tq.wait_ge(s_ld2[slot].h, x_ld)
                        tq.wait_ge(s_ldc2[slot].h, xc_ld)
                        tq.wait_ge(s_wl.h, wl_ready)
                        if idx >= 2:
                            tq.wait_ge(*psum_free[idx - 2])
                            if dual:
                                tq.wait_ge(s_v.h, ev_v_at[idx - 2])
                        mms = []
                        for si, s in enumerate(segs):
                            for kc in range(s["K"] // 128):
                                if s.get("wlayout", "rows") == "rows":
                                    lhs = s["_w"][kc][:, ncx * 128:ncx * 128 + cw]
                                else:
                                    lhs = s["_w"][jj][:, kc * 128:kc * 128 + cw]
                                mms.append((lhs, s["_x"][slot][kc][:]))
                        for mi, (lap, rap) in enumerate(mms):
                            ins = tq.matmul(p[pp][0:cw, :], lap, rap,
                                            start=(mi == 0), stop=(mi == len(mms) - 1))
                            if mi == len(mms) - 1:
                                ins.then_inc(s_t.h, 1)
                        if dual:
                            m2 = []
                            for si, s in enumerate(segs):
                                if "w2" not in s:
                                    continue
                                for kc in range(s["K"] // 128):
                                    if s.get("wlayout", "rows") == "rows":
                                        lhs = s["_w2"][kc][:, ncx * 128:ncx * 128 + cw]
                                    else:
                                        lhs = s["_w2"][jj][:, kc * 128:kc * 128 + cw]
                                    m2.append((lhs, s["_x"][slot][kc][:]))
                            for mi, (lap, rap) in enumerate(m2):
                                ins = tq.matmul(p2[pp][0:cw, :], lap, rap,
                                                start=(mi == 0), stop=(mi == len(m2) - 1))
                                if mi == len(m2) - 1:
                                    ins.then_inc(s_t.h, 1)
                    b.sec("tensor", _mm)
                    s_t.inc(2 if dual else 1)
                    mm_at[idx] = s_t.val

                    E = _EC()
                    E.cw, E.ncx, E.nw, E.idx = cw, ncx, TT, idx
                    E.col = (lambda j, cw=cw, ncx=ncx: colsb[0:cw, j * Nc + ncx:j * Nc + ncx + 1])

                    if evac_s is not None:
                        def _es(sc, E=E, pp=pp, cw=cw, idx=idx, la_tile=la_tile):
                            sc.wait_ge(s_t.h, mm_at[idx])
                            if aux:
                                sc.wait_ge(s_la2[pp].h, la_tile)
                            if cols:
                                sc.wait_ge(s_pro.h, cols_ld)
                            if idx >= 2:
                                sc.wait_ge(s_v.h, ev_v_at[idx - 2])
                                if evac_v is None:
                                    sc.wait_ge(s_st2[pp].h, st_at[idx - 2])
                            E.q = sc
                            E.step = chainer(sc, s_s)
                            E.psum = p[pp][0:cw, :]
                            E.psum2 = p2[pp][0:cw, :] if dual else None
                            E.tmp = tmp[pp][0:cw, :]
                            E.out = ob[pp][0:cw, :]
                            E.aux = [a[0:cw, :] for a in axb[pp]]
                            ins = evac_s(E)
                            if ins is not None:
                                ins.then_inc(s_s.h, 1)
                            else:
                                sc.drain()
                                sc.sem_inc(s_s.h, 1)
                        b.sec("scalar", _es)
                        s_s.inc(1)
                        psum_free[idx] = (s_s.h, s_s.val)
                        sc_at = s_s.val
                    else:
                        sc_at = None

                    if evac_v is not None:
                        def _ev(v, E=E, pp=pp, cw=cw, idx=idx, la_tile=la_tile, sc_at=sc_at):
                            if sc_at is not None:
                                v.wait_ge(s_s.h, sc_at)
                            else:
                                v.wait_ge(s_t.h, mm_at[idx])
                            if aux:
                                v.wait_ge(s_la2[pp].h, la_tile)
                            if cols:
                                v.wait_ge(s_pro.h, cols_ld)
                            if idx >= 2:
                                v.wait_ge(s_st2[pp].h, st_at[idx - 2])
                            E.q = v
                            E.step = chainer(v, s_v)
                            E.psum = p[pp][0:cw, :]
                            E.psum2 = p2[pp][0:cw, :] if dual else None
                            E.tmp = tmp[pp][0:cw, :]
                            E.out = ob[pp][0:cw, :]
                            E.aux = [a[0:cw, :] for a in axb[pp]]
                            ins = evac_v(E)
                            if ins is not None:
                                ins.then_inc(s_v.h, 1)
                            else:
                                v.drain()
                                v.sem_inc(s_v.h, 1)
                        b.sec("vector", _ev)
                        s_v.inc(1)
                        ev_v_at[idx] = s_v.val
                        if idx not in psum_free:
                            psum_free[idx] = (s_v.h, s_v.val)
                        ev_done = (s_v.h, s_v.val)
                    else:
                        ev_v_at[idx] = s_v.val
                        ev_done = psum_free[idx]
                    aux_free[idx] = ev_v_at[idx]

                    def _st(g, pp=pp, cw=cw, idx=idx, odap=odap, olo=olo, t=t,
                            ev_done=ev_done):
                        g.wait_ge(ev_done[0], ev_done[1])
                        g.dma_start(out=odap[olo:olo + cw, t * TT:(t + 1) * TT],
                                    in_=ob[pp][0:cw, :]).then_inc(s_st2[pp].h, 16)
                    b.sec("gpsimd", _st)
                    s_st2[pp].inc(16)
                    st_at[idx] = s_st2[pp].val
                    idx += 1
                xfree[slot] = mm_at[idx - 1]

        def _fin(g):
            g.wait_ge(s_st2[0].h, s_st2[0].val)
            g.wait_ge(s_st2[1].h, s_st2[1].val)
        b.sec("gpsimd", _fin)


def st_mm_tm(b, name, segs, N, out, evac_s=None, evac_v=_copy_evac,
             bias_row=None, aux=(), Tn=T, out_dt=F32):
    nblocks = cdiv(N, 512)
    with b.stage(name) as sctx:
        s_t, s_s, s_v = (b.sem() for _ in range(3))
        s_wl = b.sem_sw()
        s_ld2, s_la2 = b.sem2(), b.sem2()
        s_st2 = b.sem2_sw()
        for si, s in enumerate(segs):
            Kc = s["K"] // 128
            s["_w"] = [b.sb(sctx, f"{name}_w{si}k{kc}", [128, N], BF16) for kc in range(Kc)]
            s["_x"] = [[b.sb(sctx, f"{name}_x{si}b{t}k{kc}", [128, TT], BF16)
                        for kc in range(Kc)] for t in range(2)]
        p = [b.ps(sctx, f"{name}_p{i}", [128, 512], F32) for i in range(2)]
        ob = [b.sb(sctx, f"{name}_o{i}", [128, 512], out_dt) for i in range(2)]
        tmp = [b.sb(sctx, f"{name}_t{i}", [128, 512], F32) for i in range(2)]
        axb = [[b.sb(sctx, f"{name}_a{i}j{j}", [128, 512], aux[j][1]) for j in range(len(aux))]
               for i in range(2)]
        if bias_row is not None:
            brow = b.sb(sctx, f"{name}_br", [1, N], BF16)
            ones1 = b.sb(sctx, f"{name}_on", [1, 128], BF16)

        def _pw(g):
            if bias_row is not None:
                g.dma_start(out=brow[:], in_=bias_row).then_inc(s_wl.h, 16)
            for si, s in enumerate(segs):
                for kc in range(s["K"] // 128):
                    g.dma_start(out=s["_w"][kc][:],
                                in_=s["w"][kc * 128:(kc + 1) * 128, :]).then_inc(s_wl.h, 16)
        b.sec("gpsimd", _pw)
        s_wl.inc(16 * ((1 if bias_row is not None else 0) + sum(s["K"] // 128 for s in segs)))
        wl_ready = s_wl.val

        if bias_row is not None:
            def _von(v):
                v.memset(ones1[:], 1.0).then_inc(s_v.h, 1)
            b.sec("vector", _von)
            s_v.inc(1)
        ones_ready = s_v.val

        mm_at = {}
        ev_v_at = {}
        psum_free = {}
        st_at = {}
        aux_free = {}
        xfree = [0, 0]
        idx = 0
        for t5 in range(Tn // TT):
            slot = t5 % 2

            def _lx(sync, slot=slot, t5=t5, w=xfree[slot]):
                sync.wait_ge(s_t.h, w)
                for si, s in enumerate(segs):
                    for kc in range(s["K"] // 128):
                        sync.dma_start(out=s["_x"][slot][kc][:],
                                       in_=s["x"][kc * 128:(kc + 1) * 128,
                                                  t5 * TT:(t5 + 1) * TT]).then_inc(s_ld2[slot].h, 16)
            b.sec("sync", _lx)
            s_ld2[slot].inc(16 * sum(s["K"] // 128 for s in segs))
            x_ld = s_ld2[slot].val

            for q in range(TT // 128):
                t128 = t5 * 4 + q
                r0 = t128 * 128
                for nb in range(nblocks):
                    pp = idx % 2
                    nw = min(512, N - nb * 512)
                    n0 = nb * 512

                    if aux:
                        def _la(sync, idx=idx, pp=pp, r0=r0, n0=n0, nw=nw):
                            if idx >= 2:
                                sync.wait_ge(s_v.h, aux_free[idx - 2])
                            for j, (ad, adt) in enumerate(aux):
                                sync.dma_start(out=axb[pp][j][:, 0:nw],
                                               in_=ad[r0:r0 + 128, n0:n0 + nw]).then_inc(s_la2[pp].h, 16)
                        b.sec("sync", _la)
                        s_la2[pp].inc(16 * len(aux))
                    la_tile = s_la2[pp].val

                    def _mm(tq, idx=idx, pp=pp, q=q, slot=slot, n0=n0, nw=nw, x_ld=x_ld):
                        tq.wait_ge(s_ld2[slot].h, x_ld)
                        tq.wait_ge(s_wl.h, wl_ready)
                        if bias_row is not None:
                            tq.wait_ge(s_v.h, ones_ready)
                        if idx >= 2:
                            tq.wait_ge(*psum_free[idx - 2])
                        first = True
                        if bias_row is not None:
                            tq.matmul(p[pp][:, 0:nw], ones1[:], brow[0:1, n0:n0 + nw],
                                      start=True, stop=False)
                            first = False
                        mms = []
                        for si, s in enumerate(segs):
                            for kc in range(s["K"] // 128):
                                mms.append((s["_x"][slot][kc][:, q * 128:(q + 1) * 128],
                                            s["_w"][kc][:, n0:n0 + nw]))
                        for mi, (lap, rap) in enumerate(mms):
                            ins = tq.matmul(p[pp][:, 0:nw], lap, rap,
                                            start=(first and mi == 0), stop=(mi == len(mms) - 1))
                            if mi == len(mms) - 1:
                                ins.then_inc(s_t.h, 1)
                    b.sec("tensor", _mm)
                    s_t.inc(1)
                    mm_at[idx] = s_t.val

                    E = _EC()
                    E.cw, E.ncx, E.nw, E.idx = 128, nb, nw, idx
                    E.col = None

                    if evac_s is not None:
                        def _es(sc, E=E, pp=pp, idx=idx, la_tile=la_tile, nw=nw):
                            sc.wait_ge(s_t.h, mm_at[idx])
                            if aux:
                                sc.wait_ge(s_la2[pp].h, la_tile)
                            if idx >= 2:
                                sc.wait_ge(s_v.h, ev_v_at[idx - 2])
                                if evac_v is None:
                                    sc.wait_ge(s_st2[pp].h, st_at[idx - 2])
                            E.q = sc
                            E.step = chainer(sc, s_s)
                            E.psum = p[pp][:, 0:nw]
                            E.psum2 = None
                            E.tmp = tmp[pp][:, 0:nw]
                            E.out = ob[pp][:, 0:nw]
                            E.aux = [a[:, 0:nw] for a in axb[pp]]
                            ins = evac_s(E)
                            if ins is not None:
                                ins.then_inc(s_s.h, 1)
                            else:
                                sc.drain()
                                sc.sem_inc(s_s.h, 1)
                        b.sec("scalar", _es)
                        s_s.inc(1)
                        psum_free[idx] = (s_s.h, s_s.val)
                        sc_at = s_s.val
                    else:
                        sc_at = None

                    if evac_v is not None:
                        def _ev(v, E=E, pp=pp, idx=idx, la_tile=la_tile, nw=nw, sc_at=sc_at):
                            if sc_at is not None:
                                v.wait_ge(s_s.h, sc_at)
                            else:
                                v.wait_ge(s_t.h, mm_at[idx])
                            if aux:
                                v.wait_ge(s_la2[pp].h, la_tile)
                            if idx >= 2:
                                v.wait_ge(s_st2[pp].h, st_at[idx - 2])
                            E.q = v
                            E.step = chainer(v, s_v)
                            E.psum = p[pp][:, 0:nw]
                            E.psum2 = None
                            E.tmp = tmp[pp][:, 0:nw]
                            E.out = ob[pp][:, 0:nw]
                            E.aux = [a[:, 0:nw] for a in axb[pp]]
                            ins = evac_v(E)
                            if ins is not None:
                                ins.then_inc(s_v.h, 1)
                            else:
                                v.drain()
                                v.sem_inc(s_v.h, 1)
                        b.sec("vector", _ev)
                        s_v.inc(1)
                        ev_v_at[idx] = s_v.val
                        if idx not in psum_free:
                            psum_free[idx] = (s_v.h, s_v.val)
                        ev_done = (s_v.h, s_v.val)
                    else:
                        ev_v_at[idx] = s_v.val
                        ev_done = psum_free[idx]
                    aux_free[idx] = ev_v_at[idx]

                    def _st(g, pp=pp, idx=idx, r0=r0, n0=n0, nw=nw, ev_done=ev_done):
                        g.wait_ge(ev_done[0], ev_done[1])
                        g.dma_start(out=out[r0:r0 + 128, n0:n0 + nw],
                                    in_=ob[pp][:, 0:nw]).then_inc(s_st2[pp].h, 16)
                    b.sec("gpsimd", _st)
                    s_st2[pp].inc(16)
                    st_at[idx] = s_st2[pp].val
                    idx += 1
            xfree[slot] = mm_at[idx - 1]

        def _fin(g):
            g.wait_ge(s_st2[0].h, s_st2[0].val)
            g.wait_ge(s_st2[1].h, s_st2[1].val)
        b.sec("gpsimd", _fin)


def st_transpose_add(b, name, srcT, Nr, xr_in, xr_out, scale, ident, Tn=T,
                     out_dt=F32, src_dt=BF16):
    nch = Nr // 128
    with b.stage(name) as sctx:
        s_pro, s_t, s_v = b.sem(), b.sem(), b.sem()
        s_ld2, s_sr2 = b.sem2(), b.sem2()
        s_st2 = b.sem2_sw()
        id_sb = b.sb(sctx, "tr_id", [128, 128], src_dt)
        src = [b.sb(sctx, f"tr_s{i}", [128, 128], src_dt) for i in range(2)]
        p_tr = [b.ps(sctx, f"tr_p{i}", [128, 128], src_dt) for i in range(2)]
        xa = [b.sb(sctx, f"tr_x{i}", [128, Nr], out_dt) for i in range(2)]

        def _l0(sync):
            sync.dma_start(out=id_sb[:], in_=ident).then_inc(s_pro.h, 16)
        b.sec("sync", _l0)
        s_pro.inc(16)
        pro_ld = s_pro.val

        v_at = {}
        vtile_at = {}
        st_at = {}
        tr_at = {}
        idx = 0
        for i in range(Tn // 128):
            bi = i % 2
            r0 = i * 128

            def _lxr(sync, i=i, bi=bi, r0=r0):
                if xr_in is not None:
                    if i >= 2:
                        sync.wait_ge(s_v.h, vtile_at[i - 2])
                        sync.wait_ge(s_st2[bi].h, st_at[i - 2])
                    sync.dma_start(out=xa[bi][:], in_=xr_in[r0:r0 + 128, :]).then_inc(s_ld2[bi].h, 16)
            b.sec("sync", _lxr)
            if xr_in is not None:
                s_ld2[bi].inc(16)
            x_ld = s_ld2[bi].val

            for c in range(nch):
                pk = idx % 2

                def _ls(sync, c=c, pk=pk, idx=idx, r0=r0):
                    if idx >= 2:
                        sync.wait_ge(s_t.h, tr_at[idx - 2])
                    sync.dma_start(out=src[pk][:],
                                   in_=srcT[c * 128:(c + 1) * 128, r0:r0 + 128]).then_inc(s_sr2[pk].h, 16)
                b.sec("sync", _ls)
                s_sr2[pk].inc(16)
                sld = s_sr2[pk].val

                def _tr(tq, pk=pk, idx=idx, sld=sld):
                    tq.wait_ge(s_sr2[pk].h, sld)
                    tq.wait_ge(s_pro.h, pro_ld)
                    if idx >= 2:
                        tq.wait_ge(s_v.h, v_at[idx - 2])
                    tq.transpose(p_tr[pk][:], src[pk][:], id_sb[:]).then_inc(s_t.h, 1)
                b.sec("tensor", _tr)
                s_t.inc(1)
                tr_at[idx] = s_t.val

                def _ev(v, i=i, c=c, pk=pk, bi=bi, idx=idx, x_ld=x_ld):
                    v.wait_ge(s_t.h, tr_at[idx])
                    if xr_in is not None:
                        v.wait_ge(s_ld2[bi].h, x_ld)
                        v.scalar_tensor_tensor(out=xa[bi][:, c * 128:(c + 1) * 128],
                                               in0=p_tr[pk][:], scalar=float(scale),
                                               in1=xa[bi][:, c * 128:(c + 1) * 128],
                                               op0=OP.mult, op1=OP.add).then_inc(s_v.h, 1)
                    else:
                        if i >= 2 and c == 0:
                            v.wait_ge(s_st2[bi].h, st_at[i - 2])
                        v.tensor_scalar(out=xa[bi][:, c * 128:(c + 1) * 128],
                                        in0=p_tr[pk][:], scalar1=float(scale),
                                        scalar2=None, op0=OP.mult).then_inc(s_v.h, 1)
                b.sec("vector", _ev)
                s_v.inc(1)
                v_at[idx] = s_v.val
                idx += 1
            vtile_at[i] = s_v.val

            def _st(g, i=i, bi=bi, r0=r0):
                g.wait_ge(s_v.h, vtile_at[i])
                g.dma_start(out=xr_out[r0:r0 + 128, :], in_=xa[bi][:]).then_inc(s_st2[bi].h, 16)
            b.sec("gpsimd", _st)
            s_st2[bi].inc(16)
            st_at[i] = s_st2[bi].val

        def _fin(g):
            g.wait_ge(s_st2[0].h, s_st2[0].val)
            g.wait_ge(s_st2[1].h, s_st2[1].val)
        b.sec("gpsimd", _fin)


def host_constants():
    c = {}
    c["ident_bf"] = np.eye(128, dtype=np.float32).astype(BF16NP)
    c["ident_f32"] = np.eye(128, dtype=np.float32)
    r = np.arange(128)
    c["tri_incl"] = (r[:, None] <= r[None, :]).astype(np.float32)
    c["triU_strict"] = (r[:, None] > r[None, :]).astype(np.float32)
    c["trimask_bf"] = (r[:, None] <= r[None, :]).astype(BF16NP)
    M = np.zeros((256, 256), dtype=np.float32)
    for leaf in range(256):
        node = 0
        for level in range(8):
            bit = (leaf >> (7 - level)) & 1
            M[node, leaf] = 1.0 if bit else -1.0
            node = 2 * node + 1 + bit
        assert node - 255 == leaf
    c["fff_M2"] = (2.0 * M).astype(BF16NP)
    c["fff_cs"] = (-M.sum(axis=0).reshape(256, 1)).astype(np.float32)
    E = np.zeros((256, 4096), dtype=np.float32)
    for leaf in range(256):
        E[leaf, leaf * 16:(leaf + 1) * 16] = 1.0
    c["fff_E"] = E.astype(BF16NP)
    inv = 1.0 / (10000.0 ** (np.arange(32, dtype=np.float64) / 32.0))
    ang = np.arange(T, dtype=np.float64)[None, :] * inv[np.tile(np.arange(32), 2)][:, None]
    c["rope_cos"] = np.cos(ang).astype(np.float32)
    c["rope_sin"] = np.sin(ang).astype(np.float32)
    return c


def emit_L2F(b, W, xr_in, xr_out, uid):
    zT = b.dtmp(f"zT_{uid}", [D, T], BF16)
    h1T = b.dtmp(f"h1T_{uid}", [D, T], BF16)
    h2T = b.dtmp(f"h2T_{uid}", [D, T], BF16)
    st_ln(b, xr_in, zT=zT, ident=W["ident_bf"])
    st_mm_fm(b, f"l2f_m1_{uid}", [dict(x=zT, w=W["l2_w"], K=D)], D,
             [(h1T, 0, D)])
    st_mm_fm(b, f"l2f_m2_{uid}", [dict(x=h1T, w=W["l2_a1"], K=D)], D,
             [(h2T, 0, D)])

    def ev_s(E):
        return E.q.activation(out=E.tmp, in_=E.psum, func=AF.Tanh)

    def ev_v(E):
        return E.q.scalar_tensor_tensor(out=E.out, in0=E.tmp, scalar=0.5, in1=E.aux[0],
                                        op0=OP.mult, op1=OP.add)
    st_mm_tm(b, f"l2f_m3_{uid}", [dict(x=h2T, w=W["l2_a2"], K=D)], D, xr_out,
             evac_s=ev_s, evac_v=ev_v, bias_row=W["l2_bias"],
             aux=[(xr_in, F32)], out_dt=F32)


def emit_L1F(b, W, xr_in, xr_out, uid):
    zT = b.dtmp(f"zT_{uid}", [D, T], BF16)
    haT = b.dtmp(f"l1f_haT_{uid}", [4096, T], BF16)
    hbT = b.dtmp(f"l1f_hbT_{uid}", [4096, T], BF16)
    hcT = b.dtmp(f"l1f_hcT_{uid}", [4096, T], BF16)
    dT = b.dtmp(f"l1f_dT_{uid}", [D, T], BF16)
    st_ln(b, xr_in, zT=zT, ident=W["ident_bf"])

    def ev_s_gate(E):
        return E.q.activation(out=E.tmp, in_=E.psum, func=AF.Silu, scale=E.col(0))

    def ev_v_gate(E):
        return E.q.scalar_tensor_tensor(out=E.out, in0=E.psum2, scalar=E.col(1),
                                        in1=E.tmp, op0=OP.mult, op1=OP.mult)
    st_mm_fm(b, f"l1f_g_{uid}",
             [dict(x=zT, w=W["l1_w1T"], w2=W["l1_w2T"], K=D, wlayout="panels")],
             4096, [(haT, 0, 4096)], evac_s=ev_s_gate, evac_v=ev_v_gate,
             cols=[W["l1_w1s"], W["l1_w2s"]])

    def ev_s_gelu(E):
        return E.q.activation(out=E.out, in_=E.psum, func=AF.Gelu, scale=E.col(0))
    st_mm_fm(b, f"l1f_h1_{uid}",
             [dict(x=haT, w=W["l1_h1T"], K=4096, wlayout="panels")],
             4096, [(hbT, 0, 4096)], evac_s=ev_s_gelu, evac_v=None,
             cols=[W["l1_h1s"]])
    st_mm_fm(b, f"l1f_h2_{uid}",
             [dict(x=hbT, w=W["l1_h2T"], K=4096, wlayout="panels")],
             4096, [(hcT, 0, 4096)], evac_s=ev_s_gelu, evac_v=None,
             cols=[W["l1_h2s"]])

    def ev_v_s5(E):
        return E.q.tensor_scalar(out=E.out, in0=E.psum, scalar1=E.col(0),
                                 scalar2=None, op0=OP.mult)
    st_mm_fm(b, f"l1f_w3_{uid}",
             [dict(x=hcT, w=W["l1_w3T"], K=4096, wlayout="panels")],
             D, [(dT, 0, D)], evac_v=ev_v_s5, cols=[W["l1_w3s"]])
    st_transpose_add(b, f"l1f_tr_{uid}", dT, D, xr_in, xr_out, 0.5, W["ident_bf"])


def panels(w_kn):
    Kd, Nd = w_kn.shape
    return np.ascontiguousarray(
        w_kn.reshape(Kd, Nd // 128, 128).transpose(1, 0, 2))


def colmat(v):
    return np.ascontiguousarray(v.reshape(-1, 128).T)


def st_add(b, name, xr_in, plus, xr_out, Dd=D, Tn=T, plus_dt=None):
    plus_dt = plus_dt or [F32] * len(plus)
    with b.stage(name) as sctx:
        s_v = b.sem()
        s_ld2 = b.sem2()
        s_st2 = b.sem2_sw()
        xa = [b.sb(sctx, f"{name}_x{i}", [128, Dd], F32) for i in range(2)]
        dl = [[b.sb(sctx, f"{name}_d{i}_{j}", [128, Dd], plus_dt[j]) for j in range(len(plus))]
              for i in range(2)]
        v_at = {}
        st_at = {}
        for i in range(Tn // 128):
            bi = i % 2
            r0 = i * 128

            def _ld(sync, i=i, bi=bi, r0=r0):
                if i >= 2:
                    sync.wait_ge(s_v.h, v_at[i - 2])
                    sync.wait_ge(s_st2[bi].h, st_at[i - 2])
                sync.dma_start(out=xa[bi][:], in_=xr_in[r0:r0 + 128, :]).then_inc(s_ld2[bi].h, 16)
                for j, (dap, sc) in enumerate(plus):
                    sync.dma_start(out=dl[bi][j][:], in_=dap[r0:r0 + 128, :]).then_inc(s_ld2[bi].h, 16)
            b.sec("sync", _ld)
            s_ld2[bi].inc(16 * (1 + len(plus)))
            ld_i = s_ld2[bi].val

            def _v(v, bi=bi, ld_i=ld_i):
                st = chainer(v, s_v)
                v.wait_ge(s_ld2[bi].h, ld_i)
                for j, (dap, sc) in enumerate(plus):
                    st(v.scalar_tensor_tensor(out=xa[bi][:], in0=dl[bi][j][:],
                                              scalar=float(sc), in1=xa[bi][:],
                                              op0=OP.mult, op1=OP.add))
                v.sem_inc(s_v.h, 1)
            b.sec("vector", _v)
            s_v.inc(1)
            v_at[i] = s_v.val

            def _st(g, i=i, bi=bi, r0=r0):
                g.wait_ge(s_v.h, v_at[i])
                g.dma_start(out=xr_out[r0:r0 + 128, :], in_=xa[bi][:]).then_inc(s_st2[bi].h, 16)
            b.sec("gpsimd", _st)
            s_st2[bi].inc(16)
            st_at[i] = s_st2[bi].val

        def _fin(g):
            g.wait_ge(s_st2[0].h, s_st2[0].val)
            g.wait_ge(s_st2[1].h, s_st2[1].val)
        b.sec("gpsimd", _fin)


def st_regroup(b, name, src, dst, Nr, S=8, Tn=T):
    Gn = Tn // S
    with b.stage(name) as sctx:
        s_v = b.sem()
        s_ld2 = b.sem2()
        s_st2 = b.sem2_sw()
        a = [b.sb(sctx, f"{name}_a{i}", [128, Tn], BF16) for i in range(2)]
        o = [b.sb(sctx, f"{name}_o{i}", [128, Tn], BF16) for i in range(2)]
        v_at = {}
        st_at = {}
        for i in range(Nr // 128):
            bi = i % 2

            def _ld(sync, i=i, bi=bi):
                if i >= 2:
                    sync.wait_ge(s_v.h, v_at[i - 2])
                    sync.wait_ge(s_st2[bi].h, st_at[i - 2])
                sync.dma_start(out=a[bi][:], in_=src[i * 128:(i + 1) * 128, :]).then_inc(s_ld2[bi].h, 16)
            b.sec("sync", _ld)
            s_ld2[bi].inc(16)
            ld_i = s_ld2[bi].val

            def _v(v, bi=bi, ld_i=ld_i):
                v.wait_ge(s_ld2[bi].h, ld_i)
                for r in range(S):
                    v.tensor_copy(out=o[bi][:, r * Gn:(r + 1) * Gn],
                                  in_=a[bi][:, r:Tn:S])
                v.drain()
                v.sem_inc(s_v.h, 1)
            b.sec("vector", _v)
            s_v.inc(1)
            v_at[i] = s_v.val

            def _st(g, i=i, bi=bi):
                g.wait_ge(s_v.h, v_at[i])
                g.dma_start(out=dst[i * 128:(i + 1) * 128, :], in_=o[bi][:]).then_inc(s_st2[bi].h, 16)
            b.sec("gpsimd", _st)
            s_st2[bi].inc(16)
            st_at[i] = s_st2[bi].val

        def _fin(g):
            g.wait_ge(s_st2[0].h, s_st2[0].val)
            g.wait_ge(s_st2[1].h, s_st2[1].val)
        b.sec("gpsimd", _fin)


def emit_L3F(b, W, xr_in, xr_out, uid):
    zT = b.dtmp(f"zT_{uid}", [D, T], BF16)
    gT = b.dtmp(f"l3f_gT_{uid}", [2048, T], BF16)
    shared = b.dtmp(f"l3f_sh_{uid}", [T, D], F32)
    zgT = b.dtmp(f"l3f_zgT_{uid}", [D, T], BF16)
    hgT = b.dtmp(f"l3f_hgT_{uid}", [1024, 512], BF16)
    ym = [b.dtmp(f"l3f_ym{k}_{uid}", [T, D], F32) for k in range(2)]
    st_ln(b, xr_in, zT=zT, ident=W["ident_bf"])

    def ev_s_gate(E):
        return E.q.activation(out=E.tmp, in_=E.psum, func=AF.Silu, bias=E.col(0))

    def ev_v_gate(E):
        return E.q.scalar_tensor_tensor(out=E.out, in0=E.psum2, scalar=E.col(1),
                                        in1=E.tmp, op0=OP.add, op1=OP.mult)
    st_mm_fm(b, f"l3f_g_{uid}",
             [dict(x=zT, w=W["l3_sw1"], w2=W["l3_sw3"], K=D)],
             2048, [(gT, 0, 2048)], evac_s=ev_s_gate, evac_v=ev_v_gate,
             cols=[W["l3_sb1c"], W["l3_sb3c"]])
    st_mm_tm(b, f"l3f_sh_{uid}", [dict(x=gT, w=W["l3_sw2"], K=2048)], D,
             shared, bias_row=W["l3_sb2"], out_dt=F32)
    st_regroup(b, f"l3f_rg_{uid}", zT, zgT, D)
    for k in range(2):
        for r in range(8):
            e = (3 * r + 7 * k) % 8

            def ev_s_gelu(E, e=e):
                return E.q.activation(out=E.out, in_=E.psum, func=AF.Gelu,
                                      bias=E.col(0))
            st_mm_fm(b, f"l3f_h{k}_{r}_{uid}",
                     [dict(x=zgT[:, r * 512:(r + 1) * 512], w=W["l3_w1"][e], K=D)],
                     1024, [(hgT, 0, 1024)], evac_s=ev_s_gelu, evac_v=None,
                     cols=[W["l3_b1c"][e]], Tn=512)
            out_view = ym[k].rearrange("(g s) d -> s g d", s=8)[r]
            st_mm_tm(b, f"l3f_y{k}_{r}_{uid}",
                     [dict(x=hgT, w=W["l3_w2"][e], K=1024)], D, out_view,
                     bias_row=W["l3_b2"][e:e + 1, :], out_dt=F32, Tn=512)
    st_add(b, f"l3f_cmb_{uid}", xr_in,
           [(shared, 0.25), (ym[0], 0.125), (ym[1], 0.125)], xr_out)


def st_rope(b, name, srcT, dstT, H, cosd, sind, Tn=T):
    nch = H // 2
    with b.stage(name) as sctx:
        s_pro, s_v = b.sem(), b.sem()
        s_ld2 = b.sem2()
        s_st2 = b.sem2_sw()
        cs = b.sb(sctx, f"{name}_cs", [64, 2 * Tn], F32)
        xt = [b.sb(sctx, f"{name}_xt{i}", [64, Tn], BF16) for i in range(2)]
        xb = [b.sb(sctx, f"{name}_xb{i}", [64, Tn], BF16) for i in range(2)]
        t1 = b.sb(sctx, f"{name}_t1", [64, Tn], F32)
        t2 = b.sb(sctx, f"{name}_t2", [64, Tn], F32)
        ot = [b.sb(sctx, f"{name}_ot{i}", [64, Tn], BF16) for i in range(2)]
        obt = [b.sb(sctx, f"{name}_ob{i}", [64, Tn], BF16) for i in range(2)]
        src4 = srcT.rearrange("(hh half d) t -> half hh d t", half=2, d=32)
        dst4 = dstT.rearrange("(hh half d) t -> half hh d t", half=2, d=32)

        def _l0(sync):
            sync.dma_start(out=cs[:, 0:Tn], in_=cosd).then_inc(s_pro.h, 16)
            sync.dma_start(out=cs[:, Tn:2 * Tn], in_=sind).then_inc(s_pro.h, 16)
        b.sec("sync", _l0)
        s_pro.inc(32)
        pro_ld = s_pro.val

        v_at = {}
        st_at = {}
        for c in range(nch):
            bi = c % 2

            def _ld(sync, c=c, bi=bi):
                if c >= 2:
                    sync.wait_ge(s_v.h, v_at[c - 2])
                    sync.wait_ge(s_st2[bi].h, st_at[c - 2])
                sync.dma_start(out=xt[bi][:], in_=src4[0, 2 * c:2 * c + 2]).then_inc(s_ld2[bi].h, 16)
                sync.dma_start(out=xb[bi][:], in_=src4[1, 2 * c:2 * c + 2]).then_inc(s_ld2[bi].h, 16)
            b.sec("sync", _ld)
            s_ld2[bi].inc(32)
            ld_c = s_ld2[bi].val

            def _v(v, c=c, bi=bi, ld_c=ld_c):
                st = chainer(v, s_v)
                v.wait_ge(s_ld2[bi].h, ld_c)
                v.wait_ge(s_pro.h, pro_ld)
                cos = cs[:, 0:Tn]
                sin = cs[:, Tn:2 * Tn]
                st(v.tensor_tensor(out=t1[:], in0=xt[bi][:], in1=cos, op=OP.mult))
                st(v.tensor_tensor(out=t2[:], in0=xb[bi][:], in1=sin, op=OP.mult))
                st(v.tensor_tensor(out=ot[bi][:], in0=t1[:], in1=t2[:], op=OP.subtract))
                st(v.tensor_tensor(out=t1[:], in0=xb[bi][:], in1=cos, op=OP.mult))
                st(v.tensor_tensor(out=t2[:], in0=xt[bi][:], in1=sin, op=OP.mult))
                st(v.tensor_tensor(out=obt[bi][:], in0=t1[:], in1=t2[:], op=OP.add))
                v.sem_inc(s_v.h, 1)
            b.sec("vector", _v)
            s_v.inc(1)
            v_at[c] = s_v.val

            def _st(g, c=c, bi=bi):
                g.wait_ge(s_v.h, v_at[c])
                g.dma_start(out=dst4[0, 2 * c:2 * c + 2], in_=ot[bi][:]).then_inc(s_st2[bi].h, 16)
                g.dma_start(out=dst4[1, 2 * c:2 * c + 2], in_=obt[bi][:]).then_inc(s_st2[bi].h, 16)
            b.sec("gpsimd", _st)
            s_st2[bi].inc(32)
            st_at[c] = s_st2[bi].val

        def _fin(g):
            g.wait_ge(s_st2[0].h, s_st2[0].val)
            g.wait_ge(s_st2[1].h, s_st2[1].val)
        b.sec("gpsimd", _fin)


def st_attn(b, name, qT, kT, v, out, H, dh, dv, ident=None, KV=None, scale=1.0,
            aux_add=None, out_writer=None, out_dt=F32, Tn=T):
    KV = KV if KV is not None else H
    grp = H // KV
    dhc = cdiv(dh, 128)
    dvb = cdiv(dv, 512)
    nkt = Tn // 512
    nkc = Tn // 128
    nq = Tn // 128
    with b.stage(name) as sctx:
        s_pro, s_t, s_s, s_v = (b.sem() for _ in range(4))
        s_q2, s_a2 = b.sem2(), b.sem2()
        s_st2 = b.sem2_sw()
        s_kv = b.sem()
        id_sb = b.sb(sctx, f"{name}_id", [128, 128], BF16)
        ones8 = b.sb(sctx, f"{name}_on", [128, 8], BF16)
        kh = b.sb(sctx, f"{name}_kh", [dh if dh <= 128 else 128, Tn * dhc], BF16)
        qh = [b.sb(sctx, f"{name}_qh{i}", [dh if dh <= 128 else 128, 128 * dhc], BF16)
              for i in range(2)]
        vh = b.sb(sctx, f"{name}_vh", [128, nkc * dv], BF16)
        P_sb = b.sb(sctx, f"{name}_P", [128, Tn], BF16)
        PT_sb = b.sb(sctx, f"{name}_PT", [128, Tn], BF16)
        ob = [b.sb(sctx, f"{name}_ob{i}", [128, dv], out_dt) for i in range(2)]
        ax = [b.sb(sctx, f"{name}_ax{i}", [128, dv], F32) for i in range(2)] if aux_add is not None else None
        sc_col = b.sb(sctx, f"{name}_sc", [128, 4], F32)
        S_ps = b.ps(sctx, f"{name}_S", [128, 4096], F32)
        pt_ps = S_ps[:, 0:512].bitcast(BF16)
        pv_ps = S_ps[:, 512:512 + 512 * dvb]
        rs_ps = S_ps[:, 512 + 512 * dvb:512 + 512 * dvb + 8]

        def _pro(sync):
            sync.dma_start(out=id_sb[:], in_=ident).then_inc(s_pro.h, 16)
        b.sec("sync", _pro)
        s_pro.inc(16)

        def _von(v_):
            v_.memset(ones8[:], 1.0).then_inc(s_v.h, 1)
        b.sec("vector", _von)
        s_v.inc(1)
        pro_v = s_v.val
        pro_ld = s_pro.val

        q_ld = {}
        a_ld = {}
        ev_at = {}
        st_at = {}
        exp_at = {}
        exp_free = {}
        pT_at = {}
        out_ev = {}
        pair = 0
        smm_last = 0
        for kv in range(KV):
            kv_gate = []
            if pair > 0:
                kv_gate.append((s_t.h, exp_free[pair - 1]))

            def _lkv(sync, kv=kv, kv_gate=tuple(kv_gate)):
                for (h_, v_) in kv_gate:
                    sync.wait_ge(h_, v_)
                for c in range(dhc):
                    sync.dma_start(
                        out=kh[:, c * Tn:(c + 1) * Tn],
                        in_=kT[kv * dh + c * 128: kv * dh + min((c + 1) * 128, dh), :],
                    ).then_inc(s_kv.h, 16)
                sync.dma_start(
                    out=vh[:, :],
                    in_=v.rearrange("(kc kp) d -> kp kc d", kp=128)[:, :, kv * dv:(kv + 1) * dv],
                ).then_inc(s_kv.h, 16)
            b.sec("sync", _lkv)
            s_kv.inc(16 * (dhc + 1))
            kv_ld = s_kv.val

            for hg in range(grp):
                h = kv * grp + hg
                for qi in range(nq):
                    pp = pair % 2

                    def _lq(sync, h=h, qi=qi, pp=pp, pair=pair):
                        if pair >= 2:
                            sync.wait_ge(s_t.h, exp_free[pair - 2])
                            if aux_add is not None:
                                sync.wait_ge(s_v.h, out_ev[pair - 2])
                        for c in range(dhc):
                            sync.dma_start(
                                out=qh[pp][:, c * 128:(c + 1) * 128],
                                in_=qT[h * dh + c * 128: h * dh + min((c + 1) * 128, dh),
                                       qi * 128:(qi + 1) * 128]).then_inc(s_q2[pp].h, 16)
                        if aux_add is not None:
                            sync.dma_start(out=ax[pp][:],
                                           in_=aux_add[qi * 128:(qi + 1) * 128, :]).then_inc(s_q2[pp].h, 16)
                    b.sec("sync", _lq)
                    s_q2[pp].inc(16 * (dhc + (1 if aux_add is not None else 0)))
                    q_ld[pair] = s_q2[pp].val

                    def _smm(tq, pp=pp, pair=pair, kv_ld=kv_ld):
                        tq.wait_ge(s_q2[pp].h, q_ld[pair])
                        tq.wait_ge(s_kv.h, kv_ld)
                        if pair >= 1:
                            tq.wait_ge(s_v.h, out_ev[pair - 1])
                        for kt in range(nkt):
                            for c in range(dhc):
                                cw = min(128, dh - c * 128)
                                ins = tq.matmul(
                                    S_ps[:, kt * 512:(kt + 1) * 512],
                                    qh[pp][0:cw, c * 128:(c + 1) * 128],
                                    kh[0:cw, c * Tn + kt * 512: c * Tn + (kt + 1) * 512],
                                    start=(c == 0), stop=(c == dhc - 1))
                        ins.then_inc(s_t.h, 1)
                    b.sec("tensor", _smm)
                    s_t.inc(1)
                    smm_at = s_t.val
                    smm_last = smm_at

                    def _vm(v_, pair=pair, smm_at=smm_at):
                        st = chainer(v_, s_v)
                        v_.wait_ge(s_t.h, smm_at)
                        st(v_.reduce_max(out=sc_col[:, 0:1], in_=S_ps[:, 0:Tn],
                                         axis=AX, negate=True))
                        v_.tensor_scalar(out=sc_col[:, 1:2], in0=sc_col[:, 0:1],
                                         scalar1=float(scale), scalar2=None,
                                         op0=OP.mult).then_inc(s_v.h, 1)
                    b.sec("vector", _vm)
                    s_v.inc(1)
                    max_at = s_v.val

                    def _se(sc, pair=pair, max_at=max_at):
                        sc.wait_ge(s_v.h, max_at)
                        sc.activation(out=P_sb[:], in_=S_ps[:, 0:Tn], func=AF.Exp,
                                      scale=float(scale),
                                      bias=sc_col[:, 1:2]).then_inc(s_s.h, 1)
                    b.sec("scalar", _se)
                    s_s.inc(1)
                    exp_at[pair] = s_s.val

                    n_round = nkc // 8
                    for rnd in range(n_round):
                        def _tr(tq, rnd=rnd, pair=pair):
                            tq.wait_ge(s_s.h, exp_at[pair])
                            tq.wait_ge(s_pro.h, pro_ld)
                            if rnd >= 1:
                                tq.wait_ge(s_v.h, pT_at[(pair, rnd - 1)])
                            for j in range(8):
                                kc = rnd * 8 + j
                                tq.transpose(pt_ps[:, j * 128:(j + 1) * 128],
                                             P_sb[:, kc * 128:(kc + 1) * 128],
                                             id_sb[:]).then_inc(s_t.h, 1)
                        b.sec("tensor", _tr)
                        s_t.inc(8)
                        tr_at = s_t.val

                        def _ptv(v_, rnd=rnd, tr_at=tr_at):
                            v_.wait_ge(s_t.h, tr_at)
                            v_.tensor_copy(out=PT_sb[:, rnd * 1024:(rnd + 1) * 1024],
                                           in_=pt_ps[:, :]).then_inc(s_v.h, 1)
                        b.sec("vector", _ptv)
                        s_v.inc(1)
                        pT_at[(pair, rnd)] = s_v.val

                        def _pv(tq, rnd=rnd, pair=pair, kv_ld=kv_ld):
                            tq.wait_ge(s_v.h, pT_at[(pair, rnd)])
                            for j in range(8):
                                kc = rnd * 8 + j
                                first = (kc == 0)
                                for db in range(dvb):
                                    bw = min(512, dv - db * 512)
                                    tq.matmul(pv_ps[:, db * 512:db * 512 + bw],
                                              PT_sb[:, kc * 128:(kc + 1) * 128],
                                              vh[:, kc * dv + db * 512: kc * dv + db * 512 + bw],
                                              start=first, stop=(kc == nkc - 1))
                                ins = tq.matmul(rs_ps[:, :],
                                                PT_sb[:, kc * 128:(kc + 1) * 128],
                                                ones8[:], start=first, stop=(kc == nkc - 1))
                            ins.then_inc(s_t.h, 1)
                        b.sec("tensor", _pv)
                        s_t.inc(1)
                        pv_at = s_t.val
                    exp_free[pair] = pv_at

                    def _vo(v_, pp=pp, pair=pair, pv_at=pv_at, h=h, qi=qi):
                        st = chainer(v_, s_v)
                        v_.wait_ge(s_t.h, pv_at)
                        v_.wait_ge(s_v.h, pro_v)
                        if pair >= 2:
                            v_.wait_ge(s_st2[pp].h, st_at[pair - 2])
                        st(v_.reciprocal(out=sc_col[:, 2:3], in_=rs_ps[:, 0:1]))
                        if aux_add is not None:
                            v_.wait_ge(s_q2[pp].h, q_ld[pair])
                            st(v_.scalar_tensor_tensor(out=ob[pp][:], in0=pv_ps[:, 0:dv],
                                                       scalar=sc_col[:, 2:3], in1=ax[pp][:],
                                                       op0=OP.mult, op1=OP.add))
                        else:
                            st(v_.tensor_scalar(out=ob[pp][:], in0=pv_ps[:, 0:dv],
                                                scalar1=sc_col[:, 2:3], scalar2=None,
                                                op0=OP.mult))
                        v_.sem_inc(s_v.h, 1)
                    b.sec("vector", _vo)
                    s_v.inc(1)
                    out_ev[pair] = s_v.val

                    def _st(g, pp=pp, pair=pair, h=h, qi=qi):
                        g.wait_ge(s_v.h, out_ev[pair])
                        if out_writer is not None:
                            out_writer(g, h, qi, ob[pp])
                            g.sem_inc(s_st2[pp].h, 16)
                        else:
                            g.dma_start(out=out[qi * 128:(qi + 1) * 128,
                                                h * dv:(h + 1) * dv],
                                        in_=ob[pp][:]).then_inc(s_st2[pp].h, 16)
                    b.sec("gpsimd", _st)
                    s_st2[pp].inc(16)
                    st_at[pair] = s_st2[pp].val
                    pair += 1

        def _fin(g):
            g.wait_ge(s_st2[0].h, s_st2[0].val)
            g.wait_ge(s_st2[1].h, s_st2[1].val)
        b.sec("gpsimd", _fin)


def emit_L3A(b, W, xr_in, xr_out, uid):
    zT = b.dtmp(f"zT_{uid}", [D, T], BF16)
    qkT = b.dtmp(f"l3a_qkT_{uid}", [2048, T], BF16)
    qrT = b.dtmp(f"l3a_qrT_{uid}", [1024, T], BF16)
    krT = b.dtmp(f"l3a_krT_{uid}", [1024, T], BF16)
    vtm = b.dtmp(f"l3a_v_{uid}", [T, 1024], BF16)
    attO = b.dtmp(f"l3a_o_{uid}", [T, 1024], BF16)
    OT = b.dtmp(f"l3a_OT_{uid}", [1024, T], BF16)
    st_ln(b, xr_in, zT=zT, ident=W["ident_bf"])
    st_mm_fm(b, f"l3a_qk_{uid}", [dict(x=zT, w=W["l3_wqk"], K=D)], 2048,
             [(qkT, 0, 2048)])
    st_mm_tm(b, f"l3a_v_{uid}", [dict(x=zT, w=W["l3_wv"], K=D)], 1024, vtm,
             out_dt=BF16)
    st_rope(b, f"l3a_rq_{uid}", qkT[0:1024, :], qrT, 16, W["rope_cos"], W["rope_sin"])
    st_rope(b, f"l3a_rk_{uid}", qkT[1024:2048, :], krT, 16, W["rope_cos"], W["rope_sin"])
    st_attn(b, f"l3a_at_{uid}", qrT, krT, vtm, attO, H=16, dh=64, dv=64,
            ident=W["ident_bf"], scale=0.125, out_dt=BF16)
    st_transpose_add(b, f"l3a_tr_{uid}", attO, 4096, None, OT, 1.0,
                     W["ident_bf"], Tn=1024, out_dt=BF16)

    def ev_v(E):
        return E.q.tensor_tensor(out=E.out, in0=E.psum, in1=E.aux[0], op=OP.add)
    st_mm_tm(b, f"l3a_o_{uid}", [dict(x=OT, w=W["l3_wout"], K=D)], D, xr_out,
             evac_v=ev_v, aux=[(xr_in, F32)], out_dt=F32)


def emit_L2A(b, W, xr_in, xr_out, uid):
    zT = b.dtmp(f"zT_{uid}", [D, T], BF16)
    qT = b.dtmp(f"l2a_qT_{uid}", [1024, T], BF16)
    kT = b.dtmp(f"l2a_kT_{uid}", [256, T], BF16)
    qrT = b.dtmp(f"l2a_qrT_{uid}", [1024, T], BF16)
    krT = b.dtmp(f"l2a_krT_{uid}", [256, T], BF16)
    vtm = b.dtmp(f"l2a_v_{uid}", [T, 256], BF16)
    attO = b.dtmp(f"l2a_o_{uid}", [T, 1024], BF16)
    OT = b.dtmp(f"l2a_OT_{uid}", [1024, T], BF16)
    st_ln(b, xr_in, zT=zT, ident=W["ident_bf"])
    st_mm_fm(b, f"l2a_q_{uid}", [dict(x=zT, w=W["l2_wq"], K=D)], 1024,
             [(qT, 0, 1024)])
    st_mm_fm(b, f"l2a_k_{uid}", [dict(x=zT, w=W["l2_wk"], K=D)], 256,
             [(kT, 0, 256)])
    st_mm_tm(b, f"l2a_v_{uid}", [dict(x=zT, w=W["l2_wv"], K=D)], 256, vtm,
             out_dt=BF16)
    st_rope(b, f"l2a_rq_{uid}", qT, qrT, 16, W["rope_cos"], W["rope_sin"])
    st_rope(b, f"l2a_rk_{uid}", kT, krT, 4, W["rope_cos"], W["rope_sin"])
    st_attn(b, f"l2a_at_{uid}", qrT, krT, vtm, attO, H=16, dh=64, dv=64, KV=4,
            ident=W["ident_bf"], scale=0.125, out_dt=BF16)
    st_transpose_add(b, f"l2a_tr_{uid}", attO, 4096, None, OT, 1.0,
                     W["ident_bf"], Tn=1024, out_dt=BF16)

    def ev_v(E):
        return E.q.tensor_tensor(out=E.out, in0=E.psum, in1=E.aux[0], op=OP.add)
    st_mm_tm(b, f"l2a_o_{uid}", [dict(x=OT, w=W["l2_wo"], K=D)], D, xr_out,
             evac_v=ev_v, aux=[(xr_in, F32)], out_dt=F32)


def st_greduce(b, name, src, dst, R, Nin):
    Nout = Nin // 8
    nch = cdiv(R, 128)
    with b.stage(name) as sctx:
        s_v = b.sem()
        s_ld2 = b.sem2()
        s_st2 = b.sem2_sw()
        a = [b.sb(sctx, f"{name}_a{i}", [128, Nin], F32) for i in range(2)]
        o = [b.sb(sctx, f"{name}_o{i}", [128, Nout], F32) for i in range(2)]
        v_at = {}
        st_at = {}
        for i in range(nch):
            bi = i % 2
            cw = min(128, R - i * 128)

            def _ld(sync, i=i, bi=bi, cw=cw):
                if i >= 2:
                    sync.wait_ge(s_v.h, v_at[i - 2])
                sync.dma_start(out=a[bi][0:cw, :],
                               in_=src[i * 128:i * 128 + cw, :]).then_inc(s_ld2[bi].h, 16)
            b.sec("sync", _ld)
            s_ld2[bi].inc(16)
            ld_i = s_ld2[bi].val

            def _v(v, i=i, bi=bi, cw=cw, ld_i=ld_i):
                v.wait_ge(s_ld2[bi].h, ld_i)
                if i >= 2:
                    v.wait_ge(s_st2[bi].h, st_at[i - 2])
                v.reduce_sum(out=o[bi][0:cw, :],
                             in_=a[bi][0:cw, :].rearrange("p (g k) -> p g k", k=8),
                             axis=AX).then_inc(s_v.h, 1)
            b.sec("vector", _v)
            s_v.inc(1)
            v_at[i] = s_v.val

            def _st(g, i=i, bi=bi, cw=cw):
                g.wait_ge(s_v.h, v_at[i])
                g.dma_start(out=dst[i * 128:i * 128 + cw, :],
                            in_=o[bi][0:cw, :]).then_inc(s_st2[bi].h, 16)
            b.sec("gpsimd", _st)
            s_st2[bi].inc(16)
            st_at[i] = s_st2[bi].val

        def _fin(g):
            g.wait_ge(s_st2[0].h, s_st2[0].val)
            g.wait_ge(s_st2[1].h, s_st2[1].val)
        b.sec("gpsimd", _fin)


def emit_L1A(b, W, xr_in, xr_out, uid):
    zT = b.dtmp(f"zT_{uid}", [D, T], BF16)
    qT = b.dtmp(f"l1a_qT_{uid}", [1024, T], BF16)
    kT = b.dtmp(f"l1a_kT_{uid}", [1024, T], BF16)
    wvr = b.dtmp(f"l1a_wvr_{uid}", [D, 1024], F32)
    bvr = b.dtmp(f"l1a_bvr_{uid}", [1, 1024], F32)
    vtm = b.dtmp(f"l1a_v_{uid}", [T, 1024], BF16)
    delta = b.dtmp(f"l1a_d_{uid}", [T, D], F32)
    st_ln(b, xr_in, zT=zT, gamma=W["l1_gamma"], beta=W["l1_beta"],
          ident=W["ident_bf"])
    st_greduce(b, f"l1a_wr_{uid}", W["l1_wv"], wvr, D, 8192)
    st_greduce(b, f"l1a_br_{uid}", W["l1_bv"], bvr, 1, 8192)

    def ev_bias(E):
        return E.q.tensor_scalar(out=E.out, in0=E.psum, scalar1=E.col(0),
                                 scalar2=None, op0=OP.add)
    st_mm_fm(b, f"l1a_q_{uid}", [dict(x=zT, w=W["l1_wq"], K=D)], 1024,
             [(qT, 0, 1024)], evac_v=ev_bias, cols=[W["l1_bqc"]])
    st_mm_fm(b, f"l1a_k_{uid}", [dict(x=zT, w=W["l1_wk"], K=D)], 1024,
             [(kT, 0, 1024)], evac_v=ev_bias, cols=[W["l1_bkc"]])
    st_mm_tm(b, f"l1a_v_{uid}", [dict(x=zT, w=wvr, K=D)], 1024, vtm,
             bias_row=bvr, out_dt=BF16)

    d6 = delta.rearrange("(hh R a bb) (cc f) -> hh R a bb cc f",
                         hh=8, R=32, a=2, bb=8, cc=8)
    sem_sc = b.sem_sw()

    def writer(g, h, qi, ob):
        g.dma_start(out=d6[h, qi, 0], in_=ob[0:64, :]).then_inc(sem_sc.h, 16)
        g.dma_start(out=d6[h, qi, 1], in_=ob[64:128, :]).then_inc(sem_sc.h, 16)
        sem_sc.inc(32)
        g.wait_ge(sem_sc.h, sem_sc.val)
    st_attn(b, f"l1a_at_{uid}", qT, kT, vtm, None, H=8, dh=128, dv=128,
            ident=W["ident_bf"], scale=128.0 ** -0.5, out_writer=writer,
            out_dt=F32)
    st_add(b, f"l1a_add_{uid}", xr_in, [(delta, 1.0)], xr_out)


def emit_fff(b, W, pfx, xT32, hseg, outT_spec, act, uid, gT, ohT, maskT, HmT):
    def ev_route(E):
        return E.q.tensor_scalar(out=E.out, in0=E.psum, scalar1=E.col(0),
                                 scalar2=0.0, op0=OP.add, op1=OP.is_gt)
    st_mm_fm(b, f"{pfx}rt_{uid}", [dict(x=xT32, w=W[pfx + "nwT"], K=D, dt=F32)],
             256, [(gT, 0, 256)], evac_v=ev_route, cols=[W[pfx + "nbc"]])

    def ev_leaf(E):
        return E.q.tensor_scalar(out=E.out, in0=E.psum, scalar1=E.col(0),
                                 scalar2=7.5, op0=OP.add, op1=OP.is_gt)
    st_mm_fm(b, f"{pfx}oh_{uid}", [dict(x=gT, w=W["fff_M2"], K=256)], 256,
             [(ohT, 0, 256)], evac_v=ev_leaf, cols=[W["fff_cs"]])
    st_mm_fm(b, f"{pfx}mk_{uid}", [dict(x=ohT, w=W["fff_E"], K=256)], 4096,
             [(maskT, 0, 4096)])

    def ev_s_h(E):
        return E.q.activation(out=E.tmp, in_=E.psum, func=act, bias=E.col(0))

    def ev_v_h(E):
        return E.q.tensor_tensor(out=E.out, in0=E.tmp, in1=E.aux[0], op=OP.mult)
    st_mm_fm(b, f"{pfx}h_{uid}", [dict(list(hseg.items()) + [("w", W[pfx + "w1p"]),
                                        ("K", D), ("wlayout", "panels")])],
             4096, [(HmT, 0, 4096)], evac_s=ev_s_h, evac_v=ev_v_h,
             cols=[W[pfx + "b1c"]], aux=[(maskT, BF16)])

    segs2 = [dict(x=HmT, w=W[pfx + "w2p"], K=4096, wlayout="panels"),
             dict(x=ohT, w=W[pfx + "b2s"], K=256)]
    segs2tm = [dict(x=HmT, w=W[pfx + "w2r"], K=4096),
               dict(x=ohT, w=W[pfx + "b2s"], K=256)]
    if outT_spec[0] == "fm":
        st_mm_fm(b, f"{pfx}o_{uid}", segs2, D, [(outT_spec[1], 0, D)])
    else:
        _, outd, ev_s, ev_v, aux, odt = outT_spec
        st_mm_tm(b, f"{pfx}o_{uid}", segs2tm, D, outd, evac_s=ev_s, evac_v=ev_v,
                 aux=aux, out_dt=odt)


def emit_L0F(b, W, xr_in, xr_out, uid, upto=99):
    z0 = b.dtmp(f"l0f_z0_{uid}", [T, D], F32)
    n0 = b.dtmp(f"l0f_n0_{uid}", [T, D], F32)
    n0T = b.dtmp(f"l0f_n0T_{uid}", [D, T], BF16)
    n0T32 = b.dtmp(f"l0f_n0T32_{uid}", [D, T], F32)
    gT = b.dtmp(f"l0f_gT_{uid}", [256, T], BF16)
    ohT = b.dtmp(f"l0f_ohT_{uid}", [256, T], BF16)
    maskT = b.dtmp(f"l0f_mkT_{uid}", [4096, T], BF16)
    HmT = b.dtmp(f"l0f_HmT_{uid}", [4096, T], BF16)
    qfT = b.dtmp(f"l0f_qfT_{uid}", [D, T], BF16)
    kfT = b.dtmp(f"l0f_kfT_{uid}", [D, T], BF16)
    vf = b.dtmp(f"l0f_vf_{uid}", [T, D], BF16)
    u = b.dtmp(f"l0f_u_{uid}", [T, D], F32)
    uT32 = b.dtmp(f"l0f_uT32_{uid}", [D, T], F32)
    st_ln(b, xr_in, z=z0, ident=W["ident_bf"])
    st_ln(b, z0, variant="inner", z=n0, zT=n0T, zT32=n0T32,
          ident=W["ident_bf"], ident32=W["ident_f32"])
    hseg_n0 = dict(x=n0T)
    if upto < 1:
        st_add(b, f"l0f_dbg_{uid}", xr_in, [(z0, 0.001)], xr_out)
        return
    emit_fff(b, W, "l0_q_", n0T32, hseg_n0, ("fm", qfT), AF.Relu, uid, gT, ohT, maskT, HmT)
    if upto < 2:
        st_add(b, f"l0f_dbg_{uid}", xr_in, [(n0, 0.001)], xr_out)
        return
    emit_fff(b, W, "l0_k_", n0T32, hseg_n0, ("fm", kfT), AF.Relu, uid, gT, ohT, maskT, HmT)
    emit_fff(b, W, "l0_v_", n0T32, hseg_n0, ("tm", vf, None, _copy_evac, (), BF16),
             AF.Relu, uid, gT, ohT, maskT, HmT)
    if upto < 3:
        st_add(b, f"l0f_dbg_{uid}", xr_in, [(vf, 0.001)], xr_out, plus_dt=[BF16])
        return
    st_attn(b, f"l0f_at_{uid}", qfT, kfT, vf, u, H=1, dh=1024, dv=1024,
            ident=W["ident_bf"], scale=1.0 / 32.0, aux_add=n0, out_dt=F32)
    if upto < 4:
        st_add(b, f"l0f_dbg_{uid}", xr_in, [(u, 0.001)], xr_out)
        return
    st_transpose_add(b, f"l0f_tr_{uid}", u, 4096, None, uT32, 1.0,
                     W["ident_f32"], Tn=1024, out_dt=F32, src_dt=F32)
    if upto < 5:
        st_add(b, f"l0f_dbg_{uid}", xr_in, [(u, 0.001)], xr_out)
        return

    def ev_v_fin(E):
        E.step(E.q.scalar_tensor_tensor(out=E.tmp, in0=E.aux[1], scalar=0.5,
                                        in1=E.aux[0], op0=OP.mult, op1=OP.add))
        return E.q.scalar_tensor_tensor(out=E.out, in0=E.psum, scalar=0.5,
                                        in1=E.tmp, op0=OP.mult, op1=OP.add)
    emit_fff(b, W, "l0_f_", uT32, dict(x=uT32, cast=True),
             ("tm", xr_out, None, ev_v_fin, [(xr_in, F32), (u, F32)], F32),
             AF.Gelu, uid, gT, ohT, maskT, HmT)


def st_mamba_conv(b, name, xbcT, convT, conv_w, conv_b, R=2304, Tn=T):
    nch = R // 128
    with b.stage(name) as sctx:
        s_v, s_s = b.sem(), b.sem()
        s_ld2 = b.sem2()
        s_st2 = b.sem2_sw()
        xx = [b.sb(sctx, f"{name}_x{i}", [128, Tn], BF16) for i in range(2)]
        wc = [b.sb(sctx, f"{name}_w{i}", [128, 5], F32) for i in range(2)]
        acc = [b.sb(sctx, f"{name}_a{i}", [128, Tn], F32) for i in range(2)]
        oo = [b.sb(sctx, f"{name}_o{i}", [128, Tn], BF16) for i in range(2)]
        v_at = {}
        s_at = {}
        st_at = {}
        for i in range(nch):
            bi = i % 2

            def _ld(sync, i=i, bi=bi):
                if i >= 2:
                    sync.wait_ge(s_s.h, s_at[i - 2])
                    sync.wait_ge(s_st2[bi].h, st_at[i - 2])
                sync.dma_start(out=xx[bi][:], in_=xbcT[i * 128:(i + 1) * 128, :]).then_inc(s_ld2[bi].h, 16)
                sync.dma_start(out=wc[bi][:, 0:4], in_=conv_w[i * 128:(i + 1) * 128, :]).then_inc(s_ld2[bi].h, 16)
                sync.dma_start(out=wc[bi][:, 4:5], in_=conv_b[i * 128:(i + 1) * 128, :]).then_inc(s_ld2[bi].h, 16)
            b.sec("sync", _ld)
            s_ld2[bi].inc(48)
            ld_i = s_ld2[bi].val

            def _v(v, i=i, bi=bi, ld_i=ld_i):
                st = chainer(v, s_v)
                v.wait_ge(s_ld2[bi].h, ld_i)
                st(v.tensor_scalar(out=acc[bi][:], in0=xx[bi][:], scalar1=wc[bi][:, 3:4],
                                   scalar2=wc[bi][:, 4:5], op0=OP.mult, op1=OP.add))
                for k in range(3):
                    sh = 3 - k
                    st(v.scalar_tensor_tensor(out=acc[bi][:, sh:Tn], in0=xx[bi][:, 0:Tn - sh],
                                              scalar=wc[bi][:, k:k + 1], in1=acc[bi][:, sh:Tn],
                                              op0=OP.mult, op1=OP.add))
                v.sem_inc(s_v.h, 1)
            b.sec("vector", _v)
            s_v.inc(1)
            v_at[i] = s_v.val

            def _s(sc, i=i, bi=bi):
                sc.wait_ge(s_v.h, v_at[i])
                sc.activation(out=oo[bi][:], in_=acc[bi][:], func=AF.Silu).then_inc(s_s.h, 1)
            b.sec("scalar", _s)
            s_s.inc(1)
            s_at[i] = s_s.val

            def _st(g, i=i, bi=bi):
                g.wait_ge(s_s.h, s_at[i])
                g.dma_start(out=convT[i * 128:(i + 1) * 128, :], in_=oo[bi][:]).then_inc(s_st2[bi].h, 16)
            b.sec("gpsimd", _st)
            s_st2[bi].inc(16)
            st_at[i] = s_st2[bi].val

        def _fin(g):
            g.wait_ge(s_st2[0].h, s_st2[0].val)
            g.wait_ge(s_st2[1].h, s_st2[1].val)
        b.sec("gpsimd", _fin)


def st_mamba_dt(b, name, dtrawT, dta_tm, dt_bias, A_log, ident32, Tn=T):
    with b.stage(name) as sctx:
        s_ld, s_v, s_s, s_t = (b.sem() for _ in range(4))
        s_st2 = b.sem2_sw()
        raw = b.sb(sctx, f"{name}_r", [16, Tn], F32)
        u = b.sb(sctx, f"{name}_u", [16, Tn], F32)
        t1 = b.sb(sctx, f"{name}_t1", [16, Tn], F32)
        t2 = b.sb(sctx, f"{name}_t2", [16, Tn], F32)
        dta = b.sb(sctx, f"{name}_d", [128, Tn], F32)
        cols = b.sb(sctx, f"{name}_c", [16, 3], F32)
        id32 = b.sb(sctx, f"{name}_id", [128, 128], F32)
        ptr = [b.ps(sctx, f"{name}_p{i}", [128, 128], F32) for i in range(2)]
        otr = [b.sb(sctx, f"{name}_o{i}", [128, 48], F32) for i in range(2)]

        def _ld(sync):
            sync.dma_start(out=raw[:], in_=dtrawT[0:16, :]).then_inc(s_ld.h, 16)
            sync.dma_start(out=cols[:, 0:1], in_=dt_bias).then_inc(s_ld.h, 16)
            sync.dma_start(out=cols[:, 1:2], in_=A_log).then_inc(s_ld.h, 16)
            sync.dma_start(out=id32[:], in_=ident32).then_inc(s_ld.h, 16)
        b.sec("sync", _ld)
        s_ld.inc(64)
        ld0 = s_ld.val

        def _v1(v):
            v.wait_ge(s_ld.h, ld0)
            v.memset(dta[:], 0.0)
            v.tensor_scalar(out=u[:], in0=raw[:], scalar1=cols[:, 0:1],
                            scalar2=None, op0=OP.add).then_inc(s_v.h, 1)
        b.sec("vector", _v1)
        s_v.inc(1)
        u_at = s_v.val

        def _s1(sc):
            st = chainer(sc, s_s)
            sc.wait_ge(s_v.h, u_at)
            st(sc.activation(out=t1[:], in_=u[:], func=AF.Abs))
            st(sc.activation(out=t1[:], in_=t1[:], func=AF.Exp, scale=-1.0))
            st(sc.activation(out=t1[:], in_=t1[:], func=AF.Ln, bias=1.0))
            st(sc.activation(out=t2[:], in_=u[:], func=AF.Relu))
            st(sc.activation(out=cols[:, 2:3], in_=cols[:, 1:2], func=AF.Exp))
            sc.sem_inc(s_s.h, 1)
        b.sec("scalar", _s1)
        s_s.inc(1)
        sp_at = s_s.val

        def _v2(v):
            st = chainer(v, s_v)
            v.wait_ge(s_s.h, sp_at)
            st(v.tensor_tensor(out=dta[0:16, :], in0=t1[:], in1=t2[:], op=OP.add))
            st(v.tensor_scalar(out=cols[:, 2:3], in0=cols[:, 2:3], scalar1=-1.0,
                               scalar2=None, op0=OP.mult))
            st(v.tensor_scalar(out=dta[32:48, :], in0=dta[0:16, :],
                               scalar1=cols[:, 2:3], scalar2=None, op0=OP.mult))
            v.sem_inc(s_v.h, 1)
        b.sec("vector", _v2)
        s_v.inc(1)
        dta_at = s_v.val

        tr_at = {}
        ev_at = {}
        st_at = {}
        for c in range(Tn // 128):
            pk = c % 2

            def _tr(tq, c=c, pk=pk):
                tq.wait_ge(s_v.h, dta_at)
                tq.wait_ge(s_ld.h, ld0)
                if c >= 2:
                    tq.wait_ge(s_v.h, ev_at[c - 2])
                tq.transpose(ptr[pk][:], dta[:, c * 128:(c + 1) * 128],
                             id32[:]).then_inc(s_t.h, 1)
            b.sec("tensor", _tr)
            s_t.inc(1)
            tr_at[c] = s_t.val

            def _ev(v, c=c, pk=pk):
                v.wait_ge(s_t.h, tr_at[c])
                if c >= 2:
                    v.wait_ge(s_st2[pk].h, st_at[c - 2])
                v.tensor_copy(out=otr[pk][:], in_=ptr[pk][:, 0:48]).then_inc(s_v.h, 1)
            b.sec("vector", _ev)
            s_v.inc(1)
            ev_at[c] = s_v.val

            def _st(g, c=c, pk=pk):
                g.wait_ge(s_v.h, ev_at[c])
                g.dma_start(out=dta_tm[c * 128:(c + 1) * 128, :],
                            in_=otr[pk][:]).then_inc(s_st2[pk].h, 16)
            b.sec("gpsimd", _st)
            s_st2[pk].inc(16)
            st_at[c] = s_st2[pk].val

        def _fin(g):
            g.wait_ge(s_st2[0].h, s_st2[0].val)
            g.wait_ge(s_st2[1].h, s_st2[1].val)
        b.sec("gpsimd", _fin)


def st_mamba_ssd(b, name, convT, dta_tm, yT, Drow, tri_incl, triU, trimask,
                 ident, Tn=T, NH=16, P=128, NS=128):
    nchk = Tn // 128
    with b.stage(name) as sctx:
        s_ld, s_t, s_s, s_v = (b.sem() for _ in range(4))
        s_ld2 = b.sem2()
        s_st2 = b.sem2_sw()
        tri_sb = b.sb(sctx, f"{name}_tri", [128, 384], F32)
        trim_sb = b.sb(sctx, f"{name}_trm", [128, 128], BF16)
        id_sb = b.sb(sctx, f"{name}_id", [128, 128], BF16)
        ones_sb = b.sb(sctx, f"{name}_on", [128, 128], F32)
        ones1 = b.sb(sctx, f"{name}_o1", [1, 128], F32)
        Db = b.sb(sctx, f"{name}_Db", [128, 16], F32)
        Drow_sb = b.sb(sctx, f"{name}_Dr", [1, 16], F32)
        St = b.sb(sctx, f"{name}_St", [128, NH * P], F32)
        Stbf = b.sb(sctx, f"{name}_Sb", [128, NH * P], BF16)
        dta = [b.sb(sctx, f"{name}_dt{i}", [128, 48], F32) for i in range(2)]
        BT = [b.sb(sctx, f"{name}_B{i}", [128, 128], BF16) for i in range(2)]
        CT = [b.sb(sctx, f"{name}_C{i}", [128, 128], BF16) for i in range(2)]
        xhT = [b.sb(sctx, f"{name}_xh{i}", [128, NH * P], BF16) for i in range(2)]
        Btm = [b.sb(sctx, f"{name}_Bt{i}", [128, 128], BF16) for i in range(2)]
        csc = [b.sb(sctx, f"{name}_cs{i}", [128, 64], F32) for i in range(2)]
        acb = [b.sb(sctx, f"{name}_ac{i}", [128, 128], F32) for i in range(2)]
        dcl = [b.sb(sctx, f"{name}_dc{i}", [128, 128], F32) for i in range(2)]
        expd = [b.sb(sctx, f"{name}_ed{i}", [128, 128], F32) for i in range(2)]
        expa = [b.sb(sctx, f"{name}_ea{i}", [128, 128], F32) for i in range(2)]
        cbm = [b.sb(sctx, f"{name}_cb{i}", [128, 128], F32) for i in range(2)]
        hsb = [b.sb(sctx, f"{name}_H{i}", [128, 128], BF16) for i in range(2)]
        cp = [b.sb(sctx, f"{name}_Cp{i}", [128, 128], BF16) for i in range(2)]
        xp = [b.sb(sctx, f"{name}_Xp{i}", [128, 128], BF16) for i in range(2)]
        bw = [b.sb(sctx, f"{name}_Bw{i}", [128, 128], BF16) for i in range(2)]
        yt = [b.sb(sctx, f"{name}_yt{i}", [128, 128], BF16) for i in range(2)]
        pA = b.ps(sctx, f"{name}_pA", [128, 128], F32)
        pD = b.ps(sctx, f"{name}_pD", [128, 128], F32)
        pY = b.ps(sctx, f"{name}_pY", [128, 128], F32)
        pS = b.ps(sctx, f"{name}_pS", [128, 128], F32)
        pTr = b.ps(sctx, f"{name}_pT", [128, 128], BF16)
        pBC = b.ps(sctx, f"{name}_pBC", [128, 32], F32)
        pDb = b.ps(sctx, f"{name}_pDb", [128, 16], F32)

        def _pro(sync):
            sync.dma_start(out=tri_sb[:, 0:128], in_=tri_incl).then_inc(s_ld.h, 16)
            sync.dma_start(out=tri_sb[:, 128:256], in_=triU).then_inc(s_ld.h, 16)
            sync.dma_start(out=trim_sb[:], in_=trimask).then_inc(s_ld.h, 16)
            sync.dma_start(out=Drow_sb[:], in_=Drow).then_inc(s_ld.h, 16)
            sync.dma_start(out=id_sb[:], in_=ident).then_inc(s_ld.h, 16)
        b.sec("sync", _pro)
        s_ld.inc(80)
        pro_ld = s_ld.val

        def _v0(v):
            v.memset(ones_sb[:], 1.0)
            v.memset(ones1[:], 1.0)
            v.memset(St[:], 0.0)
            v.memset(Stbf[:], 0.0)
            v.drain()
            v.sem_inc(s_v.h, 1)
        b.sec("vector", _v0)
        s_v.inc(1)
        init_v = s_v.val

        def _t0(tq):
            tq.wait_ge(s_ld.h, pro_ld)
            tq.wait_ge(s_v.h, init_v)
            tq.matmul(pDb[:], ones1[:], Drow_sb[:], start=True, stop=True).then_inc(s_t.h, 1)
        b.sec("tensor", _t0)
        s_t.inc(1)
        db_mm = s_t.val

        def _v00(v):
            v.wait_ge(s_t.h, db_mm)
            v.tensor_copy(out=Db[:], in_=pDb[:]).then_inc(s_v.h, 1)
        b.sec("vector", _v00)
        s_v.inc(1)
        db_at = s_v.val

        last = dict(V1={}, T1={}, V2={}, S1={}, V3={}, T2={}, V4={}, G1={},
                    T0={}, V0={})
        stbf_at = [db_at] * NH
        ytst = {}

        xh3 = convT.rearrange("(hc p) t -> p hc t", p=128)
        for c in range(nchk):
            cb = c % 2

            def _ldc(sync, c=c, cb=cb):
                if c >= 2:
                    sync.wait_ge(s_t.h, last["T2"][(c - 2, NH - 1)])
                    sync.wait_ge(s_v.h, last["V4"][(c - 2, NH - 1)])
                sync.dma_start(out=dta[cb][:],
                               in_=dta_tm[c * 128:(c + 1) * 128, :]).then_inc(s_ld2[cb].h, 16)
                sync.dma_start(out=BT[cb][:],
                               in_=convT[2048:2176, c * 128:(c + 1) * 128]).then_inc(s_ld2[cb].h, 16)
                sync.dma_start(out=CT[cb][:],
                               in_=convT[2176:2304, c * 128:(c + 1) * 128]).then_inc(s_ld2[cb].h, 16)
                sync.dma_start(out=xhT[cb][:],
                               in_=xh3[:, 0:16, c * 128:(c + 1) * 128]).then_inc(s_ld2[cb].h, 16)
            b.sec("sync", _ldc)
            s_ld2[cb].inc(64)
            ld_c = s_ld2[cb].val

            def _tc(tq, c=c, cb=cb, ld_c=ld_c):
                tq.wait_ge(s_ld2[cb].h, ld_c)
                tq.wait_ge(s_ld.h, pro_ld)
                if c >= 1:
                    tq.wait_ge(s_v.h, last["V0"][c - 1])
                    tq.wait_ge(s_v.h, last["V2"][(c - 1, NH - 1)])
                    tq.wait_ge(s_v.h, last["V3"][(c - 1, NH - 1)])
                tq.matmul(pBC[:, 0:16], tri_sb[:, 0:128], dta[cb][:, 32:48],
                          start=True, stop=True)
                tq.matmul(pBC[:, 16:32], tri_sb[:, 128:256], dta[cb][:, 32:48],
                          start=True, stop=True).then_inc(s_t.h, 1)
                tq.matmul(pD[:], BT[cb][:], CT[cb][:],
                          start=True, stop=True)
                tq.transpose(pTr[:], BT[cb][:], id_sb[:]).then_inc(s_t.h, 1)
            b.sec("tensor", _tc)
            s_t.inc(2)
            last["T0"][c] = s_t.val

            def _vc(v, c=c, cb=cb):
                st = chainer(v, s_v)
                v.wait_ge(s_t.h, last["T0"][c])
                st(v.tensor_copy(out=csc[cb][:, 0:32], in_=pBC[:, 0:32]))
                st(v.tensor_tensor(out=csc[cb][:, 32:48], in0=csc[cb][:, 16:32],
                                   in1=dta[cb][:, 32:48], op=OP.add))
                st(v.tensor_copy(out=Btm[cb][:], in_=pTr[:]))
                st(v.tensor_tensor(out=cbm[cb][:], in0=pD[:], in1=trim_sb[:],
                                   op=OP.mult))
                v.sem_inc(s_v.h, 1)
            b.sec("vector", _vc)
            s_v.inc(1)
            last["V0"][c] = s_v.val

            def _sc(sc, c=c, cb=cb):
                st = chainer(sc, s_s)
                sc.wait_ge(s_v.h, last["V0"][c])
                st(sc.activation(out=csc[cb][:, 48:64], in_=csc[cb][:, 16:32],
                                 func=AF.Exp))
                st(sc.activation(out=csc[cb][:, 32:48], in_=csc[cb][:, 32:48],
                                 func=AF.Exp))
                sc.sem_inc(s_s.h, 1)
            b.sec("scalar", _sc)
            s_s.inc(1)
            sc_c = s_s.val

            for h in range(NH):
                hp = h % 2

                def _v1(v, c=c, cb=cb, h=h, hp=hp):
                    v.wait_ge(s_ld2[cb].h, ld_c)
                    if (c, h) != (0, 0) and (c * NH + h) >= 2:
                        ch2 = divmod(c * NH + h - 2, NH)
                        v.wait_ge(s_t.h, last["T1"][(ch2[0], ch2[1])])
                    v.tensor_scalar(out=acb[hp][:], in0=ones_sb[:],
                                    scalar1=dta[cb][:, 32 + h:33 + h], scalar2=None,
                                    op0=OP.mult).then_inc(s_v.h, 1)
                b.sec("vector", _v1)
                s_v.inc(1)
                last["V1"][(c, h)] = s_v.val

                def _t1(tq, c=c, cb=cb, h=h, hp=hp):
                    k = c * NH + h
                    tq.wait_ge(s_v.h, last["V1"][(c, h)])
                    tq.wait_ge(s_v.h, last["V0"][c])
                    if k >= 1:
                        kp = divmod(k - 1, NH)
                        tq.wait_ge(s_v.h, last["V2"][kp])
                        tq.wait_ge(s_s.h, last["S1"][kp])
                    tq.matmul(pA[:], acb[hp][:], tri_sb[:, 0:128],
                              start=True, stop=True)
                    tq.transpose(pTr[:], xhT[cb][:, h * 128:(h + 1) * 128],
                                 id_sb[:]).then_inc(s_t.h, 1)
                b.sec("tensor", _t1)
                s_t.inc(1)
                last["T1"][(c, h)] = s_t.val

                def _v2(v, c=c, cb=cb, h=h, hp=hp):
                    st = chainer(v, s_v)
                    v.wait_ge(s_t.h, last["T1"][(c, h)])
                    v.wait_ge(s_v.h, last["V0"][c])
                    st(v.tensor_scalar(out=xp[hp][:], in0=pTr[:],
                                       scalar1=dta[cb][:, h:h + 1], scalar2=None,
                                       op0=OP.mult))
                    st(v.tensor_scalar(out=dcl[hp][:], in0=pA[:],
                                       scalar1=csc[cb][:, h:h + 1], scalar2=0.0,
                                       op0=OP.subtract, op1=OP.min))
                    v.sem_inc(s_v.h, 1)
                b.sec("vector", _v2)
                s_v.inc(1)
                last["V2"][(c, h)] = s_v.val

                def _s1(sc, c=c, h=h, hp=hp):
                    st = chainer(sc, s_s)
                    sc.wait_ge(s_v.h, last["V2"][(c, h)])
                    st(sc.activation(out=expd[hp][:], in_=dcl[hp][:], func=AF.Exp))
                    st(sc.activation(out=expa[hp][:], in_=pA[:], func=AF.Exp))
                    sc.sem_inc(s_s.h, 1)
                b.sec("scalar", _s1)
                s_s.inc(1)
                last["S1"][(c, h)] = s_s.val

                def _v3(v, c=c, cb=cb, h=h, hp=hp):
                    st = chainer(v, s_v)
                    v.wait_ge(s_s.h, last["S1"][(c, h)])
                    v.wait_ge(s_s.h, sc_c)
                    st(v.tensor_tensor(out=hsb[hp][:], in0=cbm[cb][:],
                                       in1=expd[hp][:], op=OP.mult))
                    st(v.tensor_tensor(out=cp[hp][:], in0=CT[cb][:],
                                       in1=expa[hp][:], op=OP.mult))
                    st(v.tensor_scalar(out=bw[hp][:], in0=Btm[cb][:],
                                       scalar1=csc[cb][:, 48 + h:49 + h],
                                       scalar2=None, op0=OP.mult))
                    v.sem_inc(s_v.h, 1)
                b.sec("vector", _v3)
                s_v.inc(1)
                last["V3"][(c, h)] = s_v.val

                def _t2(tq, c=c, h=h, hp=hp):
                    k = c * NH + h
                    tq.wait_ge(s_v.h, last["V3"][(c, h)])
                    tq.wait_ge(s_v.h, stbf_at[h])
                    if k >= 1:
                        kp = divmod(k - 1, NH)
                        tq.wait_ge(s_v.h, last["V4"][kp])
                    tq.matmul(pY[:], Stbf[:, h * 128:(h + 1) * 128], cp[hp][:],
                              start=True, stop=False)
                    tq.matmul(pY[:], xp[hp][:], hsb[hp][:],
                              start=False, stop=True)
                    tq.matmul(pS[:], bw[hp][:], xp[hp][:],
                              start=True, stop=True).then_inc(s_t.h, 1)
                b.sec("tensor", _t2)
                s_t.inc(1)
                last["T2"][(c, h)] = s_t.val

                def _v4(v, c=c, cb=cb, h=h, hp=hp):
                    st = chainer(v, s_v)
                    v.wait_ge(s_t.h, last["T2"][(c, h)])
                    v.wait_ge(s_v.h, db_at)
                    k = c * NH + h
                    if k >= 2:
                        v.wait_ge(s_st2[hp].h, ytst.get(k - 2, 0))
                    st(v.scalar_tensor_tensor(out=yt[hp][:], in0=xhT[cb][:, h * 128:(h + 1) * 128],
                                              scalar=Db[:, h:h + 1], in1=pY[:],
                                              op0=OP.mult, op1=OP.add))
                    st(v.scalar_tensor_tensor(out=St[:, h * 128:(h + 1) * 128],
                                              in0=St[:, h * 128:(h + 1) * 128],
                                              scalar=csc[cb][:, 32 + h:33 + h],
                                              in1=pS[:], op0=OP.mult, op1=OP.add))
                    st(v.tensor_copy(out=Stbf[:, h * 128:(h + 1) * 128],
                                     in_=St[:, h * 128:(h + 1) * 128]))
                    v.sem_inc(s_v.h, 1)
                b.sec("vector", _v4)
                s_v.inc(1)
                last["V4"][(c, h)] = s_v.val
                stbf_at[h] = s_v.val

                def _g1(g, c=c, h=h, hp=hp):
                    g.wait_ge(s_v.h, last["V4"][(c, h)])
                    g.dma_start(out=yT[h * 128:(h + 1) * 128, c * 128:(c + 1) * 128],
                                in_=yt[hp][:]).then_inc(s_st2[hp].h, 16)
                b.sec("gpsimd", _g1)
                s_st2[hp].inc(16)
                ytst[c * NH + h] = s_st2[hp].val
                last["G1"][(c, h)] = s_st2[hp].val

        def _fin(g):
            g.wait_ge(s_st2[0].h, s_st2[0].val)
            g.wait_ge(s_st2[1].h, s_st2[1].val)
        b.sec("gpsimd", _fin)


def st_mamba_gate(b, name, yT, zxT, y2T, rmsw_cm, Tn=T, R=2048):
    nch = R // 128
    with b.stage(name) as sctx:
        s_pro, s_t, s_s, s_v = (b.sem() for _ in range(4))
        s_ld2 = b.sem2()
        s_st2 = b.sem2_sw()
        rw = b.sb(sctx, f"{name}_rw", [128, nch], F32)
        onec = b.sb(sctx, f"{name}_oc", [128, 1], BF16)
        one1 = b.sb(sctx, f"{name}_o1", [1, 128], BF16)
        yy = [[b.sb(sctx, f"{name}_y{i}c{c}", [128, TT], BF16) for c in range(nch)]
              for i in range(2)]
        zz = [[b.sb(sctx, f"{name}_z{i}c{c}", [128, TT], BF16) for c in range(nch)]
              for i in range(2)]
        gg = [[b.sb(sctx, f"{name}_g{i}c{c}", [128, TT], F32) for c in range(nch)]
              for i in range(2)]
        g2 = b.sb(sctx, f"{name}_g2", [128, TT], BF16)
        sq = [b.sb(sctx, f"{name}_sq{i}", [1, TT], F32) for i in range(2)]
        rr = [b.sb(sctx, f"{name}_rr{i}", [1, TT], BF16) for i in range(2)]
        oo = [b.sb(sctx, f"{name}_oo{i}", [128, TT], BF16) for i in range(2)]
        pq = b.ps(sctx, f"{name}_pq", [1, TT], F32)
        pb = b.ps(sctx, f"{name}_pb", [128, TT], F32)

        def _pro(sync):
            sync.dma_start(out=rw[:], in_=rmsw_cm).then_inc(s_pro.h, 16)
        b.sec("sync", _pro)
        s_pro.inc(16)
        pro_ld = s_pro.val

        def _v0(v):
            v.memset(onec[:], 1.0)
            v.memset(one1[:], 1.0)
            v.drain()
            v.sem_inc(s_v.h, 1)
        b.sec("vector", _v0)
        s_v.inc(1)
        ones_at = s_v.val

        v_done = {}
        st_done = {}
        for t5 in range(Tn // TT):
            tb = t5 % 2

            def _ld(sync, t5=t5, tb=tb):
                if t5 >= 2:
                    sync.wait_ge(s_v.h, v_done[t5 - 2])
                    sync.wait_ge(s_st2[tb].h, st_done[t5 - 2])
                for c in range(nch):
                    sync.dma_start(out=yy[tb][c][:],
                                   in_=yT[c * 128:(c + 1) * 128,
                                          t5 * TT:(t5 + 1) * TT]).then_inc(s_ld2[tb].h, 16)
                    sync.dma_start(out=zz[tb][c][:],
                                   in_=zxT[c * 128:(c + 1) * 128,
                                           t5 * TT:(t5 + 1) * TT]).then_inc(s_ld2[tb].h, 16)
            b.sec("sync", _ld)
            s_ld2[tb].inc(16 * 2 * nch)
            ld_t = s_ld2[tb].val

            g2_at = {}
            for c in range(nch):
                def _sc(sc, t5=t5, tb=tb, c=c, ld_t=ld_t):
                    sc.wait_ge(s_ld2[tb].h, ld_t)
                    sc.activation(out=gg[tb][c][:], in_=zz[tb][c][:],
                                  func=AF.Silu).then_inc(s_s.h, 1)
                b.sec("scalar", _sc)
                s_s.inc(1)
                sil = s_s.val

                def _vg(v, t5=t5, tb=tb, c=c, sil=sil):
                    st = chainer(v, s_v)
                    v.wait_ge(s_s.h, sil)
                    if c > 0 or t5 > 0:
                        v.wait_ge(s_t.h, mm_at)
                    st(v.tensor_tensor(out=gg[tb][c][:], in0=yy[tb][c][:],
                                       in1=gg[tb][c][:], op=OP.mult))
                    st(v.tensor_tensor(out=g2[:], in0=gg[tb][c][:],
                                       in1=gg[tb][c][:], op=OP.mult))
                    v.sem_inc(s_v.h, 1)
                b.sec("vector", _vg)
                s_v.inc(1)
                gv = s_v.val

                def _tm(tq, t5=t5, c=c, gv=gv):
                    tq.wait_ge(s_v.h, gv)
                    tq.wait_ge(s_v.h, ones_at)
                    if t5 >= 1 and c == 0:
                        tq.wait_ge(s_v.h, rr_ev)
                    tq.matmul(pq[:], onec[:], g2[:], start=(c == 0),
                              stop=(c == nch - 1)).then_inc(s_t.h, 1)
                b.sec("tensor", _tm)
                s_t.inc(1)
                mm_at = s_t.val

            def _vq(v, t5=t5, tb=tb):
                v.wait_ge(s_t.h, mm_at)
                v.tensor_scalar(out=sq[tb][:], in0=pq[:], scalar1=1.0 / R,
                                scalar2=None, op0=OP.mult).then_inc(s_v.h, 1)
            b.sec("vector", _vq)
            s_v.inc(1)
            sq_at = s_v.val

            def _sq(sc, t5=t5, tb=tb, sq_at=sq_at):
                sc.wait_ge(s_v.h, sq_at)
                sc.activation(out=sq[tb][:], in_=sq[tb][:], func=AF.Sqrt,
                              bias=1e-5).then_inc(s_s.h, 1)
            b.sec("scalar", _sq)
            s_s.inc(1)
            rt_at = s_s.val

            def _vr(v, t5=t5, tb=tb, rt_at=rt_at):
                st = chainer(v, s_v)
                v.wait_ge(s_s.h, rt_at)
                st(v.reciprocal(out=sq[tb][:], in_=sq[tb][:]))
                st(v.tensor_copy(out=rr[tb][:], in_=sq[tb][:]))
                v.sem_inc(s_v.h, 1)
            b.sec("vector", _vr)
            s_v.inc(1)
            rr_at = s_v.val

            def _tb(tq, t5=t5, tb=tb, rr_at=rr_at):
                tq.wait_ge(s_v.h, rr_at)
                if t5 >= 1:
                    tq.wait_ge(s_v.h, bc_used)
                tq.matmul(pb[:], one1[:], rr[tb][:], start=True,
                          stop=True).then_inc(s_t.h, 1)
            b.sec("tensor", _tb)
            s_t.inc(1)
            bc_at = s_t.val
            rr_ev = s_v.val

            ev_list = []
            for c in range(nch):
                pk = c % 2

                def _vo(v, t5=t5, tb=tb, c=c, pk=pk, bc_at=bc_at):
                    v.wait_ge(s_t.h, bc_at)
                    v.wait_ge(s_pro.h, pro_ld)
                    if c >= 2:
                        v.wait_ge(s_st2[pk].h, ev_list[c - 2][1])
                    v.scalar_tensor_tensor(out=oo[pk][:], in0=gg[tb][c][:],
                                           scalar=rw[:, c:c + 1], in1=pb[:],
                                           op0=OP.mult, op1=OP.mult).then_inc(s_v.h, 1)
                b.sec("vector", _vo)
                s_v.inc(1)
                vo_at = s_v.val

                def _st(g, t5=t5, c=c, pk=pk, vo_at=vo_at):
                    g.wait_ge(s_v.h, vo_at)
                    g.dma_start(out=y2T[c * 128:(c + 1) * 128,
                                        t5 * TT:(t5 + 1) * TT],
                                in_=oo[pk][:]).then_inc(s_st2[pk].h, 16)
                b.sec("gpsimd", _st)
                s_st2[pk].inc(16)
                ev_list.append((vo_at, s_st2[pk].val))
            bc_used = s_v.val
            v_done[t5] = s_v.val
            st_done[t5] = s_st2[tb].val

        def _fin(g):
            g.wait_ge(s_st2[0].h, s_st2[0].val)
            g.wait_ge(s_st2[1].h, s_st2[1].val)
        b.sec("gpsimd", _fin)


def emit_L0A(b, W, xr_in, xr_out, uid):
    z0 = b.dtmp(f"l0a_z0_{uid}", [T, D], F32)
    m0T = b.dtmp(f"l0a_m0T_{uid}", [D, T], BF16)
    zxT = b.dtmp(f"l0a_zxT_{uid}", [2048, T], BF16)
    xbcT = b.dtmp(f"l0a_xbcT_{uid}", [2304, T], BF16)
    dtrawT = b.dtmp(f"l0a_dtr_{uid}", [128, T], F32)
    convT = b.dtmp(f"l0a_conv_{uid}", [2304, T], BF16)
    dta_tm = b.dtmp(f"l0a_dta_{uid}", [T, 48], F32)
    yT = b.dtmp(f"l0a_yT_{uid}", [2048, T], BF16)
    y2T = b.dtmp(f"l0a_y2T_{uid}", [2048, T], BF16)
    st_ln(b, xr_in, z=z0, ident=W["ident_bf"])
    st_ln(b, z0, zT=m0T, gamma=W["l0_ln_w"], beta=W["l0_ln_b"], ident=W["ident_bf"])
    st_mm_fm(b, f"l0a_in_{uid}", [dict(x=m0T, w=W["l0_winp"], K=D, wlayout="panels")],
             4352, [(zxT, 0, 2048), (xbcT, 2048, 4352)])
    st_mm_fm(b, f"l0a_dt_{uid}", [dict(x=m0T, w=W["l0_windt"], K=D)],
             128, [(dtrawT, 0, 128)], out_dt=F32)
    st_mamba_conv(b, f"l0a_cv_{uid}", xbcT, convT, W["l0_conv_w"], W["l0_conv_b"])
    st_mamba_dt(b, f"l0a_sp_{uid}", dtrawT, dta_tm, W["l0_dt_bias"],
                W["l0_A_log"], W["ident_f32"])
    st_mamba_ssd(b, f"l0a_ssd_{uid}", convT, dta_tm, yT, W["l0_D"],
                 W["tri_incl"], W["triU"], W["trimask"], W["ident_bf"])
    st_mamba_gate(b, f"l0a_gt_{uid}", yT, zxT, y2T, W["l0_rmsw"])

    def ev_v(E):
        E.step(E.q.tensor_tensor(out=E.tmp, in0=E.psum, in1=E.aux[0], op=OP.add))
        return E.q.tensor_tensor(out=E.out, in0=E.tmp, in1=E.aux[1], op=OP.add)
    st_mm_tm(b, f"l0a_out_{uid}", [dict(x=y2T, w=W["l0_wout"], K=2048)], D,
             xr_out, evac_v=ev_v, aux=[(xr_in, F32), (z0, F32)], out_dt=F32)


class WReg:

    def __init__(self, b, host):
        self.b = b
        self.host = host

    def __getitem__(self, name):
        arr = self.host[name]
        dt = BF16 if arr.dtype == BF16NP else F32
        return self.b.din(name, list(arr.shape), dt)


def host_weights(inp):
    H = dict(host_constants())
    H["fff_cs"] = colmat(H["fff_cs"].ravel())
    H["triU"] = H.pop("triU_strict")
    H["trimask"] = H.pop("trimask_bf")
    f32 = np.float32
    asf = np.ascontiguousarray
    H["l0_ln_w"] = inp["L0_attn_ln_w"].reshape(1, -1)
    H["l0_ln_b"] = inp["L0_attn_ln_b"].reshape(1, -1)
    Win = np.asarray(inp["L0_attn_W_in"], f32)
    H["l0_winp"] = panels(asf(Win[:, :4352]))
    wdt = np.zeros((D, 128), f32)
    wdt[:, :16] = Win[:, 4352:]
    H["l0_windt"] = wdt
    H["l0_conv_w"] = np.asarray(inp["L0_attn_conv_w"], f32)
    H["l0_conv_b"] = inp["L0_attn_conv_b"].reshape(-1, 1)
    H["l0_dt_bias"] = inp["L0_attn_dt_bias"].reshape(-1, 1)
    H["l0_A_log"] = inp["L0_attn_A_log"].reshape(-1, 1)
    H["l0_D"] = inp["L0_attn_D"].reshape(1, -1)
    H["l0_rmsw"] = colmat(np.asarray(inp["L0_attn_rms_w"], f32))
    H["l0_wout"] = np.asarray(inp["L0_attn_W_out"], f32)
    for n in "qkvf":
        p = f"l0_{n}_"
        nw = np.asarray(inp[f"L0_ffn_{n}_nw"], f32)
        nb = np.asarray(inp[f"L0_ffn_{n}_nb"], f32)
        w1 = np.asarray(inp[f"L0_ffn_{n}_w1"], f32)
        b1 = np.asarray(inp[f"L0_ffn_{n}_b1"], f32)
        w2 = np.asarray(inp[f"L0_ffn_{n}_w2"], f32)
        b2 = np.asarray(inp[f"L0_ffn_{n}_b2"], f32)
        nwT = np.zeros((D, 256), f32)
        nwT[:, :255] = nw.T
        nbp = np.zeros(256, f32)
        nbp[:255] = nb
        H[p + "nwT"] = nwT
        H[p + "nbc"] = colmat(nbp)
        H[p + "w1p"] = panels(asf(w1.transpose(1, 0, 2).reshape(D, 4096)))
        H[p + "b1c"] = colmat(b1.reshape(-1))
        W2s = asf(w2.reshape(4096, D))
        H[p + "w2p"] = panels(W2s)
        H[p + "w2r"] = W2s
        H[p + "b2s"] = b2
    H["l1_wq"] = np.asarray(inp["L1_attn_Wq"], f32)
    H["l1_wk"] = np.asarray(inp["L1_attn_Wk"], f32)
    H["l1_wv"] = np.asarray(inp["L1_attn_Wv"], f32)
    H["l1_bqc"] = colmat(np.asarray(inp["L1_attn_bq"], f32))
    H["l1_bkc"] = colmat(np.asarray(inp["L1_attn_bk"], f32))
    H["l1_bv"] = inp["L1_attn_bv"].reshape(1, -1)
    H["l1_gamma"] = inp["L1_attn_gamma"].reshape(1, -1)
    H["l1_beta"] = inp["L1_attn_beta"].reshape(1, -1)
    H["l1_w1T"] = panels(asf(np.asarray(inp["L1_ffn_w1_w"], f32).T))
    H["l1_w2T"] = panels(asf(np.asarray(inp["L1_ffn_w2_w"], f32).T))
    H["l1_h1T"] = panels(asf(np.asarray(inp["L1_ffn_h1_w"], f32).T))
    H["l1_h2T"] = panels(asf(np.asarray(inp["L1_ffn_h2_w"], f32).T))
    H["l1_w3T"] = panels(asf(np.asarray(inp["L1_ffn_w3_w"], f32).T))
    H["l1_w1s"] = colmat(np.asarray(inp["L1_ffn_w1_s"], f32))
    H["l1_w2s"] = colmat(np.asarray(inp["L1_ffn_w2_s"], f32))
    H["l1_h1s"] = colmat(np.asarray(inp["L1_ffn_h1_s"], f32))
    H["l1_h2s"] = colmat(np.asarray(inp["L1_ffn_h2_s"], f32))
    H["l1_w3s"] = colmat(np.asarray(inp["L1_ffn_w3_s"], f32))
    H["l2_wq"] = np.asarray(inp["L2_attn_Wq"], f32)
    H["l2_wk"] = np.asarray(inp["L2_attn_Wk"], f32)
    H["l2_wv"] = np.asarray(inp["L2_attn_Wv"], f32)
    H["l2_wo"] = np.asarray(inp["L2_attn_Wo"], f32)
    H["l2_w"] = np.asarray(inp["L2_ffn_weight"], f32)
    H["l2_a1"] = np.asarray(inp["L2_ffn_aux_weight1"], f32)
    H["l2_a2"] = np.asarray(inp["L2_ffn_aux_weight2"], f32)
    H["l2_bias"] = inp["L2_ffn_bias"].reshape(1, -1)
    wqkv = np.asarray(inp["L3_attn_w_qkv"], f32)
    H["l3_wqk"] = asf(wqkv[:2048].T)
    H["l3_wv"] = asf(wqkv[2048:].T)
    H["l3_wout"] = asf(np.asarray(inp["L3_attn_w_out"], f32).T)
    H["l3_sw1"] = np.asarray(inp["L3_ffn_sw1"], f32)
    H["l3_sw3"] = np.asarray(inp["L3_ffn_sw3"], f32)
    H["l3_sw2"] = np.asarray(inp["L3_ffn_sw2"], f32)
    H["l3_sb1c"] = colmat(np.asarray(inp["L3_ffn_sb1"], f32))
    H["l3_sb3c"] = colmat(np.asarray(inp["L3_ffn_sb3"], f32))
    H["l3_sb2"] = inp["L3_ffn_sb2"].reshape(1, -1)
    H["l3_w1"] = np.asarray(inp["L3_ffn_w1"], f32)
    H["l3_b1c"] = np.stack([colmat(np.asarray(inp["L3_ffn_b1"][e], f32))
                            for e in range(8)])
    H["l3_w2"] = np.asarray(inp["L3_ffn_w2"], f32)
    H["l3_b2"] = np.asarray(inp["L3_ffn_b2"], f32)
    H = {k: (np.ascontiguousarray(v) if isinstance(v, np.ndarray) else v)
         for k, v in H.items()}
    return H


SUBLAYERS = [
    ("L0F", "s0"), ("L0A", "s1"), ("L0F", "s2"),
    ("L1F", "s3"), ("L1A", "s4"), ("L1F", "s5"),
    ("L2F", "s6"), ("L2A", "s7"), ("L2F", "s8"),
    ("L3F", "s9"), ("L3A", "s10"), ("L3F", "s11"),
]
EMITTERS = dict(L0F=emit_L0F, L0A=emit_L0A, L1F=emit_L1F, L1A=emit_L1A,
                L2F=emit_L2F, L2A=emit_L2A, L3F=emit_L3F, L3A=emit_L3A)

MODE = "unfused12"


def kernel(**inputs):
    H = host_weights(inputs)
    x = np.ascontiguousarray(np.asarray(inputs["L0_attn_x"], np.float32))
    shards = [np.ascontiguousarray(x[c]) for c in range(8)]

    if MODE == "fused":
        b = Build("mk")
        xr_in0 = b.din("x_in", [T, D], F32)
        xr_out_f = b.dout("x_out", [T, D], F32)
        xa = b.dtmp("xr_a", [T, D], F32)
        xb_ = b.dtmp("xr_b", [T, D], F32)
        wreg = WReg(b, H)
        b.open()
        cur = xr_in0
        bufs = [xa, xb_]
        for i, (kind, uid) in enumerate(SUBLAYERS):
            nxt = xr_out_f if i == len(SUBLAYERS) - 1 else bufs[i % 2]
            EMITTERS[kind](b, wreg, cur, nxt, uid)
            cur = nxt
        nc = b.close()
        in_maps = []
        for c in range(8):
            im = {n: H[n] for n in b.input_names if n != "x_in"}
            im["x_in"] = shards[c]
            in_maps.append(im)
        res = run_bass_kernel_spmd(nc, in_maps, core_ids=list(range(8)))
        return np.stack([np.asarray(r["x_out"], np.float32) for r in res.results])

    cur_shards = shards
    for i, (kind, uid) in enumerate(SUBLAYERS):
        b = Build(f"mk_{uid}")
        xr_in = b.din("x_in", [T, D], F32)
        xr_out = b.dout("x_out", [T, D], F32)
        wreg = WReg(b, H)
        b.open()
        EMITTERS[kind](b, wreg, xr_in, xr_out, uid)
        nc = b.close()
        in_maps = []
        for c in range(8):
            im = {n: H[n] for n in b.input_names if n != "x_in"}
            im["x_in"] = cur_shards[c]
            in_maps.append(im)
        res = run_bass_kernel_spmd(nc, in_maps, core_ids=list(range(8)))
        cur_shards = [np.ascontiguousarray(np.asarray(r["x_out"], np.float32))
                      for r in res.results]
    return np.stack(cur_shards)


# version 09
# speedup vs baseline: 1.4988x; 1.4988x over previous
import numpy as np
import ml_dtypes
import concourse.bass as bass
import concourse.mybir as mybir
from concourse.bass_utils import run_bass_kernel_spmd
from concourse.alu_op_type import AluOpType as OP
from contextlib import ExitStack, contextmanager

F32 = mybir.dt.float32
BF16 = mybir.dt.bfloat16
AF = mybir.ActivationFunctionType
AX = mybir.AxisListType.X
BF16NP = ml_dtypes.bfloat16

T = 4096
D = 1024
TT = 512
NT128 = T // 128

ENGINES = ("sync", "tensor", "scalar", "vector", "gpsimd")


class Sem:
    def __init__(self, h):
        self.h = h
        self.val = 0

    def inc(self, n=1):
        self.val += n
        return self.val


class _QProxy:

    def __init__(self, b, eng):
        self._b = b
        self._eng = eng
        self._q = None

    def _bind(self, q):
        self._q = q

    def wait_ge(self, sem, val, **kw):
        if val <= 0:
            return None
        key = (self._eng, id(sem))
        seen = self._b._waited.get(key, -1)
        if val <= seen:
            return None
        self._b._waited[key] = val
        return self._q.wait_ge(sem, val, **kw)

    def __getattr__(self, name):
        return getattr(self._q, name)


class Build:

    def __init__(self, name="mk"):
        self.nc = bass.Bass("TRN2", target_bir_lowering=False)
        self.ctx = ExitStack()
        self.dram = {}
        self.input_names = []
        self.output_names = []
        self._block = None
        self._sems = []
        self._swsems = []
        self._sem_i = 0
        self._swsem_i = 0
        self._waited = {}
        self._pend = []
        self.bar = None
        self.stage_n = 0
        self.name = name

    def din(self, name, shape, dt):
        if name in self.dram:
            return self.dram[name]
        ap = self.nc.dram_tensor(name, list(shape), dt, kind="ExternalInput").ap()
        self.dram[name] = ap
        self.input_names.append(name)
        return ap

    def dout(self, name, shape, dt):
        ap = self.nc.dram_tensor(name, list(shape), dt, kind="ExternalOutput").ap()
        self.dram[name] = ap
        self.output_names.append(name)
        return ap

    def dtmp(self, name, shape, dt):
        if name in self.dram:
            return self.dram[name]
        ap = self.nc.dram_tensor(name, list(shape), dt, kind="Internal").ap()
        self.dram[name] = ap
        return ap

    def open(self, consts=()):
        for i in range(24):
            self._sems.append(Sem(self.ctx.enter_context(self.nc.semaphore(f"sp{i}"))))
        for i in range(24):
            self._swsems.append(Sem(self.ctx.enter_context(self.nc.semaphore(f"sw{i}"))))
        self.bar = Sem(self.ctx.enter_context(self.nc.semaphore("bar")))
        self._block = self.ctx.enter_context(self.nc.Block())
        self.csem = {e: Sem(self.ctx.enter_context(self.nc.semaphore(f"ch_{e}")))
                     for e in ENGINES}
        cvals = [0.0, 1.0, 1e-5] + [v for v in consts if v not in (0.0, 1.0, 1e-5)]
        self._csb = self.ctx.enter_context(
            self.nc.sbuf_tensor("constf", [128, len(cvals)], F32))
        with self.stage("consts"):
            def _m(v):
                for j, val in enumerate(cvals):
                    v.memset(self._csb[:, j:j + 1], float(val))
            self.sec("vector", _m)
        for j, val in enumerate(cvals):
            self.nc.const_aps.register(F32, float(val), self._csb[:, j:j + 1])

    def close(self):
        self.ctx.close()
        return self.nc

    def sem(self):
        s = self._sems[self._sem_i % len(self._sems)]
        self._sem_i += 1
        return s

    def sem2(self):
        return [self.sem(), self.sem()]

    def sem_sw(self):
        s = self._swsems[self._swsem_i % len(self._swsems)]
        self._swsem_i += 1
        return s

    def sem2_sw(self):
        return [self.sem_sw(), self.sem_sw()]

    def sec(self, eng, fn):
        prox = _QProxy(self, eng)
        getattr(self._block, eng)(lambda q, f=fn, p=prox: (p._bind(q), f(p))[-1])

    def emit(self, eng, fn):
        self._pend.append((eng, fn))

    def flush(self):
        for e in ENGINES:
            fns = [f for (ee, f) in self._pend if ee == e]
            if fns:
                self.sec(e, lambda q, fns=tuple(fns): [f(q) for f in fns])
        self._pend.clear()

    @contextmanager
    def stage(self, name=""):
        k = self.stage_n
        base = 6 * k

        def _wsync(q):
            q.wait_ge(self.bar.h, base)
            for s in self._sems:
                q.sem_clear(s.h)
            for s in self.csem.values():
                q.sem_clear(s.h)
            q.sem_inc(self.bar.h, 1)
        self.sec("sync", _wsync)
        pool_ids = {id(s.h) for s in self._sems}
        pool_ids |= {id(s.h) for s in self.csem.values()}
        self._waited = {k: v for k, v in self._waited.items()
                        if k[1] not in pool_ids}
        for s in self._sems:
            s.val = 0
        for s in self.csem.values():
            s.val = 0
        for e in ENGINES:
            if e == "sync":
                continue

            def _w(q, base=base):
                q.wait_ge(self.bar.h, base + 1)
            self.sec(e, _w)
        sctx = ExitStack()
        try:
            yield sctx
        finally:
            sctx.close()
            self.stage_n += 1
            used = [(s.h, s.val) for s in self._sems if s.val > 0]
            used += [(s.h, s.val) for s in self._swsems if s.val > 0]
            used += [(s.h, s.val) for s in self.csem.values() if s.val > 0]
            for e in ENGINES:
                def _i(q, e=e):
                    for (h, v) in used:
                        q.wait_ge(h, v)
                    if e in ("vector", "scalar", "gpsimd"):
                        q.drain()
                    q.sem_inc(self.bar.h, 1)
                self.sec(e, _i)

    def sb(self, sctx, name, shape, dt):
        return sctx.enter_context(
            self.nc.sbuf_tensor(f"{name}_{self.stage_n}", list(shape), dt))

    def ps(self, sctx, name, shape, dt=F32):
        return sctx.enter_context(
            self.nc.psum_tensor(f"{name}_{self.stage_n}", list(shape), dt))


def cdiv(a, b):
    return (a + b - 1) // b


def chainer(q, s):
    def step(ins):
        tgt = s.inc(1)
        ins.then_inc(s.h, 1)
        q.wait_ge(s.h, tgt)
    return step


def st_ln(b, xr, plus=(), write_x=None, z=None, zT=None, zT32=None,
          gamma=None, beta=None, variant="ln", ident=None, ident32=None):
    with b.stage("ln") as sctx:
        xa = [b.sb(sctx, f"ln_xa{i}", [128, D], F32) for i in range(2)]
        zc = [b.sb(sctx, f"ln_zc{i}", [128, D], F32) for i in range(2)]
        sq = [b.sb(sctx, f"ln_sq{i}", [128, D], F32) for i in range(2)]
        dl = [[b.sb(sctx, f"ln_d{i}_{j}", [128, D], F32) for j in range(len(plus))]
              for i in range(2)]
        co = [b.sb(sctx, f"ln_co{i}", [128, 4], F32) for i in range(2)]
        zbf = [b.sb(sctx, f"ln_zbf{i}", [128, D], BF16) for i in range(2)] if zT is not None else None
        s_pro = b.sem()
        s_ld2 = b.sem2()
        s_st2 = b.sem2()
        s_zt2 = b.sem2()
        s_v, s_s, s_t = b.sem(), b.sem(), b.sem()

        use_gb = gamma is not None
        if use_gb:
            gb = b.sb(sctx, "ln_gb", [128, 2 * D], F32)
            ones1 = b.sb(sctx, "ln_ones", [1, 128], F32)
            grow = b.sb(sctx, "ln_grow", [1, 2 * D], F32)
            p_gb = b.ps(sctx, "ln_pgb", [128, 2048], F32)
        if zT is not None:
            id_sb = b.sb(sctx, "ln_id", [128, 128], BF16)
            p_tr = [b.ps(sctx, f"ln_ptr{i}", [128, 1024], BF16) for i in range(2)]
            ztr = [b.sb(sctx, f"ln_ztr{i}", [128, 1024], BF16) for i in range(2)]
        if zT32 is not None:
            id32_sb = b.sb(sctx, "ln_id32", [128, 128], F32)
            p_tr32 = [b.ps(sctx, f"ln_p32{i}", [128, 512], F32) for i in range(2)]
            ztr32 = [b.sb(sctx, f"ln_z32{i}", [128, 512], F32) for i in range(2)]

        def _ld0(sync):
            if use_gb:
                sync.dma_start(out=grow[0:1, 0:D], in_=gamma).then_inc(s_pro.h, 16)
                sync.dma_start(out=grow[0:1, D:2 * D], in_=beta).then_inc(s_pro.h, 16)
            if zT is not None:
                sync.dma_start(out=id_sb[:], in_=ident).then_inc(s_pro.h, 16)
            if zT32 is not None:
                sync.dma_start(out=id32_sb[:], in_=ident32).then_inc(s_pro.h, 16)
        b.sec("sync", _ld0)
        s_pro.inc(16 * ((2 if use_gb else 0) + (1 if zT is not None else 0) +
                        (1 if zT32 is not None else 0)))
        ld_pro = s_pro.val

        if use_gb:
            def _v0(v):
                v.memset(ones1[:], 1.0).then_inc(s_v.h, 1)
            b.sec("vector", _v0)
            s_v.inc(1)
            v_ones = s_v.val

            def _t0(t):
                t.wait_ge(s_pro.h, ld_pro)
                t.wait_ge(s_v.h, v_ones)
                for j in range(4):
                    t.matmul(p_gb[:, j * 512:(j + 1) * 512], ones1[:],
                             grow[0:1, j * 512:(j + 1) * 512], start=True,
                             stop=True).then_inc(s_t.h, 1)
            b.sec("tensor", _t0)
            s_t.inc(4)
            t_gb = s_t.val

            def _v1(v):
                v.wait_ge(s_t.h, t_gb)
                v.tensor_copy(out=gb[:], in_=p_gb[:]).then_inc(s_v.h, 1)
            b.sec("vector", _v1)
            s_v.inc(1)
        gb_ready = s_v.val

        v_done = []
        st_done = []
        zt_ev = []
        zt_st = []

        n_z = D // 128
        for i in range(NT128):
            bi = i % 2
            r0 = i * 128

            def _ld(sync, i=i, bi=bi, r0=r0):
                if i >= 2:
                    sync.wait_ge(s_v.h, v_done[i - 2])
                    sync.wait_ge(s_st2[bi].h, st_done[i - 2])
                sync.dma_start(out=xa[bi][:], in_=xr[r0:r0 + 128, :]).then_inc(s_ld2[bi].h, 16)
                for j in range(len(plus)):
                    sync.dma_start(out=dl[bi][j][:],
                                   in_=plus[j][0][r0:r0 + 128, :]).then_inc(s_ld2[bi].h, 16)
            b.emit("sync", _ld)
            s_ld2[bi].inc(16 * (1 + len(plus)))
            ld_i = s_ld2[bi].val

            def _va(v, bi=bi, ld_i=ld_i):
                st = chainer(v, b.csem["vector"])
                v.wait_ge(s_ld2[bi].h, ld_i)
                if use_gb:
                    v.wait_ge(s_v.h, gb_ready)
                for j in range(len(plus)):
                    st(v.scalar_tensor_tensor(out=xa[bi][:], in0=dl[bi][j][:],
                                              scalar=float(plus[j][1]), in1=xa[bi][:],
                                              op0=OP.mult, op1=OP.add))
                st(v.reduce_sum(out=co[bi][:, 0:1], in_=xa[bi][:], axis=AX))
                st(v.tensor_scalar(out=co[bi][:, 0:1], in0=co[bi][:, 0:1],
                                   scalar1=1.0 / D, scalar2=None, op0=OP.mult))
                v.tensor_scalar(out=zc[bi][:], in0=xa[bi][:], scalar1=co[bi][:, 0:1],
                                scalar2=None, op0=OP.subtract).then_inc(s_v.h, 1)
            b.emit("vector", _va)
            s_v.inc(1)
            centered = s_v.val

            def _sa(sc, bi=bi, centered=centered):
                sc.wait_ge(s_v.h, centered)
                sc.activation(out=sq[bi][:], in_=zc[bi][:], func=AF.Square).then_inc(s_s.h, 1)
            b.emit("scalar", _sa)
            s_s.inc(1)
            squared = s_s.val

            def _vb(v, bi=bi, squared=squared):
                v.wait_ge(s_s.h, squared)
                v.reduce_sum(out=co[bi][:, 1:2], in_=sq[bi][:], axis=AX).then_inc(s_v.h, 1)
            b.emit("vector", _vb)
            s_v.inc(1)
            ssq = s_v.val

            def _sb_(sc, bi=bi, ssq=ssq):
                sc.wait_ge(s_v.h, ssq)
                if variant == "ln":
                    sc.activation(out=co[bi][:, 2:3], in_=co[bi][:, 1:2],
                                  func=AF.Sqrt, scale=1.0 / D, bias=1e-5).then_inc(s_s.h, 1)
                else:
                    st = chainer(sc, b.csem["scalar"])
                    st(sc.activation(out=co[bi][:, 3:4], in_=co[bi][:, 1:2],
                                     func=AF.Sqrt, scale=1.0 / (D - 1)))
                    sc.activation(out=co[bi][:, 2:3], in_=co[bi][:, 3:4],
                                  func=AF.Sqrt, bias=1e-5).then_inc(s_s.h, 1)
            b.emit("scalar", _sb_)
            s_s.inc(1)
            denom = s_s.val

            def _vc(v, bi=bi, denom=denom):
                st = chainer(v, b.csem["vector"])
                v.wait_ge(s_s.h, denom)
                st(v.reciprocal(out=co[bi][:, 3:4], in_=co[bi][:, 2:3]))
                st(v.tensor_scalar(out=zc[bi][:], in0=zc[bi][:], scalar1=co[bi][:, 3:4],
                                   scalar2=None, op0=OP.mult))
                if use_gb:
                    st(v.tensor_tensor(out=zc[bi][:], in0=zc[bi][:], in1=gb[:, 0:D], op=OP.mult))
                    st(v.tensor_tensor(out=zc[bi][:], in0=zc[bi][:], in1=gb[:, D:2 * D], op=OP.add))
                if zbf is not None:
                    st(v.tensor_copy(out=zbf[bi][:], in_=zc[bi][:]))
                v.sem_inc(s_v.h, 1)
            b.emit("vector", _vc)
            s_v.inc(1)
            z_ready = s_v.val
            v_done.append(z_ready)

            n_st = (1 if z is not None else 0) + (1 if (write_x is not None and plus) else 0)

            def _st(sc, i=i, bi=bi, r0=r0, z_ready=z_ready):
                sc.wait_ge(s_v.h, z_ready)
                if z is not None:
                    sc.dma_start(out=z[r0:r0 + 128, :], in_=zc[bi][:]).then_inc(s_st2[bi].h, 16)
                if write_x is not None and plus:
                    sc.dma_start(out=write_x[r0:r0 + 128, :], in_=xa[bi][:]).then_inc(s_st2[bi].h, 16)
            b.emit("scalar", _st)
            s_st2[bi].inc(16 * n_st)

            for (on, dst, ptr_, ztr_, nb) in (
                    [("bf", zT, p_tr, ztr, 8)] if zT is not None else []) + (
                    [("f32", zT32, p_tr32, ztr32, 4)] if zT32 is not None else []):
                for g0 in range(0, n_z, nb):
                    k = len(zt_ev)
                    pk = k % 2

                    def _tr(t, bi=bi, g0=g0, k=k, pk=pk, z_ready=z_ready, on=on,
                            ptr_=ptr_, nb=nb):
                        t.wait_ge(s_v.h, z_ready)
                        t.wait_ge(s_pro.h, ld_pro)
                        if k >= 2:
                            t.wait_ge(s_v.h, zt_ev[k - 2])
                        srct = zbf[bi] if on == "bf" else zc[bi]
                        idt = id_sb if on == "bf" else id32_sb
                        for j in range(nb):
                            c = g0 + j
                            ins = t.transpose(ptr_[pk][:, j * 128:(j + 1) * 128],
                                              srct[:, c * 128:(c + 1) * 128],
                                              idt[:])
                        ins.then_inc(s_t.h, 1)
                    b.emit("tensor", _tr)
                    s_t.inc(1)
                    trk = s_t.val

                    def _ev(v, pk=pk, k=k, trk=trk, ptr_=ptr_, ztr_=ztr_, nb=nb):
                        v.wait_ge(s_t.h, trk)
                        if k >= 2:
                            v.wait_ge(s_zt2[pk].h, zt_st[k - 2])
                        v.tensor_copy(out=ztr_[pk][:], in_=ptr_[pk][:]).then_inc(s_v.h, 1)
                    b.emit("vector", _ev)
                    s_v.inc(1)
                    zt_ev.append(s_v.val)
                    evk = s_v.val

                    def _st2(sc, g0=g0, i=i, pk=pk, evk=evk, dst=dst, ztr_=ztr_, nb=nb):
                        sc.wait_ge(s_v.h, evk)
                        sc.dma_start(
                            out=dst[g0 * 128:(g0 + nb) * 128,
                                    i * 128:(i + 1) * 128].rearrange(
                                        "(c p) m -> p c m", p=128),
                            in_=ztr_[pk][:]).then_inc(s_zt2[pk].h, 16)
                    b.emit("scalar", _st2)
                    s_zt2[pk].inc(16)
                    zt_st.append(s_zt2[pk].val)
            st_done.append(s_st2[bi].val)
            b.flush()

        def _fin(sc):
            sc.wait_ge(s_st2[0].h, s_st2[0].val)
            sc.wait_ge(s_st2[1].h, s_st2[1].val)
            sc.wait_ge(s_zt2[0].h, s_zt2[0].val)
            sc.wait_ge(s_zt2[1].h, s_zt2[1].val)
        b.sec("scalar", _fin)


class _EC:
    pass


def _copy_evac(E):
    return E.q.tensor_copy(out=E.out, in_=E.psum)


def st_mm_fm(b, name, segs, N, outs, evac_s=None, evac_v=_copy_evac, cols=(),
             aux=(), Tn=T, out_dt=BF16, group_bytes=12 * 1024 * 1024):
    Nc = cdiv(N, 128)
    dual = any("w2" in s or s.get("alt") for s in segs)
    NB = 2 if dual else 4
    with b.stage(name) as sctx:
        s_pro, s_t, s_s, s_v = (b.sem() for _ in range(4))
        s_wl = b.sem_sw()
        s_ld2 = b.sem2()
        s_ldc2 = b.sem2_sw()
        s_la4 = [b.sem() for _ in range(NB)]
        s_st4 = [b.sem_sw() for _ in range(NB)]

        use_panels = any(s.get("wlayout") == "panels" for s in segs)
        if use_panels:
            pb = sum(s["K"] * 256 * (2 if "w2" in s else 1)
                     for s in segs if s.get("wlayout") == "panels")
            G = max(1, min(Nc, group_bytes // pb))
        else:
            G = Nc
        groups = [list(range(g, min(g + G, Nc))) for g in range(0, Nc, G)]

        for si, s in enumerate(segs):
            Kc = s["K"] // 128
            sdt = s.get("dt", BF16)
            if s.get("wlayout", "rows") == "rows":
                s["_w"] = [b.sb(sctx, f"{name}_w{si}k{kc}", [128, N], sdt) for kc in range(Kc)]
                if "w2" in s:
                    s["_w2"] = [b.sb(sctx, f"{name}_v{si}k{kc}", [128, N], sdt) for kc in range(Kc)]
            else:
                s["_w"] = [b.sb(sctx, f"{name}_wp{si}j{j}", [128, s["K"]], sdt) for j in range(G)]
                if "w2" in s:
                    s["_w2"] = [b.sb(sctx, f"{name}_vp{si}j{j}", [128, s["K"]], sdt) for j in range(G)]
            s["_xm"] = [b.sb(sctx, f"{name}_x{si}b{t}", [128, Kc * TT], sdt)
                        for t in range(2)]
            s["_x"] = [[s["_xm"][t][:, kc * TT:(kc + 1) * TT] for kc in range(Kc)]
                       for t in range(2)]

        p = [b.ps(sctx, f"{name}_p{i}", [128, TT], F32) for i in range(NB)]
        p2 = [b.ps(sctx, f"{name}_q{i}", [128, TT], F32) for i in range(NB)] if dual else None
        ob = [b.sb(sctx, f"{name}_o{i}", [128, TT], out_dt) for i in range(NB)]
        tmp = [b.sb(sctx, f"{name}_t{i}", [128, TT], F32) for i in range(NB)]
        axb = [[b.sb(sctx, f"{name}_a{i}j{j}", [128, TT], aux[j][1]) for j in range(len(aux))]
               for i in range(NB)]
        colsb = b.sb(sctx, f"{name}_c", [128, max(1, Nc * len(cols))], F32) if cols else None

        def _pro(sync):
            for j, c in enumerate(cols):
                sync.dma_start(out=colsb[:, j * Nc:(j + 1) * Nc],
                               in_=c).then_inc(s_pro.h, 16)
        b.sec("sync", _pro)
        s_pro.inc(16 * len(cols))
        cols_ld = s_pro.val

        def _prow(g):
            for si, s in enumerate(segs):
                if s.get("wlayout", "rows") == "rows":
                    for kc in range(s["K"] // 128):
                        g.dma_start(out=s["_w"][kc][:],
                                    in_=s["w"][kc * 128:(kc + 1) * 128, :]).then_inc(s_wl.h, 16)
                        if "w2" in s:
                            g.dma_start(out=s["_w2"][kc][:],
                                        in_=s["w2"][kc * 128:(kc + 1) * 128, :]).then_inc(s_wl.h, 16)
        b.sec("gpsimd", _prow)
        for s in segs:
            if s.get("wlayout", "rows") == "rows":
                s_wl.inc(16 * (s["K"] // 128) * (2 if "w2" in s else 1))

        def outmap(r):
            for (dap, g0, g1) in outs:
                if g0 <= r < g1:
                    return dap, r - g0
            raise AssertionError(f"row {r} unmapped")

        pend = []

        def emit(eng, fn):
            pend.append((eng, fn))

        def flush():
            for eng in ENGINES:
                fns = [f for (e, f) in pend if e == eng]
                if fns:
                    b.sec(eng, lambda q, fns=tuple(fns): [f(q) for f in fns])
            pend.clear()

        mm_at = {}
        ev_v_at = {}
        psum_free = {}
        st_at = {}
        aux_free = {}
        xfree = [0, 0]
        idx = 0
        plan = []
        for gi, group in enumerate(groups):
            if use_panels:
                wait_mm = mm_at[idx - 1] if idx > 0 else 0

                def _pg(g, group=tuple(group), wait_mm=wait_mm):
                    g.wait_ge(s_t.h, wait_mm)
                    for si, s in enumerate(segs):
                        if s.get("wlayout") == "panels":
                            for jj, ncx in enumerate(group):
                                g.dma_start(
                                    out=s["_w"][jj][:],
                                    in_=s["w"][ncx].rearrange("(kc kp) m -> kp kc m", kp=128),
                                ).then_inc(s_wl.h, 16)
                                if "w2" in s:
                                    g.dma_start(
                                        out=s["_w2"][jj][:],
                                        in_=s["w2"][ncx].rearrange("(kc kp) m -> kp kc m", kp=128),
                                    ).then_inc(s_wl.h, 16)
                b.sec("gpsimd", _pg)
                for s in segs:
                    if s.get("wlayout") == "panels":
                        s_wl.inc(16 * len(group) * (2 if "w2" in s else 1))
            wl_ready = s_wl.val

            for t in range(Tn // TT):
                slot = t % 2
                plain = [s for s in segs if not s.get("cast")]
                casts = [s for s in segs if s.get("cast")]

                def _lx(sync, slot=slot, t=t, w=xfree[slot]):
                    sync.wait_ge(s_t.h, w)
                    for s in plain:
                        sync.dma_start(
                            out=s["_xm"][slot][:],
                            in_=s["x"][:, t * TT:(t + 1) * TT].rearrange(
                                "(kc p) m -> p kc m", p=128)).then_inc(s_ld2[slot].h, 16)
                b.sec("sync", _lx)
                if casts:
                    def _lxc(g, slot=slot, t=t, w=xfree[slot]):
                        g.wait_ge(s_t.h, w)
                        for s in casts:
                            g.dma_start(
                                out=s["_xm"][slot][:],
                                in_=s["x"][:, t * TT:(t + 1) * TT].rearrange(
                                    "(kc p) m -> p kc m", p=128)).then_inc(s_ldc2[slot].h, 16)
                    b.sec("gpsimd", _lxc)
                s_ld2[slot].inc(16 * len(plain))
                s_ldc2[slot].inc(16 * len(casts))
                x_ld = s_ld2[slot].val
                xc_ld = s_ldc2[slot].val

                for jj, ncx in enumerate(group):
                    pp = idx % NB
                    cw = min(128, N - ncx * 128)
                    odap, olo = outmap(ncx * 128)

                    if aux:
                        def _la(sync, idx=idx, pp=pp, ncx=ncx, cw=cw, t=t):
                            if idx >= NB:
                                sync.wait_ge(s_v.h, aux_free[idx - NB])
                            for j, (ad, adt) in enumerate(aux):
                                sync.dma_start(
                                    out=axb[pp][j][0:cw, :],
                                    in_=ad[ncx * 128:ncx * 128 + cw,
                                           t * TT:(t + 1) * TT]).then_inc(s_la4[pp].h, 16)
                        emit("sync", _la)
                        s_la4[pp].inc(16 * len(aux))
                    la_tile = s_la4[pp].val

                    def _mm(tq, idx=idx, pp=pp, jj=jj, ncx=ncx, cw=cw, x_ld=x_ld,
                            wl_ready=wl_ready, slot=slot, xc_ld=xc_ld):
                        tq.wait_ge(s_ld2[slot].h, x_ld)
                        tq.wait_ge(s_ldc2[slot].h, xc_ld)
                        tq.wait_ge(s_wl.h, wl_ready)
                        if idx >= NB:
                            tq.wait_ge(*psum_free[idx - NB])
                            if dual:
                                tq.wait_ge(s_v.h, ev_v_at[idx - NB])
                        mms = []
                        for si, s in enumerate(segs):
                            if s.get("alt"):
                                continue
                            for kc in range(s["K"] // 128):
                                if s.get("wlayout", "rows") == "rows":
                                    lhs = s["_w"][kc][:, ncx * 128:ncx * 128 + cw]
                                else:
                                    lhs = s["_w"][jj][:, kc * 128:kc * 128 + cw]
                                mms.append((lhs, s["_x"][slot][kc][:]))
                        for mi, (lap, rap) in enumerate(mms):
                            ins = tq.matmul(p[pp][0:cw, :], lap, rap,
                                            start=(mi == 0), stop=(mi == len(mms) - 1))
                            if mi == len(mms) - 1:
                                ins.then_inc(s_t.h, 1)
                        if dual:
                            m2 = []
                            for si, s in enumerate(segs):
                                if "w2" in s:
                                    ws = s["_w2"]
                                elif s.get("alt"):
                                    ws = s["_w"]
                                else:
                                    continue
                                for kc in range(s["K"] // 128):
                                    if s.get("wlayout", "rows") == "rows":
                                        lhs = ws[kc][:, ncx * 128:ncx * 128 + cw]
                                    else:
                                        lhs = ws[jj][:, kc * 128:kc * 128 + cw]
                                    m2.append((lhs, s["_x"][slot][kc][:]))
                            for mi, (lap, rap) in enumerate(m2):
                                ins = tq.matmul(p2[pp][0:cw, :], lap, rap,
                                                start=(mi == 0), stop=(mi == len(m2) - 1))
                                if mi == len(m2) - 1:
                                    ins.then_inc(s_t.h, 1)
                    emit("tensor", _mm)
                    s_t.inc(2 if dual else 1)
                    mm_at[idx] = s_t.val

                    E = _EC()
                    E.cw, E.ncx, E.nw, E.idx = cw, ncx, TT, idx
                    E.col = (lambda j, cw=cw, ncx=ncx: colsb[0:cw, j * Nc + ncx:j * Nc + ncx + 1])

                    if evac_s is not None:
                        def _es(sc, E=E, pp=pp, cw=cw, idx=idx, la_tile=la_tile):
                            sc.wait_ge(s_t.h, mm_at[idx])
                            if aux:
                                sc.wait_ge(s_la4[pp].h, la_tile)
                            if cols:
                                sc.wait_ge(s_pro.h, cols_ld)
                            if idx >= NB:
                                sc.wait_ge(s_v.h, ev_v_at[idx - NB])
                                if evac_v is None:
                                    sc.wait_ge(s_st4[pp].h, st_at[idx - NB])
                            E.q = sc
                            E.step = chainer(sc, b.csem["scalar"])
                            E.psum = p[pp][0:cw, :]
                            E.psum2 = p2[pp][0:cw, :] if dual else None
                            E.tmp = tmp[pp][0:cw, :]
                            E.out = ob[pp][0:cw, :]
                            E.aux = [a[0:cw, :] for a in axb[pp]]
                            ins = evac_s(E)
                            if ins is not None:
                                ins.then_inc(s_s.h, 1)
                            else:
                                sc.drain()
                                sc.sem_inc(s_s.h, 1)
                        emit("scalar", _es)
                        s_s.inc(1)
                        psum_free[idx] = (s_s.h, s_s.val)
                        sc_at = s_s.val
                    else:
                        sc_at = None

                    if evac_v is not None:
                        def _ev(v, E=E, pp=pp, cw=cw, idx=idx, la_tile=la_tile, sc_at=sc_at):
                            if sc_at is not None:
                                v.wait_ge(s_s.h, sc_at)
                            else:
                                v.wait_ge(s_t.h, mm_at[idx])
                            if aux:
                                v.wait_ge(s_la4[pp].h, la_tile)
                            if cols:
                                v.wait_ge(s_pro.h, cols_ld)
                            if idx >= NB:
                                v.wait_ge(s_st4[pp].h, st_at[idx - NB])
                            E.q = v
                            E.step = chainer(v, b.csem["vector"])
                            E.psum = p[pp][0:cw, :]
                            E.psum2 = p2[pp][0:cw, :] if dual else None
                            E.tmp = tmp[pp][0:cw, :]
                            E.out = ob[pp][0:cw, :]
                            E.aux = [a[0:cw, :] for a in axb[pp]]
                            ins = evac_v(E)
                            if ins is not None:
                                ins.then_inc(s_v.h, 1)
                            else:
                                v.drain()
                                v.sem_inc(s_v.h, 1)
                        emit("vector", _ev)
                        s_v.inc(1)
                        ev_v_at[idx] = s_v.val
                        if idx not in psum_free:
                            psum_free[idx] = (s_v.h, s_v.val)
                        ev_done = (s_v.h, s_v.val)
                    else:
                        ev_v_at[idx] = s_v.val
                        ev_done = psum_free[idx]
                    aux_free[idx] = ev_v_at[idx]

                    def _st(g, pp=pp, cw=cw, idx=idx, odap=odap, olo=olo, t=t,
                            ev_done=ev_done):
                        g.wait_ge(ev_done[0], ev_done[1])
                        g.dma_start(out=odap[olo:olo + cw, t * TT:(t + 1) * TT],
                                    in_=ob[pp][0:cw, :]).then_inc(s_st4[pp].h, 16)
                    emit("gpsimd", _st)
                    s_st4[pp].inc(16)
                    st_at[idx] = s_st4[pp].val
                    idx += 1
                    if idx % NB == 0:
                        flush()
                xfree[slot] = mm_at[idx - 1]
        flush()

        def _fin(g):
            for ss in s_st4:
                g.wait_ge(ss.h, ss.val)
        b.sec("gpsimd", _fin)


def st_mm_tm(b, name, segs, N, out, evac_s=None, evac_v=_copy_evac,
             bias_row=None, aux=(), Tn=T, out_dt=F32):
    nblocks = cdiv(N, 512)
    NB = 4
    with b.stage(name) as sctx:
        s_t, s_s, s_v = (b.sem() for _ in range(3))
        s_wl = b.sem_sw()
        s_ld2 = b.sem2()
        s_la4 = [b.sem() for _ in range(NB)]
        s_st4 = [b.sem_sw() for _ in range(NB)]
        for si, s in enumerate(segs):
            Kc = s["K"] // 128
            s["_w"] = [b.sb(sctx, f"{name}_w{si}k{kc}", [128, N], BF16) for kc in range(Kc)]
            s["_xm"] = [b.sb(sctx, f"{name}_x{si}b{t}", [128, Kc * TT], BF16)
                        for t in range(2)]
            s["_x"] = [[s["_xm"][t][:, kc * TT:(kc + 1) * TT] for kc in range(Kc)]
                       for t in range(2)]
        p = [b.ps(sctx, f"{name}_p{i}", [128, 512], F32) for i in range(NB)]
        ob = [b.sb(sctx, f"{name}_o{i}", [128, 512], out_dt) for i in range(NB)]
        tmp = [b.sb(sctx, f"{name}_t{i}", [128, 512], F32) for i in range(NB)]
        axb = [[b.sb(sctx, f"{name}_a{i}j{j}", [128, 512], aux[j][1]) for j in range(len(aux))]
               for i in range(NB)]
        if bias_row is not None:
            brow = b.sb(sctx, f"{name}_br", [1, N], BF16)
            ones1 = b.sb(sctx, f"{name}_on", [1, 128], BF16)

        def _pw(g):
            if bias_row is not None:
                g.dma_start(out=brow[:], in_=bias_row).then_inc(s_wl.h, 16)
            for si, s in enumerate(segs):
                for kc in range(s["K"] // 128):
                    g.dma_start(out=s["_w"][kc][:],
                                in_=s["w"][kc * 128:(kc + 1) * 128, :]).then_inc(s_wl.h, 16)
        b.sec("gpsimd", _pw)
        s_wl.inc(16 * ((1 if bias_row is not None else 0) + sum(s["K"] // 128 for s in segs)))
        wl_ready = s_wl.val

        if bias_row is not None:
            def _von(v):
                v.memset(ones1[:], 1.0).then_inc(s_v.h, 1)
            b.sec("vector", _von)
            s_v.inc(1)
        ones_ready = s_v.val

        pend = []

        def emit(eng, fn):
            pend.append((eng, fn))

        def flush():
            for eng in ENGINES:
                fns = [f for (e, f) in pend if e == eng]
                if fns:
                    b.sec(eng, lambda q, fns=tuple(fns): [f(q) for f in fns])
            pend.clear()

        mm_at = {}
        ev_v_at = {}
        psum_free = {}
        st_at = {}
        aux_free = {}
        xfree = [0, 0]
        idx = 0
        for t5 in range(Tn // TT):
            slot = t5 % 2

            def _lx(sync, slot=slot, t5=t5, w=xfree[slot]):
                sync.wait_ge(s_t.h, w)
                for si, s in enumerate(segs):
                    sync.dma_start(
                        out=s["_xm"][slot][:],
                        in_=s["x"][:, t5 * TT:(t5 + 1) * TT].rearrange(
                            "(kc p) m -> p kc m", p=128)).then_inc(s_ld2[slot].h, 16)
            b.sec("sync", _lx)
            s_ld2[slot].inc(16 * len(segs))
            x_ld = s_ld2[slot].val

            for q_ in range(TT // 128):
                t128 = t5 * 4 + q_
                r0 = t128 * 128
                for nb in range(nblocks):
                    pp = idx % NB
                    nw = min(512, N - nb * 512)
                    n0 = nb * 512

                    if aux:
                        def _la(sync, idx=idx, pp=pp, r0=r0, n0=n0, nw=nw):
                            if idx >= NB:
                                sync.wait_ge(s_v.h, aux_free[idx - NB])
                            for j, (ad, adt) in enumerate(aux):
                                sync.dma_start(out=axb[pp][j][:, 0:nw],
                                               in_=ad[r0:r0 + 128, n0:n0 + nw]).then_inc(s_la4[pp].h, 16)
                        emit("sync", _la)
                        s_la4[pp].inc(16 * len(aux))
                    la_tile = s_la4[pp].val

                    def _mm(tq, idx=idx, pp=pp, q_=q_, slot=slot, n0=n0, nw=nw, x_ld=x_ld):
                        tq.wait_ge(s_ld2[slot].h, x_ld)
                        tq.wait_ge(s_wl.h, wl_ready)
                        if bias_row is not None:
                            tq.wait_ge(s_v.h, ones_ready)
                        if idx >= NB:
                            tq.wait_ge(*psum_free[idx - NB])
                        first = True
                        if bias_row is not None:
                            tq.matmul(p[pp][:, 0:nw], ones1[:], brow[0:1, n0:n0 + nw],
                                      start=True, stop=False)
                            first = False
                        mms = []
                        for si, s in enumerate(segs):
                            for kc in range(s["K"] // 128):
                                mms.append((s["_x"][slot][kc][:, q_ * 128:(q_ + 1) * 128],
                                            s["_w"][kc][:, n0:n0 + nw]))
                        for mi, (lap, rap) in enumerate(mms):
                            ins = tq.matmul(p[pp][:, 0:nw], lap, rap,
                                            start=(first and mi == 0), stop=(mi == len(mms) - 1))
                            if mi == len(mms) - 1:
                                ins.then_inc(s_t.h, 1)
                    emit("tensor", _mm)
                    s_t.inc(1)
                    mm_at[idx] = s_t.val

                    E = _EC()
                    E.cw, E.ncx, E.nw, E.idx = 128, nb, nw, idx
                    E.col = None

                    if evac_s is not None:
                        def _es(sc, E=E, pp=pp, idx=idx, la_tile=la_tile, nw=nw):
                            sc.wait_ge(s_t.h, mm_at[idx])
                            if aux:
                                sc.wait_ge(s_la4[pp].h, la_tile)
                            if idx >= NB:
                                sc.wait_ge(s_v.h, ev_v_at[idx - NB])
                                if evac_v is None:
                                    sc.wait_ge(s_st4[pp].h, st_at[idx - NB])
                            E.q = sc
                            E.step = chainer(sc, b.csem["scalar"])
                            E.psum = p[pp][:, 0:nw]
                            E.psum2 = None
                            E.tmp = tmp[pp][:, 0:nw]
                            E.out = ob[pp][:, 0:nw]
                            E.aux = [a[:, 0:nw] for a in axb[pp]]
                            ins = evac_s(E)
                            if ins is not None:
                                ins.then_inc(s_s.h, 1)
                            else:
                                sc.drain()
                                sc.sem_inc(s_s.h, 1)
                        emit("scalar", _es)
                        s_s.inc(1)
                        psum_free[idx] = (s_s.h, s_s.val)
                        sc_at = s_s.val
                    else:
                        sc_at = None

                    if evac_v is not None:
                        def _ev(v, E=E, pp=pp, idx=idx, la_tile=la_tile, nw=nw, sc_at=sc_at):
                            if sc_at is not None:
                                v.wait_ge(s_s.h, sc_at)
                            else:
                                v.wait_ge(s_t.h, mm_at[idx])
                            if aux:
                                v.wait_ge(s_la4[pp].h, la_tile)
                            if idx >= NB:
                                v.wait_ge(s_st4[pp].h, st_at[idx - NB])
                            E.q = v
                            E.step = chainer(v, b.csem["vector"])
                            E.psum = p[pp][:, 0:nw]
                            E.psum2 = None
                            E.tmp = tmp[pp][:, 0:nw]
                            E.out = ob[pp][:, 0:nw]
                            E.aux = [a[:, 0:nw] for a in axb[pp]]
                            ins = evac_v(E)
                            if ins is not None:
                                ins.then_inc(s_v.h, 1)
                            else:
                                v.drain()
                                v.sem_inc(s_v.h, 1)
                        emit("vector", _ev)
                        s_v.inc(1)
                        ev_v_at[idx] = s_v.val
                        if idx not in psum_free:
                            psum_free[idx] = (s_v.h, s_v.val)
                        ev_done = (s_v.h, s_v.val)
                    else:
                        ev_v_at[idx] = s_v.val
                        ev_done = psum_free[idx]
                    aux_free[idx] = ev_v_at[idx]

                    def _st(g, pp=pp, idx=idx, r0=r0, n0=n0, nw=nw, ev_done=ev_done):
                        g.wait_ge(ev_done[0], ev_done[1])
                        g.dma_start(out=out[r0:r0 + 128, n0:n0 + nw],
                                    in_=ob[pp][:, 0:nw]).then_inc(s_st4[pp].h, 16)
                    emit("gpsimd", _st)
                    s_st4[pp].inc(16)
                    st_at[idx] = s_st4[pp].val
                    idx += 1
                    if idx % NB == 0:
                        flush()
            xfree[slot] = mm_at[idx - 1]
        flush()

        def _fin(g):
            for ss in s_st4:
                g.wait_ge(ss.h, ss.val)
        b.sec("gpsimd", _fin)


def st_transpose_add(b, name, srcT, Nr, xr_in, xr_out, scale, ident, Tn=T,
                     out_dt=F32, src_dt=BF16):
    nch = Nr // 128
    with b.stage(name) as sctx:
        s_pro, s_t, s_v = b.sem(), b.sem(), b.sem()
        s_ld2, s_sr2 = b.sem2(), b.sem2()
        s_st2 = b.sem2_sw()
        id_sb = b.sb(sctx, "tr_id", [128, 128], src_dt)
        src = [b.sb(sctx, f"tr_s{i}", [128, 128], src_dt) for i in range(2)]
        p_tr = [b.ps(sctx, f"tr_p{i}", [128, 128], src_dt) for i in range(2)]
        xa = [b.sb(sctx, f"tr_x{i}", [128, Nr], out_dt) for i in range(2)]

        def _l0(sync):
            sync.dma_start(out=id_sb[:], in_=ident).then_inc(s_pro.h, 16)
        b.sec("sync", _l0)
        s_pro.inc(16)
        pro_ld = s_pro.val

        v_at = {}
        vtile_at = {}
        st_at = {}
        tr_at = {}
        idx = 0
        for i in range(Tn // 128):
            bi = i % 2
            r0 = i * 128

            def _lxr(sync, i=i, bi=bi, r0=r0):
                if xr_in is not None:
                    if i >= 2:
                        sync.wait_ge(s_v.h, vtile_at[i - 2])
                        sync.wait_ge(s_st2[bi].h, st_at[i - 2])
                    sync.dma_start(out=xa[bi][:], in_=xr_in[r0:r0 + 128, :]).then_inc(s_ld2[bi].h, 16)
            b.sec("sync", _lxr)
            if xr_in is not None:
                s_ld2[bi].inc(16)
            x_ld = s_ld2[bi].val

            for c in range(nch):
                pk = idx % 2

                def _ls(sync, c=c, pk=pk, idx=idx, r0=r0):
                    if idx >= 2:
                        sync.wait_ge(s_t.h, tr_at[idx - 2])
                    sync.dma_start(out=src[pk][:],
                                   in_=srcT[c * 128:(c + 1) * 128, r0:r0 + 128]).then_inc(s_sr2[pk].h, 16)
                b.sec("sync", _ls)
                s_sr2[pk].inc(16)
                sld = s_sr2[pk].val

                def _tr(tq, pk=pk, idx=idx, sld=sld):
                    tq.wait_ge(s_sr2[pk].h, sld)
                    tq.wait_ge(s_pro.h, pro_ld)
                    if idx >= 2:
                        tq.wait_ge(s_v.h, v_at[idx - 2])
                    tq.transpose(p_tr[pk][:], src[pk][:], id_sb[:]).then_inc(s_t.h, 1)
                b.sec("tensor", _tr)
                s_t.inc(1)
                tr_at[idx] = s_t.val

                def _ev(v, i=i, c=c, pk=pk, bi=bi, idx=idx, x_ld=x_ld):
                    v.wait_ge(s_t.h, tr_at[idx])
                    if xr_in is not None:
                        v.wait_ge(s_ld2[bi].h, x_ld)
                        v.scalar_tensor_tensor(out=xa[bi][:, c * 128:(c + 1) * 128],
                                               in0=p_tr[pk][:], scalar=float(scale),
                                               in1=xa[bi][:, c * 128:(c + 1) * 128],
                                               op0=OP.mult, op1=OP.add).then_inc(s_v.h, 1)
                    else:
                        if i >= 2 and c == 0:
                            v.wait_ge(s_st2[bi].h, st_at[i - 2])
                        v.tensor_scalar(out=xa[bi][:, c * 128:(c + 1) * 128],
                                        in0=p_tr[pk][:], scalar1=float(scale),
                                        scalar2=None, op0=OP.mult).then_inc(s_v.h, 1)
                b.sec("vector", _ev)
                s_v.inc(1)
                v_at[idx] = s_v.val
                idx += 1
            vtile_at[i] = s_v.val

            def _st(g, i=i, bi=bi, r0=r0):
                g.wait_ge(s_v.h, vtile_at[i])
                g.dma_start(out=xr_out[r0:r0 + 128, :], in_=xa[bi][:]).then_inc(s_st2[bi].h, 16)
            b.sec("gpsimd", _st)
            s_st2[bi].inc(16)
            st_at[i] = s_st2[bi].val

        def _fin(g):
            g.wait_ge(s_st2[0].h, s_st2[0].val)
            g.wait_ge(s_st2[1].h, s_st2[1].val)
        b.sec("gpsimd", _fin)


def host_constants():
    c = {}
    c["ident_bf"] = np.eye(128, dtype=np.float32).astype(BF16NP)
    c["ident_f32"] = np.eye(128, dtype=np.float32)
    r = np.arange(128)
    c["tri_incl"] = (r[:, None] <= r[None, :]).astype(np.float32)
    c["triU_strict"] = (r[:, None] > r[None, :]).astype(np.float32)
    c["trimask_bf"] = (r[:, None] <= r[None, :]).astype(BF16NP)
    M = np.zeros((256, 256), dtype=np.float32)
    for leaf in range(256):
        node = 0
        for level in range(8):
            bit = (leaf >> (7 - level)) & 1
            M[node, leaf] = 1.0 if bit else -1.0
            node = 2 * node + 1 + bit
        assert node - 255 == leaf
    c["fff_M2"] = (2.0 * M).astype(BF16NP)
    c["fff_cs"] = (-M.sum(axis=0).reshape(256, 1)).astype(np.float32)
    E = np.zeros((256, 4096), dtype=np.float32)
    for leaf in range(256):
        E[leaf, leaf * 16:(leaf + 1) * 16] = 1.0
    c["fff_E"] = E.astype(BF16NP)
    inv = 1.0 / (10000.0 ** (np.arange(32, dtype=np.float64) / 32.0))
    ang = np.arange(T, dtype=np.float64)[None, :] * inv[np.tile(np.arange(32), 2)][:, None]
    c["rope_cos"] = np.cos(ang).astype(np.float32)
    c["rope_sin"] = np.sin(ang).astype(np.float32)
    return c


def emit_L2F(b, W, xr_in, xr_out, uid):
    zT = b.dtmp(f"zT_{uid}", [D, T], BF16)
    h1T = b.dtmp(f"h1T_{uid}", [D, T], BF16)
    h2T = b.dtmp(f"h2T_{uid}", [D, T], BF16)
    st_ln(b, xr_in, zT=zT, ident=W["ident_bf"])
    st_mm_fm(b, f"l2f_m1_{uid}", [dict(x=zT, w=W["l2_w"], K=D)], D,
             [(h1T, 0, D)])
    st_mm_fm(b, f"l2f_m2_{uid}", [dict(x=h1T, w=W["l2_a1"], K=D)], D,
             [(h2T, 0, D)])

    def ev_s(E):
        return E.q.activation(out=E.tmp, in_=E.psum, func=AF.Tanh)

    def ev_v(E):
        return E.q.scalar_tensor_tensor(out=E.out, in0=E.tmp, scalar=0.5, in1=E.aux[0],
                                        op0=OP.mult, op1=OP.add)
    st_mm_tm(b, f"l2f_m3_{uid}", [dict(x=h2T, w=W["l2_a2"], K=D)], D, xr_out,
             evac_s=ev_s, evac_v=ev_v, bias_row=W["l2_bias"],
             aux=[(xr_in, F32)], out_dt=F32)


def emit_L1F(b, W, xr_in, xr_out, uid):
    zT = b.dtmp(f"zT_{uid}", [D, T], BF16)
    haT = b.dtmp(f"l1f_haT_{uid}", [4096, T], BF16)
    hbT = b.dtmp(f"l1f_hbT_{uid}", [4096, T], BF16)
    hcT = b.dtmp(f"l1f_hcT_{uid}", [4096, T], BF16)
    dT = b.dtmp(f"l1f_dT_{uid}", [D, T], BF16)
    st_ln(b, xr_in, zT=zT, ident=W["ident_bf"])

    def ev_s_gate(E):
        return E.q.activation(out=E.tmp, in_=E.psum, func=AF.Silu, scale=E.col(0))

    def ev_v_gate(E):
        return E.q.scalar_tensor_tensor(out=E.out, in0=E.psum2, scalar=E.col(1),
                                        in1=E.tmp, op0=OP.mult, op1=OP.mult)
    st_mm_fm(b, f"l1f_g_{uid}",
             [dict(x=zT, w=W["l1_w1T"], w2=W["l1_w2T"], K=D, wlayout="panels")],
             4096, [(haT, 0, 4096)], evac_s=ev_s_gate, evac_v=ev_v_gate,
             cols=[W["l1_w1s"], W["l1_w2s"]])

    def ev_s_gelu(E):
        return E.q.activation(out=E.out, in_=E.psum, func=AF.Gelu, scale=E.col(0))
    st_mm_fm(b, f"l1f_h1_{uid}",
             [dict(x=haT, w=W["l1_h1T"], K=4096, wlayout="panels")],
             4096, [(hbT, 0, 4096)], evac_s=ev_s_gelu, evac_v=None,
             cols=[W["l1_h1s"]])
    st_mm_fm(b, f"l1f_h2_{uid}",
             [dict(x=hbT, w=W["l1_h2T"], K=4096, wlayout="panels")],
             4096, [(hcT, 0, 4096)], evac_s=ev_s_gelu, evac_v=None,
             cols=[W["l1_h2s"]])

    def ev_v_s5(E):
        return E.q.tensor_scalar(out=E.out, in0=E.psum, scalar1=E.col(0),
                                 scalar2=None, op0=OP.mult)
    st_mm_fm(b, f"l1f_w3_{uid}",
             [dict(x=hcT, w=W["l1_w3T"], K=4096, wlayout="panels")],
             D, [(dT, 0, D)], evac_v=ev_v_s5, cols=[W["l1_w3s"]])
    st_transpose_add(b, f"l1f_tr_{uid}", dT, D, xr_in, xr_out, 0.5, W["ident_bf"])


def panels(w_kn):
    Kd, Nd = w_kn.shape
    return np.ascontiguousarray(
        w_kn.reshape(Kd, Nd // 128, 128).transpose(1, 0, 2))


def colmat(v):
    return np.ascontiguousarray(v.reshape(-1, 128).T)


def st_add(b, name, xr_in, plus, xr_out, Dd=D, Tn=T, plus_dt=None):
    plus_dt = plus_dt or [F32] * len(plus)
    with b.stage(name) as sctx:
        s_v = b.sem()
        s_ld2 = b.sem2()
        s_st2 = b.sem2_sw()
        xa = [b.sb(sctx, f"{name}_x{i}", [128, Dd], F32) for i in range(2)]
        dl = [[b.sb(sctx, f"{name}_d{i}_{j}", [128, Dd], plus_dt[j]) for j in range(len(plus))]
              for i in range(2)]
        v_at = {}
        st_at = {}
        for i in range(Tn // 128):
            bi = i % 2
            r0 = i * 128

            def _ld(sync, i=i, bi=bi, r0=r0):
                if i >= 2:
                    sync.wait_ge(s_v.h, v_at[i - 2])
                    sync.wait_ge(s_st2[bi].h, st_at[i - 2])
                sync.dma_start(out=xa[bi][:], in_=xr_in[r0:r0 + 128, :]).then_inc(s_ld2[bi].h, 16)
                for j, (dap, sc) in enumerate(plus):
                    sync.dma_start(out=dl[bi][j][:], in_=dap[r0:r0 + 128, :]).then_inc(s_ld2[bi].h, 16)
            b.sec("sync", _ld)
            s_ld2[bi].inc(16 * (1 + len(plus)))
            ld_i = s_ld2[bi].val

            def _v(v, bi=bi, ld_i=ld_i):
                st = chainer(v, b.csem["vector"])
                v.wait_ge(s_ld2[bi].h, ld_i)
                for j, (dap, sc) in enumerate(plus):
                    st(v.scalar_tensor_tensor(out=xa[bi][:], in0=dl[bi][j][:],
                                              scalar=float(sc), in1=xa[bi][:],
                                              op0=OP.mult, op1=OP.add))
                v.sem_inc(s_v.h, 1)
            b.sec("vector", _v)
            s_v.inc(1)
            v_at[i] = s_v.val

            def _st(g, i=i, bi=bi, r0=r0):
                g.wait_ge(s_v.h, v_at[i])
                g.dma_start(out=xr_out[r0:r0 + 128, :], in_=xa[bi][:]).then_inc(s_st2[bi].h, 16)
            b.sec("gpsimd", _st)
            s_st2[bi].inc(16)
            st_at[i] = s_st2[bi].val

        def _fin(g):
            g.wait_ge(s_st2[0].h, s_st2[0].val)
            g.wait_ge(s_st2[1].h, s_st2[1].val)
        b.sec("gpsimd", _fin)


def st_regroup(b, name, src, dst, Nr, S=8, Tn=T):
    Gn = Tn // S
    with b.stage(name) as sctx:
        s_v = b.sem()
        s_ld2 = b.sem2()
        s_st2 = b.sem2_sw()
        a = [b.sb(sctx, f"{name}_a{i}", [128, Tn], BF16) for i in range(2)]
        o = [b.sb(sctx, f"{name}_o{i}", [128, Tn], BF16) for i in range(2)]
        v_at = {}
        st_at = {}
        for i in range(Nr // 128):
            bi = i % 2

            def _ld(sync, i=i, bi=bi):
                if i >= 2:
                    sync.wait_ge(s_v.h, v_at[i - 2])
                    sync.wait_ge(s_st2[bi].h, st_at[i - 2])
                sync.dma_start(out=a[bi][:], in_=src[i * 128:(i + 1) * 128, :]).then_inc(s_ld2[bi].h, 16)
            b.sec("sync", _ld)
            s_ld2[bi].inc(16)
            ld_i = s_ld2[bi].val

            def _v(v, bi=bi, ld_i=ld_i):
                v.wait_ge(s_ld2[bi].h, ld_i)
                for r in range(S):
                    v.tensor_copy(out=o[bi][:, r * Gn:(r + 1) * Gn],
                                  in_=a[bi][:, r:Tn:S])
                v.drain()
                v.sem_inc(s_v.h, 1)
            b.sec("vector", _v)
            s_v.inc(1)
            v_at[i] = s_v.val

            def _st(g, i=i, bi=bi):
                g.wait_ge(s_v.h, v_at[i])
                g.dma_start(out=dst[i * 128:(i + 1) * 128, :], in_=o[bi][:]).then_inc(s_st2[bi].h, 16)
            b.sec("gpsimd", _st)
            s_st2[bi].inc(16)
            st_at[i] = s_st2[bi].val

        def _fin(g):
            g.wait_ge(s_st2[0].h, s_st2[0].val)
            g.wait_ge(s_st2[1].h, s_st2[1].val)
        b.sec("gpsimd", _fin)


def emit_L3F(b, W, xr_in, xr_out, uid):
    zT = b.dtmp(f"zT_{uid}", [D, T], BF16)
    gT = b.dtmp(f"l3f_gT_{uid}", [2048, T], BF16)
    shared = b.dtmp(f"l3f_sh_{uid}", [T, D], F32)
    zgT = b.dtmp(f"l3f_zgT_{uid}", [D, T], BF16)
    hgT = b.dtmp(f"l3f_hgT_{uid}", [1024, 512], BF16)
    ym = [b.dtmp(f"l3f_ym{k}_{uid}", [T, D], F32) for k in range(2)]
    st_ln(b, xr_in, zT=zT, ident=W["ident_bf"])

    def ev_s_gate(E):
        return E.q.activation(out=E.tmp, in_=E.psum, func=AF.Silu, bias=E.col(0))

    def ev_v_gate(E):
        return E.q.scalar_tensor_tensor(out=E.out, in0=E.psum2, scalar=E.col(1),
                                        in1=E.tmp, op0=OP.add, op1=OP.mult)
    st_mm_fm(b, f"l3f_g_{uid}",
             [dict(x=zT, w=W["l3_sw1"], w2=W["l3_sw3"], K=D)],
             2048, [(gT, 0, 2048)], evac_s=ev_s_gate, evac_v=ev_v_gate,
             cols=[W["l3_sb1c"], W["l3_sb3c"]])
    st_mm_tm(b, f"l3f_sh_{uid}", [dict(x=gT, w=W["l3_sw2"], K=2048)], D,
             shared, bias_row=W["l3_sb2"], out_dt=F32)
    st_regroup(b, f"l3f_rg_{uid}", zT, zgT, D)
    for k in range(2):
        for r in range(8):
            e = (3 * r + 7 * k) % 8

            def ev_s_gelu(E, e=e):
                return E.q.activation(out=E.out, in_=E.psum, func=AF.Gelu,
                                      bias=E.col(0))
            st_mm_fm(b, f"l3f_h{k}_{r}_{uid}",
                     [dict(x=zgT[:, r * 512:(r + 1) * 512], w=W["l3_w1"][e], K=D)],
                     1024, [(hgT, 0, 1024)], evac_s=ev_s_gelu, evac_v=None,
                     cols=[W["l3_b1c"][e]], Tn=512)
            out_view = ym[k].rearrange("(g s) d -> s g d", s=8)[r]
            st_mm_tm(b, f"l3f_y{k}_{r}_{uid}",
                     [dict(x=hgT, w=W["l3_w2"][e], K=1024)], D, out_view,
                     bias_row=W["l3_b2"][e:e + 1, :], out_dt=F32, Tn=512)
    st_add(b, f"l3f_cmb_{uid}", xr_in,
           [(shared, 0.25), (ym[0], 0.125), (ym[1], 0.125)], xr_out)


def st_rope(b, name, srcT, dstT, H, cosd, sind, Tn=T):
    nch = H // 2
    with b.stage(name) as sctx:
        s_pro, s_v = b.sem(), b.sem()
        s_ld2 = b.sem2()
        s_st2 = b.sem2_sw()
        cs = b.sb(sctx, f"{name}_cs", [64, 2 * Tn], F32)
        xt = [b.sb(sctx, f"{name}_xt{i}", [64, Tn], BF16) for i in range(2)]
        xb = [b.sb(sctx, f"{name}_xb{i}", [64, Tn], BF16) for i in range(2)]
        t1 = b.sb(sctx, f"{name}_t1", [64, Tn], F32)
        t2 = b.sb(sctx, f"{name}_t2", [64, Tn], F32)
        ot = [b.sb(sctx, f"{name}_ot{i}", [64, Tn], BF16) for i in range(2)]
        obt = [b.sb(sctx, f"{name}_ob{i}", [64, Tn], BF16) for i in range(2)]
        src4 = srcT.rearrange("(hh half d) t -> half hh d t", half=2, d=32)
        dst4 = dstT.rearrange("(hh half d) t -> half hh d t", half=2, d=32)

        def _l0(sync):
            sync.dma_start(out=cs[:, 0:Tn], in_=cosd).then_inc(s_pro.h, 16)
            sync.dma_start(out=cs[:, Tn:2 * Tn], in_=sind).then_inc(s_pro.h, 16)
        b.sec("sync", _l0)
        s_pro.inc(32)
        pro_ld = s_pro.val

        v_at = {}
        st_at = {}
        for c in range(nch):
            bi = c % 2

            def _ld(sync, c=c, bi=bi):
                if c >= 2:
                    sync.wait_ge(s_v.h, v_at[c - 2])
                    sync.wait_ge(s_st2[bi].h, st_at[c - 2])
                sync.dma_start(out=xt[bi][:], in_=src4[0, 2 * c:2 * c + 2]).then_inc(s_ld2[bi].h, 16)
                sync.dma_start(out=xb[bi][:], in_=src4[1, 2 * c:2 * c + 2]).then_inc(s_ld2[bi].h, 16)
            b.sec("sync", _ld)
            s_ld2[bi].inc(32)
            ld_c = s_ld2[bi].val

            def _v(v, c=c, bi=bi, ld_c=ld_c):
                st = chainer(v, b.csem["vector"])
                v.wait_ge(s_ld2[bi].h, ld_c)
                v.wait_ge(s_pro.h, pro_ld)
                cos = cs[:, 0:Tn]
                sin = cs[:, Tn:2 * Tn]
                st(v.tensor_tensor(out=t1[:], in0=xt[bi][:], in1=cos, op=OP.mult))
                st(v.tensor_tensor(out=t2[:], in0=xb[bi][:], in1=sin, op=OP.mult))
                st(v.tensor_tensor(out=ot[bi][:], in0=t1[:], in1=t2[:], op=OP.subtract))
                st(v.tensor_tensor(out=t1[:], in0=xb[bi][:], in1=cos, op=OP.mult))
                st(v.tensor_tensor(out=t2[:], in0=xt[bi][:], in1=sin, op=OP.mult))
                st(v.tensor_tensor(out=obt[bi][:], in0=t1[:], in1=t2[:], op=OP.add))
                v.sem_inc(s_v.h, 1)
            b.sec("vector", _v)
            s_v.inc(1)
            v_at[c] = s_v.val

            def _st(g, c=c, bi=bi):
                g.wait_ge(s_v.h, v_at[c])
                g.dma_start(out=dst4[0, 2 * c:2 * c + 2], in_=ot[bi][:]).then_inc(s_st2[bi].h, 16)
                g.dma_start(out=dst4[1, 2 * c:2 * c + 2], in_=obt[bi][:]).then_inc(s_st2[bi].h, 16)
            b.sec("gpsimd", _st)
            s_st2[bi].inc(32)
            st_at[c] = s_st2[bi].val

        def _fin(g):
            g.wait_ge(s_st2[0].h, s_st2[0].val)
            g.wait_ge(s_st2[1].h, s_st2[1].val)
        b.sec("gpsimd", _fin)


def st_attn(b, name, qT, kT, v, out, H, dh, dv, ident=None, KV=None, scale=1.0,
            aux_add=None, out_writer=None, out_dt=F32, Tn=T):
    KV = KV if KV is not None else H
    grp = H // KV
    dhc = cdiv(dh, 128)
    nkt = Tn // 512
    nkc = Tn // 128
    nq = Tn // 128
    aug = dv + 8 <= 512
    dva = dv + 8 if aug else dv
    dvb = cdiv(dva, 512)
    with b.stage(name) as sctx:
        s_pro, s_t, s_s, s_v = (b.sem() for _ in range(4))
        s_q2, s_a2 = b.sem2(), b.sem2()
        s_st2 = b.sem2_sw()
        s_kv = b.sem()
        id_sb = b.sb(sctx, f"{name}_id", [128, 128], BF16)
        ones8 = b.sb(sctx, f"{name}_on", [128, 8], BF16)
        kh = b.sb(sctx, f"{name}_kh", [dh if dh <= 128 else 128, Tn * dhc], BF16)
        qh = [b.sb(sctx, f"{name}_qh{i}", [dh if dh <= 128 else 128, 128 * dhc], BF16)
              for i in range(2)]
        vh = b.sb(sctx, f"{name}_vh", [128, nkc * dva], BF16)
        P_sb = b.sb(sctx, f"{name}_P", [128, Tn], BF16)
        PT_sb = b.sb(sctx, f"{name}_PT", [128, Tn], BF16)
        ob = [b.sb(sctx, f"{name}_ob{i}", [128, dv], out_dt) for i in range(2)]
        ax = [b.sb(sctx, f"{name}_ax{i}", [128, dv], F32) for i in range(2)] if aux_add is not None else None
        sc_col = b.sb(sctx, f"{name}_sc", [128, 4], F32)
        S_ps = b.ps(sctx, f"{name}_S", [128, 4096], F32)
        big_pt = dva <= 512 - 8 or True
        ptw = 2048 if dva <= 448 else 1024
        pt_ps = S_ps[:, 0:ptw].bitcast(BF16)
        pv_ps = S_ps[:, ptw:ptw + 512 * dvb]
        rs_ps = S_ps[:, ptw + 512 * dvb:ptw + 512 * dvb + 8]
        vh3 = vh[:].rearrange("p (kc d) -> p kc d", d=dva)

        def _pro(sync):
            sync.dma_start(out=id_sb[:], in_=ident).then_inc(s_pro.h, 16)
        b.sec("sync", _pro)
        s_pro.inc(16)

        def _von(v_):
            v_.memset(ones8[:], 1.0)
            if aug:
                v_.memset(vh[:], 1.0)
            v_.drain()
            v_.sem_inc(s_v.h, 1)
        b.sec("vector", _von)
        s_v.inc(1)
        pro_v = s_v.val
        pro_ld = s_pro.val

        q_ld = {}
        exp_at = {}
        exp_free = {}
        out_ev = {}
        st_at = {}
        pair = 0
        for kv in range(KV):
            kv_gate = [(s_v.h, pro_v)]
            if pair > 0:
                kv_gate.append((s_t.h, exp_free[pair - 1]))

            def _lkv(sync, kv=kv, kv_gate=tuple(kv_gate)):
                for (h_, v_) in kv_gate:
                    sync.wait_ge(h_, v_)
                for c in range(dhc):
                    sync.dma_start(
                        out=kh[:, c * Tn:(c + 1) * Tn],
                        in_=kT[kv * dh + c * 128: kv * dh + min((c + 1) * 128, dh), :],
                    ).then_inc(s_kv.h, 16)
                sync.dma_start(
                    out=vh3[:, :, 0:dv],
                    in_=v.rearrange("(kc kp) d -> kp kc d", kp=128)[:, :, kv * dv:(kv + 1) * dv],
                ).then_inc(s_kv.h, 16)
            b.sec("sync", _lkv)
            s_kv.inc(16 * (dhc + 1))
            kv_ld = s_kv.val

            for hg in range(grp):
                h = kv * grp + hg
                for qi in range(nq):
                    pp = pair % 2

                    def _lq(sync, h=h, qi=qi, pp=pp, pair=pair):
                        if pair >= 2:
                            sync.wait_ge(s_t.h, exp_free[pair - 2])
                            if aux_add is not None:
                                sync.wait_ge(s_v.h, out_ev[pair - 2])
                        for c in range(dhc):
                            sync.dma_start(
                                out=qh[pp][:, c * 128:(c + 1) * 128],
                                in_=qT[h * dh + c * 128: h * dh + min((c + 1) * 128, dh),
                                       qi * 128:(qi + 1) * 128]).then_inc(s_q2[pp].h, 16)
                        if aux_add is not None:
                            sync.dma_start(out=ax[pp][:],
                                           in_=aux_add[qi * 128:(qi + 1) * 128, :]).then_inc(s_q2[pp].h, 16)
                    b.emit("sync", _lq)
                    s_q2[pp].inc(16 * (dhc + (1 if aux_add is not None else 0)))
                    q_ld[pair] = s_q2[pp].val

                    def _smm(tq, pp=pp, pair=pair, kv_ld=kv_ld):
                        tq.wait_ge(s_q2[pp].h, q_ld[pair])
                        tq.wait_ge(s_kv.h, kv_ld)
                        if pair >= 1:
                            tq.wait_ge(s_v.h, out_ev[pair - 1])
                        for kt in range(nkt):
                            for c in range(dhc):
                                cw = min(128, dh - c * 128)
                                ins = tq.matmul(
                                    S_ps[:, kt * 512:(kt + 1) * 512],
                                    qh[pp][0:cw, c * 128:(c + 1) * 128],
                                    kh[0:cw, c * Tn + kt * 512: c * Tn + (kt + 1) * 512],
                                    start=(c == 0), stop=(c == dhc - 1))
                        ins.then_inc(s_t.h, 1)
                    b.emit("tensor", _smm)
                    s_t.inc(1)
                    smm_at = s_t.val

                    def _vm(v_, pair=pair, smm_at=smm_at):
                        v_.wait_ge(s_t.h, smm_at)
                        if scale == 1.0:
                            v_.reduce_max(out=sc_col[:, 1:2], in_=S_ps[:, 0:Tn],
                                          axis=AX, negate=True).then_inc(s_v.h, 1)
                        else:
                            st = chainer(v_, b.csem["vector"])
                            st(v_.reduce_max(out=sc_col[:, 0:1], in_=S_ps[:, 0:Tn],
                                             axis=AX, negate=True))
                            v_.tensor_scalar(out=sc_col[:, 1:2], in0=sc_col[:, 0:1],
                                             scalar1=float(scale), scalar2=None,
                                             op0=OP.mult).then_inc(s_v.h, 1)
                    b.emit("vector", _vm)
                    s_v.inc(1)
                    max_at = s_v.val

                    def _se(sc, pair=pair, max_at=max_at):
                        sc.wait_ge(s_v.h, max_at)
                        sc.activation(out=P_sb[:], in_=S_ps[:, 0:Tn], func=AF.Exp,
                                      scale=float(scale),
                                      bias=sc_col[:, 1:2]).then_inc(s_s.h, 1)
                    b.emit("scalar", _se)
                    s_s.inc(1)
                    exp_at[pair] = s_s.val

                    per_round = ptw * 2 // 128
                    n_round = nkc // per_round
                    for rnd in range(n_round):
                        def _tr(tq, rnd=rnd, pair=pair,
                                ptw_=(pT_ev if rnd >= 1 else 0)):
                            tq.wait_ge(s_s.h, exp_at[pair])
                            tq.wait_ge(s_pro.h, pro_ld)
                            if rnd >= 1:
                                tq.wait_ge(s_v.h, ptw_)
                            for j in range(per_round):
                                kc = rnd * per_round + j
                                ins = tq.transpose(pt_ps[:, j * 128:(j + 1) * 128],
                                                   P_sb[:, kc * 128:(kc + 1) * 128],
                                                   id_sb[:])
                            ins.then_inc(s_t.h, 1)
                        b.emit("tensor", _tr)
                        s_t.inc(1)
                        tr_at = s_t.val

                        def _ptv(v_, rnd=rnd, tr_at=tr_at):
                            v_.wait_ge(s_t.h, tr_at)
                            v_.tensor_copy(
                                out=PT_sb[:, rnd * per_round * 128:(rnd + 1) * per_round * 128],
                                in_=pt_ps[:, 0:per_round * 128]).then_inc(s_v.h, 1)
                        b.emit("vector", _ptv)
                        s_v.inc(1)
                        pT_ev = s_v.val

                        def _pv(tq, rnd=rnd, pair=pair, kv_ld=kv_ld, pT_ev=pT_ev):
                            tq.wait_ge(s_v.h, pT_ev)
                            for j in range(per_round):
                                kc = rnd * per_round + j
                                first = (kc == 0)
                                for db in range(dvb):
                                    bw = min(512, dva - db * 512)
                                    ins = tq.matmul(
                                        pv_ps[:, db * 512:db * 512 + bw],
                                        PT_sb[:, kc * 128:(kc + 1) * 128],
                                        vh[:, kc * dva + db * 512: kc * dva + db * 512 + bw],
                                        start=first, stop=(kc == nkc - 1))
                                if not aug:
                                    ins2 = tq.matmul(rs_ps[:, :],
                                                     PT_sb[:, kc * 128:(kc + 1) * 128],
                                                     ones8[:], start=first,
                                                     stop=(kc == nkc - 1))
                                    ins = ins2
                            ins.then_inc(s_t.h, 1)
                        b.emit("tensor", _pv)
                        s_t.inc(1)
                        pv_at = s_t.val
                    exp_free[pair] = pv_at

                    def _vo(v_, pp=pp, pair=pair, pv_at=pv_at, h=h, qi=qi):
                        st = chainer(v_, b.csem["vector"])
                        v_.wait_ge(s_t.h, pv_at)
                        v_.wait_ge(s_v.h, pro_v)
                        if pair >= 2:
                            v_.wait_ge(s_st2[pp].h, st_at[pair - 2])
                        rsrc = pv_ps[:, dv:dv + 1] if aug else rs_ps[:, 0:1]
                        st(v_.reciprocal(out=sc_col[:, 2:3], in_=rsrc))
                        if aux_add is not None:
                            v_.wait_ge(s_q2[pp].h, q_ld[pair])
                            st(v_.scalar_tensor_tensor(out=ob[pp][:], in0=pv_ps[:, 0:dv],
                                                       scalar=sc_col[:, 2:3], in1=ax[pp][:],
                                                       op0=OP.mult, op1=OP.add))
                        else:
                            st(v_.tensor_scalar(out=ob[pp][:], in0=pv_ps[:, 0:dv],
                                                scalar1=sc_col[:, 2:3], scalar2=None,
                                                op0=OP.mult))
                        v_.sem_inc(s_v.h, 1)
                    b.emit("vector", _vo)
                    s_v.inc(1)
                    out_ev[pair] = s_v.val

                    def _st(g, pp=pp, pair=pair, h=h, qi=qi):
                        g.wait_ge(s_v.h, out_ev[pair])
                        if out_writer is not None:
                            out_writer(g, h, qi, ob[pp])
                            g.sem_inc(s_st2[pp].h, 16)
                        else:
                            g.dma_start(out=out[qi * 128:(qi + 1) * 128,
                                                h * dv:(h + 1) * dv],
                                        in_=ob[pp][:]).then_inc(s_st2[pp].h, 16)
                    b.emit("gpsimd", _st)
                    s_st2[pp].inc(16)
                    st_at[pair] = s_st2[pp].val
                    pair += 1
                    b.flush()

        def _fin(g):
            g.wait_ge(s_st2[0].h, s_st2[0].val)
            g.wait_ge(s_st2[1].h, s_st2[1].val)
        b.sec("gpsimd", _fin)


def st_attn64(b, name, qT, kT, v, out, H, KV=None, scale=1.0, ebias=-4.0,
              ident=None, out_dt=BF16, Tn=T):
    KV = KV if KV is not None else H
    grp = H // KV
    nkc = Tn // 128
    ncc = nkc // 2
    QW = 512
    nqt = Tn // QW
    with b.stage(name) as sctx:
        s_pro, s_kv, s_ts, s_se, s_tp = (b.sem() for _ in range(5))
        s_vc, s_tt, s_vn, s_v0 = (b.sem() for _ in range(4))
        s_q2 = b.sem2()
        s_st2 = b.sem2_sw()
        id_sb = b.sb(sctx, f"{name}_id", [128, 128], BF16)
        kh = [b.sb(sctx, f"{name}_kh{i}", [64, Tn], BF16) for i in range(2)]
        vh = [b.sb(sctx, f"{name}_vh{i}", [128, nkc * 65], BF16) for i in range(2)]
        qh = [b.sb(sctx, f"{name}_qh{i}", [64, Tn], BF16) for i in range(2)]
        PT = b.sb(sctx, f"{name}_pt", [128, 4096], BF16)
        oT = [b.sb(sctx, f"{name}_ot{i}", [128, QW], BF16) for i in range(2)]
        ob = [b.sb(sctx, f"{name}_ob{i}", [128, 256], out_dt) for i in range(2)]
        rc = b.sb(sctx, f"{name}_rc", [128, 8], F32)
        S_ps = b.ps(sctx, f"{name}_S", [128, 2048], F32)
        o_ps = b.ps(sctx, f"{name}_o", [128, 1024], F32)
        tr_ps = b.ps(sctx, f"{name}_tr", [128, 1024], BF16)

        def _pro(sync):
            sync.dma_start(out=id_sb[:], in_=ident).then_inc(s_pro.h, 16)
        b.sec("sync", _pro)
        s_pro.inc(16)
        pro_ld = s_pro.val

        def _von(v_):
            v_.memset(vh[0][:], 1.0)
            v_.memset(vh[1][:], 1.0)
            v_.memset(oT[0][:], 0.0)
            v_.memset(oT[1][:], 0.0)
            v_.drain()
            v_.sem_inc(s_v0.h, 1)
        b.sec("vector", _von)
        s_v0.inc(1)
        pro_v = s_v0.val

        vh3o = [vh[i][:].rearrange("p (kc d) -> p kc d", d=65) for i in range(2)]
        v3 = v.rearrange("(kc kp) d -> kp kc d", kp=128)

        kv_ld = {}
        q_ld = {}
        kv_lastS = {}
        kv_lastP = {}
        h_lastS = {}
        slab_exp = [0, 0]
        slot_pv = [0, 0, 0, 0]
        vc_at = {-2: 0, -1: 0}
        tt_at = {}
        vn_at = {}
        st_at = {}
        pair_meta = {}

        def norm_pair(v_, p, ttv, stfree):
            tp1 = p % 2
            v_.wait_ge(s_tt.h, ttv)
            v_.wait_ge(s_st2[tp1].h, stfree)
            st = chainer(v_, b.csem["vector"])
            for j in range(4):
                ins = v_.reciprocal(
                    out=rc[:, j:j + 1],
                    in_=tr_ps[:, tp1 * 512 + j * 128 + 64: tp1 * 512 + j * 128 + 65])
            st(ins)
            for j in range(4):
                ins = v_.tensor_scalar(
                    out=ob[tp1][:, j * 64:(j + 1) * 64],
                    in0=tr_ps[:, tp1 * 512 + j * 128: tp1 * 512 + j * 128 + 64],
                    scalar1=rc[:, j:j + 1], scalar2=None, op0=OP.mult)
            ins.then_inc(s_vn.h, 1)

        def store_pair(g, p, vnv):
            h1, qt1 = pair_meta[p]
            tp1 = p % 2
            g.wait_ge(s_vn.h, vnv)
            for j in range(4):
                g.dma_start(
                    out=out[qt1 * QW + j * 128: qt1 * QW + (j + 1) * 128,
                            h1 * 64:(h1 + 1) * 64],
                    in_=ob[tp1][:, j * 64:(j + 1) * 64]).then_inc(s_st2[tp1].h, 16)

        def tr_pair(tq, p, vcv, vnfree):
            tp1 = p % 2
            tq.wait_ge(s_vc.h, vcv)
            tq.wait_ge(s_vn.h, vnfree)
            tq.wait_ge(s_pro.h, pro_ld)
            for j2 in range(4):
                ins2 = tq.transpose(
                    tr_ps[:, tp1 * 512 + j2 * 128: tp1 * 512 + (j2 + 1) * 128],
                    oT[tp1][:, j2 * 128:(j2 + 1) * 128], id_sb[:])
            ins2.then_inc(s_tt.h, 1)

        idx = 0
        for kv in range(KV):
            kvp = kv % 2

            def _lkv(sync, kv=kv, kvp=kvp,
                     wS=kv_lastS.get(kv - 2, 0), wP=kv_lastP.get(kv - 2, 0)):
                sync.wait_ge(s_v0.h, pro_v)
                sync.wait_ge(s_ts.h, wS)
                sync.wait_ge(s_tp.h, wP)
                sync.dma_start(out=kh[kvp][0:64, :],
                               in_=kT[kv * 64:(kv + 1) * 64, :]).then_inc(s_kv.h, 16)
                sync.dma_start(out=vh3o[kvp][:, :, 0:64],
                               in_=v3[:, :, kv * 64:(kv + 1) * 64]).then_inc(s_kv.h, 16)
            b.sec("sync", _lkv)
            s_kv.inc(32)
            kv_ld[kv] = s_kv.val

            for hg in range(grp):
                h = kv * grp + hg
                hp = h % 2

                def _lq(sync, h=h, hp=hp, wS=h_lastS.get(h - 2, 0)):
                    sync.wait_ge(s_ts.h, wS)
                    sync.dma_start(out=qh[hp][0:64, :],
                                   in_=qT[h * 64:(h + 1) * 64, :]).then_inc(s_q2[hp].h, 16)
                b.sec("sync", _lq)
                s_q2[hp].inc(16)
                q_ld[h] = s_q2[hp].val

                for qt in range(nqt):
                    pp = idx % 2
                    prev = idx - 1
                    S_at = [s_ts.inc() for _ in range(ncc)]
                    E_at = [s_se.inc() for _ in range(ncc)]
                    P_at = [s_tp.inc() for _ in range(ncc)]
                    sfree = []
                    ptfree = []
                    for cc in range(ncc):
                        sfree.append(slab_exp[cc % 2])
                        slab_exp[cc % 2] = E_at[cc]
                        ptfree.append(slot_pv[cc % 4])
                        slot_pv[cc % 4] = P_at[cc]
                    if prev >= 0:
                        tt_at[prev] = s_tt.inc()

                    def _tn(tq, idx=idx, pp=pp, kvp=kvp, hp=hp, qt=qt,
                            E_at=tuple(E_at), sfree=tuple(sfree),
                            kvld=kv_ld[kv], qld=q_ld[h],
                            vcfree=vc_at[idx - 2], prev=prev,
                            vc_prev=vc_at[idx - 1],
                            vn_tr=vn_at.get(idx - 3, 0)):
                        def pv(ccp):
                            tq.wait_ge(s_se.h, E_at[ccp])
                            if ccp == 0:
                                tq.wait_ge(s_vc.h, vcfree)
                            for j in range(2):
                                k = 2 * ccp + j
                                ins = tq.matmul(
                                    o_ps[0:65, pp * 512:(pp + 1) * 512],
                                    vh[kvp][:, k * 65:(k + 1) * 65],
                                    PT[:, (ccp % 4) * 1024 + j * 512:
                                        (ccp % 4) * 1024 + j * 512 + 512],
                                    start=(k == 0), stop=(k == nkc - 1))
                            ins.then_inc(s_tp.h, 1)
                        for cc in range(ncc):
                            tq.wait_ge(s_se.h, sfree[cc])
                            if cc == 0:
                                tq.wait_ge(s_q2[hp].h, qld)
                                tq.wait_ge(s_kv.h, kvld)
                            slab = cc % 2
                            for j in range(2):
                                k = 2 * cc + j
                                ins = tq.matmul(
                                    S_ps[:, slab * 1024 + j * 512:
                                         slab * 1024 + j * 512 + 512],
                                    kh[kvp][0:64, k * 128:(k + 1) * 128],
                                    qh[hp][0:64, qt * QW:(qt + 1) * QW],
                                    start=True, stop=True)
                            ins.then_inc(s_ts.h, 1)
                            if cc == 1 and prev >= 0:
                                tr_pair(tq, prev, vc_prev, vn_tr)
                            if cc >= 1:
                                pv(cc - 1)
                        pv(ncc - 1)
                    b.emit("tensor", _tn)

                    def _sc(sc, S_at=tuple(S_at), ptfree=tuple(ptfree)):
                        for cc in range(ncc):
                            sc.wait_ge(s_ts.h, S_at[cc])
                            sc.wait_ge(s_tp.h, ptfree[cc])
                            slab = cc % 2
                            sc.activation(
                                out=PT[:, (cc % 4) * 1024:(cc % 4) * 1024 + 1024],
                                in_=S_ps[:, slab * 1024: slab * 1024 + 1024],
                                func=AF.Exp, scale=float(scale),
                                bias=float(ebias)).then_inc(s_se.h, 1)
                    b.emit("scalar", _sc)

                    if prev >= 0:
                        vn_at[prev] = s_vn.inc()
                    vc_at[idx] = s_vc.inc()

                    def _ve(v_, idx=idx, pp=pp, prev=prev, pv15=P_at[ncc - 1],
                            ttprev=tt_at.get(prev, 0),
                            ttfree=tt_at.get(idx - 2, 0),
                            stfree=st_at.get(idx - 3, 0)):
                        v_.wait_ge(s_v0.h, pro_v)
                        if prev >= 0:
                            norm_pair(v_, prev, ttprev, stfree)
                        v_.wait_ge(s_tp.h, pv15)
                        v_.wait_ge(s_tt.h, ttfree)
                        v_.tensor_copy(
                            out=oT[pp][0:65, :],
                            in_=o_ps[0:65, pp * 512:(pp + 1) * 512]).then_inc(s_vc.h, 1)
                    b.emit("vector", _ve)

                    if prev >= 0:
                        def _st(g, prev=prev, vnp=vn_at[prev]):
                            store_pair(g, prev, vnp)
                        b.emit("gpsimd", _st)
                        s_st2[prev % 2].inc(64)
                        st_at[prev] = s_st2[prev % 2].val
                    pair_meta[idx] = (h, qt)
                    b.flush()
                    idx += 1
                h_lastS[h] = s_ts.val
            kv_lastS[kv] = s_ts.val
            kv_lastP[kv] = s_tp.val

        last = idx - 1
        tt_at[last] = s_tt.inc()

        def _tn_tail(tq, last=last, vcl=vc_at[last], vnfree=vn_at.get(last - 2, 0)):
            tr_pair(tq, last, vcl, vnfree)
        b.sec("tensor", _tn_tail)
        vn_at[last] = s_vn.inc()

        def _ve_tail(v_, last=last, ttv=tt_at[last], stfree=st_at.get(last - 2, 0)):
            norm_pair(v_, last, ttv, stfree)
        b.sec("vector", _ve_tail)

        def _st_tail(g, last=last, vnv=vn_at[last]):
            store_pair(g, last, vnv)
        b.sec("gpsimd", _st_tail)
        s_st2[last % 2].inc(64)
        st_at[last] = s_st2[last % 2].val

        def _fin(g):
            g.wait_ge(s_st2[0].h, s_st2[0].val)
            g.wait_ge(s_st2[1].h, s_st2[1].val)
        b.sec("gpsimd", _fin)


def emit_L3A(b, W, xr_in, xr_out, uid):
    zT = b.dtmp(f"zT_{uid}", [D, T], BF16)
    qkT = b.dtmp(f"l3a_qkT_{uid}", [2048, T], BF16)
    qrT = b.dtmp(f"l3a_qrT_{uid}", [1024, T], BF16)
    krT = b.dtmp(f"l3a_krT_{uid}", [1024, T], BF16)
    vtm = b.dtmp(f"l3a_v_{uid}", [T, 1024], BF16)
    attO = b.dtmp(f"l3a_o_{uid}", [T, 1024], BF16)
    OT = b.dtmp(f"l3a_OT_{uid}", [1024, T], BF16)
    st_ln(b, xr_in, zT=zT, ident=W["ident_bf"])

    def ev_qk(E):
        return E.q.tensor_scalar(out=E.out, in0=E.psum, scalar1=0.125 ** 0.5,
                                 scalar2=None, op0=OP.mult)
    st_mm_fm(b, f"l3a_qk_{uid}", [dict(x=zT, w=W["l3_wqk"], K=D)], 2048,
             [(qkT, 0, 2048)], evac_v=ev_qk)
    st_mm_tm(b, f"l3a_v_{uid}", [dict(x=zT, w=W["l3_wv"], K=D)], 1024, vtm,
             out_dt=BF16)
    st_rope(b, f"l3a_rq_{uid}", qkT[0:1024, :], qrT, 16, W["rope_cos"], W["rope_sin"])
    st_rope(b, f"l3a_rk_{uid}", qkT[1024:2048, :], krT, 16, W["rope_cos"], W["rope_sin"])
    st_attn64(b, f"l3a_at_{uid}", qrT, krT, vtm, attO, H=16,
              ident=W["ident_bf"], scale=1.0, out_dt=BF16)
    st_transpose_add(b, f"l3a_tr_{uid}", attO, 4096, None, OT, 1.0,
                     W["ident_bf"], Tn=1024, out_dt=BF16)

    def ev_v(E):
        return E.q.tensor_tensor(out=E.out, in0=E.psum, in1=E.aux[0], op=OP.add)
    st_mm_tm(b, f"l3a_o_{uid}", [dict(x=OT, w=W["l3_wout"], K=D)], D, xr_out,
             evac_v=ev_v, aux=[(xr_in, F32)], out_dt=F32)


def emit_L2A(b, W, xr_in, xr_out, uid):
    zT = b.dtmp(f"zT_{uid}", [D, T], BF16)
    qT = b.dtmp(f"l2a_qT_{uid}", [1024, T], BF16)
    kT = b.dtmp(f"l2a_kT_{uid}", [256, T], BF16)
    qrT = b.dtmp(f"l2a_qrT_{uid}", [1024, T], BF16)
    krT = b.dtmp(f"l2a_krT_{uid}", [256, T], BF16)
    vtm = b.dtmp(f"l2a_v_{uid}", [T, 256], BF16)
    attO = b.dtmp(f"l2a_o_{uid}", [T, 1024], BF16)
    OT = b.dtmp(f"l2a_OT_{uid}", [1024, T], BF16)
    st_ln(b, xr_in, zT=zT, ident=W["ident_bf"])

    def ev_q(E):
        return E.q.tensor_scalar(out=E.out, in0=E.psum, scalar1=0.125,
                                 scalar2=None, op0=OP.mult)
    st_mm_fm(b, f"l2a_q_{uid}", [dict(x=zT, w=W["l2_wq"], K=D)], 1024,
             [(qT, 0, 1024)], evac_v=ev_q)
    st_mm_fm(b, f"l2a_k_{uid}", [dict(x=zT, w=W["l2_wk"], K=D)], 256,
             [(kT, 0, 256)])
    st_mm_tm(b, f"l2a_v_{uid}", [dict(x=zT, w=W["l2_wv"], K=D)], 256, vtm,
             out_dt=BF16)
    st_rope(b, f"l2a_rq_{uid}", qT, qrT, 16, W["rope_cos"], W["rope_sin"])
    st_rope(b, f"l2a_rk_{uid}", kT, krT, 4, W["rope_cos"], W["rope_sin"])
    st_attn64(b, f"l2a_at_{uid}", qrT, krT, vtm, attO, H=16, KV=4,
              ident=W["ident_bf"], scale=1.0, out_dt=BF16)
    st_transpose_add(b, f"l2a_tr_{uid}", attO, 4096, None, OT, 1.0,
                     W["ident_bf"], Tn=1024, out_dt=BF16)

    def ev_v(E):
        return E.q.tensor_tensor(out=E.out, in0=E.psum, in1=E.aux[0], op=OP.add)
    st_mm_tm(b, f"l2a_o_{uid}", [dict(x=OT, w=W["l2_wo"], K=D)], D, xr_out,
             evac_v=ev_v, aux=[(xr_in, F32)], out_dt=F32)


def st_greduce(b, name, src, dst, R, Nin):
    Nout = Nin // 8
    nch = cdiv(R, 128)
    with b.stage(name) as sctx:
        s_v = b.sem()
        s_ld2 = b.sem2()
        s_st2 = b.sem2_sw()
        a = [b.sb(sctx, f"{name}_a{i}", [128, Nin], F32) for i in range(2)]
        o = [b.sb(sctx, f"{name}_o{i}", [128, Nout], F32) for i in range(2)]
        v_at = {}
        st_at = {}
        for i in range(nch):
            bi = i % 2
            cw = min(128, R - i * 128)

            def _ld(sync, i=i, bi=bi, cw=cw):
                if i >= 2:
                    sync.wait_ge(s_v.h, v_at[i - 2])
                sync.dma_start(out=a[bi][0:cw, :],
                               in_=src[i * 128:i * 128 + cw, :]).then_inc(s_ld2[bi].h, 16)
            b.sec("sync", _ld)
            s_ld2[bi].inc(16)
            ld_i = s_ld2[bi].val

            def _v(v, i=i, bi=bi, cw=cw, ld_i=ld_i):
                v.wait_ge(s_ld2[bi].h, ld_i)
                if i >= 2:
                    v.wait_ge(s_st2[bi].h, st_at[i - 2])
                v.reduce_sum(out=o[bi][0:cw, :],
                             in_=a[bi][0:cw, :].rearrange("p (g k) -> p g k", k=8),
                             axis=AX).then_inc(s_v.h, 1)
            b.sec("vector", _v)
            s_v.inc(1)
            v_at[i] = s_v.val

            def _st(g, i=i, bi=bi, cw=cw):
                g.wait_ge(s_v.h, v_at[i])
                g.dma_start(out=dst[i * 128:i * 128 + cw, :],
                            in_=o[bi][0:cw, :]).then_inc(s_st2[bi].h, 16)
            b.sec("gpsimd", _st)
            s_st2[bi].inc(16)
            st_at[i] = s_st2[bi].val

        def _fin(g):
            g.wait_ge(s_st2[0].h, s_st2[0].val)
            g.wait_ge(s_st2[1].h, s_st2[1].val)
        b.sec("gpsimd", _fin)


def emit_L1A(b, W, xr_in, xr_out, uid):
    zT = b.dtmp(f"zT_{uid}", [D, T], BF16)
    qT = b.dtmp(f"l1a_qT_{uid}", [1024, T], BF16)
    kT = b.dtmp(f"l1a_kT_{uid}", [1024, T], BF16)
    wvr = b.dtmp(f"l1a_wvr_{uid}", [D, 1024], F32)
    bvr = b.dtmp(f"l1a_bvr_{uid}", [1, 1024], F32)
    vtm = b.dtmp(f"l1a_v_{uid}", [T, 1024], BF16)
    delta = b.dtmp(f"l1a_d_{uid}", [T, D], F32)
    st_ln(b, xr_in, zT=zT, gamma=W["l1_gamma"], beta=W["l1_beta"],
          ident=W["ident_bf"])
    st_greduce(b, f"l1a_wr_{uid}", W["l1_wv"], wvr, D, 8192)
    st_greduce(b, f"l1a_br_{uid}", W["l1_bv"], bvr, 1, 8192)

    def ev_bias_q(E):
        return E.q.tensor_scalar(out=E.out, in0=E.psum, scalar1=E.col(0),
                                 scalar2=128.0 ** -0.5, op0=OP.add, op1=OP.mult)

    def ev_bias(E):
        return E.q.tensor_scalar(out=E.out, in0=E.psum, scalar1=E.col(0),
                                 scalar2=None, op0=OP.add)
    st_mm_fm(b, f"l1a_q_{uid}", [dict(x=zT, w=W["l1_wq"], K=D)], 1024,
             [(qT, 0, 1024)], evac_v=ev_bias_q, cols=[W["l1_bqc"]])
    st_mm_fm(b, f"l1a_k_{uid}", [dict(x=zT, w=W["l1_wk"], K=D)], 1024,
             [(kT, 0, 1024)], evac_v=ev_bias, cols=[W["l1_bkc"]])
    st_mm_tm(b, f"l1a_v_{uid}", [dict(x=zT, w=wvr, K=D)], 1024, vtm,
             bias_row=bvr, out_dt=BF16)

    d6 = delta.rearrange("(hh R a bb) (cc f) -> hh R a bb cc f",
                         hh=8, R=32, a=2, bb=8, cc=8)
    sem_sc = b.sem_sw()

    def writer(g, h, qi, ob):
        g.dma_start(out=d6[h, qi, 0], in_=ob[0:64, :]).then_inc(sem_sc.h, 16)
        g.dma_start(out=d6[h, qi, 1], in_=ob[64:128, :]).then_inc(sem_sc.h, 16)
        sem_sc.inc(32)
        g.wait_ge(sem_sc.h, sem_sc.val)
    st_attn(b, f"l1a_at_{uid}", qT, kT, vtm, None, H=8, dh=128, dv=128,
            ident=W["ident_bf"], scale=1.0, out_writer=writer,
            out_dt=F32)
    st_add(b, f"l1a_add_{uid}", xr_in, [(delta, 1.0)], xr_out)


def emit_fff(b, W, pfx, xT32, hseg, outT_spec, act, uid, gT, ohT, maskT, HmT):
    def ev_route(E):
        return E.q.tensor_scalar(out=E.out, in0=E.psum, scalar1=E.col(0),
                                 scalar2=0.0, op0=OP.add, op1=OP.is_gt)
    st_mm_fm(b, f"{pfx}rt_{uid}", [dict(x=xT32, w=W[pfx + "nwT"], K=D, dt=F32)],
             256, [(gT, 0, 256)], evac_v=ev_route, cols=[W[pfx + "nbc"]])

    def ev_leaf(E):
        return E.q.tensor_scalar(out=E.out, in0=E.psum, scalar1=E.col(0),
                                 scalar2=7.5, op0=OP.add, op1=OP.is_gt)
    st_mm_fm(b, f"{pfx}oh_{uid}", [dict(x=gT, w=W["fff_M2"], K=256)], 256,
             [(ohT, 0, 256)], evac_v=ev_leaf, cols=[W["fff_cs"]])

    def ev_s_h(E):
        return E.q.activation(out=E.tmp, in_=E.psum, func=act, bias=E.col(0))

    def ev_v_h(E):
        return E.q.tensor_tensor(out=E.out, in0=E.tmp, in1=E.psum2, op=OP.mult)
    st_mm_fm(b, f"{pfx}h_{uid}", [dict(list(hseg.items()) + [("w", W[pfx + "w1p"]),
                                        ("K", D), ("wlayout", "panels")]),
                                  dict(x=ohT, w=W["fff_E"], K=256, alt=True)],
             4096, [(HmT, 0, 4096)], evac_s=ev_s_h, evac_v=ev_v_h,
             cols=[W[pfx + "b1c"]])

    segs2 = [dict(x=HmT, w=W[pfx + "w2p"], K=4096, wlayout="panels"),
             dict(x=ohT, w=W[pfx + "b2s"], K=256)]
    segs2tm = [dict(x=HmT, w=W[pfx + "w2r"], K=4096),
               dict(x=ohT, w=W[pfx + "b2s"], K=256)]
    if outT_spec[0] == "fm":
        qsc = outT_spec[2] if len(outT_spec) > 2 else None
        if qsc is None:
            st_mm_fm(b, f"{pfx}o_{uid}", segs2, D, [(outT_spec[1], 0, D)])
        else:
            def ev_sc(E, qsc=qsc):
                return E.q.tensor_scalar(out=E.out, in0=E.psum, scalar1=float(qsc),
                                         scalar2=None, op0=OP.mult)
            st_mm_fm(b, f"{pfx}o_{uid}", segs2, D, [(outT_spec[1], 0, D)],
                     evac_v=ev_sc)
    else:
        _, outd, ev_s, ev_v, aux, odt = outT_spec
        st_mm_tm(b, f"{pfx}o_{uid}", segs2tm, D, outd, evac_s=ev_s, evac_v=ev_v,
                 aux=aux, out_dt=odt)


def emit_L0F(b, W, xr_in, xr_out, uid, upto=99):
    z0 = b.dtmp(f"l0f_z0_{uid}", [T, D], F32)
    n0 = b.dtmp(f"l0f_n0_{uid}", [T, D], F32)
    n0T = b.dtmp(f"l0f_n0T_{uid}", [D, T], BF16)
    n0T32 = b.dtmp(f"l0f_n0T32_{uid}", [D, T], F32)
    gT = b.dtmp(f"l0f_gT_{uid}", [256, T], BF16)
    ohT = b.dtmp(f"l0f_ohT_{uid}", [256, T], BF16)
    maskT = b.dtmp(f"l0f_mkT_{uid}", [4096, T], BF16)
    HmT = b.dtmp(f"l0f_HmT_{uid}", [4096, T], BF16)
    qfT = b.dtmp(f"l0f_qfT_{uid}", [D, T], BF16)
    kfT = b.dtmp(f"l0f_kfT_{uid}", [D, T], BF16)
    vf = b.dtmp(f"l0f_vf_{uid}", [T, D], BF16)
    u = b.dtmp(f"l0f_u_{uid}", [T, D], F32)
    uT32 = b.dtmp(f"l0f_uT32_{uid}", [D, T], F32)
    st_ln(b, xr_in, z=z0, ident=W["ident_bf"])
    st_ln(b, z0, variant="inner", z=n0, zT=n0T, zT32=n0T32,
          ident=W["ident_bf"], ident32=W["ident_f32"])
    hseg_n0 = dict(x=n0T)
    if upto < 1:
        st_add(b, f"l0f_dbg_{uid}", xr_in, [(z0, 0.001)], xr_out)
        return
    emit_fff(b, W, "l0_q_", n0T32, hseg_n0, ("fm", qfT, 1.0 / 32.0), AF.Relu, uid, gT, ohT, maskT, HmT)
    if upto < 2:
        st_add(b, f"l0f_dbg_{uid}", xr_in, [(n0, 0.001)], xr_out)
        return
    emit_fff(b, W, "l0_k_", n0T32, hseg_n0, ("fm", kfT), AF.Relu, uid, gT, ohT, maskT, HmT)
    emit_fff(b, W, "l0_v_", n0T32, hseg_n0, ("tm", vf, None, _copy_evac, (), BF16),
             AF.Relu, uid, gT, ohT, maskT, HmT)
    if upto < 3:
        st_add(b, f"l0f_dbg_{uid}", xr_in, [(vf, 0.001)], xr_out, plus_dt=[BF16])
        return
    st_attn(b, f"l0f_at_{uid}", qfT, kfT, vf, u, H=1, dh=1024, dv=1024,
            ident=W["ident_bf"], scale=1.0, aux_add=n0, out_dt=F32)
    if upto < 4:
        st_add(b, f"l0f_dbg_{uid}", xr_in, [(u, 0.001)], xr_out)
        return
    st_transpose_add(b, f"l0f_tr_{uid}", u, 4096, None, uT32, 1.0,
                     W["ident_f32"], Tn=1024, out_dt=F32, src_dt=F32)
    if upto < 5:
        st_add(b, f"l0f_dbg_{uid}", xr_in, [(u, 0.001)], xr_out)
        return

    def ev_v_fin(E):
        E.step(E.q.scalar_tensor_tensor(out=E.tmp, in0=E.aux[1], scalar=0.5,
                                        in1=E.aux[0], op0=OP.mult, op1=OP.add))
        return E.q.scalar_tensor_tensor(out=E.out, in0=E.psum, scalar=0.5,
                                        in1=E.tmp, op0=OP.mult, op1=OP.add)
    emit_fff(b, W, "l0_f_", uT32, dict(x=uT32, cast=True),
             ("tm", xr_out, None, ev_v_fin, [(xr_in, F32), (u, F32)], F32),
             AF.Gelu, uid, gT, ohT, maskT, HmT)


def st_mamba_conv(b, name, xbcT, convT, conv_w, conv_b, R=2304, Tn=T):
    nch = R // 128
    with b.stage(name) as sctx:
        s_v, s_s = b.sem(), b.sem()
        s_ld2 = b.sem2()
        s_st2 = b.sem2_sw()
        xx = [b.sb(sctx, f"{name}_x{i}", [128, Tn], BF16) for i in range(2)]
        wc = [b.sb(sctx, f"{name}_w{i}", [128, 5], F32) for i in range(2)]
        acc = [b.sb(sctx, f"{name}_a{i}", [128, Tn], F32) for i in range(2)]
        oo = [b.sb(sctx, f"{name}_o{i}", [128, Tn], BF16) for i in range(2)]
        v_at = {}
        s_at = {}
        st_at = {}
        for i in range(nch):
            bi = i % 2

            def _ld(sync, i=i, bi=bi):
                if i >= 2:
                    sync.wait_ge(s_s.h, s_at[i - 2])
                    sync.wait_ge(s_st2[bi].h, st_at[i - 2])
                sync.dma_start(out=xx[bi][:], in_=xbcT[i * 128:(i + 1) * 128, :]).then_inc(s_ld2[bi].h, 16)
                sync.dma_start(out=wc[bi][:, 0:4], in_=conv_w[i * 128:(i + 1) * 128, :]).then_inc(s_ld2[bi].h, 16)
                sync.dma_start(out=wc[bi][:, 4:5], in_=conv_b[i * 128:(i + 1) * 128, :]).then_inc(s_ld2[bi].h, 16)
            b.sec("sync", _ld)
            s_ld2[bi].inc(48)
            ld_i = s_ld2[bi].val

            def _v(v, i=i, bi=bi, ld_i=ld_i):
                st = chainer(v, b.csem["vector"])
                v.wait_ge(s_ld2[bi].h, ld_i)
                st(v.tensor_scalar(out=acc[bi][:], in0=xx[bi][:], scalar1=wc[bi][:, 3:4],
                                   scalar2=wc[bi][:, 4:5], op0=OP.mult, op1=OP.add))
                for k in range(3):
                    sh = 3 - k
                    st(v.scalar_tensor_tensor(out=acc[bi][:, sh:Tn], in0=xx[bi][:, 0:Tn - sh],
                                              scalar=wc[bi][:, k:k + 1], in1=acc[bi][:, sh:Tn],
                                              op0=OP.mult, op1=OP.add))
                v.sem_inc(s_v.h, 1)
            b.sec("vector", _v)
            s_v.inc(1)
            v_at[i] = s_v.val

            def _s(sc, i=i, bi=bi):
                sc.wait_ge(s_v.h, v_at[i])
                sc.activation(out=oo[bi][:], in_=acc[bi][:], func=AF.Silu).then_inc(s_s.h, 1)
            b.sec("scalar", _s)
            s_s.inc(1)
            s_at[i] = s_s.val

            def _st(g, i=i, bi=bi):
                g.wait_ge(s_s.h, s_at[i])
                g.dma_start(out=convT[i * 128:(i + 1) * 128, :], in_=oo[bi][:]).then_inc(s_st2[bi].h, 16)
            b.sec("gpsimd", _st)
            s_st2[bi].inc(16)
            st_at[i] = s_st2[bi].val

        def _fin(g):
            g.wait_ge(s_st2[0].h, s_st2[0].val)
            g.wait_ge(s_st2[1].h, s_st2[1].val)
        b.sec("gpsimd", _fin)


def st_mamba_dt(b, name, dtrawT, dta_tm, dt_bias, A_log, ident32, Tn=T):
    with b.stage(name) as sctx:
        s_ld, s_v, s_s, s_t = (b.sem() for _ in range(4))
        s_st2 = b.sem2_sw()
        raw = b.sb(sctx, f"{name}_r", [16, Tn], F32)
        u = b.sb(sctx, f"{name}_u", [16, Tn], F32)
        t1 = b.sb(sctx, f"{name}_t1", [16, Tn], F32)
        t2 = b.sb(sctx, f"{name}_t2", [16, Tn], F32)
        dta = b.sb(sctx, f"{name}_d", [128, Tn], F32)
        cols = b.sb(sctx, f"{name}_c", [16, 3], F32)
        id32 = b.sb(sctx, f"{name}_id", [128, 128], F32)
        ptr = [b.ps(sctx, f"{name}_p{i}", [128, 128], F32) for i in range(2)]
        otr = [b.sb(sctx, f"{name}_o{i}", [128, 48], F32) for i in range(2)]

        def _ld(sync):
            sync.dma_start(out=raw[:], in_=dtrawT[0:16, :]).then_inc(s_ld.h, 16)
            sync.dma_start(out=cols[:, 0:1], in_=dt_bias).then_inc(s_ld.h, 16)
            sync.dma_start(out=cols[:, 1:2], in_=A_log).then_inc(s_ld.h, 16)
            sync.dma_start(out=id32[:], in_=ident32).then_inc(s_ld.h, 16)
        b.sec("sync", _ld)
        s_ld.inc(64)
        ld0 = s_ld.val

        def _v1(v):
            v.wait_ge(s_ld.h, ld0)
            v.memset(dta[:], 0.0)
            v.tensor_scalar(out=u[:], in0=raw[:], scalar1=cols[:, 0:1],
                            scalar2=None, op0=OP.add).then_inc(s_v.h, 1)
        b.sec("vector", _v1)
        s_v.inc(1)
        u_at = s_v.val

        def _s1(sc):
            st = chainer(sc, b.csem["scalar"])
            sc.wait_ge(s_v.h, u_at)
            st(sc.activation(out=t1[:], in_=u[:], func=AF.Abs))
            st(sc.activation(out=t1[:], in_=t1[:], func=AF.Exp, scale=-1.0))
            st(sc.activation(out=t1[:], in_=t1[:], func=AF.Ln, bias=1.0))
            st(sc.activation(out=t2[:], in_=u[:], func=AF.Relu))
            st(sc.activation(out=cols[:, 2:3], in_=cols[:, 1:2], func=AF.Exp))
            sc.sem_inc(s_s.h, 1)
        b.sec("scalar", _s1)
        s_s.inc(1)
        sp_at = s_s.val

        def _v2(v):
            st = chainer(v, b.csem["vector"])
            v.wait_ge(s_s.h, sp_at)
            st(v.tensor_tensor(out=dta[0:16, :], in0=t1[:], in1=t2[:], op=OP.add))
            st(v.tensor_scalar(out=cols[:, 2:3], in0=cols[:, 2:3], scalar1=-1.0,
                               scalar2=None, op0=OP.mult))
            st(v.tensor_scalar(out=dta[32:48, :], in0=dta[0:16, :],
                               scalar1=cols[:, 2:3], scalar2=None, op0=OP.mult))
            v.sem_inc(s_v.h, 1)
        b.sec("vector", _v2)
        s_v.inc(1)
        dta_at = s_v.val

        tr_at = {}
        ev_at = {}
        st_at = {}
        for c in range(Tn // 128):
            pk = c % 2

            def _tr(tq, c=c, pk=pk):
                tq.wait_ge(s_v.h, dta_at)
                tq.wait_ge(s_ld.h, ld0)
                if c >= 2:
                    tq.wait_ge(s_v.h, ev_at[c - 2])
                tq.transpose(ptr[pk][:], dta[:, c * 128:(c + 1) * 128],
                             id32[:]).then_inc(s_t.h, 1)
            b.sec("tensor", _tr)
            s_t.inc(1)
            tr_at[c] = s_t.val

            def _ev(v, c=c, pk=pk):
                v.wait_ge(s_t.h, tr_at[c])
                if c >= 2:
                    v.wait_ge(s_st2[pk].h, st_at[c - 2])
                v.tensor_copy(out=otr[pk][:], in_=ptr[pk][:, 0:48]).then_inc(s_v.h, 1)
            b.sec("vector", _ev)
            s_v.inc(1)
            ev_at[c] = s_v.val

            def _st(g, c=c, pk=pk):
                g.wait_ge(s_v.h, ev_at[c])
                g.dma_start(out=dta_tm[c * 128:(c + 1) * 128, :],
                            in_=otr[pk][:]).then_inc(s_st2[pk].h, 16)
            b.sec("gpsimd", _st)
            s_st2[pk].inc(16)
            st_at[c] = s_st2[pk].val

        def _fin(g):
            g.wait_ge(s_st2[0].h, s_st2[0].val)
            g.wait_ge(s_st2[1].h, s_st2[1].val)
        b.sec("gpsimd", _fin)


def st_mamba_ssd(b, name, convT, dta_tm, yT, Drow, tri_incl, triU, trimask,
                 ident, Tn=T, NH=16, P=128, NS=128):
    nchk = Tn // 128
    with b.stage(name) as sctx:
        s_ld, s_t, s_s, s_v = (b.sem() for _ in range(4))
        s_ld2 = b.sem2()
        s_st2 = b.sem2_sw()
        tri_sb = b.sb(sctx, f"{name}_tri", [128, 384], F32)
        trim_sb = b.sb(sctx, f"{name}_trm", [128, 128], BF16)
        id_sb = b.sb(sctx, f"{name}_id", [128, 128], BF16)
        ones_sb = b.sb(sctx, f"{name}_on", [128, 128], F32)
        ones1 = b.sb(sctx, f"{name}_o1", [1, 128], F32)
        Db = b.sb(sctx, f"{name}_Db", [128, 16], F32)
        Drow_sb = b.sb(sctx, f"{name}_Dr", [1, 16], F32)
        St = b.sb(sctx, f"{name}_St", [128, NH * P], F32)
        Stbf = b.sb(sctx, f"{name}_Sb", [128, NH * P], BF16)
        dta = [b.sb(sctx, f"{name}_dt{i}", [128, 48], F32) for i in range(2)]
        BT = [b.sb(sctx, f"{name}_B{i}", [128, 128], BF16) for i in range(2)]
        CT = [b.sb(sctx, f"{name}_C{i}", [128, 128], BF16) for i in range(2)]
        xhT = [b.sb(sctx, f"{name}_xh{i}", [128, NH * P], BF16) for i in range(2)]
        Btm = [b.sb(sctx, f"{name}_Bt{i}", [128, 128], BF16) for i in range(2)]
        csc = [b.sb(sctx, f"{name}_cs{i}", [128, 64], F32) for i in range(2)]
        acb = [b.sb(sctx, f"{name}_ac{i}", [128, 128], F32) for i in range(2)]
        dcl = [b.sb(sctx, f"{name}_dc{i}", [128, 128], F32) for i in range(2)]
        expd = [b.sb(sctx, f"{name}_ed{i}", [128, 128], F32) for i in range(2)]
        expa = [b.sb(sctx, f"{name}_ea{i}", [128, 128], F32) for i in range(2)]
        cbm = [b.sb(sctx, f"{name}_cb{i}", [128, 128], F32) for i in range(2)]
        hsb = [b.sb(sctx, f"{name}_H{i}", [128, 128], BF16) for i in range(2)]
        cp = [b.sb(sctx, f"{name}_Cp{i}", [128, 128], BF16) for i in range(2)]
        xp = [b.sb(sctx, f"{name}_Xp{i}", [128, 128], BF16) for i in range(2)]
        bw = [b.sb(sctx, f"{name}_Bw{i}", [128, 128], BF16) for i in range(2)]
        yt = [b.sb(sctx, f"{name}_yt{i}", [128, 128], BF16) for i in range(2)]
        pA = b.ps(sctx, f"{name}_pA", [128, 128], F32)
        pD = b.ps(sctx, f"{name}_pD", [128, 128], F32)
        pY = b.ps(sctx, f"{name}_pY", [128, 128], F32)
        pS = b.ps(sctx, f"{name}_pS", [128, 128], F32)
        pTr = b.ps(sctx, f"{name}_pT", [128, 128], BF16)
        pBC = b.ps(sctx, f"{name}_pBC", [128, 32], F32)
        pDb = b.ps(sctx, f"{name}_pDb", [128, 16], F32)

        def _pro(sync):
            sync.dma_start(out=tri_sb[:, 0:128], in_=tri_incl).then_inc(s_ld.h, 16)
            sync.dma_start(out=tri_sb[:, 128:256], in_=triU).then_inc(s_ld.h, 16)
            sync.dma_start(out=trim_sb[:], in_=trimask).then_inc(s_ld.h, 16)
            sync.dma_start(out=Drow_sb[:], in_=Drow).then_inc(s_ld.h, 16)
            sync.dma_start(out=id_sb[:], in_=ident).then_inc(s_ld.h, 16)
        b.sec("sync", _pro)
        s_ld.inc(80)
        pro_ld = s_ld.val

        def _v0(v):
            v.memset(ones_sb[:], 1.0)
            v.memset(ones1[:], 1.0)
            v.memset(St[:], 0.0)
            v.memset(Stbf[:], 0.0)
            v.drain()
            v.sem_inc(s_v.h, 1)
        b.sec("vector", _v0)
        s_v.inc(1)
        init_v = s_v.val

        def _t0(tq):
            tq.wait_ge(s_ld.h, pro_ld)
            tq.wait_ge(s_v.h, init_v)
            tq.matmul(pDb[:], ones1[:], Drow_sb[:], start=True, stop=True).then_inc(s_t.h, 1)
        b.sec("tensor", _t0)
        s_t.inc(1)
        db_mm = s_t.val

        def _v00(v):
            v.wait_ge(s_t.h, db_mm)
            v.tensor_copy(out=Db[:], in_=pDb[:]).then_inc(s_v.h, 1)
        b.sec("vector", _v00)
        s_v.inc(1)
        db_at = s_v.val

        last = dict(V1={}, T1={}, V2={}, S1={}, V3={}, T2={}, V4={}, G1={},
                    T0={}, V0={})
        stbf_at = [db_at] * NH
        ytst = {}

        xh3 = convT.rearrange("(hc p) t -> p hc t", p=128)
        for c in range(nchk):
            cb = c % 2

            def _ldc(sync, c=c, cb=cb):
                if c >= 2:
                    sync.wait_ge(s_t.h, last["T2"][(c - 2, NH - 1)])
                    sync.wait_ge(s_v.h, last["V4"][(c - 2, NH - 1)])
                sync.dma_start(out=dta[cb][:],
                               in_=dta_tm[c * 128:(c + 1) * 128, :]).then_inc(s_ld2[cb].h, 16)
                sync.dma_start(out=BT[cb][:],
                               in_=convT[2048:2176, c * 128:(c + 1) * 128]).then_inc(s_ld2[cb].h, 16)
                sync.dma_start(out=CT[cb][:],
                               in_=convT[2176:2304, c * 128:(c + 1) * 128]).then_inc(s_ld2[cb].h, 16)
                sync.dma_start(out=xhT[cb][:],
                               in_=xh3[:, 0:16, c * 128:(c + 1) * 128]).then_inc(s_ld2[cb].h, 16)
            b.sec("sync", _ldc)
            s_ld2[cb].inc(64)
            ld_c = s_ld2[cb].val

            def _tc(tq, c=c, cb=cb, ld_c=ld_c):
                tq.wait_ge(s_ld2[cb].h, ld_c)
                tq.wait_ge(s_ld.h, pro_ld)
                if c >= 1:
                    tq.wait_ge(s_v.h, last["V0"][c - 1])
                    tq.wait_ge(s_v.h, last["V2"][(c - 1, NH - 1)])
                    tq.wait_ge(s_v.h, last["V3"][(c - 1, NH - 1)])
                tq.matmul(pBC[:, 0:16], tri_sb[:, 0:128], dta[cb][:, 32:48],
                          start=True, stop=True)
                tq.matmul(pBC[:, 16:32], tri_sb[:, 128:256], dta[cb][:, 32:48],
                          start=True, stop=True).then_inc(s_t.h, 1)
                tq.matmul(pD[:], BT[cb][:], CT[cb][:],
                          start=True, stop=True)
                tq.transpose(pTr[:], BT[cb][:], id_sb[:]).then_inc(s_t.h, 1)
            b.sec("tensor", _tc)
            s_t.inc(2)
            last["T0"][c] = s_t.val

            def _vc(v, c=c, cb=cb):
                st = chainer(v, b.csem["vector"])
                v.wait_ge(s_t.h, last["T0"][c])
                st(v.tensor_copy(out=csc[cb][:, 0:32], in_=pBC[:, 0:32]))
                st(v.tensor_tensor(out=csc[cb][:, 32:48], in0=csc[cb][:, 16:32],
                                   in1=dta[cb][:, 32:48], op=OP.add))
                st(v.tensor_copy(out=Btm[cb][:], in_=pTr[:]))
                st(v.tensor_tensor(out=cbm[cb][:], in0=pD[:], in1=trim_sb[:],
                                   op=OP.mult))
                v.sem_inc(s_v.h, 1)
            b.sec("vector", _vc)
            s_v.inc(1)
            last["V0"][c] = s_v.val

            def _sc(sc, c=c, cb=cb):
                st = chainer(sc, b.csem["scalar"])
                sc.wait_ge(s_v.h, last["V0"][c])
                st(sc.activation(out=csc[cb][:, 48:64], in_=csc[cb][:, 16:32],
                                 func=AF.Exp))
                st(sc.activation(out=csc[cb][:, 32:48], in_=csc[cb][:, 32:48],
                                 func=AF.Exp))
                sc.sem_inc(s_s.h, 1)
            b.sec("scalar", _sc)
            s_s.inc(1)
            sc_c = s_s.val

            for h in range(NH):
                hp = h % 2

                def _v1(v, c=c, cb=cb, h=h, hp=hp):
                    v.wait_ge(s_ld2[cb].h, ld_c)
                    if (c, h) != (0, 0) and (c * NH + h) >= 2:
                        ch2 = divmod(c * NH + h - 2, NH)
                        v.wait_ge(s_t.h, last["T1"][(ch2[0], ch2[1])])
                    v.tensor_scalar(out=acb[hp][:], in0=ones_sb[:],
                                    scalar1=dta[cb][:, 32 + h:33 + h], scalar2=None,
                                    op0=OP.mult).then_inc(s_v.h, 1)
                b.emit("vector", _v1)
                s_v.inc(1)
                last["V1"][(c, h)] = s_v.val

                def _t1(tq, c=c, cb=cb, h=h, hp=hp):
                    k = c * NH + h
                    tq.wait_ge(s_v.h, last["V1"][(c, h)])
                    tq.wait_ge(s_v.h, last["V0"][c])
                    if k >= 1:
                        kp = divmod(k - 1, NH)
                        tq.wait_ge(s_v.h, last["V2"][kp])
                        tq.wait_ge(s_s.h, last["S1"][kp])
                    tq.matmul(pA[:], acb[hp][:], tri_sb[:, 0:128],
                              start=True, stop=True)
                    tq.transpose(pTr[:], xhT[cb][:, h * 128:(h + 1) * 128],
                                 id_sb[:]).then_inc(s_t.h, 1)
                b.emit("tensor", _t1)
                s_t.inc(1)
                last["T1"][(c, h)] = s_t.val

                def _v2(v, c=c, cb=cb, h=h, hp=hp):
                    st = chainer(v, b.csem["vector"])
                    v.wait_ge(s_t.h, last["T1"][(c, h)])
                    v.wait_ge(s_v.h, last["V0"][c])
                    st(v.tensor_scalar(out=xp[hp][:], in0=pTr[:],
                                       scalar1=dta[cb][:, h:h + 1], scalar2=None,
                                       op0=OP.mult))
                    st(v.tensor_scalar(out=dcl[hp][:], in0=pA[:],
                                       scalar1=csc[cb][:, h:h + 1], scalar2=0.0,
                                       op0=OP.subtract, op1=OP.min))
                    v.sem_inc(s_v.h, 1)
                b.emit("vector", _v2)
                s_v.inc(1)
                last["V2"][(c, h)] = s_v.val

                def _s1(sc, c=c, h=h, hp=hp):
                    st = chainer(sc, b.csem["scalar"])
                    sc.wait_ge(s_v.h, last["V2"][(c, h)])
                    st(sc.activation(out=expd[hp][:], in_=dcl[hp][:], func=AF.Exp))
                    st(sc.activation(out=expa[hp][:], in_=pA[:], func=AF.Exp))
                    sc.sem_inc(s_s.h, 1)
                b.emit("scalar", _s1)
                s_s.inc(1)
                last["S1"][(c, h)] = s_s.val

                def _v3(v, c=c, cb=cb, h=h, hp=hp):
                    v.wait_ge(s_s.h, last["S1"][(c, h)])
                    v.wait_ge(s_s.h, sc_c)
                    v.tensor_tensor(out=hsb[hp][:], in0=cbm[cb][:],
                                    in1=expd[hp][:], op=OP.mult)
                    v.tensor_tensor(out=cp[hp][:], in0=CT[cb][:],
                                    in1=expa[hp][:], op=OP.mult)
                    v.tensor_scalar(out=bw[hp][:], in0=Btm[cb][:],
                                    scalar1=csc[cb][:, 48 + h:49 + h],
                                    scalar2=None, op0=OP.mult).then_inc(s_v.h, 1)
                b.emit("vector", _v3)
                s_v.inc(1)
                last["V3"][(c, h)] = s_v.val

                def _t2(tq, c=c, h=h, hp=hp, stw=stbf_at[h]):
                    k = c * NH + h
                    tq.wait_ge(s_v.h, last["V3"][(c, h)])
                    tq.wait_ge(s_v.h, stw)
                    if k >= 1:
                        kp = divmod(k - 1, NH)
                        tq.wait_ge(s_v.h, last["V4"][kp])
                    tq.matmul(pY[:], Stbf[:, h * 128:(h + 1) * 128], cp[hp][:],
                              start=True, stop=False)
                    tq.matmul(pY[:], xp[hp][:], hsb[hp][:],
                              start=False, stop=True)
                    tq.matmul(pS[:], bw[hp][:], xp[hp][:],
                              start=True, stop=True).then_inc(s_t.h, 1)
                b.emit("tensor", _t2)
                s_t.inc(1)
                last["T2"][(c, h)] = s_t.val

                def _v4(v, c=c, cb=cb, h=h, hp=hp):
                    st = chainer(v, b.csem["vector"])
                    v.wait_ge(s_t.h, last["T2"][(c, h)])
                    v.wait_ge(s_v.h, db_at)
                    k = c * NH + h
                    if k >= 2:
                        v.wait_ge(s_st2[hp].h, ytst.get(k - 2, 0))
                    v.scalar_tensor_tensor(out=yt[hp][:], in0=xhT[cb][:, h * 128:(h + 1) * 128],
                                           scalar=Db[:, h:h + 1], in1=pY[:],
                                           op0=OP.mult, op1=OP.add)
                    st(v.scalar_tensor_tensor(out=St[:, h * 128:(h + 1) * 128],
                                              in0=St[:, h * 128:(h + 1) * 128],
                                              scalar=csc[cb][:, 32 + h:33 + h],
                                              in1=pS[:], op0=OP.mult, op1=OP.add))
                    v.tensor_copy(out=Stbf[:, h * 128:(h + 1) * 128],
                                  in_=St[:, h * 128:(h + 1) * 128]).then_inc(s_v.h, 1)
                b.emit("vector", _v4)
                s_v.inc(1)
                last["V4"][(c, h)] = s_v.val
                stbf_at[h] = s_v.val

                def _g1(g, c=c, h=h, hp=hp):
                    g.wait_ge(s_v.h, last["V4"][(c, h)])
                    g.dma_start(out=yT[h * 128:(h + 1) * 128, c * 128:(c + 1) * 128],
                                in_=yt[hp][:]).then_inc(s_st2[hp].h, 16)
                b.emit("gpsimd", _g1)
                s_st2[hp].inc(16)
                ytst[c * NH + h] = s_st2[hp].val
                last["G1"][(c, h)] = s_st2[hp].val
                b.flush()

        def _fin(g):
            g.wait_ge(s_st2[0].h, s_st2[0].val)
            g.wait_ge(s_st2[1].h, s_st2[1].val)
        b.sec("gpsimd", _fin)


def st_mamba_gate(b, name, yT, zxT, y2T, rmsw_cm, Tn=T, R=2048):
    nch = R // 128
    with b.stage(name) as sctx:
        s_pro, s_t, s_s, s_v = (b.sem() for _ in range(4))
        s_ld2 = b.sem2()
        s_st2 = b.sem2_sw()
        rw = b.sb(sctx, f"{name}_rw", [128, nch], F32)
        onec = b.sb(sctx, f"{name}_oc", [128, 1], BF16)
        one1 = b.sb(sctx, f"{name}_o1", [1, 128], BF16)
        yy = [[b.sb(sctx, f"{name}_y{i}c{c}", [128, TT], BF16) for c in range(nch)]
              for i in range(2)]
        zz = [[b.sb(sctx, f"{name}_z{i}c{c}", [128, TT], BF16) for c in range(nch)]
              for i in range(2)]
        gg = [[b.sb(sctx, f"{name}_g{i}c{c}", [128, TT], F32) for c in range(nch)]
              for i in range(2)]
        g2 = b.sb(sctx, f"{name}_g2", [128, TT], BF16)
        sq = [b.sb(sctx, f"{name}_sq{i}", [1, TT], F32) for i in range(2)]
        rr = [b.sb(sctx, f"{name}_rr{i}", [1, TT], BF16) for i in range(2)]
        oo = [b.sb(sctx, f"{name}_oo{i}", [128, TT], BF16) for i in range(2)]
        pq = b.ps(sctx, f"{name}_pq", [1, TT], F32)
        pb = b.ps(sctx, f"{name}_pb", [128, TT], F32)

        def _pro(sync):
            sync.dma_start(out=rw[:], in_=rmsw_cm).then_inc(s_pro.h, 16)
        b.sec("sync", _pro)
        s_pro.inc(16)
        pro_ld = s_pro.val

        def _v0(v):
            v.memset(onec[:], 1.0)
            v.memset(one1[:], 1.0)
            v.drain()
            v.sem_inc(s_v.h, 1)
        b.sec("vector", _v0)
        s_v.inc(1)
        ones_at = s_v.val

        v_done = {}
        st_done = {}
        for t5 in range(Tn // TT):
            tb = t5 % 2

            def _ld(sync, t5=t5, tb=tb):
                if t5 >= 2:
                    sync.wait_ge(s_v.h, v_done[t5 - 2])
                    sync.wait_ge(s_st2[tb].h, st_done[t5 - 2])
                for c in range(nch):
                    sync.dma_start(out=yy[tb][c][:],
                                   in_=yT[c * 128:(c + 1) * 128,
                                          t5 * TT:(t5 + 1) * TT]).then_inc(s_ld2[tb].h, 16)
                    sync.dma_start(out=zz[tb][c][:],
                                   in_=zxT[c * 128:(c + 1) * 128,
                                           t5 * TT:(t5 + 1) * TT]).then_inc(s_ld2[tb].h, 16)
            b.sec("sync", _ld)
            s_ld2[tb].inc(16 * 2 * nch)
            ld_t = s_ld2[tb].val

            g2_at = {}
            for c in range(nch):
                def _sc(sc, t5=t5, tb=tb, c=c, ld_t=ld_t):
                    sc.wait_ge(s_ld2[tb].h, ld_t)
                    sc.activation(out=gg[tb][c][:], in_=zz[tb][c][:],
                                  func=AF.Silu).then_inc(s_s.h, 1)
                b.sec("scalar", _sc)
                s_s.inc(1)
                sil = s_s.val

                def _vg(v, t5=t5, tb=tb, c=c, sil=sil):
                    st = chainer(v, b.csem["vector"])
                    v.wait_ge(s_s.h, sil)
                    if c > 0 or t5 > 0:
                        v.wait_ge(s_t.h, mm_at)
                    st(v.tensor_tensor(out=gg[tb][c][:], in0=yy[tb][c][:],
                                       in1=gg[tb][c][:], op=OP.mult))
                    st(v.tensor_tensor(out=g2[:], in0=gg[tb][c][:],
                                       in1=gg[tb][c][:], op=OP.mult))
                    v.sem_inc(s_v.h, 1)
                b.sec("vector", _vg)
                s_v.inc(1)
                gv = s_v.val

                def _tm(tq, t5=t5, c=c, gv=gv):
                    tq.wait_ge(s_v.h, gv)
                    tq.wait_ge(s_v.h, ones_at)
                    if t5 >= 1 and c == 0:
                        tq.wait_ge(s_v.h, rr_ev)
                    tq.matmul(pq[:], onec[:], g2[:], start=(c == 0),
                              stop=(c == nch - 1)).then_inc(s_t.h, 1)
                b.sec("tensor", _tm)
                s_t.inc(1)
                mm_at = s_t.val

            def _vq(v, t5=t5, tb=tb):
                v.wait_ge(s_t.h, mm_at)
                v.tensor_scalar(out=sq[tb][:], in0=pq[:], scalar1=1.0 / R,
                                scalar2=None, op0=OP.mult).then_inc(s_v.h, 1)
            b.sec("vector", _vq)
            s_v.inc(1)
            sq_at = s_v.val

            def _sq(sc, t5=t5, tb=tb, sq_at=sq_at):
                sc.wait_ge(s_v.h, sq_at)
                sc.activation(out=sq[tb][:], in_=sq[tb][:], func=AF.Sqrt,
                              bias=1e-5).then_inc(s_s.h, 1)
            b.sec("scalar", _sq)
            s_s.inc(1)
            rt_at = s_s.val

            def _vr(v, t5=t5, tb=tb, rt_at=rt_at):
                st = chainer(v, b.csem["vector"])
                v.wait_ge(s_s.h, rt_at)
                st(v.reciprocal(out=sq[tb][:], in_=sq[tb][:]))
                st(v.tensor_copy(out=rr[tb][:], in_=sq[tb][:]))
                v.sem_inc(s_v.h, 1)
            b.sec("vector", _vr)
            s_v.inc(1)
            rr_at = s_v.val

            def _tb(tq, t5=t5, tb=tb, rr_at=rr_at):
                tq.wait_ge(s_v.h, rr_at)
                if t5 >= 1:
                    tq.wait_ge(s_v.h, bc_used)
                tq.matmul(pb[:], one1[:], rr[tb][:], start=True,
                          stop=True).then_inc(s_t.h, 1)
            b.sec("tensor", _tb)
            s_t.inc(1)
            bc_at = s_t.val
            rr_ev = s_v.val

            ev_list = []
            for c in range(nch):
                pk = c % 2

                def _vo(v, t5=t5, tb=tb, c=c, pk=pk, bc_at=bc_at):
                    v.wait_ge(s_t.h, bc_at)
                    v.wait_ge(s_pro.h, pro_ld)
                    if c >= 2:
                        v.wait_ge(s_st2[pk].h, ev_list[c - 2][1])
                    v.scalar_tensor_tensor(out=oo[pk][:], in0=gg[tb][c][:],
                                           scalar=rw[:, c:c + 1], in1=pb[:],
                                           op0=OP.mult, op1=OP.mult).then_inc(s_v.h, 1)
                b.sec("vector", _vo)
                s_v.inc(1)
                vo_at = s_v.val

                def _st(g, t5=t5, c=c, pk=pk, vo_at=vo_at):
                    g.wait_ge(s_v.h, vo_at)
                    g.dma_start(out=y2T[c * 128:(c + 1) * 128,
                                        t5 * TT:(t5 + 1) * TT],
                                in_=oo[pk][:]).then_inc(s_st2[pk].h, 16)
                b.sec("gpsimd", _st)
                s_st2[pk].inc(16)
                ev_list.append((vo_at, s_st2[pk].val))
            bc_used = s_v.val
            v_done[t5] = s_v.val
            st_done[t5] = s_st2[tb].val

        def _fin(g):
            g.wait_ge(s_st2[0].h, s_st2[0].val)
            g.wait_ge(s_st2[1].h, s_st2[1].val)
        b.sec("gpsimd", _fin)


def emit_L0A(b, W, xr_in, xr_out, uid):
    z0 = b.dtmp(f"l0a_z0_{uid}", [T, D], F32)
    m0T = b.dtmp(f"l0a_m0T_{uid}", [D, T], BF16)
    zxT = b.dtmp(f"l0a_zxT_{uid}", [2048, T], BF16)
    xbcT = b.dtmp(f"l0a_xbcT_{uid}", [2304, T], BF16)
    dtrawT = b.dtmp(f"l0a_dtr_{uid}", [128, T], F32)
    convT = b.dtmp(f"l0a_conv_{uid}", [2304, T], BF16)
    dta_tm = b.dtmp(f"l0a_dta_{uid}", [T, 48], F32)
    yT = b.dtmp(f"l0a_yT_{uid}", [2048, T], BF16)
    y2T = b.dtmp(f"l0a_y2T_{uid}", [2048, T], BF16)
    st_ln(b, xr_in, z=z0, ident=W["ident_bf"])
    st_ln(b, z0, zT=m0T, gamma=W["l0_ln_w"], beta=W["l0_ln_b"], ident=W["ident_bf"])
    st_mm_fm(b, f"l0a_in_{uid}", [dict(x=m0T, w=W["l0_winp"], K=D, wlayout="panels")],
             4352, [(zxT, 0, 2048), (xbcT, 2048, 4352)])
    st_mm_fm(b, f"l0a_dt_{uid}", [dict(x=m0T, w=W["l0_windt"], K=D)],
             128, [(dtrawT, 0, 128)], out_dt=F32)
    st_mamba_conv(b, f"l0a_cv_{uid}", xbcT, convT, W["l0_conv_w"], W["l0_conv_b"])
    st_mamba_dt(b, f"l0a_sp_{uid}", dtrawT, dta_tm, W["l0_dt_bias"],
                W["l0_A_log"], W["ident_f32"])
    st_mamba_ssd(b, f"l0a_ssd_{uid}", convT, dta_tm, yT, W["l0_D"],
                 W["tri_incl"], W["triU"], W["trimask"], W["ident_bf"])
    st_mamba_gate(b, f"l0a_gt_{uid}", yT, zxT, y2T, W["l0_rmsw"])

    def ev_v(E):
        E.step(E.q.tensor_tensor(out=E.tmp, in0=E.psum, in1=E.aux[0], op=OP.add))
        return E.q.tensor_tensor(out=E.out, in0=E.tmp, in1=E.aux[1], op=OP.add)
    st_mm_tm(b, f"l0a_out_{uid}", [dict(x=y2T, w=W["l0_wout"], K=2048)], D,
             xr_out, evac_v=ev_v, aux=[(xr_in, F32), (z0, F32)], out_dt=F32)


class WReg:

    def __init__(self, b, host):
        self.b = b
        self.host = host

    def __getitem__(self, name):
        arr = self.host[name]
        dt = BF16 if arr.dtype == BF16NP else F32
        return self.b.din(name, list(arr.shape), dt)


def host_weights(inp):
    H = dict(host_constants())
    H["fff_cs"] = colmat(H["fff_cs"].ravel())
    H["triU"] = H.pop("triU_strict")
    H["trimask"] = H.pop("trimask_bf")
    f32 = np.float32
    asf = np.ascontiguousarray
    H["l0_ln_w"] = inp["L0_attn_ln_w"].reshape(1, -1)
    H["l0_ln_b"] = inp["L0_attn_ln_b"].reshape(1, -1)
    Win = np.asarray(inp["L0_attn_W_in"], f32)
    H["l0_winp"] = panels(asf(Win[:, :4352]))
    wdt = np.zeros((D, 128), f32)
    wdt[:, :16] = Win[:, 4352:]
    H["l0_windt"] = wdt
    H["l0_conv_w"] = np.asarray(inp["L0_attn_conv_w"], f32)
    H["l0_conv_b"] = inp["L0_attn_conv_b"].reshape(-1, 1)
    H["l0_dt_bias"] = inp["L0_attn_dt_bias"].reshape(-1, 1)
    H["l0_A_log"] = inp["L0_attn_A_log"].reshape(-1, 1)
    H["l0_D"] = inp["L0_attn_D"].reshape(1, -1)
    H["l0_rmsw"] = colmat(np.asarray(inp["L0_attn_rms_w"], f32))
    H["l0_wout"] = np.asarray(inp["L0_attn_W_out"], f32)
    for n in "qkvf":
        p = f"l0_{n}_"
        nw = np.asarray(inp[f"L0_ffn_{n}_nw"], f32)
        nb = np.asarray(inp[f"L0_ffn_{n}_nb"], f32)
        w1 = np.asarray(inp[f"L0_ffn_{n}_w1"], f32)
        b1 = np.asarray(inp[f"L0_ffn_{n}_b1"], f32)
        w2 = np.asarray(inp[f"L0_ffn_{n}_w2"], f32)
        b2 = np.asarray(inp[f"L0_ffn_{n}_b2"], f32)
        nwT = np.zeros((D, 256), f32)
        nwT[:, :255] = nw.T
        nbp = np.zeros(256, f32)
        nbp[:255] = nb
        H[p + "nwT"] = nwT
        H[p + "nbc"] = colmat(nbp)
        H[p + "w1p"] = panels(asf(w1.transpose(1, 0, 2).reshape(D, 4096)))
        H[p + "b1c"] = colmat(b1.reshape(-1))
        W2s = asf(w2.reshape(4096, D))
        H[p + "w2p"] = panels(W2s)
        H[p + "w2r"] = W2s
        H[p + "b2s"] = b2
    H["l1_wq"] = np.asarray(inp["L1_attn_Wq"], f32)
    H["l1_wk"] = np.asarray(inp["L1_attn_Wk"], f32)
    H["l1_wv"] = np.asarray(inp["L1_attn_Wv"], f32)
    H["l1_bqc"] = colmat(np.asarray(inp["L1_attn_bq"], f32))
    H["l1_bkc"] = colmat(np.asarray(inp["L1_attn_bk"], f32))
    H["l1_bv"] = inp["L1_attn_bv"].reshape(1, -1)
    H["l1_gamma"] = inp["L1_attn_gamma"].reshape(1, -1)
    H["l1_beta"] = inp["L1_attn_beta"].reshape(1, -1)
    H["l1_w1T"] = panels(asf(np.asarray(inp["L1_ffn_w1_w"], f32).T))
    H["l1_w2T"] = panels(asf(np.asarray(inp["L1_ffn_w2_w"], f32).T))
    H["l1_h1T"] = panels(asf(np.asarray(inp["L1_ffn_h1_w"], f32).T))
    H["l1_h2T"] = panels(asf(np.asarray(inp["L1_ffn_h2_w"], f32).T))
    H["l1_w3T"] = panels(asf(np.asarray(inp["L1_ffn_w3_w"], f32).T))
    H["l1_w1s"] = colmat(np.asarray(inp["L1_ffn_w1_s"], f32))
    H["l1_w2s"] = colmat(np.asarray(inp["L1_ffn_w2_s"], f32))
    H["l1_h1s"] = colmat(np.asarray(inp["L1_ffn_h1_s"], f32))
    H["l1_h2s"] = colmat(np.asarray(inp["L1_ffn_h2_s"], f32))
    H["l1_w3s"] = colmat(np.asarray(inp["L1_ffn_w3_s"], f32))
    H["l2_wq"] = np.asarray(inp["L2_attn_Wq"], f32)
    H["l2_wk"] = np.asarray(inp["L2_attn_Wk"], f32)
    H["l2_wv"] = np.asarray(inp["L2_attn_Wv"], f32)
    H["l2_wo"] = np.asarray(inp["L2_attn_Wo"], f32)
    H["l2_w"] = np.asarray(inp["L2_ffn_weight"], f32)
    H["l2_a1"] = np.asarray(inp["L2_ffn_aux_weight1"], f32)
    H["l2_a2"] = np.asarray(inp["L2_ffn_aux_weight2"], f32)
    H["l2_bias"] = inp["L2_ffn_bias"].reshape(1, -1)
    wqkv = np.asarray(inp["L3_attn_w_qkv"], f32)
    H["l3_wqk"] = asf(wqkv[:2048].T)
    H["l3_wv"] = asf(wqkv[2048:].T)
    H["l3_wout"] = asf(np.asarray(inp["L3_attn_w_out"], f32).T)
    H["l3_sw1"] = np.asarray(inp["L3_ffn_sw1"], f32)
    H["l3_sw3"] = np.asarray(inp["L3_ffn_sw3"], f32)
    H["l3_sw2"] = np.asarray(inp["L3_ffn_sw2"], f32)
    H["l3_sb1c"] = colmat(np.asarray(inp["L3_ffn_sb1"], f32))
    H["l3_sb3c"] = colmat(np.asarray(inp["L3_ffn_sb3"], f32))
    H["l3_sb2"] = inp["L3_ffn_sb2"].reshape(1, -1)
    H["l3_w1"] = np.asarray(inp["L3_ffn_w1"], f32)
    H["l3_b1c"] = np.stack([colmat(np.asarray(inp["L3_ffn_b1"][e], f32))
                            for e in range(8)])
    H["l3_w2"] = np.asarray(inp["L3_ffn_w2"], f32)
    H["l3_b2"] = np.asarray(inp["L3_ffn_b2"], f32)
    H = {k: (np.ascontiguousarray(v) if isinstance(v, np.ndarray) else v)
         for k, v in H.items()}
    return H


SUBLAYERS = [
    ("L0F", "s0"), ("L0A", "s1"), ("L0F", "s2"),
    ("L1F", "s3"), ("L1A", "s4"), ("L1F", "s5"),
    ("L2F", "s6"), ("L2A", "s7"), ("L2F", "s8"),
    ("L3F", "s9"), ("L3A", "s10"), ("L3F", "s11"),
]
EMITTERS = dict(L0F=emit_L0F, L0A=emit_L0A, L1F=emit_L1F, L1A=emit_L1A,
                L2F=emit_L2F, L2A=emit_L2A, L3F=emit_L3F, L3A=emit_L3A)

MODE = "fused"


def kernel(**inputs):
    H = host_weights(inputs)
    x = np.ascontiguousarray(np.asarray(inputs["L0_attn_x"], np.float32))
    shards = [np.ascontiguousarray(x[c]) for c in range(8)]

    if MODE == "fused":
        b = Build("mk")
        xr_in0 = b.din("x_in", [T, D], F32)
        xr_out_f = b.dout("x_out", [T, D], F32)
        xa = b.dtmp("xr_a", [T, D], F32)
        xb_ = b.dtmp("xr_b", [T, D], F32)
        wreg = WReg(b, H)
        b.open(consts=(-4.0,))
        cur = xr_in0
        bufs = [xa, xb_]
        for i, (kind, uid) in enumerate(SUBLAYERS):
            nxt = xr_out_f if i == len(SUBLAYERS) - 1 else bufs[i % 2]
            EMITTERS[kind](b, wreg, cur, nxt, uid)
            cur = nxt
        nc = b.close()
        in_maps = []
        for c in range(8):
            im = {n: H[n] for n in b.input_names if n != "x_in"}
            im["x_in"] = shards[c]
            in_maps.append(im)
        res = run_bass_kernel_spmd(nc, in_maps, core_ids=list(range(8)))
        return np.stack([np.asarray(r["x_out"], np.float32) for r in res.results])

    cur_shards = shards
    for i, (kind, uid) in enumerate(SUBLAYERS):
        b = Build(f"mk_{uid}")
        xr_in = b.din("x_in", [T, D], F32)
        xr_out = b.dout("x_out", [T, D], F32)
        wreg = WReg(b, H)
        b.open(consts=(-4.0,))
        EMITTERS[kind](b, wreg, xr_in, xr_out, uid)
        nc = b.close()
        in_maps = []
        for c in range(8):
            im = {n: H[n] for n in b.input_names if n != "x_in"}
            im["x_in"] = cur_shards[c]
            in_maps.append(im)
        res = run_bass_kernel_spmd(nc, in_maps, core_ids=list(range(8)))
        cur_shards = [np.ascontiguousarray(np.asarray(r["x_out"], np.float32))
                      for r in res.results]
    return np.stack(cur_shards)


# version 10
# speedup vs baseline: 1.5585x; 1.0398x over previous
import numpy as np
import ml_dtypes
import concourse.bass as bass
import concourse.mybir as mybir
from concourse.bass_utils import run_bass_kernel_spmd
from concourse.alu_op_type import AluOpType as OP
from contextlib import ExitStack, contextmanager

F32 = mybir.dt.float32
BF16 = mybir.dt.bfloat16
AF = mybir.ActivationFunctionType
AX = mybir.AxisListType.X
BF16NP = ml_dtypes.bfloat16

T = 4096
D = 1024
TT = 512
NT128 = T // 128

ENGINES = ("sync", "tensor", "scalar", "vector", "gpsimd")


class Sem:
    def __init__(self, h):
        self.h = h
        self.val = 0

    def inc(self, n=1):
        self.val += n
        return self.val


class _QProxy:

    def __init__(self, b, eng):
        self._b = b
        self._eng = eng
        self._q = None

    def _bind(self, q):
        self._q = q

    def wait_ge(self, sem, val, **kw):
        if val <= 0:
            return None
        key = (self._eng, id(sem))
        seen = self._b._waited.get(key, -1)
        if val <= seen:
            return None
        self._b._waited[key] = val
        return self._q.wait_ge(sem, val, **kw)

    def __getattr__(self, name):
        return getattr(self._q, name)


class Build:

    def __init__(self, name="mk"):
        self.nc = bass.Bass("TRN2", target_bir_lowering=False)
        self.ctx = ExitStack()
        self.dram = {}
        self.input_names = []
        self.output_names = []
        self._block = None
        self._sems = []
        self._swsems = []
        self._sem_i = 0
        self._swsem_i = 0
        self._waited = {}
        self._pend = []
        self.bar = None
        self.stage_n = 0
        self.name = name

    def din(self, name, shape, dt):
        if name in self.dram:
            return self.dram[name]
        ap = self.nc.dram_tensor(name, list(shape), dt, kind="ExternalInput").ap()
        self.dram[name] = ap
        self.input_names.append(name)
        return ap

    def dout(self, name, shape, dt):
        ap = self.nc.dram_tensor(name, list(shape), dt, kind="ExternalOutput").ap()
        self.dram[name] = ap
        self.output_names.append(name)
        return ap

    def dtmp(self, name, shape, dt):
        if name in self.dram:
            return self.dram[name]
        ap = self.nc.dram_tensor(name, list(shape), dt, kind="Internal").ap()
        self.dram[name] = ap
        return ap

    def open(self, consts=()):
        for i in range(24):
            self._sems.append(Sem(self.ctx.enter_context(self.nc.semaphore(f"sp{i}"))))
        for i in range(24):
            self._swsems.append(Sem(self.ctx.enter_context(self.nc.semaphore(f"sw{i}"))))
        self.bar = Sem(self.ctx.enter_context(self.nc.semaphore("bar")))
        self._block = self.ctx.enter_context(self.nc.Block())
        self.csem = {e: Sem(self.ctx.enter_context(self.nc.semaphore(f"ch_{e}")))
                     for e in ENGINES}
        cvals = [0.0, 1.0, 1e-5] + [v for v in consts if v not in (0.0, 1.0, 1e-5)]
        self._csb = self.ctx.enter_context(
            self.nc.sbuf_tensor("constf", [128, len(cvals)], F32))
        with self.stage("consts"):
            def _m(v):
                for j, val in enumerate(cvals):
                    v.memset(self._csb[:, j:j + 1], float(val))
            self.sec("vector", _m)
        for j, val in enumerate(cvals):
            self.nc.const_aps.register(F32, float(val), self._csb[:, j:j + 1])

    def close(self):
        self.ctx.close()
        return self.nc

    def sem(self):
        s = self._sems[self._sem_i % len(self._sems)]
        self._sem_i += 1
        return s

    def sem2(self):
        return [self.sem(), self.sem()]

    def sem_sw(self):
        s = self._swsems[self._swsem_i % len(self._swsems)]
        self._swsem_i += 1
        return s

    def sem2_sw(self):
        return [self.sem_sw(), self.sem_sw()]

    def sec(self, eng, fn):
        prox = _QProxy(self, eng)
        getattr(self._block, eng)(lambda q, f=fn, p=prox: (p._bind(q), f(p))[-1])

    def emit(self, eng, fn):
        self._pend.append((eng, fn))

    def flush(self):
        for e in ENGINES:
            fns = [f for (ee, f) in self._pend if ee == e]
            if fns:
                self.sec(e, lambda q, fns=tuple(fns): [f(q) for f in fns])
        self._pend.clear()

    @contextmanager
    def stage(self, name=""):
        k = self.stage_n
        base = 6 * k

        def _wsync(q):
            q.wait_ge(self.bar.h, base)
            for s in self._sems:
                q.sem_clear(s.h)
            for s in self.csem.values():
                q.sem_clear(s.h)
            q.sem_inc(self.bar.h, 1)
        self.sec("sync", _wsync)
        pool_ids = {id(s.h) for s in self._sems}
        pool_ids |= {id(s.h) for s in self.csem.values()}
        self._waited = {k: v for k, v in self._waited.items()
                        if k[1] not in pool_ids}
        for s in self._sems:
            s.val = 0
        for s in self.csem.values():
            s.val = 0
        for e in ENGINES:
            if e == "sync":
                continue

            def _w(q, base=base):
                q.wait_ge(self.bar.h, base + 1)
            self.sec(e, _w)
        sctx = ExitStack()
        try:
            yield sctx
        finally:
            sctx.close()
            self.stage_n += 1
            used = [(s.h, s.val) for s in self._sems if s.val > 0]
            used += [(s.h, s.val) for s in self._swsems if s.val > 0]
            used += [(s.h, s.val) for s in self.csem.values() if s.val > 0]
            for e in ENGINES:
                def _i(q, e=e):
                    for (h, v) in used:
                        q.wait_ge(h, v)
                    if e in ("vector", "scalar", "gpsimd"):
                        q.drain()
                    q.sem_inc(self.bar.h, 1)
                self.sec(e, _i)

    def sb(self, sctx, name, shape, dt):
        return sctx.enter_context(
            self.nc.sbuf_tensor(f"{name}_{self.stage_n}", list(shape), dt))

    def ps(self, sctx, name, shape, dt=F32):
        return sctx.enter_context(
            self.nc.psum_tensor(f"{name}_{self.stage_n}", list(shape), dt))


def cdiv(a, b):
    return (a + b - 1) // b


def chainer(q, s):
    def step(ins):
        tgt = s.inc(1)
        ins.then_inc(s.h, 1)
        q.wait_ge(s.h, tgt)
    return step


def st_ln(b, xr, plus=(), write_x=None, z=None, zT=None, zT32=None,
          gamma=None, beta=None, variant="ln", ident=None, ident32=None):
    with b.stage("ln") as sctx:
        xa = [b.sb(sctx, f"ln_xa{i}", [128, D], F32) for i in range(2)]
        zc = [b.sb(sctx, f"ln_zc{i}", [128, D], F32) for i in range(2)]
        sq = [b.sb(sctx, f"ln_sq{i}", [128, D], F32) for i in range(2)]
        dl = [[b.sb(sctx, f"ln_d{i}_{j}", [128, D], F32) for j in range(len(plus))]
              for i in range(2)]
        co = [b.sb(sctx, f"ln_co{i}", [128, 4], F32) for i in range(2)]
        zbf = [b.sb(sctx, f"ln_zbf{i}", [128, D], BF16) for i in range(2)] if zT is not None else None
        s_pro = b.sem()
        s_ld2 = b.sem2()
        s_st2 = b.sem2()
        s_zt2 = b.sem2()
        s_v, s_s, s_t = b.sem(), b.sem(), b.sem()

        use_gb = gamma is not None
        if use_gb:
            gb = b.sb(sctx, "ln_gb", [128, 2 * D], F32)
            ones1 = b.sb(sctx, "ln_ones", [1, 128], F32)
            grow = b.sb(sctx, "ln_grow", [1, 2 * D], F32)
            p_gb = b.ps(sctx, "ln_pgb", [128, 2048], F32)
        if zT is not None:
            id_sb = b.sb(sctx, "ln_id", [128, 128], BF16)
            p_tr = [b.ps(sctx, f"ln_ptr{i}", [128, 1024], BF16) for i in range(2)]
            ztr = [b.sb(sctx, f"ln_ztr{i}", [128, 1024], BF16) for i in range(2)]
        if zT32 is not None:
            id32_sb = b.sb(sctx, "ln_id32", [128, 128], F32)
            p_tr32 = [b.ps(sctx, f"ln_p32{i}", [128, 512], F32) for i in range(2)]
            ztr32 = [b.sb(sctx, f"ln_z32{i}", [128, 512], F32) for i in range(2)]

        def _ld0(sync):
            if use_gb:
                sync.dma_start(out=grow[0:1, 0:D], in_=gamma).then_inc(s_pro.h, 16)
                sync.dma_start(out=grow[0:1, D:2 * D], in_=beta).then_inc(s_pro.h, 16)
            if zT is not None:
                sync.dma_start(out=id_sb[:], in_=ident).then_inc(s_pro.h, 16)
            if zT32 is not None:
                sync.dma_start(out=id32_sb[:], in_=ident32).then_inc(s_pro.h, 16)
        b.sec("sync", _ld0)
        s_pro.inc(16 * ((2 if use_gb else 0) + (1 if zT is not None else 0) +
                        (1 if zT32 is not None else 0)))
        ld_pro = s_pro.val

        if use_gb:
            def _v0(v):
                v.memset(ones1[:], 1.0).then_inc(s_v.h, 1)
            b.sec("vector", _v0)
            s_v.inc(1)
            v_ones = s_v.val

            def _t0(t):
                t.wait_ge(s_pro.h, ld_pro)
                t.wait_ge(s_v.h, v_ones)
                for j in range(4):
                    t.matmul(p_gb[:, j * 512:(j + 1) * 512], ones1[:],
                             grow[0:1, j * 512:(j + 1) * 512], start=True,
                             stop=True).then_inc(s_t.h, 1)
            b.sec("tensor", _t0)
            s_t.inc(4)
            t_gb = s_t.val

            def _v1(v):
                v.wait_ge(s_t.h, t_gb)
                v.tensor_copy(out=gb[:], in_=p_gb[:]).then_inc(s_v.h, 1)
            b.sec("vector", _v1)
            s_v.inc(1)
        gb_ready = s_v.val

        v_done = []
        st_done = []
        zt_ev = []
        zt_st = []

        n_z = D // 128
        for i in range(NT128):
            bi = i % 2
            r0 = i * 128

            def _ld(sync, i=i, bi=bi, r0=r0):
                if i >= 2:
                    sync.wait_ge(s_v.h, v_done[i - 2])
                    sync.wait_ge(s_st2[bi].h, st_done[i - 2])
                sync.dma_start(out=xa[bi][:], in_=xr[r0:r0 + 128, :]).then_inc(s_ld2[bi].h, 16)
                for j in range(len(plus)):
                    sync.dma_start(out=dl[bi][j][:],
                                   in_=plus[j][0][r0:r0 + 128, :]).then_inc(s_ld2[bi].h, 16)
            b.emit("sync", _ld)
            s_ld2[bi].inc(16 * (1 + len(plus)))
            ld_i = s_ld2[bi].val

            def _va(v, bi=bi, ld_i=ld_i):
                st = chainer(v, b.csem["vector"])
                v.wait_ge(s_ld2[bi].h, ld_i)
                if use_gb:
                    v.wait_ge(s_v.h, gb_ready)
                for j in range(len(plus)):
                    st(v.scalar_tensor_tensor(out=xa[bi][:], in0=dl[bi][j][:],
                                              scalar=float(plus[j][1]), in1=xa[bi][:],
                                              op0=OP.mult, op1=OP.add))
                st(v.reduce_sum(out=co[bi][:, 0:1], in_=xa[bi][:], axis=AX))
                st(v.tensor_scalar(out=co[bi][:, 0:1], in0=co[bi][:, 0:1],
                                   scalar1=1.0 / D, scalar2=None, op0=OP.mult))
                v.tensor_scalar(out=zc[bi][:], in0=xa[bi][:], scalar1=co[bi][:, 0:1],
                                scalar2=None, op0=OP.subtract).then_inc(s_v.h, 1)
            b.emit("vector", _va)
            s_v.inc(1)
            centered = s_v.val

            def _sa(sc, bi=bi, centered=centered):
                sc.wait_ge(s_v.h, centered)
                sc.activation(out=sq[bi][:], in_=zc[bi][:], func=AF.Square).then_inc(s_s.h, 1)
            b.emit("scalar", _sa)
            s_s.inc(1)
            squared = s_s.val

            def _vb(v, bi=bi, squared=squared):
                v.wait_ge(s_s.h, squared)
                v.reduce_sum(out=co[bi][:, 1:2], in_=sq[bi][:], axis=AX).then_inc(s_v.h, 1)
            b.emit("vector", _vb)
            s_v.inc(1)
            ssq = s_v.val

            def _sb_(sc, bi=bi, ssq=ssq):
                sc.wait_ge(s_v.h, ssq)
                if variant == "ln":
                    sc.activation(out=co[bi][:, 2:3], in_=co[bi][:, 1:2],
                                  func=AF.Sqrt, scale=1.0 / D, bias=1e-5).then_inc(s_s.h, 1)
                else:
                    st = chainer(sc, b.csem["scalar"])
                    st(sc.activation(out=co[bi][:, 3:4], in_=co[bi][:, 1:2],
                                     func=AF.Sqrt, scale=1.0 / (D - 1)))
                    sc.activation(out=co[bi][:, 2:3], in_=co[bi][:, 3:4],
                                  func=AF.Sqrt, bias=1e-5).then_inc(s_s.h, 1)
            b.emit("scalar", _sb_)
            s_s.inc(1)
            denom = s_s.val

            def _vc(v, bi=bi, denom=denom):
                st = chainer(v, b.csem["vector"])
                v.wait_ge(s_s.h, denom)
                st(v.reciprocal(out=co[bi][:, 3:4], in_=co[bi][:, 2:3]))
                st(v.tensor_scalar(out=zc[bi][:], in0=zc[bi][:], scalar1=co[bi][:, 3:4],
                                   scalar2=None, op0=OP.mult))
                if use_gb:
                    st(v.tensor_tensor(out=zc[bi][:], in0=zc[bi][:], in1=gb[:, 0:D], op=OP.mult))
                    st(v.tensor_tensor(out=zc[bi][:], in0=zc[bi][:], in1=gb[:, D:2 * D], op=OP.add))
                if zbf is not None:
                    st(v.tensor_copy(out=zbf[bi][:], in_=zc[bi][:]))
                v.sem_inc(s_v.h, 1)
            b.emit("vector", _vc)
            s_v.inc(1)
            z_ready = s_v.val
            v_done.append(z_ready)

            n_st = (1 if z is not None else 0) + (1 if (write_x is not None and plus) else 0)

            def _st(sc, i=i, bi=bi, r0=r0, z_ready=z_ready):
                sc.wait_ge(s_v.h, z_ready)
                if z is not None:
                    sc.dma_start(out=z[r0:r0 + 128, :], in_=zc[bi][:]).then_inc(s_st2[bi].h, 16)
                if write_x is not None and plus:
                    sc.dma_start(out=write_x[r0:r0 + 128, :], in_=xa[bi][:]).then_inc(s_st2[bi].h, 16)
            b.emit("scalar", _st)
            s_st2[bi].inc(16 * n_st)

            for (on, dst, ptr_, ztr_, nb) in (
                    [("bf", zT, p_tr, ztr, 8)] if zT is not None else []) + (
                    [("f32", zT32, p_tr32, ztr32, 4)] if zT32 is not None else []):
                for g0 in range(0, n_z, nb):
                    k = len(zt_ev)
                    pk = k % 2

                    def _tr(t, bi=bi, g0=g0, k=k, pk=pk, z_ready=z_ready, on=on,
                            ptr_=ptr_, nb=nb):
                        t.wait_ge(s_v.h, z_ready)
                        t.wait_ge(s_pro.h, ld_pro)
                        if k >= 2:
                            t.wait_ge(s_v.h, zt_ev[k - 2])
                        srct = zbf[bi] if on == "bf" else zc[bi]
                        idt = id_sb if on == "bf" else id32_sb
                        for j in range(nb):
                            c = g0 + j
                            ins = t.transpose(ptr_[pk][:, j * 128:(j + 1) * 128],
                                              srct[:, c * 128:(c + 1) * 128],
                                              idt[:])
                        ins.then_inc(s_t.h, 1)
                    b.emit("tensor", _tr)
                    s_t.inc(1)
                    trk = s_t.val

                    def _ev(v, pk=pk, k=k, trk=trk, ptr_=ptr_, ztr_=ztr_, nb=nb):
                        v.wait_ge(s_t.h, trk)
                        if k >= 2:
                            v.wait_ge(s_zt2[pk].h, zt_st[k - 2])
                        v.tensor_copy(out=ztr_[pk][:], in_=ptr_[pk][:]).then_inc(s_v.h, 1)
                    b.emit("vector", _ev)
                    s_v.inc(1)
                    zt_ev.append(s_v.val)
                    evk = s_v.val

                    def _st2(sc, g0=g0, i=i, pk=pk, evk=evk, dst=dst, ztr_=ztr_, nb=nb):
                        sc.wait_ge(s_v.h, evk)
                        sc.dma_start(
                            out=dst[g0 * 128:(g0 + nb) * 128,
                                    i * 128:(i + 1) * 128].rearrange(
                                        "(c p) m -> p c m", p=128),
                            in_=ztr_[pk][:]).then_inc(s_zt2[pk].h, 16)
                    b.emit("scalar", _st2)
                    s_zt2[pk].inc(16)
                    zt_st.append(s_zt2[pk].val)
            st_done.append(s_st2[bi].val)
            b.flush()

        def _fin(sc):
            sc.wait_ge(s_st2[0].h, s_st2[0].val)
            sc.wait_ge(s_st2[1].h, s_st2[1].val)
            sc.wait_ge(s_zt2[0].h, s_zt2[0].val)
            sc.wait_ge(s_zt2[1].h, s_zt2[1].val)
        b.sec("scalar", _fin)


class _EC:
    pass


def _copy_evac(E):
    return E.q.tensor_copy(out=E.out, in_=E.psum)


def st_mm_fm(b, name, segs, N, outs, evac_s=None, evac_v=_copy_evac, cols=(),
             aux=(), Tn=T, out_dt=BF16, group_bytes=12 * 1024 * 1024):
    Nc = cdiv(N, 128)
    dual = any("w2" in s or s.get("alt") for s in segs)
    NB = 2 if dual else 4
    with b.stage(name) as sctx:
        s_pro, s_t, s_s, s_v = (b.sem() for _ in range(4))
        s_wl = b.sem_sw()
        s_ld2 = b.sem2()
        s_ldc2 = b.sem2_sw()
        s_la4 = [b.sem() for _ in range(NB)]
        s_st4 = [b.sem_sw() for _ in range(NB)]

        use_panels = any(s.get("wlayout") == "panels" for s in segs)
        if use_panels:
            pb = sum(s["K"] * 256 * (2 if "w2" in s else 1)
                     for s in segs if s.get("wlayout") == "panels")
            G = max(1, min(Nc, group_bytes // pb))
        else:
            G = Nc
        groups = [list(range(g, min(g + G, Nc))) for g in range(0, Nc, G)]

        for si, s in enumerate(segs):
            Kc = s["K"] // 128
            sdt = s.get("dt", BF16)
            if s.get("wlayout", "rows") == "rows":
                s["_w"] = [b.sb(sctx, f"{name}_w{si}k{kc}", [128, N], sdt) for kc in range(Kc)]
                if "w2" in s:
                    s["_w2"] = [b.sb(sctx, f"{name}_v{si}k{kc}", [128, N], sdt) for kc in range(Kc)]
            else:
                s["_w"] = [b.sb(sctx, f"{name}_wp{si}j{j}", [128, s["K"]], sdt) for j in range(G)]
                if "w2" in s:
                    s["_w2"] = [b.sb(sctx, f"{name}_vp{si}j{j}", [128, s["K"]], sdt) for j in range(G)]
            s["_xm"] = [b.sb(sctx, f"{name}_x{si}b{t}", [128, Kc * TT], sdt)
                        for t in range(2)]
            s["_x"] = [[s["_xm"][t][:, kc * TT:(kc + 1) * TT] for kc in range(Kc)]
                       for t in range(2)]

        p = [b.ps(sctx, f"{name}_p{i}", [128, TT], F32) for i in range(NB)]
        p2 = [b.ps(sctx, f"{name}_q{i}", [128, TT], F32) for i in range(NB)] if dual else None
        ob = [b.sb(sctx, f"{name}_o{i}", [128, TT], out_dt) for i in range(NB)]
        tmp = [b.sb(sctx, f"{name}_t{i}", [128, TT], F32) for i in range(NB)]
        axb = [[b.sb(sctx, f"{name}_a{i}j{j}", [128, TT], aux[j][1]) for j in range(len(aux))]
               for i in range(NB)]
        colsb = b.sb(sctx, f"{name}_c", [128, max(1, Nc * len(cols))], F32) if cols else None

        def _pro(sync):
            for j, c in enumerate(cols):
                sync.dma_start(out=colsb[:, j * Nc:(j + 1) * Nc],
                               in_=c).then_inc(s_pro.h, 16)
        b.sec("sync", _pro)
        s_pro.inc(16 * len(cols))
        cols_ld = s_pro.val

        def _prow(g):
            for si, s in enumerate(segs):
                if s.get("wlayout", "rows") == "rows":
                    for kc in range(s["K"] // 128):
                        g.dma_start(out=s["_w"][kc][:],
                                    in_=s["w"][kc * 128:(kc + 1) * 128, :]).then_inc(s_wl.h, 16)
                        if "w2" in s:
                            g.dma_start(out=s["_w2"][kc][:],
                                        in_=s["w2"][kc * 128:(kc + 1) * 128, :]).then_inc(s_wl.h, 16)
        b.sec("gpsimd", _prow)
        for s in segs:
            if s.get("wlayout", "rows") == "rows":
                s_wl.inc(16 * (s["K"] // 128) * (2 if "w2" in s else 1))

        def outmap(r):
            for (dap, g0, g1) in outs:
                if g0 <= r < g1:
                    return dap, r - g0
            raise AssertionError(f"row {r} unmapped")

        pend = []

        def emit(eng, fn):
            pend.append((eng, fn))

        def flush():
            for eng in ENGINES:
                fns = [f for (e, f) in pend if e == eng]
                if fns:
                    b.sec(eng, lambda q, fns=tuple(fns): [f(q) for f in fns])
            pend.clear()

        mm_at = {}
        ev_v_at = {}
        psum_free = {}
        st_at = {}
        aux_free = {}
        xfree = [0, 0]
        idx = 0
        plan = []
        for gi, group in enumerate(groups):
            if use_panels:
                wait_mm = mm_at[idx - 1] if idx > 0 else 0

                def _pg(g, group=tuple(group), wait_mm=wait_mm):
                    g.wait_ge(s_t.h, wait_mm)
                    for si, s in enumerate(segs):
                        if s.get("wlayout") == "panels":
                            for jj, ncx in enumerate(group):
                                g.dma_start(
                                    out=s["_w"][jj][:],
                                    in_=s["w"][ncx].rearrange("(kc kp) m -> kp kc m", kp=128),
                                ).then_inc(s_wl.h, 16)
                                if "w2" in s:
                                    g.dma_start(
                                        out=s["_w2"][jj][:],
                                        in_=s["w2"][ncx].rearrange("(kc kp) m -> kp kc m", kp=128),
                                    ).then_inc(s_wl.h, 16)
                b.sec("gpsimd", _pg)
                for s in segs:
                    if s.get("wlayout") == "panels":
                        s_wl.inc(16 * len(group) * (2 if "w2" in s else 1))
            wl_ready = s_wl.val

            for t in range(Tn // TT):
                slot = t % 2
                plain = [s for s in segs if not s.get("cast")]
                casts = [s for s in segs if s.get("cast")]

                def _lx(sync, slot=slot, t=t, w=xfree[slot]):
                    sync.wait_ge(s_t.h, w)
                    for s in plain:
                        sync.dma_start(
                            out=s["_xm"][slot][:],
                            in_=s["x"][:, t * TT:(t + 1) * TT].rearrange(
                                "(kc p) m -> p kc m", p=128)).then_inc(s_ld2[slot].h, 16)
                b.sec("sync", _lx)
                if casts:
                    def _lxc(g, slot=slot, t=t, w=xfree[slot]):
                        g.wait_ge(s_t.h, w)
                        for s in casts:
                            g.dma_start(
                                out=s["_xm"][slot][:],
                                in_=s["x"][:, t * TT:(t + 1) * TT].rearrange(
                                    "(kc p) m -> p kc m", p=128)).then_inc(s_ldc2[slot].h, 16)
                    b.sec("gpsimd", _lxc)
                s_ld2[slot].inc(16 * len(plain))
                s_ldc2[slot].inc(16 * len(casts))
                x_ld = s_ld2[slot].val
                xc_ld = s_ldc2[slot].val

                for jj, ncx in enumerate(group):
                    pp = idx % NB
                    cw = min(128, N - ncx * 128)
                    odap, olo = outmap(ncx * 128)

                    if aux:
                        def _la(sync, idx=idx, pp=pp, ncx=ncx, cw=cw, t=t):
                            if idx >= NB:
                                sync.wait_ge(s_v.h, aux_free[idx - NB])
                            for j, (ad, adt) in enumerate(aux):
                                sync.dma_start(
                                    out=axb[pp][j][0:cw, :],
                                    in_=ad[ncx * 128:ncx * 128 + cw,
                                           t * TT:(t + 1) * TT]).then_inc(s_la4[pp].h, 16)
                        emit("sync", _la)
                        s_la4[pp].inc(16 * len(aux))
                    la_tile = s_la4[pp].val

                    def _mm(tq, idx=idx, pp=pp, jj=jj, ncx=ncx, cw=cw, x_ld=x_ld,
                            wl_ready=wl_ready, slot=slot, xc_ld=xc_ld):
                        tq.wait_ge(s_ld2[slot].h, x_ld)
                        tq.wait_ge(s_ldc2[slot].h, xc_ld)
                        tq.wait_ge(s_wl.h, wl_ready)
                        if idx >= NB:
                            tq.wait_ge(*psum_free[idx - NB])
                            if dual:
                                tq.wait_ge(s_v.h, ev_v_at[idx - NB])
                        mms = []
                        for si, s in enumerate(segs):
                            if s.get("alt"):
                                continue
                            for kc in range(s["K"] // 128):
                                if s.get("wlayout", "rows") == "rows":
                                    lhs = s["_w"][kc][:, ncx * 128:ncx * 128 + cw]
                                else:
                                    lhs = s["_w"][jj][:, kc * 128:kc * 128 + cw]
                                mms.append((lhs, s["_x"][slot][kc][:]))
                        for mi, (lap, rap) in enumerate(mms):
                            ins = tq.matmul(p[pp][0:cw, :], lap, rap,
                                            start=(mi == 0), stop=(mi == len(mms) - 1))
                            if mi == len(mms) - 1:
                                ins.then_inc(s_t.h, 1)
                        if dual:
                            m2 = []
                            for si, s in enumerate(segs):
                                if "w2" in s:
                                    ws = s["_w2"]
                                elif s.get("alt"):
                                    ws = s["_w"]
                                else:
                                    continue
                                for kc in range(s["K"] // 128):
                                    if s.get("wlayout", "rows") == "rows":
                                        lhs = ws[kc][:, ncx * 128:ncx * 128 + cw]
                                    else:
                                        lhs = ws[jj][:, kc * 128:kc * 128 + cw]
                                    m2.append((lhs, s["_x"][slot][kc][:]))
                            for mi, (lap, rap) in enumerate(m2):
                                ins = tq.matmul(p2[pp][0:cw, :], lap, rap,
                                                start=(mi == 0), stop=(mi == len(m2) - 1))
                                if mi == len(m2) - 1:
                                    ins.then_inc(s_t.h, 1)
                    emit("tensor", _mm)
                    s_t.inc(2 if dual else 1)
                    mm_at[idx] = s_t.val

                    E = _EC()
                    E.cw, E.ncx, E.nw, E.idx = cw, ncx, TT, idx
                    E.col = (lambda j, cw=cw, ncx=ncx: colsb[0:cw, j * Nc + ncx:j * Nc + ncx + 1])

                    if evac_s is not None:
                        def _es(sc, E=E, pp=pp, cw=cw, idx=idx, la_tile=la_tile):
                            sc.wait_ge(s_t.h, mm_at[idx])
                            if aux:
                                sc.wait_ge(s_la4[pp].h, la_tile)
                            if cols:
                                sc.wait_ge(s_pro.h, cols_ld)
                            if idx >= NB:
                                sc.wait_ge(s_v.h, ev_v_at[idx - NB])
                                if evac_v is None:
                                    sc.wait_ge(s_st4[pp].h, st_at[idx - NB])
                            E.q = sc
                            E.step = chainer(sc, b.csem["scalar"])
                            E.psum = p[pp][0:cw, :]
                            E.psum2 = p2[pp][0:cw, :] if dual else None
                            E.tmp = tmp[pp][0:cw, :]
                            E.out = ob[pp][0:cw, :]
                            E.aux = [a[0:cw, :] for a in axb[pp]]
                            ins = evac_s(E)
                            if ins is not None:
                                ins.then_inc(s_s.h, 1)
                            else:
                                sc.drain()
                                sc.sem_inc(s_s.h, 1)
                        emit("scalar", _es)
                        s_s.inc(1)
                        psum_free[idx] = (s_s.h, s_s.val)
                        sc_at = s_s.val
                    else:
                        sc_at = None

                    if evac_v is not None:
                        def _ev(v, E=E, pp=pp, cw=cw, idx=idx, la_tile=la_tile, sc_at=sc_at):
                            if sc_at is not None:
                                v.wait_ge(s_s.h, sc_at)
                            else:
                                v.wait_ge(s_t.h, mm_at[idx])
                            if aux:
                                v.wait_ge(s_la4[pp].h, la_tile)
                            if cols:
                                v.wait_ge(s_pro.h, cols_ld)
                            if idx >= NB:
                                v.wait_ge(s_st4[pp].h, st_at[idx - NB])
                            E.q = v
                            E.step = chainer(v, b.csem["vector"])
                            E.psum = p[pp][0:cw, :]
                            E.psum2 = p2[pp][0:cw, :] if dual else None
                            E.tmp = tmp[pp][0:cw, :]
                            E.out = ob[pp][0:cw, :]
                            E.aux = [a[0:cw, :] for a in axb[pp]]
                            ins = evac_v(E)
                            if ins is not None:
                                ins.then_inc(s_v.h, 1)
                            else:
                                v.drain()
                                v.sem_inc(s_v.h, 1)
                        emit("vector", _ev)
                        s_v.inc(1)
                        ev_v_at[idx] = s_v.val
                        if idx not in psum_free:
                            psum_free[idx] = (s_v.h, s_v.val)
                        ev_done = (s_v.h, s_v.val)
                    else:
                        ev_v_at[idx] = s_v.val
                        ev_done = psum_free[idx]
                    aux_free[idx] = ev_v_at[idx]

                    def _st(g, pp=pp, cw=cw, idx=idx, odap=odap, olo=olo, t=t,
                            ev_done=ev_done):
                        g.wait_ge(ev_done[0], ev_done[1])
                        g.dma_start(out=odap[olo:olo + cw, t * TT:(t + 1) * TT],
                                    in_=ob[pp][0:cw, :]).then_inc(s_st4[pp].h, 16)
                    emit("gpsimd", _st)
                    s_st4[pp].inc(16)
                    st_at[idx] = s_st4[pp].val
                    idx += 1
                    if idx % NB == 0:
                        flush()
                xfree[slot] = mm_at[idx - 1]
        flush()

        def _fin(g):
            for ss in s_st4:
                g.wait_ge(ss.h, ss.val)
        b.sec("gpsimd", _fin)


def st_mm_tm(b, name, segs, N, out, evac_s=None, evac_v=_copy_evac,
             bias_row=None, aux=(), Tn=T, out_dt=F32):
    nblocks = cdiv(N, 512)
    NB = 4
    with b.stage(name) as sctx:
        s_t, s_s, s_v = (b.sem() for _ in range(3))
        s_wl = b.sem_sw()
        s_ld2 = b.sem2()
        s_la4 = [b.sem() for _ in range(NB)]
        s_st4 = [b.sem_sw() for _ in range(NB)]
        for si, s in enumerate(segs):
            Kc = s["K"] // 128
            s["_w"] = [b.sb(sctx, f"{name}_w{si}k{kc}", [128, N], BF16) for kc in range(Kc)]
            s["_xm"] = [b.sb(sctx, f"{name}_x{si}b{t}", [128, Kc * TT], BF16)
                        for t in range(2)]
            s["_x"] = [[s["_xm"][t][:, kc * TT:(kc + 1) * TT] for kc in range(Kc)]
                       for t in range(2)]
        p = [b.ps(sctx, f"{name}_p{i}", [128, 512], F32) for i in range(NB)]
        ob = [b.sb(sctx, f"{name}_o{i}", [128, 512], out_dt) for i in range(NB)]
        tmp = [b.sb(sctx, f"{name}_t{i}", [128, 512], F32) for i in range(NB)]
        axb = [[b.sb(sctx, f"{name}_a{i}j{j}", [128, 512], aux[j][1]) for j in range(len(aux))]
               for i in range(NB)]
        if bias_row is not None:
            brow = b.sb(sctx, f"{name}_br", [1, N], BF16)
            ones1 = b.sb(sctx, f"{name}_on", [1, 128], BF16)

        def _pw(g):
            if bias_row is not None:
                g.dma_start(out=brow[:], in_=bias_row).then_inc(s_wl.h, 16)
            for si, s in enumerate(segs):
                for kc in range(s["K"] // 128):
                    g.dma_start(out=s["_w"][kc][:],
                                in_=s["w"][kc * 128:(kc + 1) * 128, :]).then_inc(s_wl.h, 16)
        b.sec("gpsimd", _pw)
        s_wl.inc(16 * ((1 if bias_row is not None else 0) + sum(s["K"] // 128 for s in segs)))
        wl_ready = s_wl.val

        if bias_row is not None:
            def _von(v):
                v.memset(ones1[:], 1.0).then_inc(s_v.h, 1)
            b.sec("vector", _von)
            s_v.inc(1)
        ones_ready = s_v.val

        pend = []

        def emit(eng, fn):
            pend.append((eng, fn))

        def flush():
            for eng in ENGINES:
                fns = [f for (e, f) in pend if e == eng]
                if fns:
                    b.sec(eng, lambda q, fns=tuple(fns): [f(q) for f in fns])
            pend.clear()

        mm_at = {}
        ev_v_at = {}
        psum_free = {}
        st_at = {}
        aux_free = {}
        xfree = [0, 0]
        idx = 0
        for t5 in range(Tn // TT):
            slot = t5 % 2

            def _lx(sync, slot=slot, t5=t5, w=xfree[slot]):
                sync.wait_ge(s_t.h, w)
                for si, s in enumerate(segs):
                    sync.dma_start(
                        out=s["_xm"][slot][:],
                        in_=s["x"][:, t5 * TT:(t5 + 1) * TT].rearrange(
                            "(kc p) m -> p kc m", p=128)).then_inc(s_ld2[slot].h, 16)
            b.sec("sync", _lx)
            s_ld2[slot].inc(16 * len(segs))
            x_ld = s_ld2[slot].val

            for q_ in range(TT // 128):
                t128 = t5 * 4 + q_
                r0 = t128 * 128
                for nb in range(nblocks):
                    pp = idx % NB
                    nw = min(512, N - nb * 512)
                    n0 = nb * 512

                    if aux:
                        def _la(sync, idx=idx, pp=pp, r0=r0, n0=n0, nw=nw):
                            if idx >= NB:
                                sync.wait_ge(s_v.h, aux_free[idx - NB])
                            for j, (ad, adt) in enumerate(aux):
                                sync.dma_start(out=axb[pp][j][:, 0:nw],
                                               in_=ad[r0:r0 + 128, n0:n0 + nw]).then_inc(s_la4[pp].h, 16)
                        emit("sync", _la)
                        s_la4[pp].inc(16 * len(aux))
                    la_tile = s_la4[pp].val

                    def _mm(tq, idx=idx, pp=pp, q_=q_, slot=slot, n0=n0, nw=nw, x_ld=x_ld):
                        tq.wait_ge(s_ld2[slot].h, x_ld)
                        tq.wait_ge(s_wl.h, wl_ready)
                        if bias_row is not None:
                            tq.wait_ge(s_v.h, ones_ready)
                        if idx >= NB:
                            tq.wait_ge(*psum_free[idx - NB])
                        first = True
                        if bias_row is not None:
                            tq.matmul(p[pp][:, 0:nw], ones1[:], brow[0:1, n0:n0 + nw],
                                      start=True, stop=False)
                            first = False
                        mms = []
                        for si, s in enumerate(segs):
                            for kc in range(s["K"] // 128):
                                mms.append((s["_x"][slot][kc][:, q_ * 128:(q_ + 1) * 128],
                                            s["_w"][kc][:, n0:n0 + nw]))
                        for mi, (lap, rap) in enumerate(mms):
                            ins = tq.matmul(p[pp][:, 0:nw], lap, rap,
                                            start=(first and mi == 0), stop=(mi == len(mms) - 1))
                            if mi == len(mms) - 1:
                                ins.then_inc(s_t.h, 1)
                    emit("tensor", _mm)
                    s_t.inc(1)
                    mm_at[idx] = s_t.val

                    E = _EC()
                    E.cw, E.ncx, E.nw, E.idx = 128, nb, nw, idx
                    E.col = None

                    if evac_s is not None:
                        def _es(sc, E=E, pp=pp, idx=idx, la_tile=la_tile, nw=nw):
                            sc.wait_ge(s_t.h, mm_at[idx])
                            if aux:
                                sc.wait_ge(s_la4[pp].h, la_tile)
                            if idx >= NB:
                                sc.wait_ge(s_v.h, ev_v_at[idx - NB])
                                if evac_v is None:
                                    sc.wait_ge(s_st4[pp].h, st_at[idx - NB])
                            E.q = sc
                            E.step = chainer(sc, b.csem["scalar"])
                            E.psum = p[pp][:, 0:nw]
                            E.psum2 = None
                            E.tmp = tmp[pp][:, 0:nw]
                            E.out = ob[pp][:, 0:nw]
                            E.aux = [a[:, 0:nw] for a in axb[pp]]
                            ins = evac_s(E)
                            if ins is not None:
                                ins.then_inc(s_s.h, 1)
                            else:
                                sc.drain()
                                sc.sem_inc(s_s.h, 1)
                        emit("scalar", _es)
                        s_s.inc(1)
                        psum_free[idx] = (s_s.h, s_s.val)
                        sc_at = s_s.val
                    else:
                        sc_at = None

                    if evac_v is not None:
                        def _ev(v, E=E, pp=pp, idx=idx, la_tile=la_tile, nw=nw, sc_at=sc_at):
                            if sc_at is not None:
                                v.wait_ge(s_s.h, sc_at)
                            else:
                                v.wait_ge(s_t.h, mm_at[idx])
                            if aux:
                                v.wait_ge(s_la4[pp].h, la_tile)
                            if idx >= NB:
                                v.wait_ge(s_st4[pp].h, st_at[idx - NB])
                            E.q = v
                            E.step = chainer(v, b.csem["vector"])
                            E.psum = p[pp][:, 0:nw]
                            E.psum2 = None
                            E.tmp = tmp[pp][:, 0:nw]
                            E.out = ob[pp][:, 0:nw]
                            E.aux = [a[:, 0:nw] for a in axb[pp]]
                            ins = evac_v(E)
                            if ins is not None:
                                ins.then_inc(s_v.h, 1)
                            else:
                                v.drain()
                                v.sem_inc(s_v.h, 1)
                        emit("vector", _ev)
                        s_v.inc(1)
                        ev_v_at[idx] = s_v.val
                        if idx not in psum_free:
                            psum_free[idx] = (s_v.h, s_v.val)
                        ev_done = (s_v.h, s_v.val)
                    else:
                        ev_v_at[idx] = s_v.val
                        ev_done = psum_free[idx]
                    aux_free[idx] = ev_v_at[idx]

                    def _st(g, pp=pp, idx=idx, r0=r0, n0=n0, nw=nw, ev_done=ev_done):
                        g.wait_ge(ev_done[0], ev_done[1])
                        g.dma_start(out=out[r0:r0 + 128, n0:n0 + nw],
                                    in_=ob[pp][:, 0:nw]).then_inc(s_st4[pp].h, 16)
                    emit("gpsimd", _st)
                    s_st4[pp].inc(16)
                    st_at[idx] = s_st4[pp].val
                    idx += 1
                    if idx % NB == 0:
                        flush()
            xfree[slot] = mm_at[idx - 1]
        flush()

        def _fin(g):
            for ss in s_st4:
                g.wait_ge(ss.h, ss.val)
        b.sec("gpsimd", _fin)


def st_transpose_add(b, name, srcT, Nr, xr_in, xr_out, scale, ident, Tn=T,
                     out_dt=F32, src_dt=BF16):
    nch = Nr // 128
    with b.stage(name) as sctx:
        s_pro, s_t, s_v = b.sem(), b.sem(), b.sem()
        s_ld2, s_sr2 = b.sem2(), b.sem2()
        s_st2 = b.sem2_sw()
        id_sb = b.sb(sctx, "tr_id", [128, 128], src_dt)
        src = [b.sb(sctx, f"tr_s{i}", [128, 128], src_dt) for i in range(2)]
        p_tr = [b.ps(sctx, f"tr_p{i}", [128, 128], src_dt) for i in range(2)]
        xa = [b.sb(sctx, f"tr_x{i}", [128, Nr], out_dt) for i in range(2)]

        def _l0(sync):
            sync.dma_start(out=id_sb[:], in_=ident).then_inc(s_pro.h, 16)
        b.sec("sync", _l0)
        s_pro.inc(16)
        pro_ld = s_pro.val

        v_at = {}
        vtile_at = {}
        st_at = {}
        tr_at = {}
        idx = 0
        for i in range(Tn // 128):
            bi = i % 2
            r0 = i * 128

            def _lxr(sync, i=i, bi=bi, r0=r0):
                if xr_in is not None:
                    if i >= 2:
                        sync.wait_ge(s_v.h, vtile_at[i - 2])
                        sync.wait_ge(s_st2[bi].h, st_at[i - 2])
                    sync.dma_start(out=xa[bi][:], in_=xr_in[r0:r0 + 128, :]).then_inc(s_ld2[bi].h, 16)
            b.sec("sync", _lxr)
            if xr_in is not None:
                s_ld2[bi].inc(16)
            x_ld = s_ld2[bi].val

            for c in range(nch):
                pk = idx % 2

                def _ls(sync, c=c, pk=pk, idx=idx, r0=r0):
                    if idx >= 2:
                        sync.wait_ge(s_t.h, tr_at[idx - 2])
                    sync.dma_start(out=src[pk][:],
                                   in_=srcT[c * 128:(c + 1) * 128, r0:r0 + 128]).then_inc(s_sr2[pk].h, 16)
                b.sec("sync", _ls)
                s_sr2[pk].inc(16)
                sld = s_sr2[pk].val

                def _tr(tq, pk=pk, idx=idx, sld=sld):
                    tq.wait_ge(s_sr2[pk].h, sld)
                    tq.wait_ge(s_pro.h, pro_ld)
                    if idx >= 2:
                        tq.wait_ge(s_v.h, v_at[idx - 2])
                    tq.transpose(p_tr[pk][:], src[pk][:], id_sb[:]).then_inc(s_t.h, 1)
                b.sec("tensor", _tr)
                s_t.inc(1)
                tr_at[idx] = s_t.val

                def _ev(v, i=i, c=c, pk=pk, bi=bi, idx=idx, x_ld=x_ld):
                    v.wait_ge(s_t.h, tr_at[idx])
                    if xr_in is not None:
                        v.wait_ge(s_ld2[bi].h, x_ld)
                        v.scalar_tensor_tensor(out=xa[bi][:, c * 128:(c + 1) * 128],
                                               in0=p_tr[pk][:], scalar=float(scale),
                                               in1=xa[bi][:, c * 128:(c + 1) * 128],
                                               op0=OP.mult, op1=OP.add).then_inc(s_v.h, 1)
                    else:
                        if i >= 2 and c == 0:
                            v.wait_ge(s_st2[bi].h, st_at[i - 2])
                        v.tensor_scalar(out=xa[bi][:, c * 128:(c + 1) * 128],
                                        in0=p_tr[pk][:], scalar1=float(scale),
                                        scalar2=None, op0=OP.mult).then_inc(s_v.h, 1)
                b.sec("vector", _ev)
                s_v.inc(1)
                v_at[idx] = s_v.val
                idx += 1
            vtile_at[i] = s_v.val

            def _st(g, i=i, bi=bi, r0=r0):
                g.wait_ge(s_v.h, vtile_at[i])
                g.dma_start(out=xr_out[r0:r0 + 128, :], in_=xa[bi][:]).then_inc(s_st2[bi].h, 16)
            b.sec("gpsimd", _st)
            s_st2[bi].inc(16)
            st_at[i] = s_st2[bi].val

        def _fin(g):
            g.wait_ge(s_st2[0].h, s_st2[0].val)
            g.wait_ge(s_st2[1].h, s_st2[1].val)
        b.sec("gpsimd", _fin)


def host_constants():
    c = {}
    c["ident_bf"] = np.eye(128, dtype=np.float32).astype(BF16NP)
    c["ident_f32"] = np.eye(128, dtype=np.float32)
    r = np.arange(128)
    c["tri_incl"] = (r[:, None] <= r[None, :]).astype(np.float32)
    c["triU_strict"] = (r[:, None] > r[None, :]).astype(np.float32)
    c["trimask_bf"] = (r[:, None] <= r[None, :]).astype(BF16NP)
    M = np.zeros((256, 256), dtype=np.float32)
    for leaf in range(256):
        node = 0
        for level in range(8):
            bit = (leaf >> (7 - level)) & 1
            M[node, leaf] = 1.0 if bit else -1.0
            node = 2 * node + 1 + bit
        assert node - 255 == leaf
    c["fff_M2"] = (2.0 * M).astype(BF16NP)
    c["fff_cs"] = (-M.sum(axis=0).reshape(256, 1)).astype(np.float32)
    E = np.zeros((256, 4096), dtype=np.float32)
    for leaf in range(256):
        E[leaf, leaf * 16:(leaf + 1) * 16] = 1.0
    c["fff_E"] = E.astype(BF16NP)
    inv = 1.0 / (10000.0 ** (np.arange(32, dtype=np.float64) / 32.0))
    ang = np.arange(T, dtype=np.float64)[None, :] * inv[np.tile(np.arange(32), 2)][:, None]
    c["rope_cos"] = np.cos(ang).astype(np.float32)
    c["rope_sin"] = np.sin(ang).astype(np.float32)
    return c


def emit_L2F(b, W, xr_in, xr_out, uid):
    zT = b.dtmp(f"zT_{uid}", [D, T], BF16)
    h1T = b.dtmp(f"h1T_{uid}", [D, T], BF16)
    h2T = b.dtmp(f"h2T_{uid}", [D, T], BF16)
    st_ln(b, xr_in, zT=zT, ident=W["ident_bf"])
    st_mm_fm(b, f"l2f_m1_{uid}", [dict(x=zT, w=W["l2_w"], K=D)], D,
             [(h1T, 0, D)])
    st_mm_fm(b, f"l2f_m2_{uid}", [dict(x=h1T, w=W["l2_a1"], K=D)], D,
             [(h2T, 0, D)])

    def ev_s(E):
        return E.q.activation(out=E.tmp, in_=E.psum, func=AF.Tanh)

    def ev_v(E):
        return E.q.scalar_tensor_tensor(out=E.out, in0=E.tmp, scalar=0.5, in1=E.aux[0],
                                        op0=OP.mult, op1=OP.add)
    st_mm_tm(b, f"l2f_m3_{uid}", [dict(x=h2T, w=W["l2_a2"], K=D)], D, xr_out,
             evac_s=ev_s, evac_v=ev_v, bias_row=W["l2_bias"],
             aux=[(xr_in, F32)], out_dt=F32)


def emit_L1F(b, W, xr_in, xr_out, uid):
    zT = b.dtmp(f"zT_{uid}", [D, T], BF16)
    haT = b.dtmp(f"l1f_haT_{uid}", [4096, T], BF16)
    hbT = b.dtmp(f"l1f_hbT_{uid}", [4096, T], BF16)
    hcT = b.dtmp(f"l1f_hcT_{uid}", [4096, T], BF16)
    dT = b.dtmp(f"l1f_dT_{uid}", [D, T], BF16)
    st_ln(b, xr_in, zT=zT, ident=W["ident_bf"])

    def ev_s_gate(E):
        return E.q.activation(out=E.tmp, in_=E.psum, func=AF.Silu, scale=E.col(0))

    def ev_v_gate(E):
        return E.q.scalar_tensor_tensor(out=E.out, in0=E.psum2, scalar=E.col(1),
                                        in1=E.tmp, op0=OP.mult, op1=OP.mult)
    st_mm_fm(b, f"l1f_g_{uid}",
             [dict(x=zT, w=W["l1_w1T"], w2=W["l1_w2T"], K=D, wlayout="panels")],
             4096, [(haT, 0, 4096)], evac_s=ev_s_gate, evac_v=ev_v_gate,
             cols=[W["l1_w1s"], W["l1_w2s"]])

    def ev_s_gelu(E):
        return E.q.activation(out=E.out, in_=E.psum, func=AF.Gelu, scale=E.col(0))
    st_mm_fm(b, f"l1f_h1_{uid}",
             [dict(x=haT, w=W["l1_h1T"], K=4096, wlayout="panels")],
             4096, [(hbT, 0, 4096)], evac_s=ev_s_gelu, evac_v=None,
             cols=[W["l1_h1s"]])
    st_mm_fm(b, f"l1f_h2_{uid}",
             [dict(x=hbT, w=W["l1_h2T"], K=4096, wlayout="panels")],
             4096, [(hcT, 0, 4096)], evac_s=ev_s_gelu, evac_v=None,
             cols=[W["l1_h2s"]])

    def ev_v_s5(E):
        return E.q.tensor_scalar(out=E.out, in0=E.psum, scalar1=E.col(0),
                                 scalar2=None, op0=OP.mult)
    st_mm_fm(b, f"l1f_w3_{uid}",
             [dict(x=hcT, w=W["l1_w3T"], K=4096, wlayout="panels")],
             D, [(dT, 0, D)], evac_v=ev_v_s5, cols=[W["l1_w3s"]])
    st_transpose_add(b, f"l1f_tr_{uid}", dT, D, xr_in, xr_out, 0.5, W["ident_bf"])


def panels(w_kn):
    Kd, Nd = w_kn.shape
    return np.ascontiguousarray(
        w_kn.reshape(Kd, Nd // 128, 128).transpose(1, 0, 2))


def colmat(v):
    return np.ascontiguousarray(v.reshape(-1, 128).T)


def st_add(b, name, xr_in, plus, xr_out, Dd=D, Tn=T, plus_dt=None):
    plus_dt = plus_dt or [F32] * len(plus)
    with b.stage(name) as sctx:
        s_v = b.sem()
        s_ld2 = b.sem2()
        s_st2 = b.sem2_sw()
        xa = [b.sb(sctx, f"{name}_x{i}", [128, Dd], F32) for i in range(2)]
        dl = [[b.sb(sctx, f"{name}_d{i}_{j}", [128, Dd], plus_dt[j]) for j in range(len(plus))]
              for i in range(2)]
        v_at = {}
        st_at = {}
        for i in range(Tn // 128):
            bi = i % 2
            r0 = i * 128

            def _ld(sync, i=i, bi=bi, r0=r0):
                if i >= 2:
                    sync.wait_ge(s_v.h, v_at[i - 2])
                    sync.wait_ge(s_st2[bi].h, st_at[i - 2])
                sync.dma_start(out=xa[bi][:], in_=xr_in[r0:r0 + 128, :]).then_inc(s_ld2[bi].h, 16)
                for j, (dap, sc) in enumerate(plus):
                    sync.dma_start(out=dl[bi][j][:], in_=dap[r0:r0 + 128, :]).then_inc(s_ld2[bi].h, 16)
            b.sec("sync", _ld)
            s_ld2[bi].inc(16 * (1 + len(plus)))
            ld_i = s_ld2[bi].val

            def _v(v, bi=bi, ld_i=ld_i):
                st = chainer(v, b.csem["vector"])
                v.wait_ge(s_ld2[bi].h, ld_i)
                for j, (dap, sc) in enumerate(plus):
                    st(v.scalar_tensor_tensor(out=xa[bi][:], in0=dl[bi][j][:],
                                              scalar=float(sc), in1=xa[bi][:],
                                              op0=OP.mult, op1=OP.add))
                v.sem_inc(s_v.h, 1)
            b.sec("vector", _v)
            s_v.inc(1)
            v_at[i] = s_v.val

            def _st(g, i=i, bi=bi, r0=r0):
                g.wait_ge(s_v.h, v_at[i])
                g.dma_start(out=xr_out[r0:r0 + 128, :], in_=xa[bi][:]).then_inc(s_st2[bi].h, 16)
            b.sec("gpsimd", _st)
            s_st2[bi].inc(16)
            st_at[i] = s_st2[bi].val

        def _fin(g):
            g.wait_ge(s_st2[0].h, s_st2[0].val)
            g.wait_ge(s_st2[1].h, s_st2[1].val)
        b.sec("gpsimd", _fin)


def st_regroup(b, name, src, dst, Nr, S=8, Tn=T):
    Gn = Tn // S
    with b.stage(name) as sctx:
        s_v = b.sem()
        s_ld2 = b.sem2()
        s_st2 = b.sem2_sw()
        a = [b.sb(sctx, f"{name}_a{i}", [128, Tn], BF16) for i in range(2)]
        o = [b.sb(sctx, f"{name}_o{i}", [128, Tn], BF16) for i in range(2)]
        v_at = {}
        st_at = {}
        for i in range(Nr // 128):
            bi = i % 2

            def _ld(sync, i=i, bi=bi):
                if i >= 2:
                    sync.wait_ge(s_v.h, v_at[i - 2])
                    sync.wait_ge(s_st2[bi].h, st_at[i - 2])
                sync.dma_start(out=a[bi][:], in_=src[i * 128:(i + 1) * 128, :]).then_inc(s_ld2[bi].h, 16)
            b.sec("sync", _ld)
            s_ld2[bi].inc(16)
            ld_i = s_ld2[bi].val

            def _v(v, bi=bi, ld_i=ld_i):
                v.wait_ge(s_ld2[bi].h, ld_i)
                for r in range(S):
                    v.tensor_copy(out=o[bi][:, r * Gn:(r + 1) * Gn],
                                  in_=a[bi][:, r:Tn:S])
                v.drain()
                v.sem_inc(s_v.h, 1)
            b.sec("vector", _v)
            s_v.inc(1)
            v_at[i] = s_v.val

            def _st(g, i=i, bi=bi):
                g.wait_ge(s_v.h, v_at[i])
                g.dma_start(out=dst[i * 128:(i + 1) * 128, :], in_=o[bi][:]).then_inc(s_st2[bi].h, 16)
            b.sec("gpsimd", _st)
            s_st2[bi].inc(16)
            st_at[i] = s_st2[bi].val

        def _fin(g):
            g.wait_ge(s_st2[0].h, s_st2[0].val)
            g.wait_ge(s_st2[1].h, s_st2[1].val)
        b.sec("gpsimd", _fin)


def emit_L3F(b, W, xr_in, xr_out, uid):
    zT = b.dtmp(f"zT_{uid}", [D, T], BF16)
    gT = b.dtmp(f"l3f_gT_{uid}", [2048, T], BF16)
    shared = b.dtmp(f"l3f_sh_{uid}", [T, D], F32)
    zgT = b.dtmp(f"l3f_zgT_{uid}", [D, T], BF16)
    hgT = b.dtmp(f"l3f_hgT_{uid}", [1024, 512], BF16)
    ym = [b.dtmp(f"l3f_ym{k}_{uid}", [T, D], F32) for k in range(2)]
    st_ln(b, xr_in, zT=zT, ident=W["ident_bf"])

    def ev_s_gate(E):
        return E.q.activation(out=E.tmp, in_=E.psum, func=AF.Silu, bias=E.col(0))

    def ev_v_gate(E):
        return E.q.scalar_tensor_tensor(out=E.out, in0=E.psum2, scalar=E.col(1),
                                        in1=E.tmp, op0=OP.add, op1=OP.mult)
    st_mm_fm(b, f"l3f_g_{uid}",
             [dict(x=zT, w=W["l3_sw1"], w2=W["l3_sw3"], K=D)],
             2048, [(gT, 0, 2048)], evac_s=ev_s_gate, evac_v=ev_v_gate,
             cols=[W["l3_sb1c"], W["l3_sb3c"]])
    st_mm_tm(b, f"l3f_sh_{uid}", [dict(x=gT, w=W["l3_sw2"], K=2048)], D,
             shared, bias_row=W["l3_sb2"], out_dt=F32)
    st_regroup(b, f"l3f_rg_{uid}", zT, zgT, D)
    for k in range(2):
        for r in range(8):
            e = (3 * r + 7 * k) % 8

            def ev_s_gelu(E, e=e):
                return E.q.activation(out=E.out, in_=E.psum, func=AF.Gelu,
                                      bias=E.col(0))
            st_mm_fm(b, f"l3f_h{k}_{r}_{uid}",
                     [dict(x=zgT[:, r * 512:(r + 1) * 512], w=W["l3_w1"][e], K=D)],
                     1024, [(hgT, 0, 1024)], evac_s=ev_s_gelu, evac_v=None,
                     cols=[W["l3_b1c"][e]], Tn=512)
            out_view = ym[k].rearrange("(g s) d -> s g d", s=8)[r]
            st_mm_tm(b, f"l3f_y{k}_{r}_{uid}",
                     [dict(x=hgT, w=W["l3_w2"][e], K=1024)], D, out_view,
                     bias_row=W["l3_b2"][e:e + 1, :], out_dt=F32, Tn=512)
    st_add(b, f"l3f_cmb_{uid}", xr_in,
           [(shared, 0.25), (ym[0], 0.125), (ym[1], 0.125)], xr_out)


def st_rope(b, name, srcT, dstT, H, cosd, sind, Tn=T):
    nch = H // 2
    with b.stage(name) as sctx:
        s_pro, s_v = b.sem(), b.sem()
        s_ld2 = b.sem2()
        s_st2 = b.sem2_sw()
        cs = b.sb(sctx, f"{name}_cs", [64, 2 * Tn], F32)
        xt = [b.sb(sctx, f"{name}_xt{i}", [64, Tn], BF16) for i in range(2)]
        xb = [b.sb(sctx, f"{name}_xb{i}", [64, Tn], BF16) for i in range(2)]
        t1 = b.sb(sctx, f"{name}_t1", [64, Tn], F32)
        t2 = b.sb(sctx, f"{name}_t2", [64, Tn], F32)
        ot = [b.sb(sctx, f"{name}_ot{i}", [64, Tn], BF16) for i in range(2)]
        obt = [b.sb(sctx, f"{name}_ob{i}", [64, Tn], BF16) for i in range(2)]
        src4 = srcT.rearrange("(hh half d) t -> half hh d t", half=2, d=32)
        dst4 = dstT.rearrange("(hh half d) t -> half hh d t", half=2, d=32)

        def _l0(sync):
            sync.dma_start(out=cs[:, 0:Tn], in_=cosd).then_inc(s_pro.h, 16)
            sync.dma_start(out=cs[:, Tn:2 * Tn], in_=sind).then_inc(s_pro.h, 16)
        b.sec("sync", _l0)
        s_pro.inc(32)
        pro_ld = s_pro.val

        v_at = {}
        st_at = {}
        for c in range(nch):
            bi = c % 2

            def _ld(sync, c=c, bi=bi):
                if c >= 2:
                    sync.wait_ge(s_v.h, v_at[c - 2])
                    sync.wait_ge(s_st2[bi].h, st_at[c - 2])
                sync.dma_start(out=xt[bi][:], in_=src4[0, 2 * c:2 * c + 2]).then_inc(s_ld2[bi].h, 16)
                sync.dma_start(out=xb[bi][:], in_=src4[1, 2 * c:2 * c + 2]).then_inc(s_ld2[bi].h, 16)
            b.sec("sync", _ld)
            s_ld2[bi].inc(32)
            ld_c = s_ld2[bi].val

            def _v(v, c=c, bi=bi, ld_c=ld_c):
                st = chainer(v, b.csem["vector"])
                v.wait_ge(s_ld2[bi].h, ld_c)
                v.wait_ge(s_pro.h, pro_ld)
                cos = cs[:, 0:Tn]
                sin = cs[:, Tn:2 * Tn]
                st(v.tensor_tensor(out=t1[:], in0=xt[bi][:], in1=cos, op=OP.mult))
                st(v.tensor_tensor(out=t2[:], in0=xb[bi][:], in1=sin, op=OP.mult))
                st(v.tensor_tensor(out=ot[bi][:], in0=t1[:], in1=t2[:], op=OP.subtract))
                st(v.tensor_tensor(out=t1[:], in0=xb[bi][:], in1=cos, op=OP.mult))
                st(v.tensor_tensor(out=t2[:], in0=xt[bi][:], in1=sin, op=OP.mult))
                st(v.tensor_tensor(out=obt[bi][:], in0=t1[:], in1=t2[:], op=OP.add))
                v.sem_inc(s_v.h, 1)
            b.sec("vector", _v)
            s_v.inc(1)
            v_at[c] = s_v.val

            def _st(g, c=c, bi=bi):
                g.wait_ge(s_v.h, v_at[c])
                g.dma_start(out=dst4[0, 2 * c:2 * c + 2], in_=ot[bi][:]).then_inc(s_st2[bi].h, 16)
                g.dma_start(out=dst4[1, 2 * c:2 * c + 2], in_=obt[bi][:]).then_inc(s_st2[bi].h, 16)
            b.sec("gpsimd", _st)
            s_st2[bi].inc(32)
            st_at[c] = s_st2[bi].val

        def _fin(g):
            g.wait_ge(s_st2[0].h, s_st2[0].val)
            g.wait_ge(s_st2[1].h, s_st2[1].val)
        b.sec("gpsimd", _fin)


def st_attn(b, name, qT, kT, v, out, H, dh, dv, ident=None, KV=None, scale=1.0,
            aux_add=None, out_writer=None, out_dt=F32, Tn=T):
    KV = KV if KV is not None else H
    grp = H // KV
    dhc = cdiv(dh, 128)
    nkt = Tn // 512
    nkc = Tn // 128
    nq = Tn // 128
    aug = dv + 8 <= 512
    dva = dv + 8 if aug else dv
    dvb = cdiv(dva, 512)
    with b.stage(name) as sctx:
        s_pro, s_t, s_s, s_v = (b.sem() for _ in range(4))
        s_q2, s_a2 = b.sem2(), b.sem2()
        s_st2 = b.sem2_sw()
        s_kv = b.sem()
        id_sb = b.sb(sctx, f"{name}_id", [128, 128], BF16)
        ones8 = b.sb(sctx, f"{name}_on", [128, 8], BF16)
        kh = b.sb(sctx, f"{name}_kh", [dh if dh <= 128 else 128, Tn * dhc], BF16)
        qh = [b.sb(sctx, f"{name}_qh{i}", [dh if dh <= 128 else 128, 128 * dhc], BF16)
              for i in range(2)]
        vh = b.sb(sctx, f"{name}_vh", [128, nkc * dva], BF16)
        P_sb = b.sb(sctx, f"{name}_P", [128, Tn], BF16)
        PT_sb = b.sb(sctx, f"{name}_PT", [128, Tn], BF16)
        ob = [b.sb(sctx, f"{name}_ob{i}", [128, dv], out_dt) for i in range(2)]
        ax = [b.sb(sctx, f"{name}_ax{i}", [128, dv], F32) for i in range(2)] if aux_add is not None else None
        sc_col = b.sb(sctx, f"{name}_sc", [128, 4], F32)
        S_ps = b.ps(sctx, f"{name}_S", [128, 4096], F32)
        big_pt = dva <= 512 - 8 or True
        ptw = 2048 if dva <= 448 else 1024
        pt_ps = S_ps[:, 0:ptw].bitcast(BF16)
        pv_ps = S_ps[:, ptw:ptw + 512 * dvb]
        rs_ps = S_ps[:, ptw + 512 * dvb:ptw + 512 * dvb + 8]
        vh3 = vh[:].rearrange("p (kc d) -> p kc d", d=dva)

        def _pro(sync):
            sync.dma_start(out=id_sb[:], in_=ident).then_inc(s_pro.h, 16)
        b.sec("sync", _pro)
        s_pro.inc(16)

        def _von(v_):
            v_.memset(ones8[:], 1.0)
            if aug:
                v_.memset(vh[:], 1.0)
            v_.drain()
            v_.sem_inc(s_v.h, 1)
        b.sec("vector", _von)
        s_v.inc(1)
        pro_v = s_v.val
        pro_ld = s_pro.val

        q_ld = {}
        exp_at = {}
        exp_free = {}
        out_ev = {}
        st_at = {}
        pair = 0
        for kv in range(KV):
            kv_gate = [(s_v.h, pro_v)]
            if pair > 0:
                kv_gate.append((s_t.h, exp_free[pair - 1]))

            def _lkv(sync, kv=kv, kv_gate=tuple(kv_gate)):
                for (h_, v_) in kv_gate:
                    sync.wait_ge(h_, v_)
                for c in range(dhc):
                    sync.dma_start(
                        out=kh[:, c * Tn:(c + 1) * Tn],
                        in_=kT[kv * dh + c * 128: kv * dh + min((c + 1) * 128, dh), :],
                    ).then_inc(s_kv.h, 16)
                sync.dma_start(
                    out=vh3[:, :, 0:dv],
                    in_=v.rearrange("(kc kp) d -> kp kc d", kp=128)[:, :, kv * dv:(kv + 1) * dv],
                ).then_inc(s_kv.h, 16)
            b.sec("sync", _lkv)
            s_kv.inc(16 * (dhc + 1))
            kv_ld = s_kv.val

            for hg in range(grp):
                h = kv * grp + hg
                for qi in range(nq):
                    pp = pair % 2

                    def _lq(sync, h=h, qi=qi, pp=pp, pair=pair):
                        if pair >= 2:
                            sync.wait_ge(s_t.h, exp_free[pair - 2])
                            if aux_add is not None:
                                sync.wait_ge(s_v.h, out_ev[pair - 2])
                        for c in range(dhc):
                            sync.dma_start(
                                out=qh[pp][:, c * 128:(c + 1) * 128],
                                in_=qT[h * dh + c * 128: h * dh + min((c + 1) * 128, dh),
                                       qi * 128:(qi + 1) * 128]).then_inc(s_q2[pp].h, 16)
                        if aux_add is not None:
                            sync.dma_start(out=ax[pp][:],
                                           in_=aux_add[qi * 128:(qi + 1) * 128, :]).then_inc(s_q2[pp].h, 16)
                    b.emit("sync", _lq)
                    s_q2[pp].inc(16 * (dhc + (1 if aux_add is not None else 0)))
                    q_ld[pair] = s_q2[pp].val

                    def _smm(tq, pp=pp, pair=pair, kv_ld=kv_ld):
                        tq.wait_ge(s_q2[pp].h, q_ld[pair])
                        tq.wait_ge(s_kv.h, kv_ld)
                        if pair >= 1:
                            tq.wait_ge(s_v.h, out_ev[pair - 1])
                        for kt in range(nkt):
                            for c in range(dhc):
                                cw = min(128, dh - c * 128)
                                ins = tq.matmul(
                                    S_ps[:, kt * 512:(kt + 1) * 512],
                                    qh[pp][0:cw, c * 128:(c + 1) * 128],
                                    kh[0:cw, c * Tn + kt * 512: c * Tn + (kt + 1) * 512],
                                    start=(c == 0), stop=(c == dhc - 1))
                        ins.then_inc(s_t.h, 1)
                    b.emit("tensor", _smm)
                    s_t.inc(1)
                    smm_at = s_t.val

                    def _se(sc, pair=pair, smm_at=smm_at):
                        sc.wait_ge(s_t.h, smm_at)
                        sc.activation(out=P_sb[:], in_=S_ps[:, 0:Tn], func=AF.Exp,
                                      scale=float(scale),
                                      bias=-4.0).then_inc(s_s.h, 1)
                    b.emit("scalar", _se)
                    s_s.inc(1)
                    exp_at[pair] = s_s.val

                    per_round = ptw * 2 // 128
                    n_round = nkc // per_round
                    for rnd in range(n_round):
                        def _tr(tq, rnd=rnd, pair=pair,
                                ptw_=(pT_ev if rnd >= 1 else 0)):
                            tq.wait_ge(s_s.h, exp_at[pair])
                            tq.wait_ge(s_pro.h, pro_ld)
                            if rnd >= 1:
                                tq.wait_ge(s_v.h, ptw_)
                            for j in range(per_round):
                                kc = rnd * per_round + j
                                ins = tq.transpose(pt_ps[:, j * 128:(j + 1) * 128],
                                                   P_sb[:, kc * 128:(kc + 1) * 128],
                                                   id_sb[:])
                            ins.then_inc(s_t.h, 1)
                        b.emit("tensor", _tr)
                        s_t.inc(1)
                        tr_at = s_t.val

                        def _ptv(v_, rnd=rnd, tr_at=tr_at):
                            v_.wait_ge(s_t.h, tr_at)
                            v_.tensor_copy(
                                out=PT_sb[:, rnd * per_round * 128:(rnd + 1) * per_round * 128],
                                in_=pt_ps[:, 0:per_round * 128]).then_inc(s_v.h, 1)
                        b.emit("vector", _ptv)
                        s_v.inc(1)
                        pT_ev = s_v.val

                        def _pv(tq, rnd=rnd, pair=pair, kv_ld=kv_ld, pT_ev=pT_ev):
                            tq.wait_ge(s_v.h, pT_ev)
                            for j in range(per_round):
                                kc = rnd * per_round + j
                                first = (kc == 0)
                                for db in range(dvb):
                                    bw = min(512, dva - db * 512)
                                    ins = tq.matmul(
                                        pv_ps[:, db * 512:db * 512 + bw],
                                        PT_sb[:, kc * 128:(kc + 1) * 128],
                                        vh[:, kc * dva + db * 512: kc * dva + db * 512 + bw],
                                        start=first, stop=(kc == nkc - 1))
                                if not aug:
                                    ins2 = tq.matmul(rs_ps[:, :],
                                                     PT_sb[:, kc * 128:(kc + 1) * 128],
                                                     ones8[:], start=first,
                                                     stop=(kc == nkc - 1))
                                    ins = ins2
                            ins.then_inc(s_t.h, 1)
                        b.emit("tensor", _pv)
                        s_t.inc(1)
                        pv_at = s_t.val
                    exp_free[pair] = pv_at

                    def _vo(v_, pp=pp, pair=pair, pv_at=pv_at, h=h, qi=qi):
                        st = chainer(v_, b.csem["vector"])
                        v_.wait_ge(s_t.h, pv_at)
                        v_.wait_ge(s_v.h, pro_v)
                        if pair >= 2:
                            v_.wait_ge(s_st2[pp].h, st_at[pair - 2])
                        rsrc = pv_ps[:, dv:dv + 1] if aug else rs_ps[:, 0:1]
                        st(v_.reciprocal(out=sc_col[:, 2:3], in_=rsrc))
                        if aux_add is not None:
                            v_.wait_ge(s_q2[pp].h, q_ld[pair])
                            st(v_.scalar_tensor_tensor(out=ob[pp][:], in0=pv_ps[:, 0:dv],
                                                       scalar=sc_col[:, 2:3], in1=ax[pp][:],
                                                       op0=OP.mult, op1=OP.add))
                        else:
                            st(v_.tensor_scalar(out=ob[pp][:], in0=pv_ps[:, 0:dv],
                                                scalar1=sc_col[:, 2:3], scalar2=None,
                                                op0=OP.mult))
                        v_.sem_inc(s_v.h, 1)
                    b.emit("vector", _vo)
                    s_v.inc(1)
                    out_ev[pair] = s_v.val

                    def _st(g, pp=pp, pair=pair, h=h, qi=qi):
                        g.wait_ge(s_v.h, out_ev[pair])
                        if out_writer is not None:
                            out_writer(g, h, qi, ob[pp])
                            g.sem_inc(s_st2[pp].h, 16)
                        else:
                            g.dma_start(out=out[qi * 128:(qi + 1) * 128,
                                                h * dv:(h + 1) * dv],
                                        in_=ob[pp][:]).then_inc(s_st2[pp].h, 16)
                    b.emit("gpsimd", _st)
                    s_st2[pp].inc(16)
                    st_at[pair] = s_st2[pp].val
                    pair += 1
                    b.flush()

        def _fin(g):
            g.wait_ge(s_st2[0].h, s_st2[0].val)
            g.wait_ge(s_st2[1].h, s_st2[1].val)
        b.sec("gpsimd", _fin)


def st_attn64(b, name, qT, kT, v, out, H, KV=None, scale=1.0, ebias=-4.0,
              ident=None, out_dt=BF16, Tn=T):
    KV = KV if KV is not None else H
    grp = H // KV
    nkc = Tn // 128
    ncc = nkc // 2
    QW = 512
    nqt = Tn // QW
    with b.stage(name) as sctx:
        s_pro, s_kv, s_ts, s_se, s_tp = (b.sem() for _ in range(5))
        s_vc, s_tt, s_vn, s_v0 = (b.sem() for _ in range(4))
        s_q2 = b.sem2()
        s_st2 = b.sem2_sw()
        id_sb = b.sb(sctx, f"{name}_id", [128, 128], BF16)
        kh = [b.sb(sctx, f"{name}_kh{i}", [64, Tn], BF16) for i in range(2)]
        vh = [b.sb(sctx, f"{name}_vh{i}", [128, nkc * 65], BF16) for i in range(2)]
        qh = [b.sb(sctx, f"{name}_qh{i}", [64, Tn], BF16) for i in range(2)]
        PT = [b.sb(sctx, f"{name}_pt{i}", [128, nkc * 512], BF16)
              for i in range(2)]
        oT = [b.sb(sctx, f"{name}_ot{i}", [128, QW], BF16) for i in range(2)]
        ob = [b.sb(sctx, f"{name}_ob{i}", [128, 256], out_dt) for i in range(2)]
        rc = b.sb(sctx, f"{name}_rc", [128, 8], F32)
        S_ps = b.ps(sctx, f"{name}_S", [128, 2048], F32)
        o_ps = b.ps(sctx, f"{name}_o", [128, 1024], F32)
        tr_ps = b.ps(sctx, f"{name}_tr", [128, 1024], BF16)

        def _pro(sync):
            sync.dma_start(out=id_sb[:], in_=ident).then_inc(s_pro.h, 16)
        b.sec("sync", _pro)
        s_pro.inc(16)
        pro_ld = s_pro.val

        def _von(v_):
            v_.memset(vh[0][:], 1.0)
            v_.memset(vh[1][:], 1.0)
            v_.memset(oT[0][:], 0.0)
            v_.memset(oT[1][:], 0.0)
            v_.drain()
            v_.sem_inc(s_v0.h, 1)
        b.sec("vector", _von)
        s_v0.inc(1)
        pro_v = s_v0.val

        vh3o = [vh[i][:].rearrange("p (kc d) -> p kc d", d=65) for i in range(2)]
        v3 = v.rearrange("(kc kp) d -> kp kc d", kp=128)

        kv_ld = {}
        q_ld = {}
        kv_lastS = {}
        kv_lastP = {}
        h_lastS = {}
        slab_exp = [0, 0]
        slot_pv = [0, 0, 0, 0]
        vc_at = {-2: 0, -1: 0}
        tt_at = {}
        vn_at = {}
        st_at = {}
        pair_meta = {}

        def norm_pair(v_, p, ttv, stfree):
            tp1 = p % 2
            v_.wait_ge(s_tt.h, ttv)
            v_.wait_ge(s_st2[tp1].h, stfree)
            st = chainer(v_, b.csem["vector"])
            for j in range(4):
                ins = v_.reciprocal(
                    out=rc[:, j:j + 1],
                    in_=tr_ps[:, tp1 * 512 + j * 128 + 64: tp1 * 512 + j * 128 + 65])
            st(ins)
            for j in range(4):
                ins = v_.tensor_scalar(
                    out=ob[tp1][:, j * 64:(j + 1) * 64],
                    in0=tr_ps[:, tp1 * 512 + j * 128: tp1 * 512 + j * 128 + 64],
                    scalar1=rc[:, j:j + 1], scalar2=None, op0=OP.mult)
            ins.then_inc(s_vn.h, 1)

        def store_pair(g, p, vnv):
            h1, qt1 = pair_meta[p]
            tp1 = p % 2
            g.wait_ge(s_vn.h, vnv)
            for j in range(4):
                g.dma_start(
                    out=out[qt1 * QW + j * 128: qt1 * QW + (j + 1) * 128,
                            h1 * 64:(h1 + 1) * 64],
                    in_=ob[tp1][:, j * 64:(j + 1) * 64]).then_inc(s_st2[tp1].h, 16)

        def tr_pair(tq, p, vcv, vnfree):
            tp1 = p % 2
            tq.wait_ge(s_vc.h, vcv)
            tq.wait_ge(s_vn.h, vnfree)
            tq.wait_ge(s_pro.h, pro_ld)
            for j2 in range(4):
                ins2 = tq.transpose(
                    tr_ps[:, tp1 * 512 + j2 * 128: tp1 * 512 + (j2 + 1) * 128],
                    oT[tp1][:, j2 * 128:(j2 + 1) * 128], id_sb[:])
            ins2.then_inc(s_tt.h, 1)

        pv_at = {}
        pair_kv = {}
        npair = H * nqt

        def emit_pv(tq, p1):
            kv1, kvp1, kvld1 = pair_kv[p1]
            tq.wait_ge(s_se.h, E_all[p1])
            tq.wait_ge(s_vc.h, vc_at[p1 - 2])
            tq.wait_ge(s_kv.h, kvld1)
            pb = PT[p1 % 2]
            for k in range(nkc):
                ins = tq.matmul(
                    o_ps[0:65, (p1 % 2) * 512:(p1 % 2) * 512 + 512],
                    vh[kvp1][:, k * 65:(k + 1) * 65],
                    pb[:, k * 512:(k + 1) * 512],
                    start=(k == 0), stop=(k == nkc - 1))
            ins.then_inc(s_tp.h, 1)

        def emit_vcopy(v_, p1):
            v_.wait_ge(s_tp.h, pv_at[p1])
            v_.wait_ge(s_tt.h, tt_at.get(p1 - 2, 0))
            v_.tensor_copy(
                out=oT[p1 % 2][0:65, :],
                in_=o_ps[0:65, (p1 % 2) * 512:(p1 % 2) * 512 + 512]).then_inc(s_vc.h, 1)

        E_all = {}
        idx = 0
        for kv in range(KV):
            kvp = kv % 2

            def _lkv(sync, kv=kv, kvp=kvp,
                     wS=kv_lastS.get(kv - 2, 0), wP=kv_lastP.get(kv - 2, 0)):
                sync.wait_ge(s_v0.h, pro_v)
                sync.wait_ge(s_ts.h, wS)
                sync.wait_ge(s_tp.h, wP)
                sync.dma_start(out=kh[kvp][0:64, :],
                               in_=kT[kv * 64:(kv + 1) * 64, :]).then_inc(s_kv.h, 16)
                sync.dma_start(out=vh3o[kvp][:, :, 0:64],
                               in_=v3[:, :, kv * 64:(kv + 1) * 64]).then_inc(s_kv.h, 16)
            b.sec("sync", _lkv)
            s_kv.inc(32)
            kv_ld[kv] = s_kv.val

            for hg in range(grp):
                h = kv * grp + hg
                hp = h % 2

                def _lq(sync, h=h, hp=hp, wS=h_lastS.get(h - 2, 0)):
                    sync.wait_ge(s_ts.h, wS)
                    sync.dma_start(out=qh[hp][0:64, :],
                                   in_=qT[h * 64:(h + 1) * 64, :]).then_inc(s_q2[hp].h, 16)
                b.sec("sync", _lq)
                s_q2[hp].inc(16)
                q_ld[h] = s_q2[hp].val

                for qt in range(nqt):
                    pp = idx % 2
                    p1, p2 = idx - 1, idx - 2
                    pair_meta[idx] = (h, qt)
                    pair_kv[idx] = (kv, kvp, kv_ld[kv])
                    S_at = [s_ts.inc() for _ in range(ncc)]
                    if p2 >= 0:
                        tt_at[p2] = s_tt.inc()
                    if p1 >= 0:
                        pv_at[p1] = s_tp.inc()
                        kv_lastP[pair_kv[p1][0]] = pv_at[p1]
                    E_at = [s_se.inc() for _ in range(ncc)]
                    E_all[idx] = E_at[-1]
                    sfree = []
                    for cc in range(ncc):
                        sfree.append(slab_exp[cc % 2])
                        slab_exp[cc % 2] = E_at[cc]
                    if p2 >= 0:
                        vn_at[p2] = s_vn.inc()
                    if p1 >= 0:
                        vc_at[p1] = s_vc.inc()

                    def _tn(tq, idx=idx, kvp=kvp, hp=hp, qt=qt,
                            sfree=tuple(sfree), kvld=kv_ld[kv], qld=q_ld[h],
                            p1=p1, p2=p2,
                            vc_p2=vc_at.get(p2, 0), vn_tr=vn_at.get(idx - 4, 0)):
                        for cc in range(ncc):
                            tq.wait_ge(s_se.h, sfree[cc])
                            if cc == 0:
                                tq.wait_ge(s_q2[hp].h, qld)
                                tq.wait_ge(s_kv.h, kvld)
                            slab = cc % 2
                            for j in range(2):
                                k = 2 * cc + j
                                ins = tq.matmul(
                                    S_ps[:, slab * 1024 + j * 512:
                                         slab * 1024 + j * 512 + 512],
                                    kh[kvp][0:64, k * 128:(k + 1) * 128],
                                    qh[hp][0:64, qt * QW:(qt + 1) * QW],
                                    start=True, stop=True)
                            ins.then_inc(s_ts.h, 1)
                        if p2 >= 0:
                            tr_pair(tq, p2, vc_p2, vn_tr)
                        if p1 >= 0:
                            emit_pv(tq, p1)
                    b.emit("tensor", _tn)

                    def _sc(sc, pp=pp, S_at=tuple(S_at),
                            ptfree=pv_at.get(idx - 2, 0)):
                        for cc in range(ncc):
                            sc.wait_ge(s_ts.h, S_at[cc])
                            sc.wait_ge(s_tp.h, ptfree)
                            slab = cc % 2
                            sc.activation(
                                out=PT[pp][:, cc * 1024:(cc + 1) * 1024],
                                in_=S_ps[:, slab * 1024: slab * 1024 + 1024],
                                func=AF.Exp, scale=float(scale),
                                bias=float(ebias)).then_inc(s_se.h, 1)
                    b.emit("scalar", _sc)

                    def _ve(v_, p1=p1, p2=p2,
                            ttp2=tt_at.get(p2, 0),
                            stfree=st_at.get(idx - 4, 0)):
                        v_.wait_ge(s_v0.h, pro_v)
                        if p2 >= 0:
                            norm_pair(v_, p2, ttp2, stfree)
                        if p1 >= 0:
                            emit_vcopy(v_, p1)
                    b.emit("vector", _ve)

                    if p2 >= 0:
                        def _st(g, p2=p2, vnp=vn_at[p2]):
                            store_pair(g, p2, vnp)
                        b.emit("gpsimd", _st)
                        s_st2[p2 % 2].inc(64)
                        st_at[p2] = s_st2[p2 % 2].val
                    b.flush()
                    idx += 1
                h_lastS[h] = s_ts.val
            kv_lastS[kv] = s_ts.val

        pA, pB = idx - 2, idx - 1
        tt_at[pA] = s_tt.inc()
        pv_at[pB] = s_tp.inc()
        vn_at[pA] = s_vn.inc()
        vc_at[pB] = s_vc.inc()

        def _tn_t1(tq, pA=pA, pB=pB, vc_pA=vc_at.get(pA, 0),
                   vn_tr=vn_at.get(pA - 2, 0)):
            tr_pair(tq, pA, vc_pA, vn_tr)
            emit_pv(tq, pB)
        b.sec("tensor", _tn_t1)

        def _ve_t1(v_, pA=pA, pB=pB, ttpA=tt_at[pA],
                   stfree=st_at.get(pA - 2, 0)):
            norm_pair(v_, pA, ttpA, stfree)
            emit_vcopy(v_, pB)
        b.sec("vector", _ve_t1)

        def _st_t1(g, pA=pA, vnp=vn_at[pA]):
            store_pair(g, pA, vnp)
        b.sec("gpsimd", _st_t1)
        s_st2[pA % 2].inc(64)
        st_at[pA] = s_st2[pA % 2].val

        tt_at[pB] = s_tt.inc()
        vn_at[pB] = s_vn.inc()

        def _tn_t2(tq, pB=pB, vc_pB=vc_at[pB], vn_tr=vn_at.get(pB - 2, 0)):
            tr_pair(tq, pB, vc_pB, vn_tr)
        b.sec("tensor", _tn_t2)

        def _ve_t2(v_, pB=pB, ttpB=tt_at[pB], stfree=st_at.get(pB - 2, 0)):
            norm_pair(v_, pB, ttpB, stfree)
        b.sec("vector", _ve_t2)

        def _st_t2(g, pB=pB, vnp=vn_at[pB]):
            store_pair(g, pB, vnp)
        b.sec("gpsimd", _st_t2)
        s_st2[pB % 2].inc(64)
        st_at[pB] = s_st2[pB % 2].val

        def _fin(g):
            g.wait_ge(s_st2[0].h, s_st2[0].val)
            g.wait_ge(s_st2[1].h, s_st2[1].val)
        b.sec("gpsimd", _fin)


def emit_L3A(b, W, xr_in, xr_out, uid):
    zT = b.dtmp(f"zT_{uid}", [D, T], BF16)
    qkT = b.dtmp(f"l3a_qkT_{uid}", [2048, T], BF16)
    qrT = b.dtmp(f"l3a_qrT_{uid}", [1024, T], BF16)
    krT = b.dtmp(f"l3a_krT_{uid}", [1024, T], BF16)
    vtm = b.dtmp(f"l3a_v_{uid}", [T, 1024], BF16)
    attO = b.dtmp(f"l3a_o_{uid}", [T, 1024], BF16)
    OT = b.dtmp(f"l3a_OT_{uid}", [1024, T], BF16)
    st_ln(b, xr_in, zT=zT, ident=W["ident_bf"])

    def ev_qk(E):
        return E.q.tensor_scalar(out=E.out, in0=E.psum, scalar1=0.125 ** 0.5,
                                 scalar2=None, op0=OP.mult)
    st_mm_fm(b, f"l3a_qk_{uid}", [dict(x=zT, w=W["l3_wqk"], K=D)], 2048,
             [(qkT, 0, 2048)], evac_v=ev_qk)
    st_mm_tm(b, f"l3a_v_{uid}", [dict(x=zT, w=W["l3_wv"], K=D)], 1024, vtm,
             out_dt=BF16)
    st_rope(b, f"l3a_rq_{uid}", qkT[0:1024, :], qrT, 16, W["rope_cos"], W["rope_sin"])
    st_rope(b, f"l3a_rk_{uid}", qkT[1024:2048, :], krT, 16, W["rope_cos"], W["rope_sin"])
    st_attn64(b, f"l3a_at_{uid}", qrT, krT, vtm, attO, H=16,
              ident=W["ident_bf"], scale=1.0, out_dt=BF16)
    st_transpose_add(b, f"l3a_tr_{uid}", attO, 4096, None, OT, 1.0,
                     W["ident_bf"], Tn=1024, out_dt=BF16)

    def ev_v(E):
        return E.q.tensor_tensor(out=E.out, in0=E.psum, in1=E.aux[0], op=OP.add)
    st_mm_tm(b, f"l3a_o_{uid}", [dict(x=OT, w=W["l3_wout"], K=D)], D, xr_out,
             evac_v=ev_v, aux=[(xr_in, F32)], out_dt=F32)


def emit_L2A(b, W, xr_in, xr_out, uid):
    zT = b.dtmp(f"zT_{uid}", [D, T], BF16)
    qT = b.dtmp(f"l2a_qT_{uid}", [1024, T], BF16)
    kT = b.dtmp(f"l2a_kT_{uid}", [256, T], BF16)
    qrT = b.dtmp(f"l2a_qrT_{uid}", [1024, T], BF16)
    krT = b.dtmp(f"l2a_krT_{uid}", [256, T], BF16)
    vtm = b.dtmp(f"l2a_v_{uid}", [T, 256], BF16)
    attO = b.dtmp(f"l2a_o_{uid}", [T, 1024], BF16)
    OT = b.dtmp(f"l2a_OT_{uid}", [1024, T], BF16)
    st_ln(b, xr_in, zT=zT, ident=W["ident_bf"])

    def ev_q(E):
        return E.q.tensor_scalar(out=E.out, in0=E.psum, scalar1=0.125,
                                 scalar2=None, op0=OP.mult)
    st_mm_fm(b, f"l2a_q_{uid}", [dict(x=zT, w=W["l2_wq"], K=D)], 1024,
             [(qT, 0, 1024)], evac_v=ev_q)
    st_mm_fm(b, f"l2a_k_{uid}", [dict(x=zT, w=W["l2_wk"], K=D)], 256,
             [(kT, 0, 256)])
    st_mm_tm(b, f"l2a_v_{uid}", [dict(x=zT, w=W["l2_wv"], K=D)], 256, vtm,
             out_dt=BF16)
    st_rope(b, f"l2a_rq_{uid}", qT, qrT, 16, W["rope_cos"], W["rope_sin"])
    st_rope(b, f"l2a_rk_{uid}", kT, krT, 4, W["rope_cos"], W["rope_sin"])
    st_attn64(b, f"l2a_at_{uid}", qrT, krT, vtm, attO, H=16, KV=4,
              ident=W["ident_bf"], scale=1.0, out_dt=BF16)
    st_transpose_add(b, f"l2a_tr_{uid}", attO, 4096, None, OT, 1.0,
                     W["ident_bf"], Tn=1024, out_dt=BF16)

    def ev_v(E):
        return E.q.tensor_tensor(out=E.out, in0=E.psum, in1=E.aux[0], op=OP.add)
    st_mm_tm(b, f"l2a_o_{uid}", [dict(x=OT, w=W["l2_wo"], K=D)], D, xr_out,
             evac_v=ev_v, aux=[(xr_in, F32)], out_dt=F32)


def st_greduce(b, name, src, dst, R, Nin):
    Nout = Nin // 8
    nch = cdiv(R, 128)
    with b.stage(name) as sctx:
        s_v = b.sem()
        s_ld2 = b.sem2()
        s_st2 = b.sem2_sw()
        a = [b.sb(sctx, f"{name}_a{i}", [128, Nin], F32) for i in range(2)]
        o = [b.sb(sctx, f"{name}_o{i}", [128, Nout], F32) for i in range(2)]
        v_at = {}
        st_at = {}
        for i in range(nch):
            bi = i % 2
            cw = min(128, R - i * 128)

            def _ld(sync, i=i, bi=bi, cw=cw):
                if i >= 2:
                    sync.wait_ge(s_v.h, v_at[i - 2])
                sync.dma_start(out=a[bi][0:cw, :],
                               in_=src[i * 128:i * 128 + cw, :]).then_inc(s_ld2[bi].h, 16)
            b.sec("sync", _ld)
            s_ld2[bi].inc(16)
            ld_i = s_ld2[bi].val

            def _v(v, i=i, bi=bi, cw=cw, ld_i=ld_i):
                v.wait_ge(s_ld2[bi].h, ld_i)
                if i >= 2:
                    v.wait_ge(s_st2[bi].h, st_at[i - 2])
                v.reduce_sum(out=o[bi][0:cw, :],
                             in_=a[bi][0:cw, :].rearrange("p (g k) -> p g k", k=8),
                             axis=AX).then_inc(s_v.h, 1)
            b.sec("vector", _v)
            s_v.inc(1)
            v_at[i] = s_v.val

            def _st(g, i=i, bi=bi, cw=cw):
                g.wait_ge(s_v.h, v_at[i])
                g.dma_start(out=dst[i * 128:i * 128 + cw, :],
                            in_=o[bi][0:cw, :]).then_inc(s_st2[bi].h, 16)
            b.sec("gpsimd", _st)
            s_st2[bi].inc(16)
            st_at[i] = s_st2[bi].val

        def _fin(g):
            g.wait_ge(s_st2[0].h, s_st2[0].val)
            g.wait_ge(s_st2[1].h, s_st2[1].val)
        b.sec("gpsimd", _fin)


def emit_L1A(b, W, xr_in, xr_out, uid):
    zT = b.dtmp(f"zT_{uid}", [D, T], BF16)
    qT = b.dtmp(f"l1a_qT_{uid}", [1024, T], BF16)
    kT = b.dtmp(f"l1a_kT_{uid}", [1024, T], BF16)
    wvr = b.dtmp(f"l1a_wvr_{uid}", [D, 1024], F32)
    bvr = b.dtmp(f"l1a_bvr_{uid}", [1, 1024], F32)
    vtm = b.dtmp(f"l1a_v_{uid}", [T, 1024], BF16)
    delta = b.dtmp(f"l1a_d_{uid}", [T, D], F32)
    st_ln(b, xr_in, zT=zT, gamma=W["l1_gamma"], beta=W["l1_beta"],
          ident=W["ident_bf"])
    st_greduce(b, f"l1a_wr_{uid}", W["l1_wv"], wvr, D, 8192)
    st_greduce(b, f"l1a_br_{uid}", W["l1_bv"], bvr, 1, 8192)

    def ev_bias_q(E):
        return E.q.tensor_scalar(out=E.out, in0=E.psum, scalar1=E.col(0),
                                 scalar2=128.0 ** -0.5, op0=OP.add, op1=OP.mult)

    def ev_bias(E):
        return E.q.tensor_scalar(out=E.out, in0=E.psum, scalar1=E.col(0),
                                 scalar2=None, op0=OP.add)
    st_mm_fm(b, f"l1a_q_{uid}", [dict(x=zT, w=W["l1_wq"], K=D)], 1024,
             [(qT, 0, 1024)], evac_v=ev_bias_q, cols=[W["l1_bqc"]])
    st_mm_fm(b, f"l1a_k_{uid}", [dict(x=zT, w=W["l1_wk"], K=D)], 1024,
             [(kT, 0, 1024)], evac_v=ev_bias, cols=[W["l1_bkc"]])
    st_mm_tm(b, f"l1a_v_{uid}", [dict(x=zT, w=wvr, K=D)], 1024, vtm,
             bias_row=bvr, out_dt=BF16)

    d6 = delta.rearrange("(hh R a bb) (cc f) -> hh R a bb cc f",
                         hh=8, R=32, a=2, bb=8, cc=8)
    sem_sc = b.sem_sw()

    def writer(g, h, qi, ob):
        g.dma_start(out=d6[h, qi, 0], in_=ob[0:64, :]).then_inc(sem_sc.h, 16)
        g.dma_start(out=d6[h, qi, 1], in_=ob[64:128, :]).then_inc(sem_sc.h, 16)
        sem_sc.inc(32)
        g.wait_ge(sem_sc.h, sem_sc.val)
    st_attn(b, f"l1a_at_{uid}", qT, kT, vtm, None, H=8, dh=128, dv=128,
            ident=W["ident_bf"], scale=1.0, out_writer=writer,
            out_dt=F32)
    st_add(b, f"l1a_add_{uid}", xr_in, [(delta, 1.0)], xr_out)


def emit_fff(b, W, pfx, xT32, hseg, outT_spec, act, uid, gT, ohT, maskT, HmT):
    def ev_route(E):
        return E.q.tensor_scalar(out=E.out, in0=E.psum, scalar1=E.col(0),
                                 scalar2=0.0, op0=OP.add, op1=OP.is_gt)
    st_mm_fm(b, f"{pfx}rt_{uid}", [dict(x=xT32, w=W[pfx + "nwT"], K=D, dt=F32)],
             256, [(gT, 0, 256)], evac_v=ev_route, cols=[W[pfx + "nbc"]])

    def ev_leaf(E):
        return E.q.tensor_scalar(out=E.out, in0=E.psum, scalar1=E.col(0),
                                 scalar2=7.5, op0=OP.add, op1=OP.is_gt)
    st_mm_fm(b, f"{pfx}oh_{uid}", [dict(x=gT, w=W["fff_M2"], K=256)], 256,
             [(ohT, 0, 256)], evac_v=ev_leaf, cols=[W["fff_cs"]])

    def ev_s_h(E):
        return E.q.activation(out=E.tmp, in_=E.psum, func=act, bias=E.col(0))

    def ev_v_h(E):
        return E.q.tensor_tensor(out=E.out, in0=E.tmp, in1=E.psum2, op=OP.mult)
    st_mm_fm(b, f"{pfx}h_{uid}", [dict(list(hseg.items()) + [("w", W[pfx + "w1p"]),
                                        ("K", D), ("wlayout", "panels")]),
                                  dict(x=ohT, w=W["fff_E"], K=256, alt=True)],
             4096, [(HmT, 0, 4096)], evac_s=ev_s_h, evac_v=ev_v_h,
             cols=[W[pfx + "b1c"]])

    segs2 = [dict(x=HmT, w=W[pfx + "w2p"], K=4096, wlayout="panels"),
             dict(x=ohT, w=W[pfx + "b2s"], K=256)]
    segs2tm = [dict(x=HmT, w=W[pfx + "w2r"], K=4096),
               dict(x=ohT, w=W[pfx + "b2s"], K=256)]
    if outT_spec[0] == "fm":
        qsc = outT_spec[2] if len(outT_spec) > 2 else None
        if qsc is None:
            st_mm_fm(b, f"{pfx}o_{uid}", segs2, D, [(outT_spec[1], 0, D)])
        else:
            def ev_sc(E, qsc=qsc):
                return E.q.tensor_scalar(out=E.out, in0=E.psum, scalar1=float(qsc),
                                         scalar2=None, op0=OP.mult)
            st_mm_fm(b, f"{pfx}o_{uid}", segs2, D, [(outT_spec[1], 0, D)],
                     evac_v=ev_sc)
    else:
        _, outd, ev_s, ev_v, aux, odt = outT_spec
        st_mm_tm(b, f"{pfx}o_{uid}", segs2tm, D, outd, evac_s=ev_s, evac_v=ev_v,
                 aux=aux, out_dt=odt)


def emit_L0F(b, W, xr_in, xr_out, uid, upto=99):
    z0 = b.dtmp(f"l0f_z0_{uid}", [T, D], F32)
    n0 = b.dtmp(f"l0f_n0_{uid}", [T, D], F32)
    n0T = b.dtmp(f"l0f_n0T_{uid}", [D, T], BF16)
    n0T32 = b.dtmp(f"l0f_n0T32_{uid}", [D, T], F32)
    gT = b.dtmp(f"l0f_gT_{uid}", [256, T], BF16)
    ohT = b.dtmp(f"l0f_ohT_{uid}", [256, T], BF16)
    maskT = b.dtmp(f"l0f_mkT_{uid}", [4096, T], BF16)
    HmT = b.dtmp(f"l0f_HmT_{uid}", [4096, T], BF16)
    qfT = b.dtmp(f"l0f_qfT_{uid}", [D, T], BF16)
    kfT = b.dtmp(f"l0f_kfT_{uid}", [D, T], BF16)
    vf = b.dtmp(f"l0f_vf_{uid}", [T, D], BF16)
    u = b.dtmp(f"l0f_u_{uid}", [T, D], F32)
    uT32 = b.dtmp(f"l0f_uT32_{uid}", [D, T], F32)
    st_ln(b, xr_in, z=z0, ident=W["ident_bf"])
    st_ln(b, z0, variant="inner", z=n0, zT=n0T, zT32=n0T32,
          ident=W["ident_bf"], ident32=W["ident_f32"])
    hseg_n0 = dict(x=n0T)
    if upto < 1:
        st_add(b, f"l0f_dbg_{uid}", xr_in, [(z0, 0.001)], xr_out)
        return
    emit_fff(b, W, "l0_q_", n0T32, hseg_n0, ("fm", qfT, 1.0 / 32.0), AF.Relu, uid, gT, ohT, maskT, HmT)
    if upto < 2:
        st_add(b, f"l0f_dbg_{uid}", xr_in, [(n0, 0.001)], xr_out)
        return
    emit_fff(b, W, "l0_k_", n0T32, hseg_n0, ("fm", kfT), AF.Relu, uid, gT, ohT, maskT, HmT)
    emit_fff(b, W, "l0_v_", n0T32, hseg_n0, ("tm", vf, None, _copy_evac, (), BF16),
             AF.Relu, uid, gT, ohT, maskT, HmT)
    if upto < 3:
        st_add(b, f"l0f_dbg_{uid}", xr_in, [(vf, 0.001)], xr_out, plus_dt=[BF16])
        return
    st_attn(b, f"l0f_at_{uid}", qfT, kfT, vf, u, H=1, dh=1024, dv=1024,
            ident=W["ident_bf"], scale=1.0, aux_add=n0, out_dt=F32)
    if upto < 4:
        st_add(b, f"l0f_dbg_{uid}", xr_in, [(u, 0.001)], xr_out)
        return
    st_transpose_add(b, f"l0f_tr_{uid}", u, 4096, None, uT32, 1.0,
                     W["ident_f32"], Tn=1024, out_dt=F32, src_dt=F32)
    if upto < 5:
        st_add(b, f"l0f_dbg_{uid}", xr_in, [(u, 0.001)], xr_out)
        return

    def ev_v_fin(E):
        E.step(E.q.scalar_tensor_tensor(out=E.tmp, in0=E.aux[1], scalar=0.5,
                                        in1=E.aux[0], op0=OP.mult, op1=OP.add))
        return E.q.scalar_tensor_tensor(out=E.out, in0=E.psum, scalar=0.5,
                                        in1=E.tmp, op0=OP.mult, op1=OP.add)
    emit_fff(b, W, "l0_f_", uT32, dict(x=uT32, cast=True),
             ("tm", xr_out, None, ev_v_fin, [(xr_in, F32), (u, F32)], F32),
             AF.Gelu, uid, gT, ohT, maskT, HmT)


def st_mamba_conv(b, name, xbcT, convT, conv_w, conv_b, R=2304, Tn=T):
    nch = R // 128
    with b.stage(name) as sctx:
        s_v, s_s = b.sem(), b.sem()
        s_ld2 = b.sem2()
        s_st2 = b.sem2_sw()
        xx = [b.sb(sctx, f"{name}_x{i}", [128, Tn], BF16) for i in range(2)]
        wc = [b.sb(sctx, f"{name}_w{i}", [128, 5], F32) for i in range(2)]
        acc = [b.sb(sctx, f"{name}_a{i}", [128, Tn], F32) for i in range(2)]
        oo = [b.sb(sctx, f"{name}_o{i}", [128, Tn], BF16) for i in range(2)]
        v_at = {}
        s_at = {}
        st_at = {}
        for i in range(nch):
            bi = i % 2

            def _ld(sync, i=i, bi=bi):
                if i >= 2:
                    sync.wait_ge(s_s.h, s_at[i - 2])
                    sync.wait_ge(s_st2[bi].h, st_at[i - 2])
                sync.dma_start(out=xx[bi][:], in_=xbcT[i * 128:(i + 1) * 128, :]).then_inc(s_ld2[bi].h, 16)
                sync.dma_start(out=wc[bi][:, 0:4], in_=conv_w[i * 128:(i + 1) * 128, :]).then_inc(s_ld2[bi].h, 16)
                sync.dma_start(out=wc[bi][:, 4:5], in_=conv_b[i * 128:(i + 1) * 128, :]).then_inc(s_ld2[bi].h, 16)
            b.sec("sync", _ld)
            s_ld2[bi].inc(48)
            ld_i = s_ld2[bi].val

            def _v(v, i=i, bi=bi, ld_i=ld_i):
                st = chainer(v, b.csem["vector"])
                v.wait_ge(s_ld2[bi].h, ld_i)
                st(v.tensor_scalar(out=acc[bi][:], in0=xx[bi][:], scalar1=wc[bi][:, 3:4],
                                   scalar2=wc[bi][:, 4:5], op0=OP.mult, op1=OP.add))
                for k in range(3):
                    sh = 3 - k
                    st(v.scalar_tensor_tensor(out=acc[bi][:, sh:Tn], in0=xx[bi][:, 0:Tn - sh],
                                              scalar=wc[bi][:, k:k + 1], in1=acc[bi][:, sh:Tn],
                                              op0=OP.mult, op1=OP.add))
                v.sem_inc(s_v.h, 1)
            b.sec("vector", _v)
            s_v.inc(1)
            v_at[i] = s_v.val

            def _s(sc, i=i, bi=bi):
                sc.wait_ge(s_v.h, v_at[i])
                sc.activation(out=oo[bi][:], in_=acc[bi][:], func=AF.Silu).then_inc(s_s.h, 1)
            b.sec("scalar", _s)
            s_s.inc(1)
            s_at[i] = s_s.val

            def _st(g, i=i, bi=bi):
                g.wait_ge(s_s.h, s_at[i])
                g.dma_start(out=convT[i * 128:(i + 1) * 128, :], in_=oo[bi][:]).then_inc(s_st2[bi].h, 16)
            b.sec("gpsimd", _st)
            s_st2[bi].inc(16)
            st_at[i] = s_st2[bi].val

        def _fin(g):
            g.wait_ge(s_st2[0].h, s_st2[0].val)
            g.wait_ge(s_st2[1].h, s_st2[1].val)
        b.sec("gpsimd", _fin)


def st_mamba_dt(b, name, dtrawT, dta_tm, dt_bias, A_log, ident32, Tn=T):
    with b.stage(name) as sctx:
        s_ld, s_v, s_s, s_t = (b.sem() for _ in range(4))
        s_st2 = b.sem2_sw()
        raw = b.sb(sctx, f"{name}_r", [16, Tn], F32)
        u = b.sb(sctx, f"{name}_u", [16, Tn], F32)
        t1 = b.sb(sctx, f"{name}_t1", [16, Tn], F32)
        t2 = b.sb(sctx, f"{name}_t2", [16, Tn], F32)
        dta = b.sb(sctx, f"{name}_d", [128, Tn], F32)
        cols = b.sb(sctx, f"{name}_c", [16, 3], F32)
        id32 = b.sb(sctx, f"{name}_id", [128, 128], F32)
        ptr = [b.ps(sctx, f"{name}_p{i}", [128, 128], F32) for i in range(2)]
        otr = [b.sb(sctx, f"{name}_o{i}", [128, 48], F32) for i in range(2)]

        def _ld(sync):
            sync.dma_start(out=raw[:], in_=dtrawT[0:16, :]).then_inc(s_ld.h, 16)
            sync.dma_start(out=cols[:, 0:1], in_=dt_bias).then_inc(s_ld.h, 16)
            sync.dma_start(out=cols[:, 1:2], in_=A_log).then_inc(s_ld.h, 16)
            sync.dma_start(out=id32[:], in_=ident32).then_inc(s_ld.h, 16)
        b.sec("sync", _ld)
        s_ld.inc(64)
        ld0 = s_ld.val

        def _v1(v):
            v.wait_ge(s_ld.h, ld0)
            v.memset(dta[:], 0.0)
            v.tensor_scalar(out=u[:], in0=raw[:], scalar1=cols[:, 0:1],
                            scalar2=None, op0=OP.add).then_inc(s_v.h, 1)
        b.sec("vector", _v1)
        s_v.inc(1)
        u_at = s_v.val

        def _s1(sc):
            st = chainer(sc, b.csem["scalar"])
            sc.wait_ge(s_v.h, u_at)
            st(sc.activation(out=t1[:], in_=u[:], func=AF.Abs))
            st(sc.activation(out=t1[:], in_=t1[:], func=AF.Exp, scale=-1.0))
            st(sc.activation(out=t1[:], in_=t1[:], func=AF.Ln, bias=1.0))
            st(sc.activation(out=t2[:], in_=u[:], func=AF.Relu))
            st(sc.activation(out=cols[:, 2:3], in_=cols[:, 1:2], func=AF.Exp))
            sc.sem_inc(s_s.h, 1)
        b.sec("scalar", _s1)
        s_s.inc(1)
        sp_at = s_s.val

        def _v2(v):
            st = chainer(v, b.csem["vector"])
            v.wait_ge(s_s.h, sp_at)
            st(v.tensor_tensor(out=dta[0:16, :], in0=t1[:], in1=t2[:], op=OP.add))
            st(v.tensor_scalar(out=cols[:, 2:3], in0=cols[:, 2:3], scalar1=-1.0,
                               scalar2=None, op0=OP.mult))
            st(v.tensor_scalar(out=dta[32:48, :], in0=dta[0:16, :],
                               scalar1=cols[:, 2:3], scalar2=None, op0=OP.mult))
            v.sem_inc(s_v.h, 1)
        b.sec("vector", _v2)
        s_v.inc(1)
        dta_at = s_v.val

        tr_at = {}
        ev_at = {}
        st_at = {}
        for c in range(Tn // 128):
            pk = c % 2

            def _tr(tq, c=c, pk=pk):
                tq.wait_ge(s_v.h, dta_at)
                tq.wait_ge(s_ld.h, ld0)
                if c >= 2:
                    tq.wait_ge(s_v.h, ev_at[c - 2])
                tq.transpose(ptr[pk][:], dta[:, c * 128:(c + 1) * 128],
                             id32[:]).then_inc(s_t.h, 1)
            b.sec("tensor", _tr)
            s_t.inc(1)
            tr_at[c] = s_t.val

            def _ev(v, c=c, pk=pk):
                v.wait_ge(s_t.h, tr_at[c])
                if c >= 2:
                    v.wait_ge(s_st2[pk].h, st_at[c - 2])
                v.tensor_copy(out=otr[pk][:], in_=ptr[pk][:, 0:48]).then_inc(s_v.h, 1)
            b.sec("vector", _ev)
            s_v.inc(1)
            ev_at[c] = s_v.val

            def _st(g, c=c, pk=pk):
                g.wait_ge(s_v.h, ev_at[c])
                g.dma_start(out=dta_tm[c * 128:(c + 1) * 128, :],
                            in_=otr[pk][:]).then_inc(s_st2[pk].h, 16)
            b.sec("gpsimd", _st)
            s_st2[pk].inc(16)
            st_at[c] = s_st2[pk].val

        def _fin(g):
            g.wait_ge(s_st2[0].h, s_st2[0].val)
            g.wait_ge(s_st2[1].h, s_st2[1].val)
        b.sec("gpsimd", _fin)


def st_mamba_ssd(b, name, convT, dta_tm, yT, Drow, tri_incl, triU, trimask,
                 ident, Tn=T, NH=16, P=128, NS=128):
    nchk = Tn // 128
    with b.stage(name) as sctx:
        s_ld, s_t, s_s, s_v = (b.sem() for _ in range(4))
        s_ld2 = b.sem2()
        s_st2 = b.sem2_sw()
        tri_sb = b.sb(sctx, f"{name}_tri", [128, 384], F32)
        trim_sb = b.sb(sctx, f"{name}_trm", [128, 128], BF16)
        id_sb = b.sb(sctx, f"{name}_id", [128, 128], BF16)
        ones_sb = b.sb(sctx, f"{name}_on", [128, 128], F32)
        ones1 = b.sb(sctx, f"{name}_o1", [1, 128], F32)
        Db = b.sb(sctx, f"{name}_Db", [128, 16], F32)
        Drow_sb = b.sb(sctx, f"{name}_Dr", [1, 16], F32)
        St = b.sb(sctx, f"{name}_St", [128, NH * P], F32)
        Stbf = b.sb(sctx, f"{name}_Sb", [128, NH * P], BF16)
        dta = [b.sb(sctx, f"{name}_dt{i}", [128, 48], F32) for i in range(2)]
        BT = [b.sb(sctx, f"{name}_B{i}", [128, 128], BF16) for i in range(2)]
        CT = [b.sb(sctx, f"{name}_C{i}", [128, 128], BF16) for i in range(2)]
        xhT = [b.sb(sctx, f"{name}_xh{i}", [128, NH * P], BF16) for i in range(2)]
        Btm = [b.sb(sctx, f"{name}_Bt{i}", [128, 128], BF16) for i in range(2)]
        csc = [b.sb(sctx, f"{name}_cs{i}", [128, 64], F32) for i in range(2)]
        acb = [b.sb(sctx, f"{name}_ac{i}", [128, 128], F32) for i in range(2)]
        dcl = [b.sb(sctx, f"{name}_dc{i}", [128, 128], F32) for i in range(2)]
        expd = [b.sb(sctx, f"{name}_ed{i}", [128, 128], F32) for i in range(2)]
        expa = [b.sb(sctx, f"{name}_ea{i}", [128, 128], F32) for i in range(2)]
        cbm = [b.sb(sctx, f"{name}_cb{i}", [128, 128], F32) for i in range(2)]
        hsb = [b.sb(sctx, f"{name}_H{i}", [128, 128], BF16) for i in range(2)]
        cp = [b.sb(sctx, f"{name}_Cp{i}", [128, 128], BF16) for i in range(2)]
        xp = [b.sb(sctx, f"{name}_Xp{i}", [128, 128], BF16) for i in range(2)]
        bw = [b.sb(sctx, f"{name}_Bw{i}", [128, 128], BF16) for i in range(2)]
        yt = [b.sb(sctx, f"{name}_yt{i}", [128, 128], BF16) for i in range(2)]
        pA = b.ps(sctx, f"{name}_pA", [128, 128], F32)
        pD = b.ps(sctx, f"{name}_pD", [128, 128], F32)
        pY = b.ps(sctx, f"{name}_pY", [128, 128], F32)
        pS = b.ps(sctx, f"{name}_pS", [128, 128], F32)
        pTr = b.ps(sctx, f"{name}_pT", [128, 128], BF16)
        pBC = b.ps(sctx, f"{name}_pBC", [128, 32], F32)
        pDb = b.ps(sctx, f"{name}_pDb", [128, 16], F32)

        def _pro(sync):
            sync.dma_start(out=tri_sb[:, 0:128], in_=tri_incl).then_inc(s_ld.h, 16)
            sync.dma_start(out=tri_sb[:, 128:256], in_=triU).then_inc(s_ld.h, 16)
            sync.dma_start(out=trim_sb[:], in_=trimask).then_inc(s_ld.h, 16)
            sync.dma_start(out=Drow_sb[:], in_=Drow).then_inc(s_ld.h, 16)
            sync.dma_start(out=id_sb[:], in_=ident).then_inc(s_ld.h, 16)
        b.sec("sync", _pro)
        s_ld.inc(80)
        pro_ld = s_ld.val

        def _v0(v):
            v.memset(ones_sb[:], 1.0)
            v.memset(ones1[:], 1.0)
            v.memset(St[:], 0.0)
            v.memset(Stbf[:], 0.0)
            v.drain()
            v.sem_inc(s_v.h, 1)
        b.sec("vector", _v0)
        s_v.inc(1)
        init_v = s_v.val

        def _t0(tq):
            tq.wait_ge(s_ld.h, pro_ld)
            tq.wait_ge(s_v.h, init_v)
            tq.matmul(pDb[:], ones1[:], Drow_sb[:], start=True, stop=True).then_inc(s_t.h, 1)
        b.sec("tensor", _t0)
        s_t.inc(1)
        db_mm = s_t.val

        def _v00(v):
            v.wait_ge(s_t.h, db_mm)
            v.tensor_copy(out=Db[:], in_=pDb[:]).then_inc(s_v.h, 1)
        b.sec("vector", _v00)
        s_v.inc(1)
        db_at = s_v.val

        last = dict(V1={}, T1={}, V2={}, S1={}, V3={}, T2={}, V4={}, G1={},
                    T0={}, V0={})
        stbf_at = [db_at] * NH
        ytst = {}

        xh3 = convT.rearrange("(hc p) t -> p hc t", p=128)
        for c in range(nchk):
            cb = c % 2

            def _ldc(sync, c=c, cb=cb):
                if c >= 2:
                    sync.wait_ge(s_t.h, last["T2"][(c - 2, NH - 1)])
                    sync.wait_ge(s_v.h, last["V4"][(c - 2, NH - 1)])
                sync.dma_start(out=dta[cb][:],
                               in_=dta_tm[c * 128:(c + 1) * 128, :]).then_inc(s_ld2[cb].h, 16)
                sync.dma_start(out=BT[cb][:],
                               in_=convT[2048:2176, c * 128:(c + 1) * 128]).then_inc(s_ld2[cb].h, 16)
                sync.dma_start(out=CT[cb][:],
                               in_=convT[2176:2304, c * 128:(c + 1) * 128]).then_inc(s_ld2[cb].h, 16)
                sync.dma_start(out=xhT[cb][:],
                               in_=xh3[:, 0:16, c * 128:(c + 1) * 128]).then_inc(s_ld2[cb].h, 16)
            b.sec("sync", _ldc)
            s_ld2[cb].inc(64)
            ld_c = s_ld2[cb].val

            def _tc(tq, c=c, cb=cb, ld_c=ld_c):
                tq.wait_ge(s_ld2[cb].h, ld_c)
                tq.wait_ge(s_ld.h, pro_ld)
                if c >= 1:
                    tq.wait_ge(s_v.h, last["V0"][c - 1])
                    tq.wait_ge(s_v.h, last["V2"][(c - 1, NH - 1)])
                    tq.wait_ge(s_v.h, last["V3"][(c - 1, NH - 1)])
                tq.matmul(pBC[:, 0:16], tri_sb[:, 0:128], dta[cb][:, 32:48],
                          start=True, stop=True)
                tq.matmul(pBC[:, 16:32], tri_sb[:, 128:256], dta[cb][:, 32:48],
                          start=True, stop=True).then_inc(s_t.h, 1)
                tq.matmul(pD[:], BT[cb][:], CT[cb][:],
                          start=True, stop=True)
                tq.transpose(pTr[:], BT[cb][:], id_sb[:]).then_inc(s_t.h, 1)
            b.sec("tensor", _tc)
            s_t.inc(2)
            last["T0"][c] = s_t.val

            def _vc(v, c=c, cb=cb):
                st = chainer(v, b.csem["vector"])
                v.wait_ge(s_t.h, last["T0"][c])
                st(v.tensor_copy(out=csc[cb][:, 0:32], in_=pBC[:, 0:32]))
                st(v.tensor_tensor(out=csc[cb][:, 32:48], in0=csc[cb][:, 16:32],
                                   in1=dta[cb][:, 32:48], op=OP.add))
                st(v.tensor_copy(out=Btm[cb][:], in_=pTr[:]))
                st(v.tensor_tensor(out=cbm[cb][:], in0=pD[:], in1=trim_sb[:],
                                   op=OP.mult))
                v.sem_inc(s_v.h, 1)
            b.sec("vector", _vc)
            s_v.inc(1)
            last["V0"][c] = s_v.val

            def _sc(sc, c=c, cb=cb):
                st = chainer(sc, b.csem["scalar"])
                sc.wait_ge(s_v.h, last["V0"][c])
                st(sc.activation(out=csc[cb][:, 48:64], in_=csc[cb][:, 16:32],
                                 func=AF.Exp))
                st(sc.activation(out=csc[cb][:, 32:48], in_=csc[cb][:, 32:48],
                                 func=AF.Exp))
                sc.sem_inc(s_s.h, 1)
            b.sec("scalar", _sc)
            s_s.inc(1)
            sc_c = s_s.val

            for h in range(NH):
                hp = h % 2

                def _v1(v, c=c, cb=cb, h=h, hp=hp):
                    v.wait_ge(s_ld2[cb].h, ld_c)
                    if (c, h) != (0, 0) and (c * NH + h) >= 2:
                        ch2 = divmod(c * NH + h - 2, NH)
                        v.wait_ge(s_t.h, last["T1"][(ch2[0], ch2[1])])
                    v.tensor_scalar(out=acb[hp][:], in0=ones_sb[:],
                                    scalar1=dta[cb][:, 32 + h:33 + h], scalar2=None,
                                    op0=OP.mult).then_inc(s_v.h, 1)
                b.emit("vector", _v1)
                s_v.inc(1)
                last["V1"][(c, h)] = s_v.val

                def _t1(tq, c=c, cb=cb, h=h, hp=hp):
                    k = c * NH + h
                    tq.wait_ge(s_v.h, last["V1"][(c, h)])
                    tq.wait_ge(s_v.h, last["V0"][c])
                    if k >= 1:
                        kp = divmod(k - 1, NH)
                        tq.wait_ge(s_v.h, last["V2"][kp])
                        tq.wait_ge(s_s.h, last["S1"][kp])
                    tq.matmul(pA[:], acb[hp][:], tri_sb[:, 0:128],
                              start=True, stop=True)
                    tq.transpose(pTr[:], xhT[cb][:, h * 128:(h + 1) * 128],
                                 id_sb[:]).then_inc(s_t.h, 1)
                b.emit("tensor", _t1)
                s_t.inc(1)
                last["T1"][(c, h)] = s_t.val

                def _v2(v, c=c, cb=cb, h=h, hp=hp):
                    st = chainer(v, b.csem["vector"])
                    v.wait_ge(s_t.h, last["T1"][(c, h)])
                    v.wait_ge(s_v.h, last["V0"][c])
                    st(v.tensor_scalar(out=xp[hp][:], in0=pTr[:],
                                       scalar1=dta[cb][:, h:h + 1], scalar2=None,
                                       op0=OP.mult))
                    st(v.tensor_scalar(out=dcl[hp][:], in0=pA[:],
                                       scalar1=csc[cb][:, h:h + 1], scalar2=0.0,
                                       op0=OP.subtract, op1=OP.min))
                    v.sem_inc(s_v.h, 1)
                b.emit("vector", _v2)
                s_v.inc(1)
                last["V2"][(c, h)] = s_v.val

                def _s1(sc, c=c, h=h, hp=hp):
                    st = chainer(sc, b.csem["scalar"])
                    sc.wait_ge(s_v.h, last["V2"][(c, h)])
                    st(sc.activation(out=expd[hp][:], in_=dcl[hp][:], func=AF.Exp))
                    st(sc.activation(out=expa[hp][:], in_=pA[:], func=AF.Exp))
                    sc.sem_inc(s_s.h, 1)
                b.emit("scalar", _s1)
                s_s.inc(1)
                last["S1"][(c, h)] = s_s.val

                def _v3(v, c=c, cb=cb, h=h, hp=hp):
                    v.wait_ge(s_s.h, last["S1"][(c, h)])
                    v.wait_ge(s_s.h, sc_c)
                    v.tensor_tensor(out=hsb[hp][:], in0=cbm[cb][:],
                                    in1=expd[hp][:], op=OP.mult)
                    v.tensor_tensor(out=cp[hp][:], in0=CT[cb][:],
                                    in1=expa[hp][:], op=OP.mult)
                    v.tensor_scalar(out=bw[hp][:], in0=Btm[cb][:],
                                    scalar1=csc[cb][:, 48 + h:49 + h],
                                    scalar2=None, op0=OP.mult).then_inc(s_v.h, 1)
                b.emit("vector", _v3)
                s_v.inc(1)
                last["V3"][(c, h)] = s_v.val

                def _t2(tq, c=c, h=h, hp=hp, stw=stbf_at[h]):
                    k = c * NH + h
                    tq.wait_ge(s_v.h, last["V3"][(c, h)])
                    tq.wait_ge(s_v.h, stw)
                    if k >= 1:
                        kp = divmod(k - 1, NH)
                        tq.wait_ge(s_v.h, last["V4"][kp])
                    tq.matmul(pY[:], Stbf[:, h * 128:(h + 1) * 128], cp[hp][:],
                              start=True, stop=False)
                    tq.matmul(pY[:], xp[hp][:], hsb[hp][:],
                              start=False, stop=True)
                    tq.matmul(pS[:], bw[hp][:], xp[hp][:],
                              start=True, stop=True).then_inc(s_t.h, 1)
                b.emit("tensor", _t2)
                s_t.inc(1)
                last["T2"][(c, h)] = s_t.val

                def _v4(v, c=c, cb=cb, h=h, hp=hp):
                    st = chainer(v, b.csem["vector"])
                    v.wait_ge(s_t.h, last["T2"][(c, h)])
                    v.wait_ge(s_v.h, db_at)
                    k = c * NH + h
                    if k >= 2:
                        v.wait_ge(s_st2[hp].h, ytst.get(k - 2, 0))
                    v.scalar_tensor_tensor(out=yt[hp][:], in0=xhT[cb][:, h * 128:(h + 1) * 128],
                                           scalar=Db[:, h:h + 1], in1=pY[:],
                                           op0=OP.mult, op1=OP.add)
                    st(v.scalar_tensor_tensor(out=St[:, h * 128:(h + 1) * 128],
                                              in0=St[:, h * 128:(h + 1) * 128],
                                              scalar=csc[cb][:, 32 + h:33 + h],
                                              in1=pS[:], op0=OP.mult, op1=OP.add))
                    v.tensor_copy(out=Stbf[:, h * 128:(h + 1) * 128],
                                  in_=St[:, h * 128:(h + 1) * 128]).then_inc(s_v.h, 1)
                b.emit("vector", _v4)
                s_v.inc(1)
                last["V4"][(c, h)] = s_v.val
                stbf_at[h] = s_v.val

                def _g1(g, c=c, h=h, hp=hp):
                    g.wait_ge(s_v.h, last["V4"][(c, h)])
                    g.dma_start(out=yT[h * 128:(h + 1) * 128, c * 128:(c + 1) * 128],
                                in_=yt[hp][:]).then_inc(s_st2[hp].h, 16)
                b.emit("gpsimd", _g1)
                s_st2[hp].inc(16)
                ytst[c * NH + h] = s_st2[hp].val
                last["G1"][(c, h)] = s_st2[hp].val
                b.flush()

        def _fin(g):
            g.wait_ge(s_st2[0].h, s_st2[0].val)
            g.wait_ge(s_st2[1].h, s_st2[1].val)
        b.sec("gpsimd", _fin)


def st_mamba_gate(b, name, yT, zxT, y2T, rmsw_cm, Tn=T, R=2048):
    nch = R // 128
    with b.stage(name) as sctx:
        s_pro, s_t, s_s, s_v = (b.sem() for _ in range(4))
        s_ld2 = b.sem2()
        s_st2 = b.sem2_sw()
        rw = b.sb(sctx, f"{name}_rw", [128, nch], F32)
        onec = b.sb(sctx, f"{name}_oc", [128, 1], BF16)
        one1 = b.sb(sctx, f"{name}_o1", [1, 128], BF16)
        yy = [[b.sb(sctx, f"{name}_y{i}c{c}", [128, TT], BF16) for c in range(nch)]
              for i in range(2)]
        zz = [[b.sb(sctx, f"{name}_z{i}c{c}", [128, TT], BF16) for c in range(nch)]
              for i in range(2)]
        gg = [[b.sb(sctx, f"{name}_g{i}c{c}", [128, TT], F32) for c in range(nch)]
              for i in range(2)]
        g2 = b.sb(sctx, f"{name}_g2", [128, TT], BF16)
        sq = [b.sb(sctx, f"{name}_sq{i}", [1, TT], F32) for i in range(2)]
        rr = [b.sb(sctx, f"{name}_rr{i}", [1, TT], BF16) for i in range(2)]
        oo = [b.sb(sctx, f"{name}_oo{i}", [128, TT], BF16) for i in range(2)]
        pq = b.ps(sctx, f"{name}_pq", [1, TT], F32)
        pb = b.ps(sctx, f"{name}_pb", [128, TT], F32)

        def _pro(sync):
            sync.dma_start(out=rw[:], in_=rmsw_cm).then_inc(s_pro.h, 16)
        b.sec("sync", _pro)
        s_pro.inc(16)
        pro_ld = s_pro.val

        def _v0(v):
            v.memset(onec[:], 1.0)
            v.memset(one1[:], 1.0)
            v.drain()
            v.sem_inc(s_v.h, 1)
        b.sec("vector", _v0)
        s_v.inc(1)
        ones_at = s_v.val

        v_done = {}
        st_done = {}
        for t5 in range(Tn // TT):
            tb = t5 % 2

            def _ld(sync, t5=t5, tb=tb):
                if t5 >= 2:
                    sync.wait_ge(s_v.h, v_done[t5 - 2])
                    sync.wait_ge(s_st2[tb].h, st_done[t5 - 2])
                for c in range(nch):
                    sync.dma_start(out=yy[tb][c][:],
                                   in_=yT[c * 128:(c + 1) * 128,
                                          t5 * TT:(t5 + 1) * TT]).then_inc(s_ld2[tb].h, 16)
                    sync.dma_start(out=zz[tb][c][:],
                                   in_=zxT[c * 128:(c + 1) * 128,
                                           t5 * TT:(t5 + 1) * TT]).then_inc(s_ld2[tb].h, 16)
            b.sec("sync", _ld)
            s_ld2[tb].inc(16 * 2 * nch)
            ld_t = s_ld2[tb].val

            g2_at = {}
            for c in range(nch):
                def _sc(sc, t5=t5, tb=tb, c=c, ld_t=ld_t):
                    sc.wait_ge(s_ld2[tb].h, ld_t)
                    sc.activation(out=gg[tb][c][:], in_=zz[tb][c][:],
                                  func=AF.Silu).then_inc(s_s.h, 1)
                b.sec("scalar", _sc)
                s_s.inc(1)
                sil = s_s.val

                def _vg(v, t5=t5, tb=tb, c=c, sil=sil):
                    st = chainer(v, b.csem["vector"])
                    v.wait_ge(s_s.h, sil)
                    if c > 0 or t5 > 0:
                        v.wait_ge(s_t.h, mm_at)
                    st(v.tensor_tensor(out=gg[tb][c][:], in0=yy[tb][c][:],
                                       in1=gg[tb][c][:], op=OP.mult))
                    st(v.tensor_tensor(out=g2[:], in0=gg[tb][c][:],
                                       in1=gg[tb][c][:], op=OP.mult))
                    v.sem_inc(s_v.h, 1)
                b.sec("vector", _vg)
                s_v.inc(1)
                gv = s_v.val

                def _tm(tq, t5=t5, c=c, gv=gv):
                    tq.wait_ge(s_v.h, gv)
                    tq.wait_ge(s_v.h, ones_at)
                    if t5 >= 1 and c == 0:
                        tq.wait_ge(s_v.h, rr_ev)
                    tq.matmul(pq[:], onec[:], g2[:], start=(c == 0),
                              stop=(c == nch - 1)).then_inc(s_t.h, 1)
                b.sec("tensor", _tm)
                s_t.inc(1)
                mm_at = s_t.val

            def _vq(v, t5=t5, tb=tb):
                v.wait_ge(s_t.h, mm_at)
                v.tensor_scalar(out=sq[tb][:], in0=pq[:], scalar1=1.0 / R,
                                scalar2=None, op0=OP.mult).then_inc(s_v.h, 1)
            b.sec("vector", _vq)
            s_v.inc(1)
            sq_at = s_v.val

            def _sq(sc, t5=t5, tb=tb, sq_at=sq_at):
                sc.wait_ge(s_v.h, sq_at)
                sc.activation(out=sq[tb][:], in_=sq[tb][:], func=AF.Sqrt,
                              bias=1e-5).then_inc(s_s.h, 1)
            b.sec("scalar", _sq)
            s_s.inc(1)
            rt_at = s_s.val

            def _vr(v, t5=t5, tb=tb, rt_at=rt_at):
                st = chainer(v, b.csem["vector"])
                v.wait_ge(s_s.h, rt_at)
                st(v.reciprocal(out=sq[tb][:], in_=sq[tb][:]))
                st(v.tensor_copy(out=rr[tb][:], in_=sq[tb][:]))
                v.sem_inc(s_v.h, 1)
            b.sec("vector", _vr)
            s_v.inc(1)
            rr_at = s_v.val

            def _tb(tq, t5=t5, tb=tb, rr_at=rr_at):
                tq.wait_ge(s_v.h, rr_at)
                if t5 >= 1:
                    tq.wait_ge(s_v.h, bc_used)
                tq.matmul(pb[:], one1[:], rr[tb][:], start=True,
                          stop=True).then_inc(s_t.h, 1)
            b.sec("tensor", _tb)
            s_t.inc(1)
            bc_at = s_t.val
            rr_ev = s_v.val

            ev_list = []
            for c in range(nch):
                pk = c % 2

                def _vo(v, t5=t5, tb=tb, c=c, pk=pk, bc_at=bc_at):
                    v.wait_ge(s_t.h, bc_at)
                    v.wait_ge(s_pro.h, pro_ld)
                    if c >= 2:
                        v.wait_ge(s_st2[pk].h, ev_list[c - 2][1])
                    v.scalar_tensor_tensor(out=oo[pk][:], in0=gg[tb][c][:],
                                           scalar=rw[:, c:c + 1], in1=pb[:],
                                           op0=OP.mult, op1=OP.mult).then_inc(s_v.h, 1)
                b.sec("vector", _vo)
                s_v.inc(1)
                vo_at = s_v.val

                def _st(g, t5=t5, c=c, pk=pk, vo_at=vo_at):
                    g.wait_ge(s_v.h, vo_at)
                    g.dma_start(out=y2T[c * 128:(c + 1) * 128,
                                        t5 * TT:(t5 + 1) * TT],
                                in_=oo[pk][:]).then_inc(s_st2[pk].h, 16)
                b.sec("gpsimd", _st)
                s_st2[pk].inc(16)
                ev_list.append((vo_at, s_st2[pk].val))
            bc_used = s_v.val
            v_done[t5] = s_v.val
            st_done[t5] = s_st2[tb].val

        def _fin(g):
            g.wait_ge(s_st2[0].h, s_st2[0].val)
            g.wait_ge(s_st2[1].h, s_st2[1].val)
        b.sec("gpsimd", _fin)


def emit_L0A(b, W, xr_in, xr_out, uid):
    z0 = b.dtmp(f"l0a_z0_{uid}", [T, D], F32)
    m0T = b.dtmp(f"l0a_m0T_{uid}", [D, T], BF16)
    zxT = b.dtmp(f"l0a_zxT_{uid}", [2048, T], BF16)
    xbcT = b.dtmp(f"l0a_xbcT_{uid}", [2304, T], BF16)
    dtrawT = b.dtmp(f"l0a_dtr_{uid}", [128, T], F32)
    convT = b.dtmp(f"l0a_conv_{uid}", [2304, T], BF16)
    dta_tm = b.dtmp(f"l0a_dta_{uid}", [T, 48], F32)
    yT = b.dtmp(f"l0a_yT_{uid}", [2048, T], BF16)
    y2T = b.dtmp(f"l0a_y2T_{uid}", [2048, T], BF16)
    st_ln(b, xr_in, z=z0, ident=W["ident_bf"])
    st_ln(b, z0, zT=m0T, gamma=W["l0_ln_w"], beta=W["l0_ln_b"], ident=W["ident_bf"])
    st_mm_fm(b, f"l0a_in_{uid}", [dict(x=m0T, w=W["l0_winp"], K=D, wlayout="panels")],
             4352, [(zxT, 0, 2048), (xbcT, 2048, 4352)])
    st_mm_fm(b, f"l0a_dt_{uid}", [dict(x=m0T, w=W["l0_windt"], K=D)],
             128, [(dtrawT, 0, 128)], out_dt=F32)
    st_mamba_conv(b, f"l0a_cv_{uid}", xbcT, convT, W["l0_conv_w"], W["l0_conv_b"])
    st_mamba_dt(b, f"l0a_sp_{uid}", dtrawT, dta_tm, W["l0_dt_bias"],
                W["l0_A_log"], W["ident_f32"])
    st_mamba_ssd(b, f"l0a_ssd_{uid}", convT, dta_tm, yT, W["l0_D"],
                 W["tri_incl"], W["triU"], W["trimask"], W["ident_bf"])
    st_mamba_gate(b, f"l0a_gt_{uid}", yT, zxT, y2T, W["l0_rmsw"])

    def ev_v(E):
        E.step(E.q.tensor_tensor(out=E.tmp, in0=E.psum, in1=E.aux[0], op=OP.add))
        return E.q.tensor_tensor(out=E.out, in0=E.tmp, in1=E.aux[1], op=OP.add)
    st_mm_tm(b, f"l0a_out_{uid}", [dict(x=y2T, w=W["l0_wout"], K=2048)], D,
             xr_out, evac_v=ev_v, aux=[(xr_in, F32), (z0, F32)], out_dt=F32)


class WReg:

    def __init__(self, b, host):
        self.b = b
        self.host = host

    def __getitem__(self, name):
        arr = self.host[name]
        dt = BF16 if arr.dtype == BF16NP else F32
        return self.b.din(name, list(arr.shape), dt)


def host_weights(inp):
    H = dict(host_constants())
    H["fff_cs"] = colmat(H["fff_cs"].ravel())
    H["triU"] = H.pop("triU_strict")
    H["trimask"] = H.pop("trimask_bf")
    f32 = np.float32
    asf = np.ascontiguousarray
    H["l0_ln_w"] = inp["L0_attn_ln_w"].reshape(1, -1)
    H["l0_ln_b"] = inp["L0_attn_ln_b"].reshape(1, -1)
    Win = np.asarray(inp["L0_attn_W_in"], f32)
    H["l0_winp"] = panels(asf(Win[:, :4352]))
    wdt = np.zeros((D, 128), f32)
    wdt[:, :16] = Win[:, 4352:]
    H["l0_windt"] = wdt
    H["l0_conv_w"] = np.asarray(inp["L0_attn_conv_w"], f32)
    H["l0_conv_b"] = inp["L0_attn_conv_b"].reshape(-1, 1)
    H["l0_dt_bias"] = inp["L0_attn_dt_bias"].reshape(-1, 1)
    H["l0_A_log"] = inp["L0_attn_A_log"].reshape(-1, 1)
    H["l0_D"] = inp["L0_attn_D"].reshape(1, -1)
    H["l0_rmsw"] = colmat(np.asarray(inp["L0_attn_rms_w"], f32))
    H["l0_wout"] = np.asarray(inp["L0_attn_W_out"], f32)
    for n in "qkvf":
        p = f"l0_{n}_"
        nw = np.asarray(inp[f"L0_ffn_{n}_nw"], f32)
        nb = np.asarray(inp[f"L0_ffn_{n}_nb"], f32)
        w1 = np.asarray(inp[f"L0_ffn_{n}_w1"], f32)
        b1 = np.asarray(inp[f"L0_ffn_{n}_b1"], f32)
        w2 = np.asarray(inp[f"L0_ffn_{n}_w2"], f32)
        b2 = np.asarray(inp[f"L0_ffn_{n}_b2"], f32)
        nwT = np.zeros((D, 256), f32)
        nwT[:, :255] = nw.T
        nbp = np.zeros(256, f32)
        nbp[:255] = nb
        H[p + "nwT"] = nwT
        H[p + "nbc"] = colmat(nbp)
        H[p + "w1p"] = panels(asf(w1.transpose(1, 0, 2).reshape(D, 4096)))
        H[p + "b1c"] = colmat(b1.reshape(-1))
        W2s = asf(w2.reshape(4096, D))
        H[p + "w2p"] = panels(W2s)
        H[p + "w2r"] = W2s
        H[p + "b2s"] = b2
    H["l1_wq"] = np.asarray(inp["L1_attn_Wq"], f32)
    H["l1_wk"] = np.asarray(inp["L1_attn_Wk"], f32)
    H["l1_wv"] = np.asarray(inp["L1_attn_Wv"], f32)
    H["l1_bqc"] = colmat(np.asarray(inp["L1_attn_bq"], f32))
    H["l1_bkc"] = colmat(np.asarray(inp["L1_attn_bk"], f32))
    H["l1_bv"] = inp["L1_attn_bv"].reshape(1, -1)
    H["l1_gamma"] = inp["L1_attn_gamma"].reshape(1, -1)
    H["l1_beta"] = inp["L1_attn_beta"].reshape(1, -1)
    H["l1_w1T"] = panels(asf(np.asarray(inp["L1_ffn_w1_w"], f32).T))
    H["l1_w2T"] = panels(asf(np.asarray(inp["L1_ffn_w2_w"], f32).T))
    H["l1_h1T"] = panels(asf(np.asarray(inp["L1_ffn_h1_w"], f32).T))
    H["l1_h2T"] = panels(asf(np.asarray(inp["L1_ffn_h2_w"], f32).T))
    H["l1_w3T"] = panels(asf(np.asarray(inp["L1_ffn_w3_w"], f32).T))
    H["l1_w1s"] = colmat(np.asarray(inp["L1_ffn_w1_s"], f32))
    H["l1_w2s"] = colmat(np.asarray(inp["L1_ffn_w2_s"], f32))
    H["l1_h1s"] = colmat(np.asarray(inp["L1_ffn_h1_s"], f32))
    H["l1_h2s"] = colmat(np.asarray(inp["L1_ffn_h2_s"], f32))
    H["l1_w3s"] = colmat(np.asarray(inp["L1_ffn_w3_s"], f32))
    H["l2_wq"] = np.asarray(inp["L2_attn_Wq"], f32)
    H["l2_wk"] = np.asarray(inp["L2_attn_Wk"], f32)
    H["l2_wv"] = np.asarray(inp["L2_attn_Wv"], f32)
    H["l2_wo"] = np.asarray(inp["L2_attn_Wo"], f32)
    H["l2_w"] = np.asarray(inp["L2_ffn_weight"], f32)
    H["l2_a1"] = np.asarray(inp["L2_ffn_aux_weight1"], f32)
    H["l2_a2"] = np.asarray(inp["L2_ffn_aux_weight2"], f32)
    H["l2_bias"] = inp["L2_ffn_bias"].reshape(1, -1)
    wqkv = np.asarray(inp["L3_attn_w_qkv"], f32)
    H["l3_wqk"] = asf(wqkv[:2048].T)
    H["l3_wv"] = asf(wqkv[2048:].T)
    H["l3_wout"] = asf(np.asarray(inp["L3_attn_w_out"], f32).T)
    H["l3_sw1"] = np.asarray(inp["L3_ffn_sw1"], f32)
    H["l3_sw3"] = np.asarray(inp["L3_ffn_sw3"], f32)
    H["l3_sw2"] = np.asarray(inp["L3_ffn_sw2"], f32)
    H["l3_sb1c"] = colmat(np.asarray(inp["L3_ffn_sb1"], f32))
    H["l3_sb3c"] = colmat(np.asarray(inp["L3_ffn_sb3"], f32))
    H["l3_sb2"] = inp["L3_ffn_sb2"].reshape(1, -1)
    H["l3_w1"] = np.asarray(inp["L3_ffn_w1"], f32)
    H["l3_b1c"] = np.stack([colmat(np.asarray(inp["L3_ffn_b1"][e], f32))
                            for e in range(8)])
    H["l3_w2"] = np.asarray(inp["L3_ffn_w2"], f32)
    H["l3_b2"] = np.asarray(inp["L3_ffn_b2"], f32)
    H = {k: (np.ascontiguousarray(v) if isinstance(v, np.ndarray) else v)
         for k, v in H.items()}
    return H


SUBLAYERS = [
    ("L0F", "s0"), ("L0A", "s1"), ("L0F", "s2"),
    ("L1F", "s3"), ("L1A", "s4"), ("L1F", "s5"),
    ("L2F", "s6"), ("L2A", "s7"), ("L2F", "s8"),
    ("L3F", "s9"), ("L3A", "s10"), ("L3F", "s11"),
]
EMITTERS = dict(L0F=emit_L0F, L0A=emit_L0A, L1F=emit_L1F, L1A=emit_L1A,
                L2F=emit_L2F, L2A=emit_L2A, L3F=emit_L3F, L3A=emit_L3A)

MODE = "fused"


def kernel(**inputs):
    H = host_weights(inputs)
    x = np.ascontiguousarray(np.asarray(inputs["L0_attn_x"], np.float32))
    shards = [np.ascontiguousarray(x[c]) for c in range(8)]

    if MODE == "fused":
        b = Build("mk")
        xr_in0 = b.din("x_in", [T, D], F32)
        xr_out_f = b.dout("x_out", [T, D], F32)
        xa = b.dtmp("xr_a", [T, D], F32)
        xb_ = b.dtmp("xr_b", [T, D], F32)
        wreg = WReg(b, H)
        b.open(consts=(-4.0,))
        cur = xr_in0
        bufs = [xa, xb_]
        for i, (kind, uid) in enumerate(SUBLAYERS):
            nxt = xr_out_f if i == len(SUBLAYERS) - 1 else bufs[i % 2]
            EMITTERS[kind](b, wreg, cur, nxt, uid)
            cur = nxt
        nc = b.close()
        in_maps = []
        for c in range(8):
            im = {n: H[n] for n in b.input_names if n != "x_in"}
            im["x_in"] = shards[c]
            in_maps.append(im)
        res = run_bass_kernel_spmd(nc, in_maps, core_ids=list(range(8)))
        return np.stack([np.asarray(r["x_out"], np.float32) for r in res.results])

    cur_shards = shards
    for i, (kind, uid) in enumerate(SUBLAYERS):
        b = Build(f"mk_{uid}")
        xr_in = b.din("x_in", [T, D], F32)
        xr_out = b.dout("x_out", [T, D], F32)
        wreg = WReg(b, H)
        b.open(consts=(-4.0,))
        EMITTERS[kind](b, wreg, xr_in, xr_out, uid)
        nc = b.close()
        in_maps = []
        for c in range(8):
            im = {n: H[n] for n in b.input_names if n != "x_in"}
            im["x_in"] = cur_shards[c]
            in_maps.append(im)
        res = run_bass_kernel_spmd(nc, in_maps, core_ids=list(range(8)))
        cur_shards = [np.ascontiguousarray(np.asarray(r["x_out"], np.float32))
                      for r in res.results]
    return np.stack(cur_shards)
